# version 05
# baseline (speedup 1.0000x reference)
import numpy as np
import ml_dtypes
import concourse.bass as bass
import concourse.mybir as mybir
from concourse.bass_utils import run_bass_kernel_spmd

F32 = mybir.dt.float32
BF16 = mybir.dt.bfloat16
AF = mybir.ActivationFunctionType
ALU = mybir.AluOpType
AX = mybir.AxisListType

S, D, NT, HD = 2048, 1024, 16, 64
HEADS_O, HEADS_I, INNER = 16, 8, 512
HID, NHT, HID_LAST = 2730, 22, 42
EPS = 1e-5

PHASE_LIMIT = "all"
NT_LIM = NT
DEBUG_TAP = None

ENGINES = ("pe", "act", "dve", "pool", "sp")
QSEM = {"sp": "q_sp", "pool": "q_pool", "act": "q_act"}


class Buf:
    __slots__ = ("name", "w", "readers")

    def __init__(self, name):
        self.name = name
        self.w = None
        self.readers = []


class Emitter:
    def __init__(self, nc):
        self.nc = nc
        self.prog = {e: [] for e in ENGINES}
        self.sems = {}
        self.counts = {n: 0 for n in ("pe", "act", "dve", "pool", "sp")}
        self.lastwait = {}
        self.bufs = {}

    def make_sems(self, ctx):
        for nm in list(self.counts):
            self.sems[nm] = self.nc.alloc_semaphore("sem_" + nm)

    def dma_sem(self, bufname):
        nm = "d_" + bufname
        if nm not in self.sems:
            self.sems[nm] = self.nc.alloc_semaphore("sem_" + nm)
            self.counts[nm] = 0
        return nm

    def buf(self, name):
        if name not in self.bufs:
            self.bufs[name] = Buf(name)
        return self.bufs[name]

    PSUM_BUFS = frozenset(("pP0", "pP1", "pP2", "pP3", "simP", "simX", "tbf", "tr32"))

    def op(self, eng, fn, r=(), w=(), dma=False):
        reads = [self.buf(n) for n in r if n not in self.PSUM_BUFS]
        writes = [self.buf(n) for n in w] + [self.buf(n) for n in r
                                             if n in self.PSUM_BUFS]
        deps = []
        for b in reads:
            if b.w is not None:
                deps.append(b.w)
        for b in writes:
            if b.w is not None:
                deps.append(b.w)
            deps.extend(b.readers)
        mysem = self.dma_sem(w[0]) if dma else eng
        need = {}
        for (sem, val) in deps:
            if dma and sem == mysem:
                continue
            if sem == eng == "pe":
                continue
            if need.get(sem, 0) < val:
                need[sem] = val
        for sem, val in sorted(need.items()):
            if self.lastwait.get((eng, sem), 0) >= val:
                continue
            self.lastwait[(eng, sem)] = val
            self.prog[eng].append(
                lambda e, _s=sem, _v=val: e.wait_ge(self.sems[_s], _v))
        amt = 16 if dma else 1
        self.counts[mysem] += amt
        val = self.counts[mysem]
        self.prog[eng].append(
            lambda e, _f=fn, _s=mysem, _a=amt: _f(e).then_inc(self.sems[_s], _a))
        tok = (mysem, val)
        for b in reads:
            b.readers.append(tok)
        for b in writes:
            b.w = tok
            b.readers = []
        return tok

    def fence(self, old_names, new_names):
        toks = {}
        for n in old_names:
            b = self.bufs.get(n)
            if b is None:
                continue
            for t in ([b.w] if b.w else []) + b.readers:
                if toks.get(t[0], 0) < t[1]:
                    toks[t[0]] = t[1]
        tl = list(toks.items())
        for n in new_names:
            b = self.buf(n)
            b.readers = b.readers + tl
        for n in old_names:
            if n in self.bufs:
                self.bufs[n].w = None
                self.bufs[n].readers = []

    def final_wait(self, eng, tok):
        self.prog[eng].append(lambda e, _t=tok: e.wait_ge(self.sems[_t[0]], _t[1]))

    def replay(self, block):
        for eng, attr in (("sp", "sync"), ("pe", "tensor"), ("act", "scalar"),
                          ("dve", "vector"), ("pool", "gpsimd")):
            seq = self.prog[eng]

            def make(seq_):
                def run(e):
                    for f in seq_:
                        f(e)
                return run
            getattr(block, attr)(make(seq))


def _host_tables():
    half = np.arange(0, HD, 2, dtype=np.float64)
    inv_freq = 1.0 / 10000.0 ** (half / HD)
    t = np.arange(S, dtype=np.float64)
    freqs = t[:, None] * inv_freq[None, :]
    freqs = np.concatenate([freqs, freqs], axis=1)
    base = (half + 0.4 * HD) / (1.4 * HD)
    power = (t - S // 2) / 256.0
    scale = base[None, :] ** power[:, None]
    scale = np.concatenate([scale, scale], axis=1)
    cq = (np.cos(freqs) * scale).astype(np.float32)
    sq = (np.sin(freqs) * scale).astype(np.float32)
    ck = (np.cos(freqs) / scale).astype(np.float32)
    sk = (np.sin(freqs) / scale).astype(np.float32)
    packed = np.concatenate([cq, sq, ck, sk], axis=1)
    return np.ascontiguousarray(
        packed.reshape(NT, 128, 4 * HD).transpose(1, 0, 2).reshape(128, NT * 4 * HD))


def _masks():
    i = np.arange(128)[:, None]
    j = np.arange(128)[None, :]
    neg = np.float32(-1.0e30)
    bf = ml_dtypes.bfloat16
    low_i = np.where(i >= j, 0.0, neg).astype(np.float32).astype(bf)
    up_i = np.where(j >= i, 0.0, neg).astype(np.float32).astype(bf)
    up_s = np.where(j > i, 0.0, neg).astype(np.float32).astype(bf)
    return low_i, up_i, up_s


def build():
    nc = bass.Bass("TRN2", target_bir_lowering=False)
    dram = {}

    def din(name, shape, dt=F32):
        dram[name] = nc.dram_tensor(name, list(shape), dt, kind="ExternalInput").ap()

    din("x", (S, D))
    for w in ("wqT", "wkT", "wvT", "woT"):
        din(w, (D, D))
    din("brow4", (1, 4 * D))
    din("qn2", (128, 1))
    din("kn2", (128, 1))
    din("rotpk", (128, NT * 4 * HD))
    din("m_low", (128, 128), BF16)
    din("m_upi", (128, 128), BF16)
    din("m_ups", (128, 128), BF16)
    din("id32", (128, 128), F32)
    din("id16", (128, 128), BF16)
    for l in range(2):
        din(f"wqkv{l}", (D, 3 * INNER))
        din(f"wo{l}", (INNER, D))
        din(f"w1p{l}", (D, 2 * HID))
        din(f"b1a{l}", (128, NHT))
        din(f"b1g{l}", (128, NHT))
        din(f"w2_{l}", (HID, D))
        din(f"b2_{l}", (1, D))
        din(f"lng{l}", (128, 8))
        din(f"lnb{l}", (128, 8))
        din(f"qsrows{l}", (1, 4 * HD))
    d_out = nc.dram_tensor("out", [S, D], F32, kind="ExternalOutput").ap()
    d_x1 = nc.dram_tensor("x1_scratch", [128, NT * D], F32, kind="Internal").ap()
    d_w1img = nc.dram_tensor("w1img", [2, 2 * NHT, 128, 1024], BF16, kind="Internal").ap()

    from contextlib import ExitStack
    with ExitStack() as ctx:
        em = Emitter(nc)
        em.make_sems(ctx)

        def sba(name, shape, dt=F32):
            return nc.alloc_sbuf_tensor(name, list(shape), dt)

        x_sb = sba("x_sb", (128, NT * D))
        ln_scr = sba("ln_scr", (128, D))
        sq_scr = sba("sq_scr", (128, D))
        cols = sba("cols", (128, 384))
        m_low = sba("m_low_sb", (128, 128), BF16)
        m_upi = sba("m_upi_sb", (128, 128), BF16)
        m_ups = sba("m_ups_sb", (128, 128), BF16)
        id32 = sba("id32_sb", (128, 128))
        id16 = sba("id16_sb", (128, 128), BF16)
        ones_bf = sba("ones_bf", (1, 128), BF16)
        qn2 = sba("qn2_sb", (128, 1))
        kn2 = sba("kn2_sb", (128, 1))
        lng = [sba(f"lng{l}_sb", (128, 8)) for l in range(2)]
        lnb = [sba(f"lnb{l}_sb", (128, 8)) for l in range(2)]
        b1a = [sba(f"b1a{l}_sb", (128, NHT)) for l in range(2)]
        b1g = [sba(f"b1g{l}_sb", (128, NHT)) for l in range(2)]
        wld = [sba(f"wld{i}", (128, 1024)) for i in range(2)]
        w1stage = sba("w1stage", (128, 1024), BF16)

        ARENA = 122880
        lo, hi = nc.bump_sbuf(ARENA)
        assert hi - lo >= ARENA

        def ar(name, shape, dt, off):
            sz = int(np.prod(shape[1:])) * mybir.dt.size(dt)
            assert off + sz <= ARENA, (name, off, sz)
            return nc.alloc_sbuf_tensor_at(name, list(shape), dt, offset=lo + off)

        wT_bf = {w: ar(w + "_bf", (128, 8 * D), BF16, k * 16384)
                 for k, w in enumerate(("wqT", "wkT", "wvT", "woT"))}
        q_scrO = ar("q_scrO", (128, D), F32, 65536)
        k_scrO = ar("k_scrO", (128, D), F32, 69632)
        ltO = [ar(f"ltO{i}", (128, D), BF16, 73728 + i * 2048) for i in range(2)]
        qhTO = [ar(f"qhTO{i}", (128, D), BF16, 77824 + i * 2048) for i in range(2)]
        khT3 = ar("khT3", (128, 4 * D), BF16, 81920)
        v3 = ar("v3", (128, 4 * D), BF16, 90112)
        pbO = [ar(f"pbO{i}", (128, 512), BF16, 98304 + i * 1024) for i in range(2)]
        ptO = [ar(f"ptO{i}", (128, 512), BF16, 100352 + i * 1024) for i in range(2)]
        obO = [ar(f"obO{i}", (128, D), BF16, 102400 + i * 2048) for i in range(2)]
        otO = [ar(f"otO{i}", (128, D), BF16, 106496 + i * 2048) for i in range(2)]
        brow4_hi = ar("brow4_hi", (1, 4 * D), BF16, 110592)
        OUTER_T = (["wqT_bf", "wkT_bf", "wvT_bf", "woT_bf", "q_scrO", "k_scrO",
                    "khT3", "v3", "brow4_hi"] +
                   [f"{p}{i}" for p in ("ltO", "qhTO", "pbO", "ptO", "obO", "otO")
                    for i in range(2)])

        def innerA():
            t = {}
            t["wqkv"] = ar("wqkv_bf", (128, 8 * 3 * INNER), BF16, 0)
            t["woin"] = ar("woin_bf", (128, 4 * D), BF16, 24576)
            t["q_scr"] = ar("q_scrI", (128, INNER), F32, 32768)
            t["k_scr"] = ar("k_scrI", (128, INNER), F32, 34816)
            t["rh"] = ar("rhI", (128, INNER), F32, 36864)
            t["lt"] = [ar(f"ltI_{i}", (128, D), BF16, 38912 + i * 2048) for i in range(2)]
            t["qhT"] = [ar(f"qhTI_{i}", (128, INNER), BF16, 43008 + i * 1024) for i in range(2)]
            t["khT5"] = ar("khT5", (128, 6 * INNER), BF16, 45056)
            t["v5"] = ar("v5", (128, 6 * INNER), BF16, 51200)
            t["pb"] = [ar(f"pbI_{i}", (128, 640), BF16, 57344 + i * 1280) for i in range(2)]
            t["pt"] = [ar(f"ptI_{i}", (128, 640), BF16, 59904 + i * 1280) for i in range(2)]
            t["ob"] = [ar(f"obI_{i}", (128, INNER), BF16, 62464 + i * 1024) for i in range(2)]
            t["ot"] = [ar(f"otI_{i}", (128, INNER), BF16, 64512 + i * 1024) for i in range(2)]
            t["h2"] = [ar(f"h2A_{i}", (128, INNER), F32, 66560 + i * 2048) for i in range(2)]
            t["rb"] = [ar(f"rotb_{i}", (128, 4 * HD), F32, 70656 + i * 1024) for i in range(2)]
            t["abt"] = ar("abt", (128, 4 * HD), F32, 72704)
            names = (["wqkv_bf", "woin_bf", "q_scrI", "k_scrI",
                      "rhI", "khT5", "v5", "abt",
                      "w2bf", "b2hi"] +
                     [f"{p}_{i}" for p in ("ltI", "qhTI", "pbI", "ptI", "obI", "otI",
                                           "h2A", "rotb") for i in range(2)])
            return t, names

        def innerB():
            t = {}
            t["w2"] = ar("w2bf", (128, NHT * D), BF16, 73728)
            t["gg"] = ar("geglu", (128, NHT * 512), BF16, 0)
            t["ln2T"] = ar("ln2T", (128, 8 * 512), BF16, 22528)
            t["w1"] = [ar(f"w1bf_{i}", (128, 1024), BF16, 30720 + i * 2048) for i in range(2)]
            t["ag"] = ar("agscr", (128, 2048), F32, 34816)
            t["h2"] = [ar(f"h2B_{i}", (128, INNER), F32, 43008 + i * 2048) for i in range(2)]
            t["b2hi"] = ar("b2hi", (1, D), BF16, 118784)
            names = (["geglu", "ln2T", "agscr"] +
                     [] +
                     [f"w1bf_{i}" for i in range(2)] + [f"h2B_{i}" for i in range(2)])
            return t, names

        x1b = [ar(f"x1b{i}", (128, D), F32, i * 4096) for i in range(2)]
        FINAL_T = ["x1b0", "x1b1"]

        pP = [nc.alloc_psum_tensor(f"pP{i}", [128, 512], F32) for i in range(4)]
        simP = nc.alloc_psum_tensor("simP", [128, 512], F32)
        simX = nc.alloc_psum_tensor("simX", [128, 128], F32)
        tbf = nc.alloc_psum_tensor("tbf", [128, 1024], BF16)
        tr32 = nc.alloc_psum_tensor("tr32", [128, 512], F32)
        PN = {id(pP[0]): "pP0", id(pP[1]): "pP1", id(pP[2]): "pP2", id(pP[3]): "pP3",
              id(simP): "simP", id(simX): "simX", id(tbf): "tbf", id(tr32): "tr32"}

        op = em.op

        def stats_bank(p):
            base = 8 * p
            return [cols[:, base + i:base + i + 1] for i in range(8)]
        sums_bank = [cols[:, 16:32], cols[:, 32:48]]
        c_rdens = cols[:, 48:64]
        c_nmax = [cols[:, 64 + i:65 + i] for i in range(2)]
        c_nmaxx = [cols[:, 66 + i:67 + i] for i in range(2)]
        c_denx = [cols[:, 68 + i:69 + i] for i in range(2)]
        qn_eff = cols[:, 70:71]
        c_eight = cols[:, 71:72]
        c_eps = cols[:, 72:73]
        rq_bank = [cols[:, 96:112], cols[:, 112:128]]
        c_s1, c_mu, c_m2, c_var, c_rstd, c_nmr, c_t6, c_t7 = stats_bank(0)

        x_sb3 = bass.AP(x_sb, 0, [[NT * D, 128], [D, NT], [1, D]])
        op("sp", lambda e: e.dma_start(
            x_sb3, dram["x"].rearrange("(n p) d -> p n d", p=128)),
            w=["x_sb"], dma=True)
        for t, n in ((m_low, "m_low"), (m_upi, "m_upi"), (m_ups, "m_ups"),
                     (id32, "id32"), (id16, "id16"), (qn2, "qn2"), (kn2, "kn2")):
            op("sp", lambda e, _t=t, _n=n: e.dma_start(_t[:], dram[_n]), w=[n], dma=True)
        for l in range(2):
            for n, t in ((f"lng{l}", lng[l]), (f"lnb{l}", lnb[l]),
                         (f"b1a{l}", b1a[l]), (f"b1g{l}", b1g[l])):
                op("sp", lambda e, _t=t, _n=n: e.dma_start(_t[:], dram[_n]), w=[n], dma=True)
        op("dve", lambda e: e.memset(ones_bf[:], 1.0), w=["ones_bf"])
        op("dve", lambda e: e.memset(c_eps, EPS), w=["c_eps"])
        op("dve", lambda e: e.memset(c_eight, 8.0), w=["c_eight"])
        op("dve", lambda e: e.tensor_scalar_mul(qn_eff, qn2[:], 0.125), r=["qn2"], w=["c_qe"])

        def load_cast(dst_bf, dst_name, dram_name, n_rows, n_cols, cast_eng="pool"):
            nj = (n_rows + 127) // 128
            for j in range(nj):
                rows = min(128, n_rows - j * 128)
                for c0 in range(0, n_cols, 1024):
                    cw = min(1024, n_cols - c0)
                    slot = load_cast.k % 2
                    load_cast.k += 1
                    op("sp", lambda e, _j=j, _c0=c0, _cw=cw, _r=rows, _s=slot,
                       _d=dram_name: e.dma_start(
                           wld[_s][0:_r, 0:_cw],
                           dram[_d][_j * 128:_j * 128 + _r, _c0:_c0 + _cw]),
                       w=["wld%d" % slot], dma=True)
                    op(cast_eng, lambda e, _j=j, _c0=c0, _cw=cw, _r=rows, _s=slot:
                       e.tensor_copy(
                           dst_bf[0:_r, _j * n_cols + _c0:_j * n_cols + _c0 + _cw],
                           wld[_s][0:_r, 0:_cw]),
                       r=["wld%d" % slot], w=[dst_name])
        load_cast.k = 0

        def ln_stats(srcs, total_d, pb=0):
            c_s1, c_mu, c_m2, c_var, c_rstd, c_nmr, c_t6, c_t7 = stats_bank(pb)
            sx = str(pb)
            for idx, (apv, wdt, rn) in enumerate(srcs):
                if idx == 0:
                    op("dve", lambda e, _a=apv, _c=c_s1: e.tensor_reduce(_c, _a, AX.X, ALU.add),
                       r=[rn], w=["c_s1" + sx])
                    op("act", lambda e, _a=apv, _w=wdt, _c=c_t6: e.activation(
                        sq_scr[:, 0:_w], _a, AF.Square, accum_out=_c),
                       r=[rn], w=["sq_scr", "c_t6" + sx])
                else:
                    op("dve", lambda e, _a=apv, _c=c_t7: e.tensor_reduce(_c, _a, AX.X, ALU.add),
                       r=[rn], w=["c_t7" + sx])
                    op("dve", lambda e, _c1=c_s1, _c2=c_t7: e.tensor_add(_c1, _c1, _c2),
                       r=["c_t7" + sx, "c_s1" + sx], w=["c_s1" + sx])
                    op("act", lambda e, _a=apv, _w=wdt, _c=c_t7: e.activation(
                        sq_scr[:, 0:_w], _a, AF.Square, accum_out=_c),
                       r=[rn], w=["sq_scr", "c_t7" + sx])
                    op("dve", lambda e, _c1=c_t6, _c2=c_t7: e.tensor_add(_c1, _c1, _c2),
                       r=["c_t7" + sx, "c_t6" + sx], w=["c_t6" + sx])
            inv = 1.0 / total_d
            op("dve", lambda e, _a=c_mu, _b=c_s1: e.tensor_scalar_mul(_a, _b, inv),
               r=["c_s1" + sx], w=["c_mu" + sx])
            op("dve", lambda e, _a=c_m2, _b=c_mu: e.tensor_mul(_a, _b, _b),
               r=["c_mu" + sx], w=["c_m2" + sx])
            op("dve", lambda e, _a=c_var, _b=c_t6: e.tensor_scalar_mul(_a, _b, inv),
               r=["c_t6" + sx], w=["c_var" + sx])
            op("dve", lambda e, _a=c_var, _b=c_m2: e.tensor_sub(_a, _a, _b),
               r=["c_var" + sx, "c_m2" + sx], w=["c_var" + sx])
            op("act", lambda e, _a=c_var: e.activation(_a, _a, AF.Sqrt, bias=c_eps, scale=1.0),
               r=["c_var" + sx, "c_eps"], w=["c_var" + sx])
            op("dve", lambda e, _a=c_rstd, _b=c_var: e.reciprocal(_a, _b),
               r=["c_var" + sx], w=["c_rstd" + sx])
            op("dve", lambda e, _a=c_nmr, _b=c_mu, _c=c_rstd: e.scalar_tensor_tensor(
                _a, _b, -1.0, _c, ALU.mult, ALU.mult),
               r=["c_mu" + sx, "c_rstd" + sx], w=["c_nmr" + sx])
            return c_rstd, c_nmr

        def tr_f32(chunks, evacs):
            for g0 in range(0, len(chunks), 4):
                grp = chunks[g0:g0 + 4]
                for gi, (src, rn) in enumerate(grp):
                    op("pe", lambda e, _s=src, _g=gi: e.transpose(
                        tr32[:, _g * 128:_g * 128 + 128], _s, id32[:]),
                       r=[rn, "id32"], w=["tr32"])
                evacs(g0, len(grp))

        for k, w in enumerate(("wqT", "wkT", "wvT", "woT")):
            load_cast(wT_bf[w], w + "_bf", w, D, D)
        for k in range(4):
            slot = load_cast.k % 2
            load_cast.k += 1
            op("sp", lambda e, _k=k, _s=slot: e.dma_start(
                wld[_s][0:1, 0:D], dram["brow4"][0:1, _k * D:(_k + 1) * D]),
               w=["wld%d" % slot], dma=True)
            op("pool", lambda e, _k=k, _s=slot: e.tensor_copy(
                brow4_hi[0:1, _k * D:(_k + 1) * D], wld[_s][0:1, 0:D]),
               r=["wld%d" % slot], w=["brow4_hi"])
        w1_off_pre = []
        pos = 0
        for m in range(NHT):
            aw = min(128, HID - m * 128)
            w1_off_pre.append((pos, aw))
            w1_off_pre.append((pos + aw, aw))
            pos += 2 * aw
        for l in range(2):
            for ot in range(2 * NHT):
                col0, ow = w1_off_pre[ot]
                slot = load_cast.k % 2
                load_cast.k += 1
                op("sp", lambda e, _c=col0, _w=ow, _s=slot, _l=l: e.dma_start(
                    bass.AP(wld[_s], 0, [[1024, 128], [_w, 8], [1, _w]]),
                    dram[f"w1p{_l}"][:, _c:_c + _w].rearrange("(a p) o -> p a o", p=128)),
                   w=["wld%d" % slot], dma=True)
                op("pool", lambda e, _s=slot, _w=ow: e.tensor_copy(
                    w1stage[:, 0:8 * _w], wld[_s][:, 0:8 * _w]),
                   r=["wld%d" % slot], w=["w1stage"])
                op("sp", lambda e, _l=l, _o=ot, _w=ow: e.dma_start(
                    d_w1img[_l, _o, :, 0:8 * _w], w1stage[:, 0:8 * _w]),
                   r=["w1stage"], w=["w1img_d"], dma=True)

        def proj_1024(lt_t, lt_name, w_bf, w_name, psA, psB, bias_idx):
            for n in range(2):
                ps = (psA, psB)[n]
                psn = PN[id(ps)]
                for j in range(8):
                    op("pe", lambda e, _n=n, _j=j, _p=ps, _lt=lt_t: e.matmul(
                        _p[:, 0:512], _lt[:, _j * 128:_j * 128 + 128],
                        w_bf[:, _j * D + _n * 512:_j * D + _n * 512 + 512],
                        start=(_j == 0), stop=False),
                       r=[lt_name, w_name], w=[psn])
                op("pe", lambda e, _n=n, _p=ps, _b=bias_idx: e.matmul(
                    _p[:, 0:512], ones_bf[:],
                    brow4_hi[0:1, _b * D + _n * 512:_b * D + _n * 512 + 512],
                    start=False, stop=True),
                   r=["ones_bf", "brow4_hi"], w=[psn])

        def rmsnorm16(scr, scr_name, pb, rq=False):
            if rq:
                su = rq_bank[pb]
                sn = f"c_rq{pb}"
            else:
                su = sums_bank[pb]
                sn = f"c_sums{pb}"
            op("dve", lambda e, _s=scr: e.tensor_scalar_min(_s[:], _s[:], 1.0e4),
               r=[scr_name], w=[scr_name])
            op("dve", lambda e, _s=scr: e.tensor_scalar_max(_s[:], _s[:], -1.0e4),
               r=[scr_name], w=[scr_name])
            op("act", lambda e, _s=scr: e.activation(sq_scr[:], _s[:], AF.Square),
               r=[scr_name], w=["sq_scr"])
            red3 = bass.AP(sq_scr, 0, [[D, 128], [64, 16], [1, 64]])
            op("dve", lambda e, _r=red3, _u=su: e.tensor_reduce(_u, _r, AX.X, ALU.add),
               r=["sq_scr"], w=[sn])
            op("dve", lambda e, _u=su: e.tensor_scalar(
                _u, _u, 1.0 / 64, 1.0e-5, ALU.mult, ALU.max),
               r=[sn], w=[sn])
            op("act", lambda e, _u=su: e.activation(_u, _u, AF.Sqrt, bias=c_eps, scale=1.0),
               r=[sn, "c_eps"], w=[sn])
            op("dve", lambda e, _u=su: e.reciprocal(_u, _u), r=[sn], w=[sn])
            if not rq:
                for h in range(16):
                    op("dve", lambda e, _s=scr, _h=h, _u=su: e.tensor_scalar_mul(
                        _s[:, _h * 64:_h * 64 + 64], _s[:, _h * 64:_h * 64 + 64],
                        _u[:, _h:_h + 1]), r=[scr_name, sn], w=[scr_name])

        def P1oA(i):
            pb = i % 2
            xt = x_sb[:, i * D:(i + 1) * D]
            rstd, nmr = ln_stats([(xt, D, "x_sb")], D, pb)
            op("act", lambda e, _x=xt, _r=rstd, _n=nmr: e.activation(
                ln_scr[:], _x, AF.Identity, bias=_n, scale=_r),
               r=["x_sb", f"c_rstd{pb}", f"c_nmr{pb}"], w=["ln_scr"])

        def P1oB(i):
            sl = i % 2
            k4 = i % 4
            pb = i % 2
            xt = x_sb[:, i * D:(i + 1) * D]

            def ev_lt(g0, glen, _sl=sl):
                op("act", lambda e, _g0=g0, _n=glen * 128: e.activation(
                    ltO[_sl][:, _g0 * 128:_g0 * 128 + _n], tr32[:, 0:_n], AF.Identity),
                   r=["tr32"], w=[f"ltO{_sl}"])
            tr_f32([(ln_scr[:, j * 128:j * 128 + 128], "ln_scr") for j in range(8)], ev_lt)
            proj_1024(ltO[sl], f"ltO{sl}", wT_bf["wqT"], "wqT_bf", pP[0], pP[1], 0)
            proj_1024(ltO[sl], f"ltO{sl}", wT_bf["wkT"], "wkT_bf", pP[2], pP[3], 1)
            op("dve", lambda e: e.tensor_copy(q_scrO[:, 0:512], pP[0][:]), r=["pP0"], w=["q_scrO"])
            op("dve", lambda e: e.tensor_copy(q_scrO[:, 512:1024], pP[1][:]), r=["pP1"], w=["q_scrO"])
            op("dve", lambda e: e.tensor_copy(k_scrO[:, 0:512], pP[2][:]), r=["pP2"], w=["k_scrO"])
            op("dve", lambda e: e.tensor_copy(k_scrO[:, 512:1024], pP[3][:]), r=["pP3"], w=["k_scrO"])
            proj_1024(ltO[sl], f"ltO{sl}", wT_bf["wvT"], "wvT_bf", pP[0], pP[1], 2)
            op("act", lambda e, _k=k4: e.activation(
                v3[:, _k * D:_k * D + 512], pP[0][:], AF.Identity), r=["pP0"], w=["v3"])
            op("act", lambda e, _k=k4: e.activation(
                v3[:, _k * D + 512:_k * D + 1024], pP[1][:], AF.Identity), r=["pP1"], w=["v3"])
            rmsnorm16(q_scrO, "q_scrO", pb, rq=True)
            rmsnorm16(k_scrO, "k_scrO", pb)

        def P1oB2(i):
            sl = i % 2
            k4 = i % 4

            def ev_q(g0, glen, _sl=sl):
                for gi in range(glen):
                    op("act", lambda e, _g=gi, _o=(g0 + gi) * 128: e.activation(
                        qhTO[_sl][:, _o:_o + 128], tr32[:, _g * 128:_g * 128 + 128],
                        AF.Identity, scale=qn_eff),
                       r=["tr32", "c_qe"], w=[f"qhTO{_sl}"])
            tr_f32([(q_scrO[:, j * 128:j * 128 + 128], "q_scrO") for j in range(8)], ev_q)

            def ev_k(g0, glen, _k4=k4):
                for gi in range(glen):
                    op("act", lambda e, _g=gi, _o=_k4 * D + (g0 + gi) * 128: e.activation(
                        khT3[:, _o:_o + 128], tr32[:, _g * 128:_g * 128 + 128],
                        AF.Identity, scale=kn2[:]),
                       r=["tr32", "kn2"], w=["khT3"])
            tr_f32([(k_scrO[:, j * 128:j * 128 + 128], "k_scrO") for j in range(8)], ev_k)

        def P2o(i):
            sl = i % 2
            pb = i % 2
            xt = x_sb[:, i * D:(i + 1) * D]
            kstart = max(0, i - 2)
            nk = i - kstart + 1
            ncols = nk * 128

            def emit_ptav(h):
                hs = h % 2
                for c in range(nk):
                    op("pe", lambda e, _c=c, _hs=hs: e.transpose(
                        tbf[:, _c * 128:_c * 128 + 128],
                        pbO[_hs][:, _c * 128:_c * 128 + 128], id16[:]),
                       r=[f"pbO{hs}", "id16"], w=["tbf"])
                op("dve", lambda e, _n=ncols, _hs=hs: e.tensor_copy(
                    ptO[_hs][:, 0:_n], tbf[:, 0:_n]), r=["tbf"], w=[f"ptO{hs}"])
                bank = pP[0] if h < 8 else pP[1]
                bn = PN[id(bank)]
                for c in range(nk):
                    slot = (kstart + c) % 4
                    st_ = (h % 8 == 0) and (c == 0)
                    sp_ = (h % 8 == 7) and (c == nk - 1)
                    op("pe", lambda e, _c=c, _hs=hs, _b=bank, _co=(h % 8) * 64,
                       _st=st_, _sp=sp_, _slot=slot, _h=h: e.matmul(
                        _b[:, _co:_co + 64], ptO[_hs][:, _c * 128:_c * 128 + 128],
                        v3[:, _slot * D + _h * 64:_slot * D + _h * 64 + 64],
                        start=_st, stop=_sp),
                       r=[f"ptO{hs}", "v3"], w=[bn])

            pend = None
            for h in range(HEADS_O):
                hp = (h % 2) * 64
                hc = (h // 2) * 128
                hs = h % 2
                for c in range(nk):
                    slot = (kstart + c) % 4
                    op("pe", lambda e, _c=c, _s=slot, _hp=hp, _hc=hc, _sl=sl,
                       _st=(c == 0): e.matmul(
                        simP[:, _c * 128:_c * 128 + 128],
                        qhTO[_sl][_hp:_hp + 64, _hc:_hc + 128],
                        khT3[_hp:_hp + 64, _s * D + _hc:_s * D + _hc + 128],
                        start=_st, stop=False),
                       r=[f"qhTO{sl}", "khT3"], w=["simP"])
                if nk == 3:
                    op("pe", lambda e: e.matmul(simP[:, 0:128], id16[:], m_ups[:],
                                                start=False, stop=False),
                       r=["id16", "m_ups"], w=["simP"])
                op("pe", lambda e, _o=(nk - 1) * 128: e.matmul(
                    simP[:, _o:_o + 128], id16[:], m_low[:], start=False, stop=True),
                   r=["id16", "m_low"], w=["simP"])
                if pend is not None:
                    emit_ptav(pend)
                nmv = c_nmax[h % 2]
                nmn = f"c_nmax{h % 2}"
                rdn = f"c_rd{h}"
                rqc = rq_bank[pb][:, h:h + 1]
                rqn = f"c_rq{pb}"
                op("dve", lambda e, _n=ncols, _m=nmv: e.tensor_reduce(
                    _m, simP[:, 0:_n], AX.X, ALU.max, negate=True),
                   r=["simP"], w=[nmn])
                op("dve", lambda e, _m=nmv, _q=rqc: e.tensor_mul(_m, _m, _q),
                   r=[nmn, rqn], w=[nmn])
                op("act", lambda e, _n=ncols, _hs=hs, _m=nmv, _h=h, _q=rqc: e.activation(
                    pbO[_hs][:, 0:_n], simP[:, 0:_n], AF.Exp, bias=_m, scale=_q,
                    accum_out=c_rdens[:, _h:_h + 1]),
                   r=["simP", nmn, rqn], w=[f"pbO{hs}", rdn])
                op("dve", lambda e, _h=h: e.reciprocal(
                    c_rdens[:, _h:_h + 1], c_rdens[:, _h:_h + 1]), r=[rdn], w=[rdn])
                pend = h
            emit_ptav(pend)
            for h in range(HEADS_O):
                bank = pP[0] if h < 8 else pP[1]
                bn = PN[id(bank)]
                op("act", lambda e, _b=bank, _co=(h % 8) * 64, _h=h, _s=sl: e.activation(
                    obO[_s][:, _h * 64:_h * 64 + 64], _b[:, _co:_co + 64], AF.Identity,
                    scale=c_rdens[:, _h:_h + 1]),
                   r=[bn, f"c_rd{h}"], w=[f"obO{sl}"])
            for j in range(8):
                op("pe", lambda e, _j=j, _s=sl: e.transpose(
                    tbf[:, _j * 128:_j * 128 + 128],
                    obO[_s][:, _j * 128:_j * 128 + 128], id16[:]),
                   r=[f"obO{sl}", "id16"], w=["tbf"])
            op("dve", lambda e, _s=sl: e.tensor_copy(otO[_s][:], tbf[:]),
               r=["tbf"], w=[f"otO{sl}"])
            proj_1024(otO[sl], f"otO{sl}", wT_bf["woT"], "woT_bf", pP[2], pP[3], 3)
            rstd, nmr = ln_stats([(pP[2][:], 512, "pP2"), (pP[3][:], 512, "pP3")], D, pb)
            op("dve", lambda e, _x=xt, _n=nmr: e.tensor_scalar_add(_x, _x, _n),
               r=["x_sb", f"c_nmr{pb}"], w=["x_sb"])
            op("dve", lambda e, _x=xt, _r=rstd: e.scalar_tensor_tensor(
                _x[:, 0:512], pP[2][:], _r, _x[:, 0:512], ALU.mult, ALU.add),
               r=["pP2", f"c_rstd{pb}", "x_sb"], w=["x_sb"])
            op("dve", lambda e, _x=xt, _r=rstd: e.scalar_tensor_tensor(
                _x[:, 512:1024], pP[3][:], _r, _x[:, 512:1024], ALU.mult, ALU.add),
               r=["pP3", f"c_rstd{pb}", "x_sb"], w=["x_sb"])

        NTL = min(NT, NT_LIM)
        if NTL:
            P1oA(0)
            P1oB(0)
            P1oB2(0)
            if NTL > 1:
                P1oA(1)
            for i in range(NTL):
                if i + 1 < NTL:
                    P1oB(i + 1)
                if i + 2 < NTL:
                    P1oA(i + 2)
                P2o(i)
                if i + 1 < NTL:
                    P1oB2(i + 1)

        op("pool", lambda e: e.dma_start(d_x1, x_sb[:]), r=["x_sb"], w=["x1d"], dma=True)
        if not (PHASE_LIMIT == "outer" and DEBUG_TAP):
            for i in range(NT):
                xt = x_sb[:, i * D:(i + 1) * D]
                pb = i % 2
                rstd, nmr = ln_stats([(xt, D, "x_sb")], D, pb)
                op("act", lambda e, _x=xt, _r=rstd, _n=nmr: e.activation(
                    _x, _x, AF.Identity, bias=_n, scale=_r),
                   r=["x_sb", f"c_rstd{pb}", f"c_nmr{pb}", "x1d"], w=["x_sb"])

        n_inner = {"outer": 0, "layer0": 1, "all": 2}[PHASE_LIMIT]
        prev_names = OUTER_T
        tA, namesA = (innerA() if n_inner else (None, None))
        tB, namesB = (innerB() if n_inner else (None, None))
        for l in range(n_inner):
            em.fence(prev_names, namesA)
            prev_names = namesA
            load_cast(tA["wqkv"], "wqkv_bf", f"wqkv{l}", D, 3 * INNER)
            load_cast(tA["woin"], "woin_bf", f"wo{l}", INNER, D)
            qsb_ap = cols[:, 128:384]
            slot = load_cast.k % 2
            load_cast.k += 1
            op("sp", lambda e, _l=l, _s=slot: e.dma_start(
                wld[_s][0:1, 0:256], dram[f"qsrows{_l}"]), w=["wld%d" % slot], dma=True)
            op("dve", lambda e: e.memset(sq_scr[0:1, 0:128], 1.0), w=["sq_scr"])
            op("pe", lambda e, _s=slot: e.matmul(
                tr32[:, 0:256], sq_scr[0:1, 0:128], wld[_s][0:1, 0:256],
                start=True, stop=True),
               r=["sq_scr", "wld%d" % slot], w=["tr32"])
            op("dve", lambda e: e.tensor_copy(qsb_ap, tr32[:, 0:256]),
               r=["tr32"], w=["qsb"])

            def shift_tile(i, dst, dst_name):
                op("pool", lambda e, _i=i, _d=dst: e.dma_start(
                    _d[1:128, :], x_sb[0:127, _i * D + 512:_i * D + 1024]),
                   r=["x_sb"], w=[dst_name], dma=True)
                if i == 0:
                    op("dve", lambda e, _d=dst: e.memset(_d[0:1, :], 0.0), w=[dst_name])
                else:
                    op("pool", lambda e, _i=i, _d=dst: e.dma_start(
                        _d[0:1, :], x_sb[127:128, (_i - 1) * D + 512:(_i - 1) * D + 1024]),
                       r=["x_sb"], w=[dst_name], dma=True)

            load_cast(tB["w2"], "w2bf", f"w2_{l}", HID, D)
            slot = load_cast.k % 2
            load_cast.k += 1
            op("sp", lambda e, _l=l, _s=slot: e.dma_start(
                wld[_s][0:1, 0:D], dram[f"b2_{_l}"]), w=["wld%d" % slot], dma=True)
            op("pool", lambda e, _s=slot: e.tensor_copy(
                tB["b2hi"][0:1, :], wld[_s][0:1, 0:D]),
               r=["wld%d" % slot], w=["b2hi"])
            def l2rot(scr, scr_name, a_off, b_off, pb, rq=False):
                if rq:
                    sums8 = rq_bank[pb][:, 0:8]
                    sn = f"c_rq{pb}"
                else:
                    sums8 = sums_bank[pb][:, 0:8]
                    sn = f"c_sums{pb}"
                op("act", lambda e, _s=scr: e.activation(
                    sq_scr[:, 0:512], _s[:], AF.Square), r=[scr_name], w=["sq_scr"])
                red3 = bass.AP(sq_scr, 0, [[D, 128], [64, 8], [1, 64]])
                op("dve", lambda e, _r=red3, _u=sums8: e.tensor_reduce(_u, _r, AX.X, ALU.add),
                   r=["sq_scr"], w=[sn])
                op("act", lambda e, _u=sums8: e.activation(_u, _u, AF.Sqrt),
                   r=[sn], w=[sn])
                op("dve", lambda e, _u=sums8: e.tensor_scalar_max(_u, _u, 1.0e-12),
                   r=[sn], w=[sn])
                op("dve", lambda e, _u=sums8: e.reciprocal(_u, _u), r=[sn], w=[sn])
                if not rq:
                    for h in range(8):
                        op("dve", lambda e, _s=scr, _h=h, _u=sums8: e.tensor_scalar_mul(
                            _s[:, _h * 64:_h * 64 + 64], _s[:, _h * 64:_h * 64 + 64],
                            _u[:, _h:_h + 1]), r=[scr_name, sn], w=[scr_name])
                vh = lambda t, off: bass.AP(t, off, [[t.ap().ap[0][0], 128], [64, 8], [1, 32]])
                a_bc = bass.AP(tA["abt"], a_off, [[4 * HD, 128], [0, 8], [1, 64]])
                b_bc = bass.AP(tA["abt"], b_off, [[4 * HD, 128], [0, 8], [1, 64]])
                op("dve", lambda e, _s=scr: e.tensor_scalar_mul(
                    vh(tA["rh"], 0), vh(_s, 32), -1.0), r=[scr_name], w=["rhI"])
                op("dve", lambda e, _s=scr: e.tensor_copy(vh(tA["rh"], 32), vh(_s, 0)),
                   r=[scr_name], w=["rhI"])
                op("dve", lambda e, _s=scr, _a=a_bc: e.tensor_mul(_s[:], _s[:], _a),
                   r=[scr_name, "abt"], w=[scr_name])
                op("dve", lambda e, _b=b_bc: e.tensor_mul(
                    tA["rh"][:], tA["rh"][:], _b), r=["rhI", "abt"], w=["rhI"])
                op("dve", lambda e, _s=scr: e.tensor_add(_s[:], _s[:], tA["rh"][:]),
                   r=[scr_name, "rhI"], w=[scr_name])

            def P1i(i):
                sl = i % 2
                k6 = i % 6
                pb = i % 2
                xt = x_sb[:, i * D:(i + 1) * D]
                shift_tile(i, tA["h2"][sl], "h2A_%d" % sl)
                op("sp", lambda e, _i=i, _s=sl: e.dma_start(
                    tA["rb"][_s][:], dram["rotpk"][:, _i * 256:(_i + 1) * 256]),
                   w=["rotb_%d" % sl], dma=True)
                op("dve", lambda e, _s=sl: e.tensor_mul(
                    tA["abt"][:], tA["rb"][_s][:], qsb_ap),
                   r=["rotb_%d" % sl, "qsb"], w=["abt"])

                def ev_lt(g0, glen, _sl=sl):
                    op("act", lambda e, _g0=g0, _n=glen * 128: e.activation(
                        tA["lt"][_sl][:, _g0 * 128:_g0 * 128 + _n], tr32[:, 0:_n], AF.Identity),
                       r=["tr32"], w=["ltI_%d" % _sl])
                tr_f32([(xt[:, j * 128:j * 128 + 128], "x_sb") for j in range(4)] +
                       [(tA["h2"][sl][:, j * 128:j * 128 + 128], "h2A_%d" % sl)
                        for j in range(4)], ev_lt)
                for n in range(3):
                    ps = pP[n]
                    psn = PN[id(ps)]
                    for j in range(8):
                        op("pe", lambda e, _n=n, _j=j, _p=ps, _s=sl: e.matmul(
                            _p[:, 0:512], tA["lt"][_s][:, _j * 128:_j * 128 + 128],
                            tA["wqkv"][:, _j * 1536 + _n * 512:_j * 1536 + _n * 512 + 512],
                            start=(_j == 0), stop=(_j == 7)),
                           r=["ltI_%d" % sl, "wqkv_bf"], w=[psn])
                op("dve", lambda e: e.tensor_copy(tA["q_scr"][:], pP[0][:]),
                   r=["pP0"], w=["q_scrI"])
                op("dve", lambda e: e.tensor_copy(tA["k_scr"][:], pP[1][:]),
                   r=["pP1"], w=["k_scrI"])
                op("act", lambda e, _k=k6: e.activation(
                    tA["v5"][:, _k * INNER:(_k + 1) * INNER], pP[2][:], AF.Identity),
                   r=["pP2"], w=["v5"])
                l2rot(tA["q_scr"], "q_scrI", 0, HD, pb, rq=True)
                l2rot(tA["k_scr"], "k_scrI", 2 * HD, 3 * HD, pb)

            def P1i2(i):
                sl = i % 2
                k6 = i % 6
                for gi in range(4):
                    op("pe", lambda e, _g=gi: e.transpose(
                        tr32[:, _g * 128:_g * 128 + 128],
                        tA["q_scr"][:, _g * 128:_g * 128 + 128], id32[:]),
                       r=["q_scrI", "id32"], w=["tr32"])
                op("act", lambda e, _s=sl: e.activation(
                    tA["qhT"][_s][:], tr32[:], AF.Identity, scale=c_eight),
                   r=["tr32", "c_eight"], w=["qhTI_%d" % sl])
                for gi in range(4):
                    op("pe", lambda e, _g=gi: e.transpose(
                        tr32[:, _g * 128:_g * 128 + 128],
                        tA["k_scr"][:, _g * 128:_g * 128 + 128], id32[:]),
                       r=["k_scrI", "id32"], w=["tr32"])
                op("act", lambda e, _k=k6: e.activation(
                    tA["khT5"][:, _k * INNER:(_k + 1) * INNER], tr32[:], AF.Identity),
                   r=["tr32"], w=["khT5"])

            def P2i(i):
                sl = i % 2
                xt = x_sb[:, i * D:(i + 1) * D]
                kstart = max(0, i - 4)
                nk = i - kstart + 1
                first_av = [True]

                def emit_ptav(h):
                    hs = h % 2
                    for c in range(nk):
                        op("pe", lambda e, _c=c, _hs=hs: e.transpose(
                            tbf[:, _c * 128:_c * 128 + 128],
                            tA["pb"][_hs][:, _c * 128:_c * 128 + 128], id16[:]),
                           r=["pbI_%d" % hs, "id16"], w=["tbf"])
                    op("dve", lambda e, _n=nk * 128, _hs=hs: e.tensor_copy(
                        tA["pt"][_hs][:, 0:_n], tbf[:, 0:_n]), r=["tbf"], w=["ptI_%d" % hs])
                    for c in range(nk):
                        slot = (kstart + c) % 6
                        st_ = first_av[0]
                        first_av[0] = False
                        sp_ = (h == HEADS_I - 1) and (c == nk - 1)
                        op("pe", lambda e, _c=c, _hs=hs, _st=st_, _sp=sp_, _slot=slot, _h=h:
                           e.matmul(pP[3][:, _h * 64:_h * 64 + 64],
                                    tA["pt"][_hs][:, _c * 128:_c * 128 + 128],
                                    tA["v5"][:, _slot * INNER + _h * 64:_slot * INNER + _h * 64 + 64],
                                    start=_st, stop=_sp),
                           r=["ptI_%d" % hs, "v5"], w=["pP3"])

                pend = None
                for h in range(HEADS_I):
                    hp = (h % 2) * 64
                    hc = (h // 2) * 128
                    hs = h % 2
                    nP = min(nk, 4)
                    for c in range(nP):
                        slot = (kstart + c) % 6
                        op("pe", lambda e, _o=c * 128, _s=slot, _hp=hp, _hc=hc,
                           _sl=sl, _st=(c == 0): e.matmul(
                            simP[:, _o:_o + 128],
                            tA["qhT"][_sl][_hp:_hp + 64, _hc:_hc + 128],
                            tA["khT5"][_hp:_hp + 64, _s * INNER + _hc:_s * INNER + _hc + 128],
                            start=_st, stop=False),
                           r=["qhTI_%d" % sl, "khT5"], w=["simP"])
                    if nk == 5:
                        op("pe", lambda e: e.matmul(simP[:, 0:128], id16[:], m_upi[:],
                                                    start=False, stop=True),
                           r=["id16", "m_upi"], w=["simP"])
                        slot = i % 6
                        op("pe", lambda e, _s=slot, _hp=hp, _hc=hc, _sl=sl: e.matmul(
                            simX[:, 0:128],
                            tA["qhT"][_sl][_hp:_hp + 64, _hc:_hc + 128],
                            tA["khT5"][_hp:_hp + 64, _s * INNER + _hc:_s * INNER + _hc + 128],
                            start=True, stop=False),
                           r=["qhTI_%d" % sl, "khT5"], w=["simX"])
                        op("pe", lambda e: e.matmul(simX[:, 0:128], id16[:], m_low[:],
                                                    start=False, stop=True),
                           r=["id16", "m_low"], w=["simX"])
                    else:
                        op("pe", lambda e, _o=(nk - 1) * 128: e.matmul(
                            simP[:, _o:_o + 128], id16[:], m_low[:], start=False, stop=True),
                           r=["id16", "m_low"], w=["simP"])
                    if pend is not None:
                        emit_ptav(pend)
                    nPc = min(nk, 4) * 128
                    nmv = c_nmax[h % 2]
                    nmn = f"c_nmax{h % 2}"
                    rdn = f"c_rd{h}"
                    rdc = c_rdens[:, h:h + 1]
                    pbp = i % 2
                    rqc = rq_bank[pbp][:, h:h + 1]
                    rqn = f"c_rq{pbp}"
                    op("dve", lambda e, _n=nPc, _m=nmv: e.tensor_reduce(
                        _m, simP[:, 0:_n], AX.X, ALU.max, negate=True),
                       r=["simP"], w=[nmn])
                    if nk == 5:
                        xv = c_nmaxx[h % 2]
                        xn = f"c_nmaxx{h % 2}"
                        op("dve", lambda e, _x=xv: e.tensor_reduce(
                            _x, simX[:], AX.X, ALU.max, negate=True), r=["simX"], w=[xn])
                        op("dve", lambda e, _m=nmv, _x=xv: e.tensor_tensor(
                            _m, _m, _x, ALU.min), r=[nmn, xn], w=[nmn])
                    op("dve", lambda e, _m=nmv, _q=rqc: e.tensor_mul(_m, _m, _q),
                       r=[nmn, rqn], w=[nmn])
                    op("act", lambda e, _n=nPc, _hs=hs, _m=nmv, _c=rdc, _q=rqc: e.activation(
                        tA["pb"][_hs][:, 0:_n], simP[:, 0:_n], AF.Exp, bias=_m,
                        scale=_q, accum_out=_c),
                       r=["simP", nmn, rqn], w=["pbI_%d" % hs, rdn])
                    if nk == 5:
                        dxv = c_denx[h % 2]
                        dxn = f"c_denx{h % 2}"
                        op("act", lambda e, _hs=hs, _m=nmv, _x=dxv, _q=rqc: e.activation(
                            tA["pb"][_hs][:, 512:640], simX[:], AF.Exp, bias=_m,
                            scale=_q, accum_out=_x),
                           r=["simX", nmn, rqn], w=["pbI_%d" % hs, dxn])
                        op("dve", lambda e, _c=rdc, _x=dxv: e.tensor_add(_c, _c, _x),
                           r=[rdn, dxn], w=[rdn])
                    op("dve", lambda e, _c=rdc: e.reciprocal(_c, _c), r=[rdn], w=[rdn])
                    pend = h
                emit_ptav(pend)
                for h in range(HEADS_I):
                    op("act", lambda e, _h=h, _s=sl: e.activation(
                        tA["ob"][_s][:, _h * 64:_h * 64 + 64], pP[3][:, _h * 64:_h * 64 + 64],
                        AF.Identity, scale=c_rdens[:, _h:_h + 1]),
                       r=["pP3", f"c_rd{h}"], w=["obI_%d" % sl])
                for c in range(4):
                    op("pe", lambda e, _c=c, _s=sl: e.transpose(
                        tbf[:, _c * 128:_c * 128 + 128],
                        tA["ob"][_s][:, _c * 128:_c * 128 + 128], id16[:]),
                       r=["obI_%d" % sl, "id16"], w=["tbf"])
                op("dve", lambda e, _s=sl: e.tensor_copy(tA["ot"][_s][:], tbf[:, 0:512]),
                   r=["tbf"], w=["otI_%d" % sl])
                for n in range(2):
                    ps = pP[n]
                    psn = PN[id(ps)]
                    for j in range(4):
                        op("pe", lambda e, _n=n, _j=j, _p=ps, _s=sl: e.matmul(
                            _p[:, 0:512], tA["ot"][_s][:, _j * 128:_j * 128 + 128],
                            tA["woin"][:, _j * D + _n * 512:_j * D + _n * 512 + 512],
                            start=(_j == 0), stop=(_j == 3)),
                           r=["otI_%d" % sl, "woin_bf"], w=[psn])
                op("dve", lambda e, _x=xt: e.tensor_add(_x[:, 0:512], _x[:, 0:512], pP[0][:]),
                   r=["pP0", "x_sb"], w=["x_sb"])
                op("dve", lambda e, _x=xt: e.tensor_add(
                    _x[:, 512:1024], _x[:, 512:1024], pP[1][:]),
                   r=["pP1", "x_sb"], w=["x_sb"])

            P1i(0)
            P1i2(0)
            for i in range(NT):
                if i + 1 < NT:
                    P1i(i + 1)
                P2i(i)
                if i + 1 < NT:
                    P1i2(i + 1)

            em.fence(prev_names, namesB)
            prev_names = namesB
            w1_off = []
            pos = 0
            for m in range(NHT):
                aw = min(128, HID - m * 128)
                w1_off.append((pos, aw))
                w1_off.append((pos + aw, aw))
                pos += 2 * aw

            for tc in range(4):
                for ii in range(4):
                    i = tc * 4 + ii
                    sl = i % 2
                    xt = x_sb[:, i * D:(i + 1) * D]
                    shift_tile(i, tB["h2"][sl], "h2B_%d" % sl)
                    pb = i % 2
                    rstd, nmr = ln_stats([(xt[:, 0:512], 512, "x_sb"),
                              (tB["h2"][sl][:], 512, "h2B_%d" % sl)], D, pb)
                    op("act", lambda e, _x=xt, _r=rstd, _n=nmr: e.activation(
                        ln_scr[:, 0:512], _x[:, 0:512], AF.Identity, bias=_n, scale=_r),
                       r=["x_sb", f"c_rstd{pb}", f"c_nmr{pb}"], w=["ln_scr"])
                    op("act", lambda e, _s=sl, _r=rstd, _n=nmr: e.activation(
                        ln_scr[:, 512:1024], tB["h2"][_s][:], AF.Identity,
                        bias=_n, scale=_r),
                       r=["h2B_%d" % sl, f"c_rstd{pb}", f"c_nmr{pb}"], w=["ln_scr"])
                    for g0 in (0, 4):
                        for gi in range(4):
                            j = g0 + gi
                            op("pe", lambda e, _j=j, _g=gi: e.transpose(
                                tr32[:, _g * 128:_g * 128 + 128],
                                ln_scr[:, _j * 128:_j * 128 + 128], id32[:]),
                               r=["ln_scr", "id32"], w=["tr32"])
                        for gi in range(4):
                            j = g0 + gi
                            op("act", lambda e, _j=j, _g=gi, _ii=ii, _l=l: e.activation(
                                tB["ln2T"][:, _j * 512 + _ii * 128:_j * 512 + _ii * 128 + 128],
                                tr32[:, _g * 128:_g * 128 + 128], AF.Identity,
                                bias=lnb[_l][:, _j:_j + 1], scale=lng[_l][:, _j:_j + 1]),
                               r=["tr32", f"lng{l}", f"lnb{l}"], w=["ln2T"])
                for ot in range(2 * NHT):
                    col0, ow = w1_off[ot]
                    wb = ot % 2
                    op("sp", lambda e, _l=l, _o=ot, _w=ow, _b=wb: e.dma_start(
                        tB["w1"][_b][:, 0:8 * _w], d_w1img[_l, _o, :, 0:8 * _w],),
                       r=["w1img_d"], w=["w1bf_%d" % wb], dma=True)
                    pa = pP[2 + wb]
                    pan = PN[id(pa)]
                    for j in range(8):
                        op("pe", lambda e, _j=j, _p=pa, _b=wb, _w=ow: e.matmul(
                            _p[0:_w, 0:512], tB["w1"][_b][:, _j * _w:(_j + 1) * _w],
                            tB["ln2T"][:, _j * 512:_j * 512 + 512],
                            start=(_j == 0), stop=(_j == 7)),
                           r=["w1bf_%d" % wb, "ln2T"], w=[pan])
                    ht = ot // 2
                    if wb == 0:
                        op("act", lambda e, _p=pa, _w=ow, _h=ht, _l=l: e.activation(
                            tB["ag"][0:_w, 0:512], _p[0:_w, 0:512], AF.Identity,
                            bias=b1a[_l][0:_w, _h:_h + 1]),
                           r=[pan, f"b1a{l}"], w=["ag_a"])
                    else:
                        op("act", lambda e, _p=pa, _w=ow, _h=ht, _l=l: e.activation(
                            tB["ag"][0:_w, 1024:1536], _p[0:_w, 0:512], AF.Gelu,
                            bias=b1g[_l][0:_w, _h:_h + 1]),
                           r=[pan, f"b1g{l}"], w=["ag_g"])
                        op("dve", lambda e, _w=ow, _h=ht: e.tensor_mul(
                            tB["gg"][0:_w, _h * 512:_h * 512 + 512],
                            tB["ag"][0:_w, 0:512], tB["ag"][0:_w, 1024:1536]),
                           r=["ag_a", "ag_g"], w=["geglu"])
                for ii in range(4):
                    i = tc * 4 + ii
                    xt = x_sb[:, i * D:(i + 1) * D]
                    for n in range(2):
                        ps = pP[n]
                        psn = PN[id(ps)]
                        for kk in range(NHT):
                            kw = 128 if kk < NHT - 1 else HID_LAST
                            op("pe", lambda e, _n=n, _k=kk, _p=ps, _ii=ii, _kw=kw: e.matmul(
                                _p[:, 0:512],
                                tB["gg"][0:_kw, _k * 512 + _ii * 128:_k * 512 + _ii * 128 + 128],
                                tB["w2"][0:_kw, _k * D + _n * 512:_k * D + _n * 512 + 512],
                                start=(_k == 0), stop=False),
                               r=["geglu", "w2bf"], w=[psn])
                        op("pe", lambda e, _n=n, _p=ps, _l=l: e.matmul(
                            _p[:, 0:512], ones_bf[:],
                            tB["b2hi"][0:1, _n * 512:_n * 512 + 512],
                            start=False, stop=True),
                           r=["ones_bf", "b2hi"], w=[psn])
                    op("dve", lambda e, _x=xt: e.tensor_add(
                        _x[:, 0:512], _x[:, 0:512], pP[0][:]), r=["pP0", "x_sb"], w=["x_sb"])
                    op("dve", lambda e, _x=xt: e.tensor_add(
                        _x[:, 512:1024], _x[:, 512:1024], pP[1][:]),
                       r=["pP1", "x_sb"], w=["x_sb"])

        if PHASE_LIMIT == "all" and DEBUG_TAP is None:
            em.fence(prev_names, FINAL_T)
            for i in range(NT):
                sl = i % 2
                xt = x_sb[:, i * D:(i + 1) * D]
                op("sp", lambda e, _i=i, _s=sl: e.dma_start(
                    x1b[_s][:], d_x1[:, _i * D:(_i + 1) * D]),
                   r=["x1d"], w=[f"x1b{sl}"], dma=True)
                pb = i % 2
                rstd, nmr = ln_stats([(xt, D, "x_sb")], D, pb)
                op("dve", lambda e, _s=sl, _n=nmr: e.tensor_scalar_add(
                    x1b[_s][:], x1b[_s][:], _n),
                   r=[f"x1b{sl}", f"c_nmr{pb}"], w=[f"x1b{sl}"])
                op("dve", lambda e, _x=xt, _s=sl, _r=rstd: e.scalar_tensor_tensor(
                    _x, _x, _r, x1b[_s][:], ALU.mult, ALU.add),
                   r=[f"x1b{sl}", f"c_rstd{pb}", "x_sb"], w=["x_sb"])
        tok = op("sp", lambda e: e.dma_start(
            d_out.rearrange("(n p) d -> p n d", p=128), x_sb3),
            r=["x_sb"], w=["out_dram"], dma=True)
        em.final_wait("sp", tok)

        with nc.Block() as block:
            em.replay(block)
    return nc


def _prep_host(inputs):
    f = np.float32
    low_i, up_i, up_s = _masks()
    common = {
        "wqT": np.ascontiguousarray(inputs["L0_attn_wq_w"].T),
        "wkT": np.ascontiguousarray(inputs["L0_attn_wk_w"].T),
        "wvT": np.ascontiguousarray(inputs["L0_attn_wv_w"].T),
        "woT": np.ascontiguousarray(inputs["L0_attn_wo_w"].T),
        "brow4": np.concatenate([
            inputs["L0_attn_wq_b"], inputs["L0_attn_wk_b"],
            inputs["L0_attn_wv_b"], inputs["L0_attn_wo_b"]]).reshape(1, 4 * D),
        "qn2": np.tile(inputs["L0_attn_qn_w"], 2).reshape(128, 1),
        "kn2": np.tile(inputs["L0_attn_kn_w"], 2).reshape(128, 1),
        "rotpk": _host_tables(),
        "m_low": low_i, "m_upi": up_i, "m_ups": up_s,
        "id32": np.eye(128, dtype=f),
        "id16": np.eye(128, dtype=ml_dtypes.bfloat16),
    }
    for l in range(2):
        w1 = inputs["L0_ffn_w1"][l]
        b1 = inputs["L0_ffn_b1"][l]
        perm = []
        for m in range(NHT):
            aw = min(128, HID - m * 128)
            perm.extend(range(m * 128, m * 128 + aw))
            perm.extend(range(HID + m * 128, HID + m * 128 + aw))
        perm = np.asarray(perm)
        w1p = np.ascontiguousarray(w1[:, perm])
        b1p = b1[perm]
        b1av = np.zeros((128, NHT), f)
        b1gv = np.zeros((128, NHT), f)
        pos = 0
        for m in range(NHT):
            aw = min(128, HID - m * 128)
            b1av[0:aw, m] = b1p[pos:pos + aw]
            b1gv[0:aw, m] = b1p[pos + aw:pos + 2 * aw]
            pos += 2 * aw
        qs = inputs["L0_ffn_q_scale"][l]
        ks = inputs["L0_ffn_k_scale"][l]
        common.update({
            f"wqkv{l}": np.ascontiguousarray(inputs["L0_ffn_wqkv"][l]),
            f"wo{l}": np.ascontiguousarray(inputs["L0_ffn_wo"][l]),
            f"w1p{l}": w1p,
            f"b1a{l}": b1av, f"b1g{l}": b1gv,
            f"w2_{l}": np.ascontiguousarray(inputs["L0_ffn_w2"][l]),
            f"b2_{l}": inputs["L0_ffn_b2"][l].reshape(1, D),
            f"lng{l}": np.ascontiguousarray(inputs["L0_ffn_ln_g"][l].reshape(8, 128).T),
            f"lnb{l}": np.ascontiguousarray(inputs["L0_ffn_ln_b"][l].reshape(8, 128).T),
            f"qsrows{l}": np.concatenate([
                qs, np.concatenate([qs[32:], qs[:32]]),
                ks, np.concatenate([ks[32:], ks[:32]])]).reshape(1, 4 * HD),
        })
    return common


def kernel(**inputs):
    inputs = {k: np.asarray(v) for k, v in inputs.items()}
    common = _prep_host(inputs)
    nc = build()
    in_maps = []
    for c in range(8):
        m = dict(common)
        m["x"] = np.ascontiguousarray(inputs["L0_attn_x"][c])
        in_maps.append(m)
    res = run_bass_kernel_spmd(nc, in_maps, core_ids=list(range(8)))
    return np.stack([res.results[c]["out"] for c in range(8)], axis=0)


# version 06
# speedup vs baseline: 1.0260x; 1.0260x over previous
import numpy as np
import ml_dtypes
import concourse.bass as bass
import concourse.mybir as mybir
from concourse.bass_utils import run_bass_kernel_spmd

F32 = mybir.dt.float32
BF16 = mybir.dt.bfloat16
AF = mybir.ActivationFunctionType
ALU = mybir.AluOpType
AX = mybir.AxisListType

S, D, NT, HD = 2048, 1024, 16, 64
HEADS_O, HEADS_I, INNER = 16, 8, 512
HID, NHT, HID_LAST = 2730, 22, 42
EPS = 1e-5

PHASE_LIMIT = "all"
NT_LIM = NT
DEBUG_TAP = None

ENGINES = ("pe", "act", "dve", "pool", "sp")
QSEM = {"sp": "q_sp", "pool": "q_pool", "act": "q_act"}


class Buf:
    __slots__ = ("name", "w", "readers")

    def __init__(self, name):
        self.name = name
        self.w = None
        self.readers = []


class Emitter:
    def __init__(self, nc):
        self.nc = nc
        self.prog = {e: [] for e in ENGINES}
        self.sems = {}
        self.counts = {n: 0 for n in ("pe", "act", "dve", "pool", "sp")}
        self.lastwait = {}
        self.bufs = {}

    def make_sems(self, ctx):
        for nm in list(self.counts):
            self.sems[nm] = self.nc.alloc_semaphore("sem_" + nm)

    def dma_sem(self, bufname):
        nm = "d_" + bufname
        if nm not in self.sems:
            self.sems[nm] = self.nc.alloc_semaphore("sem_" + nm)
            self.counts[nm] = 0
        return nm

    def buf(self, name):
        if name not in self.bufs:
            self.bufs[name] = Buf(name)
        return self.bufs[name]

    PSUM_BUFS = frozenset(("pP0", "pP1", "pP2", "pP3", "simP", "simX", "tbf", "tr32"))

    def op(self, eng, fn, r=(), w=(), dma=False):
        reads = [self.buf(n) for n in r if n not in self.PSUM_BUFS]
        writes = [self.buf(n) for n in w] + [self.buf(n) for n in r
                                             if n in self.PSUM_BUFS]
        deps = []
        for b in reads:
            if b.w is not None:
                deps.append(b.w)
        for b in writes:
            if b.w is not None:
                deps.append(b.w)
            deps.extend(b.readers)
        mysem = self.dma_sem(w[0]) if dma else eng
        need = {}
        for (sem, val) in deps:
            if dma and sem == mysem:
                continue
            if sem == eng == "pe":
                continue
            if need.get(sem, 0) < val:
                need[sem] = val
        for sem, val in sorted(need.items()):
            if self.lastwait.get((eng, sem), 0) >= val:
                continue
            self.lastwait[(eng, sem)] = val
            self.prog[eng].append(
                lambda e, _s=sem, _v=val: e.wait_ge(self.sems[_s], _v))
        amt = 16 if dma else 1
        self.counts[mysem] += amt
        val = self.counts[mysem]
        self.prog[eng].append(
            lambda e, _f=fn, _s=mysem, _a=amt: _f(e).then_inc(self.sems[_s], _a))
        tok = (mysem, val)
        for b in reads:
            b.readers.append(tok)
        for b in writes:
            b.w = tok
            b.readers = []
        return tok

    def fence(self, old_names, new_names):
        toks = {}
        for n in old_names:
            b = self.bufs.get(n)
            if b is None:
                continue
            for t in ([b.w] if b.w else []) + b.readers:
                if toks.get(t[0], 0) < t[1]:
                    toks[t[0]] = t[1]
        tl = list(toks.items())
        for n in new_names:
            b = self.buf(n)
            b.readers = b.readers + tl
        for n in old_names:
            if n in self.bufs:
                self.bufs[n].w = None
                self.bufs[n].readers = []

    def final_wait(self, eng, tok):
        self.prog[eng].append(lambda e, _t=tok: e.wait_ge(self.sems[_t[0]], _t[1]))

    def replay(self, block):
        for eng, attr in (("sp", "sync"), ("pe", "tensor"), ("act", "scalar"),
                          ("dve", "vector"), ("pool", "gpsimd")):
            seq = self.prog[eng]

            def make(seq_):
                def run(e):
                    for f in seq_:
                        f(e)
                return run
            getattr(block, attr)(make(seq))


def _host_tables():
    half = np.arange(0, HD, 2, dtype=np.float64)
    inv_freq = 1.0 / 10000.0 ** (half / HD)
    t = np.arange(S, dtype=np.float64)
    freqs = t[:, None] * inv_freq[None, :]
    freqs = np.concatenate([freqs, freqs], axis=1)
    base = (half + 0.4 * HD) / (1.4 * HD)
    power = (t - S // 2) / 256.0
    scale = base[None, :] ** power[:, None]
    scale = np.concatenate([scale, scale], axis=1)
    cq = (np.cos(freqs) * scale).astype(np.float32)
    sq = (np.sin(freqs) * scale).astype(np.float32)
    ck = (np.cos(freqs) / scale).astype(np.float32)
    sk = (np.sin(freqs) / scale).astype(np.float32)
    packed = np.concatenate([cq, sq, ck, sk], axis=1)
    return np.ascontiguousarray(
        packed.reshape(NT, 128, 4 * HD).transpose(1, 0, 2).reshape(128, NT * 4 * HD))


def _masks():
    i = np.arange(128)[:, None]
    j = np.arange(128)[None, :]
    neg = np.float32(-1.0e30)
    bf = ml_dtypes.bfloat16
    low_i = np.where(i >= j, 0.0, neg).astype(np.float32).astype(bf)
    up_i = np.where(j >= i, 0.0, neg).astype(np.float32).astype(bf)
    up_s = np.where(j > i, 0.0, neg).astype(np.float32).astype(bf)
    return low_i, up_i, up_s


def build():
    nc = bass.Bass("TRN2", target_bir_lowering=False)
    dram = {}

    def din(name, shape, dt=F32):
        dram[name] = nc.dram_tensor(name, list(shape), dt, kind="ExternalInput").ap()

    din("x", (S, D))
    for w in ("wqT", "wkT", "wvT", "woT"):
        din(w, (D, D))
    din("brow4", (1, 4 * D))
    din("qn2", (128, 1))
    din("kn2", (128, 1))
    din("rotpk", (128, NT * 4 * HD))
    din("m_low", (128, 128), BF16)
    din("m_upi", (128, 128), BF16)
    din("m_ups", (128, 128), BF16)
    din("id32", (128, 128), F32)
    din("id16", (128, 128), BF16)
    for l in range(2):
        din(f"wqkv{l}", (D, 3 * INNER))
        din(f"wo{l}", (INNER, D))
        din(f"w1p{l}", (D, 2 * HID))
        din(f"b1a{l}", (128, NHT))
        din(f"b1g{l}", (128, NHT))
        din(f"w2_{l}", (HID, D))
        din(f"b2_{l}", (1, D))
        din(f"lng{l}", (128, 8))
        din(f"lnb{l}", (128, 8))
        din(f"qsrows{l}", (1, 4 * HD))
    d_out = nc.dram_tensor("out", [S, D], F32, kind="ExternalOutput").ap()
    d_x1 = nc.dram_tensor("x1_scratch", [128, NT * D], F32, kind="Internal").ap()
    d_w1img = nc.dram_tensor("w1img", [2, 2 * NHT, 128, 1024], BF16, kind="Internal").ap()

    from contextlib import ExitStack
    with ExitStack() as ctx:
        em = Emitter(nc)
        em.make_sems(ctx)

        def sba(name, shape, dt=F32):
            return nc.alloc_sbuf_tensor(name, list(shape), dt)

        x_sb = sba("x_sb", (128, NT * D))
        ln_scr = sba("ln_scr", (128, D))
        sq_scr = sba("sq_scr", (128, D))
        cols = sba("cols", (128, 384))
        m_low = sba("m_low_sb", (128, 128), BF16)
        m_upi = sba("m_upi_sb", (128, 128), BF16)
        m_ups = sba("m_ups_sb", (128, 128), BF16)
        id32 = sba("id32_sb", (128, 128))
        id16 = sba("id16_sb", (128, 128), BF16)
        ones_bf = sba("ones_bf", (1, 128), BF16)
        qn2 = sba("qn2_sb", (128, 1))
        kn2 = sba("kn2_sb", (128, 1))
        lng = [sba(f"lng{l}_sb", (128, 8)) for l in range(2)]
        lnb = [sba(f"lnb{l}_sb", (128, 8)) for l in range(2)]
        b1a = [sba(f"b1a{l}_sb", (128, NHT)) for l in range(2)]
        b1g = [sba(f"b1g{l}_sb", (128, NHT)) for l in range(2)]
        wld = [sba(f"wld{i}", (128, 1024)) for i in range(2)]
        w1stage = sba("w1stage", (128, 1024), BF16)

        ARENA = 122880
        lo, hi = nc.bump_sbuf(ARENA)
        assert hi - lo >= ARENA

        def ar(name, shape, dt, off):
            sz = int(np.prod(shape[1:])) * mybir.dt.size(dt)
            assert off + sz <= ARENA, (name, off, sz)
            return nc.alloc_sbuf_tensor_at(name, list(shape), dt, offset=lo + off)

        wT_bf = {w: ar(w + "_bf", (128, 8 * D), BF16, k * 16384)
                 for k, w in enumerate(("wqT", "wkT", "wvT", "woT"))}
        q_scrO = ar("q_scrO", (128, D), F32, 65536)
        k_scrO = ar("k_scrO", (128, D), F32, 69632)
        ltO = [ar(f"ltO{i}", (128, D), BF16, 73728 + i * 2048) for i in range(2)]
        qhTO = [ar(f"qhTO{i}", (128, D), BF16, 77824 + i * 2048) for i in range(2)]
        khT3 = ar("khT3", (128, 4 * D), BF16, 81920)
        v3 = ar("v3", (128, 4 * D), BF16, 90112)
        pbO = [ar(f"pbO{i}", (128, 512), BF16, 98304 + i * 1024) for i in range(2)]
        ptO = [ar(f"ptO{i}", (128, 512), BF16, 100352 + i * 1024) for i in range(2)]
        obO = [ar(f"obO{i}", (128, D), BF16, 102400 + i * 2048) for i in range(2)]
        otO = [ar(f"otO{i}", (128, D), BF16, 106496 + i * 2048) for i in range(2)]
        brow4_hi = ar("brow4_hi", (1, 4 * D), BF16, 110592)
        OUTER_T = (["wqT_bf", "wkT_bf", "wvT_bf", "woT_bf", "q_scrO", "k_scrO",
                    "khT3", "v3", "brow4_hi"] +
                   [f"{p}{i}" for p in ("ltO", "qhTO", "pbO", "ptO", "obO", "otO")
                    for i in range(2)])

        def innerA():
            t = {}
            t["wqkv"] = ar("wqkv_bf", (128, 8 * 3 * INNER), BF16, 0)
            t["woin"] = ar("woin_bf", (128, 4 * D), BF16, 24576)
            t["q_scr"] = ar("q_scrI", (128, INNER), F32, 32768)
            t["k_scr"] = ar("k_scrI", (128, INNER), F32, 34816)
            t["rh"] = ar("rhI", (128, INNER), F32, 36864)
            t["lt"] = [ar(f"ltI_{i}", (128, D), BF16, 38912 + i * 2048) for i in range(2)]
            t["qhT"] = [ar(f"qhTI_{i}", (128, INNER), BF16, 43008 + i * 1024) for i in range(2)]
            t["khT5"] = ar("khT5", (128, 6 * INNER), BF16, 45056)
            t["v5"] = ar("v5", (128, 6 * INNER), BF16, 51200)
            t["pb"] = [ar(f"pbI_{i}", (128, 640), BF16, 57344 + i * 1280) for i in range(2)]
            t["pt"] = [ar(f"ptI_{i}", (128, 640), BF16, 59904 + i * 1280) for i in range(2)]
            t["ob"] = [ar(f"obI_{i}", (128, INNER), BF16, 62464 + i * 1024) for i in range(2)]
            t["ot"] = [ar(f"otI_{i}", (128, INNER), BF16, 64512 + i * 1024) for i in range(2)]
            t["h2"] = [ar(f"h2A_{i}", (128, INNER), F32, 66560 + i * 2048) for i in range(2)]
            t["rb"] = [ar(f"rotb_{i}", (128, 4 * HD), F32, 70656 + i * 1024) for i in range(2)]
            t["abt"] = ar("abt", (128, 4 * HD), F32, 72704)
            names = (["wqkv_bf", "woin_bf", "q_scrI", "k_scrI",
                      "rhI", "khT5", "v5", "abt",
                      "w2bf", "b2hi"] +
                     [f"{p}_{i}" for p in ("ltI", "qhTI", "pbI", "ptI", "obI", "otI",
                                           "h2A", "rotb") for i in range(2)])
            return t, names

        def innerB():
            t = {}
            t["w2"] = ar("w2bf", (128, NHT * D), BF16, 73728)
            t["gg"] = ar("geglu", (128, NHT * 512), BF16, 0)
            t["ln2T"] = ar("ln2T", (128, 8 * 512), BF16, 22528)
            t["w1"] = [ar(f"w1bf_{i}", (128, 1024), BF16, 30720 + i * 2048) for i in range(2)]
            t["ag"] = ar("agscr", (128, 2048), F32, 34816)
            t["h2"] = [ar(f"h2B_{i}", (128, INNER), F32, 43008 + i * 2048) for i in range(2)]
            t["b2hi"] = ar("b2hi", (1, D), BF16, 118784)
            names = (["geglu", "ln2T", "agscr"] +
                     [] +
                     [f"w1bf_{i}" for i in range(2)] + [f"h2B_{i}" for i in range(2)])
            return t, names

        x1b = [ar(f"x1b{i}", (128, D), F32, i * 4096) for i in range(2)]
        FINAL_T = ["x1b0", "x1b1"]

        pP = [nc.alloc_psum_tensor(f"pP{i}", [128, 512], F32) for i in range(4)]
        simP = nc.alloc_psum_tensor("simP", [128, 512], F32)
        simX = nc.alloc_psum_tensor("simX", [128, 128], F32)
        tbf = nc.alloc_psum_tensor("tbf", [128, 1024], BF16)
        tr32 = nc.alloc_psum_tensor("tr32", [128, 512], F32)
        PN = {id(pP[0]): "pP0", id(pP[1]): "pP1", id(pP[2]): "pP2", id(pP[3]): "pP3",
              id(simP): "simP", id(simX): "simX", id(tbf): "tbf", id(tr32): "tr32"}

        op = em.op

        def stats_bank(p):
            base = 8 * p
            return [cols[:, base + i:base + i + 1] for i in range(8)]
        sums_bank = [cols[:, 16:32], cols[:, 32:48]]
        c_rdens = cols[:, 48:64]
        c_nmax = [cols[:, 64 + i:65 + i] for i in range(2)]
        c_nmaxx = [cols[:, 66 + i:67 + i] for i in range(2)]
        c_denx = [cols[:, 68 + i:69 + i] for i in range(2)]
        qn_eff = cols[:, 70:71]
        c_eight = cols[:, 71:72]
        c_eps = cols[:, 72:73]
        rq_bank = [cols[:, 96:112], cols[:, 112:128]]
        c_s1, c_mu, c_m2, c_var, c_rstd, c_nmr, c_t6, c_t7 = stats_bank(0)

        x_sb3 = bass.AP(x_sb, 0, [[NT * D, 128], [D, NT], [1, D]])
        op("sp", lambda e: e.dma_start(
            x_sb3, dram["x"].rearrange("(n p) d -> p n d", p=128)),
            w=["x_sb"], dma=True)
        for t, n in ((m_low, "m_low"), (m_upi, "m_upi"), (m_ups, "m_ups"),
                     (id32, "id32"), (id16, "id16"), (qn2, "qn2"), (kn2, "kn2")):
            op("sp", lambda e, _t=t, _n=n: e.dma_start(_t[:], dram[_n]), w=[n], dma=True)
        for l in range(2):
            for n, t in ((f"lng{l}", lng[l]), (f"lnb{l}", lnb[l]),
                         (f"b1a{l}", b1a[l]), (f"b1g{l}", b1g[l])):
                op("sp", lambda e, _t=t, _n=n: e.dma_start(_t[:], dram[_n]), w=[n], dma=True)
        op("dve", lambda e: e.memset(ones_bf[:], 1.0), w=["ones_bf"])
        op("dve", lambda e: e.memset(c_eps, EPS), w=["c_eps"])
        op("dve", lambda e: e.memset(c_eight, 8.0), w=["c_eight"])
        op("dve", lambda e: e.tensor_scalar_mul(qn_eff, qn2[:], 0.125), r=["qn2"], w=["c_qe"])

        def load_cast(dst_bf, dst_name, dram_name, n_rows, n_cols, cast_eng="pool"):
            nj = (n_rows + 127) // 128
            for j in range(nj):
                rows = min(128, n_rows - j * 128)
                for c0 in range(0, n_cols, 1024):
                    cw = min(1024, n_cols - c0)
                    slot = load_cast.k % 2
                    load_cast.k += 1
                    op("sp", lambda e, _j=j, _c0=c0, _cw=cw, _r=rows, _s=slot,
                       _d=dram_name: e.dma_start(
                           wld[_s][0:_r, 0:_cw],
                           dram[_d][_j * 128:_j * 128 + _r, _c0:_c0 + _cw]),
                       w=["wld%d" % slot], dma=True)
                    op(cast_eng, lambda e, _j=j, _c0=c0, _cw=cw, _r=rows, _s=slot:
                       e.tensor_copy(
                           dst_bf[0:_r, _j * n_cols + _c0:_j * n_cols + _c0 + _cw],
                           wld[_s][0:_r, 0:_cw]),
                       r=["wld%d" % slot], w=[dst_name])
        load_cast.k = 0

        def ln_stats(srcs, total_d, pb=0):
            c_s1, c_mu, c_m2, c_var, c_rstd, c_nmr, c_t6, c_t7 = stats_bank(pb)
            sx = str(pb)
            for idx, (apv, wdt, rn) in enumerate(srcs):
                if idx == 0:
                    op("dve", lambda e, _a=apv, _c=c_s1: e.tensor_reduce(_c, _a, AX.X, ALU.add),
                       r=[rn], w=["c_s1" + sx])
                    op("act", lambda e, _a=apv, _w=wdt, _c=c_t6: e.activation(
                        sq_scr[:, 0:_w], _a, AF.Square, accum_out=_c),
                       r=[rn], w=["sq_scr", "c_t6" + sx])
                else:
                    op("dve", lambda e, _a=apv, _c=c_t7: e.tensor_reduce(_c, _a, AX.X, ALU.add),
                       r=[rn], w=["c_t7" + sx])
                    op("dve", lambda e, _c1=c_s1, _c2=c_t7: e.tensor_add(_c1, _c1, _c2),
                       r=["c_t7" + sx, "c_s1" + sx], w=["c_s1" + sx])
                    op("act", lambda e, _a=apv, _w=wdt, _c=c_t7: e.activation(
                        sq_scr[:, 0:_w], _a, AF.Square, accum_out=_c),
                       r=[rn], w=["sq_scr", "c_t7" + sx])
                    op("dve", lambda e, _c1=c_t6, _c2=c_t7: e.tensor_add(_c1, _c1, _c2),
                       r=["c_t7" + sx, "c_t6" + sx], w=["c_t6" + sx])
            inv = 1.0 / total_d
            op("dve", lambda e, _a=c_mu, _b=c_s1: e.tensor_scalar_mul(_a, _b, inv),
               r=["c_s1" + sx], w=["c_mu" + sx])
            op("dve", lambda e, _a=c_m2, _b=c_mu: e.tensor_mul(_a, _b, _b),
               r=["c_mu" + sx], w=["c_m2" + sx])
            op("dve", lambda e, _a=c_var, _b=c_t6: e.tensor_scalar_mul(_a, _b, inv),
               r=["c_t6" + sx], w=["c_var" + sx])
            op("dve", lambda e, _a=c_var, _b=c_m2: e.tensor_sub(_a, _a, _b),
               r=["c_var" + sx, "c_m2" + sx], w=["c_var" + sx])
            op("act", lambda e, _a=c_var: e.activation(_a, _a, AF.Sqrt, bias=c_eps, scale=1.0),
               r=["c_var" + sx, "c_eps"], w=["c_var" + sx])
            op("dve", lambda e, _a=c_rstd, _b=c_var: e.reciprocal(_a, _b),
               r=["c_var" + sx], w=["c_rstd" + sx])
            op("dve", lambda e, _a=c_nmr, _b=c_mu, _c=c_rstd: e.scalar_tensor_tensor(
                _a, _b, -1.0, _c, ALU.mult, ALU.mult),
               r=["c_mu" + sx, "c_rstd" + sx], w=["c_nmr" + sx])
            return c_rstd, c_nmr

        def tr_f32(chunks, evacs):
            for g0 in range(0, len(chunks), 4):
                grp = chunks[g0:g0 + 4]
                for gi, (src, rn) in enumerate(grp):
                    op("pe", lambda e, _s=src, _g=gi: e.transpose(
                        tr32[:, _g * 128:_g * 128 + 128], _s, id32[:]),
                       r=[rn, "id32"], w=["tr32"])
                evacs(g0, len(grp))

        for k, w in enumerate(("wqT", "wkT", "wvT", "woT")):
            load_cast(wT_bf[w], w + "_bf", w, D, D)
        for k in range(4):
            slot = load_cast.k % 2
            load_cast.k += 1
            op("sp", lambda e, _k=k, _s=slot: e.dma_start(
                wld[_s][0:1, 0:D], dram["brow4"][0:1, _k * D:(_k + 1) * D]),
               w=["wld%d" % slot], dma=True)
            op("pool", lambda e, _k=k, _s=slot: e.tensor_copy(
                brow4_hi[0:1, _k * D:(_k + 1) * D], wld[_s][0:1, 0:D]),
               r=["wld%d" % slot], w=["brow4_hi"])
        w1_off_pre = []
        pos = 0
        for m in range(NHT):
            aw = min(128, HID - m * 128)
            w1_off_pre.append((pos, aw))
            w1_off_pre.append((pos + aw, aw))
            pos += 2 * aw
        for l in range(2):
            for ot in range(2 * NHT):
                col0, ow = w1_off_pre[ot]
                slot = load_cast.k % 2
                load_cast.k += 1
                op("sp", lambda e, _c=col0, _w=ow, _s=slot, _l=l: e.dma_start(
                    bass.AP(wld[_s], 0, [[1024, 128], [_w, 8], [1, _w]]),
                    dram[f"w1p{_l}"][:, _c:_c + _w].rearrange("(a p) o -> p a o", p=128)),
                   w=["wld%d" % slot], dma=True)
                op("pool", lambda e, _s=slot, _w=ow: e.tensor_copy(
                    w1stage[:, 0:8 * _w], wld[_s][:, 0:8 * _w]),
                   r=["wld%d" % slot], w=["w1stage"])
                op("sp", lambda e, _l=l, _o=ot, _w=ow: e.dma_start(
                    d_w1img[_l, _o, :, 0:8 * _w], w1stage[:, 0:8 * _w]),
                   r=["w1stage"], w=["w1img_d"], dma=True)

        def proj_1024(lt_t, lt_name, w_bf, w_name, psA, psB, bias_idx):
            for n in range(2):
                ps = (psA, psB)[n]
                psn = PN[id(ps)]
                for j in range(8):
                    op("pe", lambda e, _n=n, _j=j, _p=ps, _lt=lt_t: e.matmul(
                        _p[:, 0:512], _lt[:, _j * 128:_j * 128 + 128],
                        w_bf[:, _j * D + _n * 512:_j * D + _n * 512 + 512],
                        start=(_j == 0), stop=False),
                       r=[lt_name, w_name], w=[psn])
                op("pe", lambda e, _n=n, _p=ps, _b=bias_idx: e.matmul(
                    _p[:, 0:512], ones_bf[:],
                    brow4_hi[0:1, _b * D + _n * 512:_b * D + _n * 512 + 512],
                    start=False, stop=True),
                   r=["ones_bf", "brow4_hi"], w=[psn])

        def rmsnorm16(scr, scr_name, pb, rq=False):
            if rq:
                su = rq_bank[pb]
                sn = f"c_rq{pb}"
            else:
                su = sums_bank[pb]
                sn = f"c_sums{pb}"
            op("pool", lambda e, _s=scr: e.tensor_scalar_min(_s[:], _s[:], 1.0e4),
               r=[scr_name], w=[scr_name])
            op("pool", lambda e, _s=scr: e.tensor_scalar_max(_s[:], _s[:], -1.0e4),
               r=[scr_name], w=[scr_name])
            op("act", lambda e, _s=scr: e.activation(sq_scr[:], _s[:], AF.Square),
               r=[scr_name], w=["sq_scr"])
            red3 = bass.AP(sq_scr, 0, [[D, 128], [64, 16], [1, 64]])
            op("dve", lambda e, _r=red3, _u=su: e.tensor_reduce(_u, _r, AX.X, ALU.add),
               r=["sq_scr"], w=[sn])
            op("dve", lambda e, _u=su: e.tensor_scalar(
                _u, _u, 1.0 / 64, 1.0e-5, ALU.mult, ALU.max),
               r=[sn], w=[sn])
            op("act", lambda e, _u=su: e.activation(_u, _u, AF.Sqrt, bias=c_eps, scale=1.0),
               r=[sn, "c_eps"], w=[sn])
            op("dve", lambda e, _u=su: e.reciprocal(_u, _u), r=[sn], w=[sn])
            if not rq:
                for h in range(16):
                    op("dve", lambda e, _s=scr, _h=h, _u=su: e.tensor_scalar_mul(
                        _s[:, _h * 64:_h * 64 + 64], _s[:, _h * 64:_h * 64 + 64],
                        _u[:, _h:_h + 1]), r=[scr_name, sn], w=[scr_name])

        def P1oA(i):
            pb = i % 2
            xt = x_sb[:, i * D:(i + 1) * D]
            rstd, nmr = ln_stats([(xt, D, "x_sb")], D, pb)
            op("act", lambda e, _x=xt, _r=rstd, _n=nmr: e.activation(
                ln_scr[:], _x, AF.Identity, bias=_n, scale=_r),
               r=["x_sb", f"c_rstd{pb}", f"c_nmr{pb}"], w=["ln_scr"])

        def P1oB(i):
            sl = i % 2
            k4 = i % 4
            pb = i % 2
            xt = x_sb[:, i * D:(i + 1) * D]

            def ev_lt(g0, glen, _sl=sl):
                op("act", lambda e, _g0=g0, _n=glen * 128: e.activation(
                    ltO[_sl][:, _g0 * 128:_g0 * 128 + _n], tr32[:, 0:_n], AF.Identity),
                   r=["tr32"], w=[f"ltO{_sl}"])
            tr_f32([(ln_scr[:, j * 128:j * 128 + 128], "ln_scr") for j in range(8)], ev_lt)
            proj_1024(ltO[sl], f"ltO{sl}", wT_bf["wqT"], "wqT_bf", pP[0], pP[1], 0)
            proj_1024(ltO[sl], f"ltO{sl}", wT_bf["wkT"], "wkT_bf", pP[2], pP[3], 1)
            op("dve", lambda e: e.tensor_copy(q_scrO[:, 0:512], pP[0][:]), r=["pP0"], w=["q_scrO"])
            op("dve", lambda e: e.tensor_copy(q_scrO[:, 512:1024], pP[1][:]), r=["pP1"], w=["q_scrO"])
            op("act", lambda e: e.activation(k_scrO[:, 0:512], pP[2][:], AF.Identity),
               r=["pP2"], w=["k_scrO"])
            op("act", lambda e: e.activation(k_scrO[:, 512:1024], pP[3][:], AF.Identity),
               r=["pP3"], w=["k_scrO"])
            proj_1024(ltO[sl], f"ltO{sl}", wT_bf["wvT"], "wvT_bf", pP[0], pP[1], 2)
            op("act", lambda e, _k=k4: e.activation(
                v3[:, _k * D:_k * D + 512], pP[0][:], AF.Identity), r=["pP0"], w=["v3"])
            op("act", lambda e, _k=k4: e.activation(
                v3[:, _k * D + 512:_k * D + 1024], pP[1][:], AF.Identity), r=["pP1"], w=["v3"])
            rmsnorm16(q_scrO, "q_scrO", pb, rq=True)
            rmsnorm16(k_scrO, "k_scrO", pb)

        def P1oB2(i):
            sl = i % 2
            k4 = i % 4

            def ev_q(g0, glen, _sl=sl):
                op("act", lambda e, _g0=g0, _n=glen * 128: e.activation(
                    qhTO[_sl][:, _g0 * 128:_g0 * 128 + _n], tr32[:, 0:_n],
                    AF.Identity, scale=qn_eff),
                   r=["tr32", "c_qe"], w=[f"qhTO{_sl}"])
            tr_f32([(q_scrO[:, j * 128:j * 128 + 128], "q_scrO") for j in range(8)], ev_q)

            def ev_k(g0, glen, _k4=k4):
                op("act", lambda e, _g0=g0, _n=glen * 128: e.activation(
                    khT3[:, _k4 * D + _g0 * 128:_k4 * D + _g0 * 128 + _n], tr32[:, 0:_n],
                    AF.Identity, scale=kn2[:]),
                   r=["tr32", "kn2"], w=["khT3"])
            tr_f32([(k_scrO[:, j * 128:j * 128 + 128], "k_scrO") for j in range(8)], ev_k)

        def P2o(i):
            sl = i % 2
            pb = i % 2
            xt = x_sb[:, i * D:(i + 1) * D]
            kstart = max(0, i - 2)
            nk = i - kstart + 1
            ncols = nk * 128

            def emit_ptav(h):
                hs = h % 2
                for c in range(nk):
                    op("pe", lambda e, _c=c, _hs=hs: e.transpose(
                        tbf[:, _c * 128:_c * 128 + 128],
                        pbO[_hs][:, _c * 128:_c * 128 + 128], id16[:]),
                       r=[f"pbO{hs}", "id16"], w=["tbf"])
                op("act", lambda e, _n=ncols, _hs=hs: e.activation(
                    ptO[_hs][:, 0:_n], tbf[:, 0:_n], AF.Identity),
                   r=["tbf"], w=[f"ptO{hs}"])
                bank = pP[0] if h < 8 else pP[1]
                bn = PN[id(bank)]
                for c in range(nk):
                    slot = (kstart + c) % 4
                    st_ = (h % 8 == 0) and (c == 0)
                    sp_ = (h % 8 == 7) and (c == nk - 1)
                    op("pe", lambda e, _c=c, _hs=hs, _b=bank, _co=(h % 8) * 64,
                       _st=st_, _sp=sp_, _slot=slot, _h=h: e.matmul(
                        _b[:, _co:_co + 64], ptO[_hs][:, _c * 128:_c * 128 + 128],
                        v3[:, _slot * D + _h * 64:_slot * D + _h * 64 + 64],
                        start=_st, stop=_sp),
                       r=[f"ptO{hs}", "v3"], w=[bn])

            pend = None
            for h in range(HEADS_O):
                hp = (h % 2) * 64
                hc = (h // 2) * 128
                hs = h % 2
                for c in range(nk):
                    slot = (kstart + c) % 4
                    op("pe", lambda e, _c=c, _s=slot, _hp=hp, _hc=hc, _sl=sl,
                       _st=(c == 0): e.matmul(
                        simP[:, _c * 128:_c * 128 + 128],
                        qhTO[_sl][_hp:_hp + 64, _hc:_hc + 128],
                        khT3[_hp:_hp + 64, _s * D + _hc:_s * D + _hc + 128],
                        start=_st, stop=False),
                       r=[f"qhTO{sl}", "khT3"], w=["simP"])
                if nk == 3:
                    op("pe", lambda e: e.matmul(simP[:, 0:128], id16[:], m_ups[:],
                                                start=False, stop=False),
                       r=["id16", "m_ups"], w=["simP"])
                op("pe", lambda e, _o=(nk - 1) * 128: e.matmul(
                    simP[:, _o:_o + 128], id16[:], m_low[:], start=False, stop=True),
                   r=["id16", "m_low"], w=["simP"])
                if pend is not None:
                    emit_ptav(pend)
                nmv = c_nmax[h % 2]
                nmn = f"c_nmax{h % 2}"
                rdn = f"c_rd{h}"
                rqc = rq_bank[pb][:, h:h + 1]
                rqn = f"c_rq{pb}"
                op("dve", lambda e, _n=ncols, _m=nmv: e.tensor_reduce(
                    _m, simP[:, 0:_n], AX.X, ALU.max, negate=True),
                   r=["simP"], w=[nmn])
                op("dve", lambda e, _m=nmv, _q=rqc: e.tensor_mul(_m, _m, _q),
                   r=[nmn, rqn], w=[nmn])
                op("act", lambda e, _n=ncols, _hs=hs, _m=nmv, _h=h, _q=rqc: e.activation(
                    pbO[_hs][:, 0:_n], simP[:, 0:_n], AF.Exp, bias=_m, scale=_q,
                    accum_out=c_rdens[:, _h:_h + 1]),
                   r=["simP", nmn, rqn], w=[f"pbO{hs}", rdn])
                op("dve", lambda e, _h=h: e.reciprocal(
                    c_rdens[:, _h:_h + 1], c_rdens[:, _h:_h + 1]), r=[rdn], w=[rdn])
                pend = h
            emit_ptav(pend)
            for h in range(HEADS_O):
                bank = pP[0] if h < 8 else pP[1]
                bn = PN[id(bank)]
                op("act", lambda e, _b=bank, _co=(h % 8) * 64, _h=h, _s=sl: e.activation(
                    obO[_s][:, _h * 64:_h * 64 + 64], _b[:, _co:_co + 64], AF.Identity,
                    scale=c_rdens[:, _h:_h + 1]),
                   r=[bn, f"c_rd{h}"], w=[f"obO{sl}"])
            for j in range(8):
                op("pe", lambda e, _j=j, _s=sl: e.transpose(
                    tbf[:, _j * 128:_j * 128 + 128],
                    obO[_s][:, _j * 128:_j * 128 + 128], id16[:]),
                   r=[f"obO{sl}", "id16"], w=["tbf"])
            op("dve", lambda e, _s=sl: e.tensor_copy(otO[_s][:], tbf[:]),
               r=["tbf"], w=[f"otO{sl}"])
            proj_1024(otO[sl], f"otO{sl}", wT_bf["woT"], "woT_bf", pP[2], pP[3], 3)
            rstd, nmr = ln_stats([(pP[2][:], 512, "pP2"), (pP[3][:], 512, "pP3")], D, pb)
            op("dve", lambda e, _x=xt, _n=nmr: e.tensor_scalar_add(_x, _x, _n),
               r=["x_sb", f"c_nmr{pb}"], w=["x_sb"])
            op("dve", lambda e, _x=xt, _r=rstd: e.scalar_tensor_tensor(
                _x[:, 0:512], pP[2][:], _r, _x[:, 0:512], ALU.mult, ALU.add),
               r=["pP2", f"c_rstd{pb}", "x_sb"], w=["x_sb"])
            op("dve", lambda e, _x=xt, _r=rstd: e.scalar_tensor_tensor(
                _x[:, 512:1024], pP[3][:], _r, _x[:, 512:1024], ALU.mult, ALU.add),
               r=["pP3", f"c_rstd{pb}", "x_sb"], w=["x_sb"])

        NTL = min(NT, NT_LIM)
        if NTL:
            P1oA(0)
            P1oB(0)
            P1oB2(0)
            if NTL > 1:
                P1oA(1)
            for i in range(NTL):
                if i + 1 < NTL:
                    P1oB(i + 1)
                if i + 2 < NTL:
                    P1oA(i + 2)
                P2o(i)
                if i + 1 < NTL:
                    P1oB2(i + 1)

        op("pool", lambda e: e.dma_start(d_x1, x_sb[:]), r=["x_sb"], w=["x1d"], dma=True)
        if not (PHASE_LIMIT == "outer" and DEBUG_TAP):
            for i in range(NT):
                xt = x_sb[:, i * D:(i + 1) * D]
                pb = i % 2
                rstd, nmr = ln_stats([(xt, D, "x_sb")], D, pb)
                op("act", lambda e, _x=xt, _r=rstd, _n=nmr: e.activation(
                    _x, _x, AF.Identity, bias=_n, scale=_r),
                   r=["x_sb", f"c_rstd{pb}", f"c_nmr{pb}", "x1d"], w=["x_sb"])

        n_inner = {"outer": 0, "layer0": 1, "all": 2}[PHASE_LIMIT]
        prev_names = OUTER_T
        tA, namesA = (innerA() if n_inner else (None, None))
        tB, namesB = (innerB() if n_inner else (None, None))
        for l in range(n_inner):
            em.fence(prev_names, namesA)
            prev_names = namesA
            load_cast(tA["wqkv"], "wqkv_bf", f"wqkv{l}", D, 3 * INNER)
            load_cast(tA["woin"], "woin_bf", f"wo{l}", INNER, D)
            qsb_ap = cols[:, 128:384]
            slot = load_cast.k % 2
            load_cast.k += 1
            op("sp", lambda e, _l=l, _s=slot: e.dma_start(
                wld[_s][0:1, 0:256], dram[f"qsrows{_l}"]), w=["wld%d" % slot], dma=True)
            op("dve", lambda e: e.memset(sq_scr[0:1, 0:128], 1.0), w=["sq_scr"])
            op("pe", lambda e, _s=slot: e.matmul(
                tr32[:, 0:256], sq_scr[0:1, 0:128], wld[_s][0:1, 0:256],
                start=True, stop=True),
               r=["sq_scr", "wld%d" % slot], w=["tr32"])
            op("dve", lambda e: e.tensor_copy(qsb_ap, tr32[:, 0:256]),
               r=["tr32"], w=["qsb"])

            def shift_tile(i, dst, dst_name):
                op("pool", lambda e, _i=i, _d=dst: e.dma_start(
                    _d[1:128, :], x_sb[0:127, _i * D + 512:_i * D + 1024]),
                   r=["x_sb"], w=[dst_name], dma=True)
                if i == 0:
                    op("dve", lambda e, _d=dst: e.memset(_d[0:1, :], 0.0), w=[dst_name])
                else:
                    op("pool", lambda e, _i=i, _d=dst: e.dma_start(
                        _d[0:1, :], x_sb[127:128, (_i - 1) * D + 512:(_i - 1) * D + 1024]),
                       r=["x_sb"], w=[dst_name], dma=True)

            load_cast(tB["w2"], "w2bf", f"w2_{l}", HID, D)
            slot = load_cast.k % 2
            load_cast.k += 1
            op("sp", lambda e, _l=l, _s=slot: e.dma_start(
                wld[_s][0:1, 0:D], dram[f"b2_{_l}"]), w=["wld%d" % slot], dma=True)
            op("pool", lambda e, _s=slot: e.tensor_copy(
                tB["b2hi"][0:1, :], wld[_s][0:1, 0:D]),
               r=["wld%d" % slot], w=["b2hi"])
            def l2rot(scr, scr_name, a_off, b_off, pb, rq=False):
                if rq:
                    sums8 = rq_bank[pb][:, 0:8]
                    sn = f"c_rq{pb}"
                else:
                    sums8 = sums_bank[pb][:, 0:8]
                    sn = f"c_sums{pb}"
                op("act", lambda e, _s=scr: e.activation(
                    sq_scr[:, 0:512], _s[:], AF.Square), r=[scr_name], w=["sq_scr"])
                red3 = bass.AP(sq_scr, 0, [[D, 128], [64, 8], [1, 64]])
                op("dve", lambda e, _r=red3, _u=sums8: e.tensor_reduce(_u, _r, AX.X, ALU.add),
                   r=["sq_scr"], w=[sn])
                op("act", lambda e, _u=sums8: e.activation(_u, _u, AF.Sqrt),
                   r=[sn], w=[sn])
                op("dve", lambda e, _u=sums8: e.tensor_scalar_max(_u, _u, 1.0e-12),
                   r=[sn], w=[sn])
                op("dve", lambda e, _u=sums8: e.reciprocal(_u, _u), r=[sn], w=[sn])
                if not rq:
                    for h in range(8):
                        op("dve", lambda e, _s=scr, _h=h, _u=sums8: e.tensor_scalar_mul(
                            _s[:, _h * 64:_h * 64 + 64], _s[:, _h * 64:_h * 64 + 64],
                            _u[:, _h:_h + 1]), r=[scr_name, sn], w=[scr_name])
                vh = lambda t, off: bass.AP(t, off, [[t.ap().ap[0][0], 128], [64, 8], [1, 32]])
                a_bc = bass.AP(tA["abt"], a_off, [[4 * HD, 128], [0, 8], [1, 64]])
                b_bc = bass.AP(tA["abt"], b_off, [[4 * HD, 128], [0, 8], [1, 64]])
                op("dve", lambda e, _s=scr: e.tensor_scalar_mul(
                    vh(tA["rh"], 0), vh(_s, 32), -1.0), r=[scr_name], w=["rhI"])
                op("dve", lambda e, _s=scr: e.tensor_copy(vh(tA["rh"], 32), vh(_s, 0)),
                   r=[scr_name], w=["rhI"])
                op("dve", lambda e, _s=scr, _a=a_bc: e.tensor_mul(_s[:], _s[:], _a),
                   r=[scr_name, "abt"], w=[scr_name])
                op("dve", lambda e, _b=b_bc: e.tensor_mul(
                    tA["rh"][:], tA["rh"][:], _b), r=["rhI", "abt"], w=["rhI"])
                op("dve", lambda e, _s=scr: e.tensor_add(_s[:], _s[:], tA["rh"][:]),
                   r=[scr_name, "rhI"], w=[scr_name])

            def P1i(i):
                sl = i % 2
                k6 = i % 6
                pb = i % 2
                xt = x_sb[:, i * D:(i + 1) * D]
                shift_tile(i, tA["h2"][sl], "h2A_%d" % sl)
                op("sp", lambda e, _i=i, _s=sl: e.dma_start(
                    tA["rb"][_s][:], dram["rotpk"][:, _i * 256:(_i + 1) * 256]),
                   w=["rotb_%d" % sl], dma=True)
                op("dve", lambda e, _s=sl: e.tensor_mul(
                    tA["abt"][:], tA["rb"][_s][:], qsb_ap),
                   r=["rotb_%d" % sl, "qsb"], w=["abt"])

                def ev_lt(g0, glen, _sl=sl):
                    op("act", lambda e, _g0=g0, _n=glen * 128: e.activation(
                        tA["lt"][_sl][:, _g0 * 128:_g0 * 128 + _n], tr32[:, 0:_n], AF.Identity),
                       r=["tr32"], w=["ltI_%d" % _sl])
                tr_f32([(xt[:, j * 128:j * 128 + 128], "x_sb") for j in range(4)] +
                       [(tA["h2"][sl][:, j * 128:j * 128 + 128], "h2A_%d" % sl)
                        for j in range(4)], ev_lt)
                for n in range(3):
                    ps = pP[n]
                    psn = PN[id(ps)]
                    for j in range(8):
                        op("pe", lambda e, _n=n, _j=j, _p=ps, _s=sl: e.matmul(
                            _p[:, 0:512], tA["lt"][_s][:, _j * 128:_j * 128 + 128],
                            tA["wqkv"][:, _j * 1536 + _n * 512:_j * 1536 + _n * 512 + 512],
                            start=(_j == 0), stop=(_j == 7)),
                           r=["ltI_%d" % sl, "wqkv_bf"], w=[psn])
                op("dve", lambda e: e.tensor_copy(tA["q_scr"][:], pP[0][:]),
                   r=["pP0"], w=["q_scrI"])
                op("act", lambda e: e.activation(tA["k_scr"][:], pP[1][:], AF.Identity),
                   r=["pP1"], w=["k_scrI"])
                op("act", lambda e, _k=k6: e.activation(
                    tA["v5"][:, _k * INNER:(_k + 1) * INNER], pP[2][:], AF.Identity),
                   r=["pP2"], w=["v5"])
                l2rot(tA["q_scr"], "q_scrI", 0, HD, pb, rq=True)
                l2rot(tA["k_scr"], "k_scrI", 2 * HD, 3 * HD, pb)

            def P1i2(i):
                sl = i % 2
                k6 = i % 6
                for gi in range(4):
                    op("pe", lambda e, _g=gi: e.transpose(
                        tr32[:, _g * 128:_g * 128 + 128],
                        tA["q_scr"][:, _g * 128:_g * 128 + 128], id32[:]),
                       r=["q_scrI", "id32"], w=["tr32"])
                op("act", lambda e, _s=sl: e.activation(
                    tA["qhT"][_s][:], tr32[:], AF.Identity, scale=c_eight),
                   r=["tr32", "c_eight"], w=["qhTI_%d" % sl])
                for gi in range(4):
                    op("pe", lambda e, _g=gi: e.transpose(
                        tr32[:, _g * 128:_g * 128 + 128],
                        tA["k_scr"][:, _g * 128:_g * 128 + 128], id32[:]),
                       r=["k_scrI", "id32"], w=["tr32"])
                op("act", lambda e, _k=k6: e.activation(
                    tA["khT5"][:, _k * INNER:(_k + 1) * INNER], tr32[:], AF.Identity),
                   r=["tr32"], w=["khT5"])

            def P2i(i):
                sl = i % 2
                xt = x_sb[:, i * D:(i + 1) * D]
                kstart = max(0, i - 4)
                nk = i - kstart + 1
                first_av = [True]

                def emit_ptav(h):
                    hs = h % 2
                    for c in range(nk):
                        op("pe", lambda e, _c=c, _hs=hs: e.transpose(
                            tbf[:, _c * 128:_c * 128 + 128],
                            tA["pb"][_hs][:, _c * 128:_c * 128 + 128], id16[:]),
                           r=["pbI_%d" % hs, "id16"], w=["tbf"])
                    op("act", lambda e, _n=nk * 128, _hs=hs: e.activation(
                        tA["pt"][_hs][:, 0:_n], tbf[:, 0:_n], AF.Identity),
                       r=["tbf"], w=["ptI_%d" % hs])
                    for c in range(nk):
                        slot = (kstart + c) % 6
                        st_ = first_av[0]
                        first_av[0] = False
                        sp_ = (h == HEADS_I - 1) and (c == nk - 1)
                        op("pe", lambda e, _c=c, _hs=hs, _st=st_, _sp=sp_, _slot=slot, _h=h:
                           e.matmul(pP[3][:, _h * 64:_h * 64 + 64],
                                    tA["pt"][_hs][:, _c * 128:_c * 128 + 128],
                                    tA["v5"][:, _slot * INNER + _h * 64:_slot * INNER + _h * 64 + 64],
                                    start=_st, stop=_sp),
                           r=["ptI_%d" % hs, "v5"], w=["pP3"])

                pend = None
                for h in range(HEADS_I):
                    hp = (h % 2) * 64
                    hc = (h // 2) * 128
                    hs = h % 2
                    nP = min(nk, 4)
                    for c in range(nP):
                        slot = (kstart + c) % 6
                        op("pe", lambda e, _o=c * 128, _s=slot, _hp=hp, _hc=hc,
                           _sl=sl, _st=(c == 0): e.matmul(
                            simP[:, _o:_o + 128],
                            tA["qhT"][_sl][_hp:_hp + 64, _hc:_hc + 128],
                            tA["khT5"][_hp:_hp + 64, _s * INNER + _hc:_s * INNER + _hc + 128],
                            start=_st, stop=False),
                           r=["qhTI_%d" % sl, "khT5"], w=["simP"])
                    if nk == 5:
                        op("pe", lambda e: e.matmul(simP[:, 0:128], id16[:], m_upi[:],
                                                    start=False, stop=True),
                           r=["id16", "m_upi"], w=["simP"])
                        slot = i % 6
                        op("pe", lambda e, _s=slot, _hp=hp, _hc=hc, _sl=sl: e.matmul(
                            simX[:, 0:128],
                            tA["qhT"][_sl][_hp:_hp + 64, _hc:_hc + 128],
                            tA["khT5"][_hp:_hp + 64, _s * INNER + _hc:_s * INNER + _hc + 128],
                            start=True, stop=False),
                           r=["qhTI_%d" % sl, "khT5"], w=["simX"])
                        op("pe", lambda e: e.matmul(simX[:, 0:128], id16[:], m_low[:],
                                                    start=False, stop=True),
                           r=["id16", "m_low"], w=["simX"])
                    else:
                        op("pe", lambda e, _o=(nk - 1) * 128: e.matmul(
                            simP[:, _o:_o + 128], id16[:], m_low[:], start=False, stop=True),
                           r=["id16", "m_low"], w=["simP"])
                    if pend is not None:
                        emit_ptav(pend)
                    nPc = min(nk, 4) * 128
                    nmv = c_nmax[h % 2]
                    nmn = f"c_nmax{h % 2}"
                    rdn = f"c_rd{h}"
                    rdc = c_rdens[:, h:h + 1]
                    pbp = i % 2
                    rqc = rq_bank[pbp][:, h:h + 1]
                    rqn = f"c_rq{pbp}"
                    op("dve", lambda e, _n=nPc, _m=nmv: e.tensor_reduce(
                        _m, simP[:, 0:_n], AX.X, ALU.max, negate=True),
                       r=["simP"], w=[nmn])
                    if nk == 5:
                        xv = c_nmaxx[h % 2]
                        xn = f"c_nmaxx{h % 2}"
                        op("dve", lambda e, _x=xv: e.tensor_reduce(
                            _x, simX[:], AX.X, ALU.max, negate=True), r=["simX"], w=[xn])
                        op("dve", lambda e, _m=nmv, _x=xv: e.tensor_tensor(
                            _m, _m, _x, ALU.min), r=[nmn, xn], w=[nmn])
                    op("dve", lambda e, _m=nmv, _q=rqc: e.tensor_mul(_m, _m, _q),
                       r=[nmn, rqn], w=[nmn])
                    op("act", lambda e, _n=nPc, _hs=hs, _m=nmv, _c=rdc, _q=rqc: e.activation(
                        tA["pb"][_hs][:, 0:_n], simP[:, 0:_n], AF.Exp, bias=_m,
                        scale=_q, accum_out=_c),
                       r=["simP", nmn, rqn], w=["pbI_%d" % hs, rdn])
                    if nk == 5:
                        dxv = c_denx[h % 2]
                        dxn = f"c_denx{h % 2}"
                        op("act", lambda e, _hs=hs, _m=nmv, _x=dxv, _q=rqc: e.activation(
                            tA["pb"][_hs][:, 512:640], simX[:], AF.Exp, bias=_m,
                            scale=_q, accum_out=_x),
                           r=["simX", nmn, rqn], w=["pbI_%d" % hs, dxn])
                        op("dve", lambda e, _c=rdc, _x=dxv: e.tensor_add(_c, _c, _x),
                           r=[rdn, dxn], w=[rdn])
                    op("dve", lambda e, _c=rdc: e.reciprocal(_c, _c), r=[rdn], w=[rdn])
                    pend = h
                emit_ptav(pend)
                for h in range(HEADS_I):
                    op("act", lambda e, _h=h, _s=sl: e.activation(
                        tA["ob"][_s][:, _h * 64:_h * 64 + 64], pP[3][:, _h * 64:_h * 64 + 64],
                        AF.Identity, scale=c_rdens[:, _h:_h + 1]),
                       r=["pP3", f"c_rd{h}"], w=["obI_%d" % sl])
                for c in range(4):
                    op("pe", lambda e, _c=c, _s=sl: e.transpose(
                        tbf[:, _c * 128:_c * 128 + 128],
                        tA["ob"][_s][:, _c * 128:_c * 128 + 128], id16[:]),
                       r=["obI_%d" % sl, "id16"], w=["tbf"])
                op("dve", lambda e, _s=sl: e.tensor_copy(tA["ot"][_s][:], tbf[:, 0:512]),
                   r=["tbf"], w=["otI_%d" % sl])
                for n in range(2):
                    ps = pP[n]
                    psn = PN[id(ps)]
                    for j in range(4):
                        op("pe", lambda e, _n=n, _j=j, _p=ps, _s=sl: e.matmul(
                            _p[:, 0:512], tA["ot"][_s][:, _j * 128:_j * 128 + 128],
                            tA["woin"][:, _j * D + _n * 512:_j * D + _n * 512 + 512],
                            start=(_j == 0), stop=(_j == 3)),
                           r=["otI_%d" % sl, "woin_bf"], w=[psn])
                op("dve", lambda e, _x=xt: e.tensor_add(_x[:, 0:512], _x[:, 0:512], pP[0][:]),
                   r=["pP0", "x_sb"], w=["x_sb"])
                op("dve", lambda e, _x=xt: e.tensor_add(
                    _x[:, 512:1024], _x[:, 512:1024], pP[1][:]),
                   r=["pP1", "x_sb"], w=["x_sb"])

            P1i(0)
            P1i2(0)
            for i in range(NT):
                if i + 1 < NT:
                    P1i(i + 1)
                P2i(i)
                if i + 1 < NT:
                    P1i2(i + 1)

            em.fence(prev_names, namesB)
            prev_names = namesB
            w1_off = []
            pos = 0
            for m in range(NHT):
                aw = min(128, HID - m * 128)
                w1_off.append((pos, aw))
                w1_off.append((pos + aw, aw))
                pos += 2 * aw

            for tc in range(4):
                for ii in range(4):
                    i = tc * 4 + ii
                    sl = i % 2
                    xt = x_sb[:, i * D:(i + 1) * D]
                    shift_tile(i, tB["h2"][sl], "h2B_%d" % sl)
                    pb = i % 2
                    rstd, nmr = ln_stats([(xt[:, 0:512], 512, "x_sb"),
                              (tB["h2"][sl][:], 512, "h2B_%d" % sl)], D, pb)
                    op("act", lambda e, _x=xt, _r=rstd, _n=nmr: e.activation(
                        ln_scr[:, 0:512], _x[:, 0:512], AF.Identity, bias=_n, scale=_r),
                       r=["x_sb", f"c_rstd{pb}", f"c_nmr{pb}"], w=["ln_scr"])
                    op("act", lambda e, _s=sl, _r=rstd, _n=nmr: e.activation(
                        ln_scr[:, 512:1024], tB["h2"][_s][:], AF.Identity,
                        bias=_n, scale=_r),
                       r=["h2B_%d" % sl, f"c_rstd{pb}", f"c_nmr{pb}"], w=["ln_scr"])
                    for g0 in (0, 4):
                        for gi in range(4):
                            j = g0 + gi
                            op("pe", lambda e, _j=j, _g=gi: e.transpose(
                                tr32[:, _g * 128:_g * 128 + 128],
                                ln_scr[:, _j * 128:_j * 128 + 128], id32[:]),
                               r=["ln_scr", "id32"], w=["tr32"])
                        for gi in range(4):
                            j = g0 + gi
                            op("act", lambda e, _j=j, _g=gi, _ii=ii, _l=l: e.activation(
                                tB["ln2T"][:, _j * 512 + _ii * 128:_j * 512 + _ii * 128 + 128],
                                tr32[:, _g * 128:_g * 128 + 128], AF.Identity,
                                bias=lnb[_l][:, _j:_j + 1], scale=lng[_l][:, _j:_j + 1]),
                               r=["tr32", f"lng{l}", f"lnb{l}"], w=["ln2T"])
                for ot in range(2 * NHT):
                    col0, ow = w1_off[ot]
                    wb = ot % 2
                    op("sp", lambda e, _l=l, _o=ot, _w=ow, _b=wb: e.dma_start(
                        tB["w1"][_b][:, 0:8 * _w], d_w1img[_l, _o, :, 0:8 * _w],),
                       r=["w1img_d"], w=["w1bf_%d" % wb], dma=True)
                    pa = pP[2 + wb]
                    pan = PN[id(pa)]
                    for j in range(8):
                        op("pe", lambda e, _j=j, _p=pa, _b=wb, _w=ow: e.matmul(
                            _p[0:_w, 0:512], tB["w1"][_b][:, _j * _w:(_j + 1) * _w],
                            tB["ln2T"][:, _j * 512:_j * 512 + 512],
                            start=(_j == 0), stop=(_j == 7)),
                           r=["w1bf_%d" % wb, "ln2T"], w=[pan])
                    ht = ot // 2
                    if wb == 0:
                        op("act", lambda e, _p=pa, _w=ow, _h=ht, _l=l: e.activation(
                            tB["ag"][0:_w, 0:512], _p[0:_w, 0:512], AF.Identity,
                            bias=b1a[_l][0:_w, _h:_h + 1]),
                           r=[pan, f"b1a{l}"], w=["ag_a"])
                    else:
                        op("act", lambda e, _p=pa, _w=ow, _h=ht, _l=l: e.activation(
                            tB["ag"][0:_w, 1024:1536], _p[0:_w, 0:512], AF.Gelu,
                            bias=b1g[_l][0:_w, _h:_h + 1]),
                           r=[pan, f"b1g{l}"], w=["ag_g"])
                        op("dve", lambda e, _w=ow, _h=ht: e.tensor_mul(
                            tB["gg"][0:_w, _h * 512:_h * 512 + 512],
                            tB["ag"][0:_w, 0:512], tB["ag"][0:_w, 1024:1536]),
                           r=["ag_a", "ag_g"], w=["geglu"])
                for ii in range(4):
                    i = tc * 4 + ii
                    xt = x_sb[:, i * D:(i + 1) * D]
                    for n in range(2):
                        ps = pP[n]
                        psn = PN[id(ps)]
                        for kk in range(NHT):
                            kw = 128 if kk < NHT - 1 else HID_LAST
                            op("pe", lambda e, _n=n, _k=kk, _p=ps, _ii=ii, _kw=kw: e.matmul(
                                _p[:, 0:512],
                                tB["gg"][0:_kw, _k * 512 + _ii * 128:_k * 512 + _ii * 128 + 128],
                                tB["w2"][0:_kw, _k * D + _n * 512:_k * D + _n * 512 + 512],
                                start=(_k == 0), stop=False),
                               r=["geglu", "w2bf"], w=[psn])
                        op("pe", lambda e, _n=n, _p=ps, _l=l: e.matmul(
                            _p[:, 0:512], ones_bf[:],
                            tB["b2hi"][0:1, _n * 512:_n * 512 + 512],
                            start=False, stop=True),
                           r=["ones_bf", "b2hi"], w=[psn])
                    op("dve", lambda e, _x=xt: e.tensor_add(
                        _x[:, 0:512], _x[:, 0:512], pP[0][:]), r=["pP0", "x_sb"], w=["x_sb"])
                    op("dve", lambda e, _x=xt: e.tensor_add(
                        _x[:, 512:1024], _x[:, 512:1024], pP[1][:]),
                       r=["pP1", "x_sb"], w=["x_sb"])

        if PHASE_LIMIT == "all" and DEBUG_TAP is None:
            em.fence(prev_names, FINAL_T)
            for i in range(NT):
                sl = i % 2
                xt = x_sb[:, i * D:(i + 1) * D]
                op("sp", lambda e, _i=i, _s=sl: e.dma_start(
                    x1b[_s][:], d_x1[:, _i * D:(_i + 1) * D]),
                   r=["x1d"], w=[f"x1b{sl}"], dma=True)
                pb = i % 2
                rstd, nmr = ln_stats([(xt, D, "x_sb")], D, pb)
                op("dve", lambda e, _s=sl, _n=nmr: e.tensor_scalar_add(
                    x1b[_s][:], x1b[_s][:], _n),
                   r=[f"x1b{sl}", f"c_nmr{pb}"], w=[f"x1b{sl}"])
                op("dve", lambda e, _x=xt, _s=sl, _r=rstd: e.scalar_tensor_tensor(
                    _x, _x, _r, x1b[_s][:], ALU.mult, ALU.add),
                   r=[f"x1b{sl}", f"c_rstd{pb}", "x_sb"], w=["x_sb"])
        tok = op("sp", lambda e: e.dma_start(
            d_out.rearrange("(n p) d -> p n d", p=128), x_sb3),
            r=["x_sb"], w=["out_dram"], dma=True)
        em.final_wait("sp", tok)

        with nc.Block() as block:
            em.replay(block)
    return nc


def _prep_host(inputs):
    f = np.float32
    low_i, up_i, up_s = _masks()
    common = {
        "wqT": np.ascontiguousarray(inputs["L0_attn_wq_w"].T),
        "wkT": np.ascontiguousarray(inputs["L0_attn_wk_w"].T),
        "wvT": np.ascontiguousarray(inputs["L0_attn_wv_w"].T),
        "woT": np.ascontiguousarray(inputs["L0_attn_wo_w"].T),
        "brow4": np.concatenate([
            inputs["L0_attn_wq_b"], inputs["L0_attn_wk_b"],
            inputs["L0_attn_wv_b"], inputs["L0_attn_wo_b"]]).reshape(1, 4 * D),
        "qn2": np.tile(inputs["L0_attn_qn_w"], 2).reshape(128, 1),
        "kn2": np.tile(inputs["L0_attn_kn_w"], 2).reshape(128, 1),
        "rotpk": _host_tables(),
        "m_low": low_i, "m_upi": up_i, "m_ups": up_s,
        "id32": np.eye(128, dtype=f),
        "id16": np.eye(128, dtype=ml_dtypes.bfloat16),
    }
    for l in range(2):
        w1 = inputs["L0_ffn_w1"][l]
        b1 = inputs["L0_ffn_b1"][l]
        perm = []
        for m in range(NHT):
            aw = min(128, HID - m * 128)
            perm.extend(range(m * 128, m * 128 + aw))
            perm.extend(range(HID + m * 128, HID + m * 128 + aw))
        perm = np.asarray(perm)
        w1p = np.ascontiguousarray(w1[:, perm])
        b1p = b1[perm]
        b1av = np.zeros((128, NHT), f)
        b1gv = np.zeros((128, NHT), f)
        pos = 0
        for m in range(NHT):
            aw = min(128, HID - m * 128)
            b1av[0:aw, m] = b1p[pos:pos + aw]
            b1gv[0:aw, m] = b1p[pos + aw:pos + 2 * aw]
            pos += 2 * aw
        qs = inputs["L0_ffn_q_scale"][l]
        ks = inputs["L0_ffn_k_scale"][l]
        common.update({
            f"wqkv{l}": np.ascontiguousarray(inputs["L0_ffn_wqkv"][l]),
            f"wo{l}": np.ascontiguousarray(inputs["L0_ffn_wo"][l]),
            f"w1p{l}": w1p,
            f"b1a{l}": b1av, f"b1g{l}": b1gv,
            f"w2_{l}": np.ascontiguousarray(inputs["L0_ffn_w2"][l]),
            f"b2_{l}": inputs["L0_ffn_b2"][l].reshape(1, D),
            f"lng{l}": np.ascontiguousarray(inputs["L0_ffn_ln_g"][l].reshape(8, 128).T),
            f"lnb{l}": np.ascontiguousarray(inputs["L0_ffn_ln_b"][l].reshape(8, 128).T),
            f"qsrows{l}": np.concatenate([
                qs, np.concatenate([qs[32:], qs[:32]]),
                ks, np.concatenate([ks[32:], ks[:32]])]).reshape(1, 4 * HD),
        })
    return common


def kernel(**inputs):
    inputs = {k: np.asarray(v) for k, v in inputs.items()}
    common = _prep_host(inputs)
    nc = build()
    in_maps = []
    for c in range(8):
        m = dict(common)
        m["x"] = np.ascontiguousarray(inputs["L0_attn_x"][c])
        in_maps.append(m)
    res = run_bass_kernel_spmd(nc, in_maps, core_ids=list(range(8)))
    return np.stack([res.results[c]["out"] for c in range(8)], axis=0)


# version 07
# speedup vs baseline: 1.1585x; 1.1291x over previous
import numpy as np
import ml_dtypes
import concourse.bass as bass
import concourse.mybir as mybir
from concourse.bass_utils import run_bass_kernel_spmd

F32 = mybir.dt.float32
BF16 = mybir.dt.bfloat16
AF = mybir.ActivationFunctionType
ALU = mybir.AluOpType
AX = mybir.AxisListType

S, D, NT, HD = 2048, 1024, 16, 64
HEADS_O, HEADS_I, INNER = 16, 8, 512
HID, NHT, HID_LAST = 2730, 22, 42
EPS = 1e-5

PHASE_LIMIT = "all"
NT_LIM = NT
DEBUG_TAP = None

ENGINES = ("pe", "act", "dve", "pool", "sp")
QSEM = {"sp": "q_sp", "pool": "q_pool", "act": "q_act"}


class Buf:
    __slots__ = ("name", "w", "readers")

    def __init__(self, name):
        self.name = name
        self.w = None
        self.readers = []


class Emitter:
    def __init__(self, nc):
        self.nc = nc
        self.prog = {e: [] for e in ENGINES}
        self.sems = {}
        self.counts = {n: 0 for n in ("pe", "act", "dve", "pool", "sp")}
        self.lastwait = {}
        self.bufs = {}

    def make_sems(self, ctx):
        for nm in list(self.counts):
            self.sems[nm] = self.nc.alloc_semaphore("sem_" + nm)

    def dma_sem(self, bufname):
        nm = "d_" + bufname
        if nm not in self.sems:
            self.sems[nm] = self.nc.alloc_semaphore("sem_" + nm)
            self.counts[nm] = 0
        return nm

    def buf(self, name):
        if name not in self.bufs:
            self.bufs[name] = Buf(name)
        return self.bufs[name]

    PSUM_BUFS = frozenset(("pP0", "pP1", "pP2", "pP3", "simP", "simX", "tbf", "tr32"))

    def op(self, eng, fn, r=(), w=(), dma=False):
        reads = [self.buf(n) for n in r if n not in self.PSUM_BUFS]
        writes = [self.buf(n) for n in w] + [self.buf(n) for n in r
                                             if n in self.PSUM_BUFS]
        deps = []
        for b in reads:
            if b.w is not None:
                deps.append(b.w)
        for b in writes:
            if b.w is not None:
                deps.append(b.w)
            deps.extend(b.readers)
        mysem = self.dma_sem(w[0]) if dma else eng
        need = {}
        for (sem, val) in deps:
            if dma and sem == mysem:
                continue
            if sem == eng == "pe":
                continue
            if need.get(sem, 0) < val:
                need[sem] = val
        for sem, val in sorted(need.items()):
            if self.lastwait.get((eng, sem), 0) >= val:
                continue
            self.lastwait[(eng, sem)] = val
            self.prog[eng].append(
                lambda e, _s=sem, _v=val: e.wait_ge(self.sems[_s], _v))
        amt = 16 if dma else 1
        self.counts[mysem] += amt
        val = self.counts[mysem]
        self.prog[eng].append(
            lambda e, _f=fn, _s=mysem, _a=amt: _f(e).then_inc(self.sems[_s], _a))
        tok = (mysem, val)
        for b in reads:
            b.readers.append(tok)
        for b in writes:
            b.w = tok
            b.readers = []
        return tok

    def fence(self, old_names, new_names):
        toks = {}
        for n in old_names:
            b = self.bufs.get(n)
            if b is None:
                continue
            for t in ([b.w] if b.w else []) + b.readers:
                if toks.get(t[0], 0) < t[1]:
                    toks[t[0]] = t[1]
        tl = list(toks.items())
        for n in new_names:
            b = self.buf(n)
            b.readers = b.readers + tl
        for n in old_names:
            if n in self.bufs:
                self.bufs[n].w = None
                self.bufs[n].readers = []

    def final_wait(self, eng, tok):
        self.prog[eng].append(lambda e, _t=tok: e.wait_ge(self.sems[_t[0]], _t[1]))

    def replay(self, block):
        for eng, attr in (("sp", "sync"), ("pe", "tensor"), ("act", "scalar"),
                          ("dve", "vector"), ("pool", "gpsimd")):
            seq = self.prog[eng]

            def make(seq_):
                def run(e):
                    for f in seq_:
                        f(e)
                return run
            getattr(block, attr)(make(seq))


def _host_tables():
    half = np.arange(0, HD, 2, dtype=np.float64)
    inv_freq = 1.0 / 10000.0 ** (half / HD)
    t = np.arange(S, dtype=np.float64)
    freqs = t[:, None] * inv_freq[None, :]
    freqs = np.concatenate([freqs, freqs], axis=1)
    base = (half + 0.4 * HD) / (1.4 * HD)
    power = (t - S // 2) / 256.0
    scale = base[None, :] ** power[:, None]
    scale = np.concatenate([scale, scale], axis=1)
    cq = (np.cos(freqs) * scale).astype(np.float32)
    sq = (np.sin(freqs) * scale).astype(np.float32)
    ck = (np.cos(freqs) / scale).astype(np.float32)
    sk = (np.sin(freqs) / scale).astype(np.float32)
    packed = np.concatenate([cq, sq, ck, sk], axis=1)
    return np.ascontiguousarray(
        packed.reshape(NT, 128, 4 * HD).transpose(1, 0, 2).reshape(128, NT * 4 * HD))


def _masks():
    i = np.arange(128)[:, None]
    j = np.arange(128)[None, :]
    neg = np.float32(-1.0e30)
    bf = ml_dtypes.bfloat16
    low_i = np.where(i >= j, 0.0, neg).astype(np.float32).astype(bf)
    up_i = np.where(j >= i, 0.0, neg).astype(np.float32).astype(bf)
    up_s = np.where(j > i, 0.0, neg).astype(np.float32).astype(bf)
    return low_i, up_i, up_s


def build():
    nc = bass.Bass("TRN2", target_bir_lowering=False)
    dram = {}

    def din(name, shape, dt=F32):
        dram[name] = nc.dram_tensor(name, list(shape), dt, kind="ExternalInput").ap()

    din("x", (S, D))
    for w in ("wqT", "wkT", "wvT", "woT"):
        din(w, (D, D))
    din("brow4", (1, 4 * D))
    din("qn2", (128, 1))
    din("kn2", (128, 1))
    din("rotpk", (128, NT * 4 * HD))
    din("m_low", (128, 128), BF16)
    din("m_upi", (128, 128), BF16)
    din("m_ups", (128, 128), BF16)
    din("id32", (128, 128), F32)
    din("id16", (128, 128), BF16)
    for l in range(2):
        din(f"wqkv{l}", (D, 3 * INNER))
        din(f"wo{l}", (INNER, D))
        din(f"w1p{l}", (D, 2 * HID))
        din(f"b1a{l}", (128, NHT))
        din(f"b1g{l}", (128, NHT))
        din(f"w2_{l}", (HID, D))
        din(f"b2_{l}", (1, D))
        din(f"lng{l}", (128, 8))
        din(f"lnb{l}", (128, 8))
        din(f"qsrows{l}", (1, 4 * HD))
    d_out = nc.dram_tensor("out", [S, D], F32, kind="ExternalOutput").ap()
    d_x1 = nc.dram_tensor("x1_scratch", [128, NT * D], F32, kind="Internal").ap()
    d_w1img = nc.dram_tensor("w1img", [2, 2 * NHT, 128, 1024], BF16, kind="Internal").ap()

    from contextlib import ExitStack
    with ExitStack() as ctx:
        em = Emitter(nc)
        em.make_sems(ctx)

        def sba(name, shape, dt=F32):
            return nc.alloc_sbuf_tensor(name, list(shape), dt)

        x_sb = sba("x_sb", (128, NT * D))
        ln_scr = sba("ln_scr", (128, D))
        sq_scr = sba("sq_scr", (128, D))
        cols = sba("cols", (128, 384))
        m_low = sba("m_low_sb", (128, 128), BF16)
        m_upi = sba("m_upi_sb", (128, 128), BF16)
        m_ups = sba("m_ups_sb", (128, 128), BF16)
        id32 = sba("id32_sb", (128, 128))
        id16 = sba("id16_sb", (128, 128), BF16)
        ones_bf = sba("ones_bf", (1, 128), BF16)
        qn2 = sba("qn2_sb", (128, 1))
        kn2 = sba("kn2_sb", (128, 1))
        lng = [sba(f"lng{l}_sb", (128, 8)) for l in range(2)]
        lnb = [sba(f"lnb{l}_sb", (128, 8)) for l in range(2)]
        b1a = [sba(f"b1a{l}_sb", (128, NHT)) for l in range(2)]
        b1g = [sba(f"b1g{l}_sb", (128, NHT)) for l in range(2)]
        wld = [sba(f"wld{i}", (128, 1024)) for i in range(2)]
        w1stage = sba("w1stage", (128, 1024), BF16)

        ARENA = 122880
        lo, hi = nc.bump_sbuf(ARENA)
        assert hi - lo >= ARENA

        def ar(name, shape, dt, off):
            sz = int(np.prod(shape[1:])) * mybir.dt.size(dt)
            assert off + sz <= ARENA, (name, off, sz)
            return nc.alloc_sbuf_tensor_at(name, list(shape), dt, offset=lo + off)

        wT_bf = {w: ar(w + "_bf", (128, 8 * D), BF16, k * 16384)
                 for k, w in enumerate(("wqT", "wkT", "wvT", "woT"))}
        q_scrO = ar("q_scrO", (128, D), F32, 65536)
        k_scrO = ar("k_scrO", (128, D), F32, 69632)
        ltO = [ar(f"ltO{i}", (128, D), BF16, 73728 + i * 2048) for i in range(2)]
        qhTO = [ar(f"qhTO{i}", (128, D), BF16, 77824 + i * 2048) for i in range(2)]
        khT3 = ar("khT3", (128, 4 * D), BF16, 81920)
        v3 = ar("v3", (128, 4 * D), BF16, 90112)
        pbO = [ar(f"pbO{i}", (128, 512), BF16, 98304 + i * 1024) for i in range(2)]
        ptO = [ar(f"ptO{i}", (128, 512), BF16, 100352 + i * 1024) for i in range(2)]
        obO = [ar(f"obO{i}", (128, D), BF16, 102400 + i * 2048) for i in range(2)]
        otO = [ar(f"otO{i}", (128, D), BF16, 106496 + i * 2048) for i in range(2)]
        brow4_hi = ar("brow4_hi", (1, 4 * D), BF16, 110592)
        OUTER_T = (["wqT_bf", "wkT_bf", "wvT_bf", "woT_bf", "q_scrO", "k_scrO",
                    "khT3", "v3", "brow4_hi"] +
                   [f"{p}{i}" for p in ("ltO", "qhTO", "pbO", "ptO", "obO", "otO")
                    for i in range(2)])

        def innerA():
            t = {}
            t["wqkv"] = ar("wqkv_bf", (128, 8 * 3 * INNER), BF16, 0)
            t["woin"] = ar("woin_bf", (128, 4 * D), BF16, 24576)
            t["q_scr"] = ar("q_scrI", (128, INNER), F32, 32768)
            t["k_scr"] = ar("k_scrI", (128, INNER), F32, 34816)
            t["rh"] = ar("rhI", (128, INNER), F32, 36864)
            t["lt"] = [ar(f"ltI_{i}", (128, D), BF16, 38912 + i * 2048) for i in range(2)]
            t["qhT"] = [ar(f"qhTI_{i}", (128, INNER), BF16, 43008 + i * 1024) for i in range(2)]
            t["khT5"] = ar("khT5", (128, 6 * INNER), BF16, 45056)
            t["v5"] = ar("v5", (128, 6 * INNER), BF16, 51200)
            t["pb"] = [ar(f"pbI_{i}", (128, 640), BF16, 57344 + i * 1280) for i in range(2)]
            t["pt"] = [ar(f"ptI_{i}", (128, 640), BF16, 59904 + i * 1280) for i in range(2)]
            t["ob"] = [ar(f"obI_{i}", (128, INNER), BF16, 62464 + i * 1024) for i in range(2)]
            t["ot"] = [ar(f"otI_{i}", (128, INNER), BF16, 64512 + i * 1024) for i in range(2)]
            t["h2"] = [ar(f"h2A_{i}", (128, INNER), F32, 66560 + i * 2048) for i in range(2)]
            t["rb"] = [ar(f"rotb_{i}", (128, 4 * HD), F32, 70656 + i * 1024) for i in range(2)]
            t["abt"] = ar("abt", (128, 4 * HD), F32, 72704)
            names = (["wqkv_bf", "woin_bf", "q_scrI", "k_scrI",
                      "rhI", "khT5", "v5", "abt",
                      "w2bf", "b2hi"] +
                     [f"{p}_{i}" for p in ("ltI", "qhTI", "pbI", "ptI", "obI", "otI",
                                           "h2A", "rotb") for i in range(2)])
            return t, names

        def innerB():
            t = {}
            t["w2"] = ar("w2bf", (128, NHT * D), BF16, 73728)
            t["gg"] = ar("geglu", (128, NHT * 512), BF16, 0)
            t["ln2T"] = ar("ln2T", (128, 8 * 512), BF16, 22528)
            t["w1"] = [ar(f"w1bf_{i}", (128, 1024), BF16, 30720 + i * 2048) for i in range(2)]
            t["ag"] = ar("agscr", (128, 2048), F32, 34816)
            t["h2"] = [ar(f"h2B_{i}", (128, INNER), F32, 43008 + i * 2048) for i in range(2)]
            t["b2hi"] = ar("b2hi", (1, D), BF16, 118784)
            names = (["geglu", "ln2T", "agscr"] +
                     [] +
                     [f"w1bf_{i}" for i in range(2)] + [f"h2B_{i}" for i in range(2)])
            return t, names

        x1b = [ar(f"x1b{i}", (128, D), F32, i * 4096) for i in range(2)]
        FINAL_T = ["x1b0", "x1b1"]

        pP = [nc.alloc_psum_tensor(f"pP{i}", [128, 512], F32) for i in range(4)]
        simP = nc.alloc_psum_tensor("simP", [128, 512], F32)
        simX = nc.alloc_psum_tensor("simX", [128, 128], F32)
        tbf = nc.alloc_psum_tensor("tbf", [128, 1024], BF16)
        tr32 = nc.alloc_psum_tensor("tr32", [128, 512], F32)
        PN = {id(pP[0]): "pP0", id(pP[1]): "pP1", id(pP[2]): "pP2", id(pP[3]): "pP3",
              id(simP): "simP", id(simX): "simX", id(tbf): "tbf", id(tr32): "tr32"}

        op = em.op

        def stats_bank(p):
            base = 8 * p
            return [cols[:, base + i:base + i + 1] for i in range(8)]
        sums_bank = [cols[:, 16:32], cols[:, 32:48]]
        c_rdens = cols[:, 48:64]
        c_nmax = [cols[:, 64 + i:65 + i] for i in range(2)]
        c_nmaxx = [cols[:, 66 + i:67 + i] for i in range(2)]
        c_denx = [cols[:, 68 + i:69 + i] for i in range(2)]
        qn_eff = cols[:, 70:71]
        c_eight = cols[:, 71:72]
        c_eps = cols[:, 72:73]
        c_s1, c_mu, c_m2, c_var, c_rstd, c_nmr, c_t6, c_t7 = stats_bank(0)

        x_sb3 = bass.AP(x_sb, 0, [[NT * D, 128], [D, NT], [1, D]])
        op("sp", lambda e: e.dma_start(
            x_sb3, dram["x"].rearrange("(n p) d -> p n d", p=128)),
            w=["x_sb"], dma=True)
        for t, n in ((m_low, "m_low"), (m_upi, "m_upi"), (m_ups, "m_ups"),
                     (id32, "id32"), (id16, "id16"), (qn2, "qn2"), (kn2, "kn2")):
            op("sp", lambda e, _t=t, _n=n: e.dma_start(_t[:], dram[_n]), w=[n], dma=True)
        for l in range(2):
            for n, t in ((f"lng{l}", lng[l]), (f"lnb{l}", lnb[l]),
                         (f"b1a{l}", b1a[l]), (f"b1g{l}", b1g[l])):
                op("sp", lambda e, _t=t, _n=n: e.dma_start(_t[:], dram[_n]), w=[n], dma=True)
        op("dve", lambda e: e.memset(ones_bf[:], 1.0), w=["ones_bf"])
        op("dve", lambda e: e.memset(c_eps, EPS), w=["c_eps"])
        op("dve", lambda e: e.memset(c_eight, 8.0), w=["c_eight"])
        op("dve", lambda e: e.tensor_scalar_mul(qn_eff, qn2[:], 0.125), r=["qn2"], w=["c_qe"])

        def load_cast(dst_bf, dst_name, dram_name, n_rows, n_cols, cast_eng="pool"):
            nj = (n_rows + 127) // 128
            for j in range(nj):
                rows = min(128, n_rows - j * 128)
                for c0 in range(0, n_cols, 1024):
                    cw = min(1024, n_cols - c0)
                    slot = load_cast.k % 2
                    load_cast.k += 1
                    op("sp", lambda e, _j=j, _c0=c0, _cw=cw, _r=rows, _s=slot,
                       _d=dram_name: e.dma_start(
                           wld[_s][0:_r, 0:_cw],
                           dram[_d][_j * 128:_j * 128 + _r, _c0:_c0 + _cw]),
                       w=["wld%d" % slot], dma=True)
                    op(cast_eng, lambda e, _j=j, _c0=c0, _cw=cw, _r=rows, _s=slot:
                       e.tensor_copy(
                           dst_bf[0:_r, _j * n_cols + _c0:_j * n_cols + _c0 + _cw],
                           wld[_s][0:_r, 0:_cw]),
                       r=["wld%d" % slot], w=[dst_name])
        load_cast.k = 0

        def ln_stats(srcs, total_d, pb=0):
            c_s1, c_mu, c_m2, c_var, c_rstd, c_nmr, c_t6, c_t7 = stats_bank(pb)
            sx = str(pb)
            for idx, (apv, wdt, rn) in enumerate(srcs):
                if idx == 0:
                    op("dve", lambda e, _a=apv, _c=c_s1: e.tensor_reduce(_c, _a, AX.X, ALU.add),
                       r=[rn], w=["c_s1" + sx])
                    op("act", lambda e, _a=apv, _w=wdt, _c=c_t6: e.activation(
                        sq_scr[:, 0:_w], _a, AF.Square, accum_out=_c),
                       r=[rn], w=["sq_scr", "c_t6" + sx])
                else:
                    op("dve", lambda e, _a=apv, _c=c_t7: e.tensor_reduce(_c, _a, AX.X, ALU.add),
                       r=[rn], w=["c_t7" + sx])
                    op("dve", lambda e, _c1=c_s1, _c2=c_t7: e.tensor_add(_c1, _c1, _c2),
                       r=["c_t7" + sx, "c_s1" + sx], w=["c_s1" + sx])
                    op("act", lambda e, _a=apv, _w=wdt, _c=c_t7: e.activation(
                        sq_scr[:, 0:_w], _a, AF.Square, accum_out=_c),
                       r=[rn], w=["sq_scr", "c_t7" + sx])
                    op("dve", lambda e, _c1=c_t6, _c2=c_t7: e.tensor_add(_c1, _c1, _c2),
                       r=["c_t7" + sx, "c_t6" + sx], w=["c_t6" + sx])
            inv = 1.0 / total_d
            op("dve", lambda e, _a=c_mu, _b=c_s1: e.tensor_scalar_mul(_a, _b, inv),
               r=["c_s1" + sx], w=["c_mu" + sx])
            op("dve", lambda e, _a=c_m2, _b=c_mu: e.tensor_mul(_a, _b, _b),
               r=["c_mu" + sx], w=["c_m2" + sx])
            op("dve", lambda e, _a=c_var, _b=c_t6: e.tensor_scalar_mul(_a, _b, inv),
               r=["c_t6" + sx], w=["c_var" + sx])
            op("dve", lambda e, _a=c_var, _b=c_m2: e.tensor_sub(_a, _a, _b),
               r=["c_var" + sx, "c_m2" + sx], w=["c_var" + sx])
            op("act", lambda e, _a=c_var: e.activation(_a, _a, AF.Sqrt, bias=c_eps, scale=1.0),
               r=["c_var" + sx, "c_eps"], w=["c_var" + sx])
            op("dve", lambda e, _a=c_rstd, _b=c_var: e.reciprocal(_a, _b),
               r=["c_var" + sx], w=["c_rstd" + sx])
            op("dve", lambda e, _a=c_nmr, _b=c_mu, _c=c_rstd: e.scalar_tensor_tensor(
                _a, _b, -1.0, _c, ALU.mult, ALU.mult),
               r=["c_mu" + sx, "c_rstd" + sx], w=["c_nmr" + sx])
            return c_rstd, c_nmr

        def tr_f32(chunks, evacs):
            for g0 in range(0, len(chunks), 4):
                grp = chunks[g0:g0 + 4]
                for gi, (src, rn) in enumerate(grp):
                    op("pe", lambda e, _s=src, _g=gi: e.transpose(
                        tr32[:, _g * 128:_g * 128 + 128], _s, id32[:]),
                       r=[rn, "id32"], w=["tr32"])
                evacs(g0, len(grp))

        for k, w in enumerate(("wqT", "wkT", "wvT", "woT")):
            load_cast(wT_bf[w], w + "_bf", w, D, D)
        for k in range(4):
            slot = load_cast.k % 2
            load_cast.k += 1
            op("sp", lambda e, _k=k, _s=slot: e.dma_start(
                wld[_s][0:1, 0:D], dram["brow4"][0:1, _k * D:(_k + 1) * D]),
               w=["wld%d" % slot], dma=True)
            op("pool", lambda e, _k=k, _s=slot: e.tensor_copy(
                brow4_hi[0:1, _k * D:(_k + 1) * D], wld[_s][0:1, 0:D]),
               r=["wld%d" % slot], w=["brow4_hi"])
        w1_off_pre = []
        pos = 0
        for m in range(NHT):
            aw = min(128, HID - m * 128)
            w1_off_pre.append((pos, aw))
            w1_off_pre.append((pos + aw, aw))
            pos += 2 * aw
        for l in range(2):
            for ot in range(2 * NHT):
                col0, ow = w1_off_pre[ot]
                slot = load_cast.k % 2
                load_cast.k += 1
                op("sp", lambda e, _c=col0, _w=ow, _s=slot, _l=l: e.dma_start(
                    bass.AP(wld[_s], 0, [[1024, 128], [_w, 8], [1, _w]]),
                    dram[f"w1p{_l}"][:, _c:_c + _w].rearrange("(a p) o -> p a o", p=128)),
                   w=["wld%d" % slot], dma=True)
                op("pool", lambda e, _s=slot, _w=ow: e.tensor_copy(
                    w1stage[:, 0:8 * _w], wld[_s][:, 0:8 * _w]),
                   r=["wld%d" % slot], w=["w1stage"])
                op("sp", lambda e, _l=l, _o=ot, _w=ow: e.dma_start(
                    d_w1img[_l, _o, :, 0:8 * _w], w1stage[:, 0:8 * _w]),
                   r=["w1stage"], w=["w1img_d"], dma=True)

        def proj_1024(lt_t, lt_name, w_bf, w_name, psA, psB, bias_idx):
            for n in range(2):
                ps = (psA, psB)[n]
                psn = PN[id(ps)]
                for j in range(8):
                    op("pe", lambda e, _n=n, _j=j, _p=ps, _lt=lt_t: e.matmul(
                        _p[:, 0:512], _lt[:, _j * 128:_j * 128 + 128],
                        w_bf[:, _j * D + _n * 512:_j * D + _n * 512 + 512],
                        start=(_j == 0), stop=False),
                       r=[lt_name, w_name], w=[psn])
                op("pe", lambda e, _n=n, _p=ps, _b=bias_idx: e.matmul(
                    _p[:, 0:512], ones_bf[:],
                    brow4_hi[0:1, _b * D + _n * 512:_b * D + _n * 512 + 512],
                    start=False, stop=True),
                   r=["ones_bf", "brow4_hi"], w=[psn])

        def rmsnorm16(scr, scr_name, pb):
            su = sums_bank[pb]
            sn = f"c_sums{pb}"
            op("dve", lambda e, _s=scr: e.tensor_scalar_min(_s[:], _s[:], 1.0e4),
               r=[scr_name], w=[scr_name])
            op("dve", lambda e, _s=scr: e.tensor_scalar_max(_s[:], _s[:], -1.0e4),
               r=[scr_name], w=[scr_name])
            op("act", lambda e, _s=scr: e.activation(sq_scr[:], _s[:], AF.Square),
               r=[scr_name], w=["sq_scr"])
            red3 = bass.AP(sq_scr, 0, [[D, 128], [64, 16], [1, 64]])
            op("dve", lambda e, _r=red3, _u=su: e.tensor_reduce(_u, _r, AX.X, ALU.add),
               r=["sq_scr"], w=[sn])
            op("dve", lambda e, _u=su: e.tensor_scalar(
                _u, _u, 1.0 / 64, 1.0e-5, ALU.mult, ALU.max),
               r=[sn], w=[sn])
            op("act", lambda e, _u=su: e.activation(_u, _u, AF.Sqrt, bias=c_eps, scale=1.0),
               r=[sn, "c_eps"], w=[sn])
            op("dve", lambda e, _u=su: e.reciprocal(_u, _u), r=[sn], w=[sn])
            for h in range(16):
                op("dve", lambda e, _s=scr, _h=h, _u=su: e.tensor_scalar_mul(
                    _s[:, _h * 64:_h * 64 + 64], _s[:, _h * 64:_h * 64 + 64],
                    _u[:, _h:_h + 1]), r=[scr_name, sn], w=[scr_name])

        def P1o(i):
            sl = i % 2
            k4 = i % 4
            pb = i % 2
            xt = x_sb[:, i * D:(i + 1) * D]
            rstd, nmr = ln_stats([(xt, D, "x_sb")], D, pb)
            op("act", lambda e, _x=xt, _r=rstd, _n=nmr: e.activation(
                ln_scr[:], _x, AF.Identity, bias=_n, scale=_r),
               r=["x_sb", f"c_rstd{pb}", f"c_nmr{pb}"], w=["ln_scr"])

            def ev_lt(g0, glen, _sl=sl):
                op("act", lambda e, _g0=g0, _n=glen * 128: e.activation(
                    ltO[_sl][:, _g0 * 128:_g0 * 128 + _n], tr32[:, 0:_n], AF.Identity),
                   r=["tr32"], w=[f"ltO{_sl}"])
            tr_f32([(ln_scr[:, j * 128:j * 128 + 128], "ln_scr") for j in range(8)], ev_lt)
            proj_1024(ltO[sl], f"ltO{sl}", wT_bf["wqT"], "wqT_bf", pP[0], pP[1], 0)
            proj_1024(ltO[sl], f"ltO{sl}", wT_bf["wkT"], "wkT_bf", pP[2], pP[3], 1)
            op("dve", lambda e: e.tensor_copy(q_scrO[:, 0:512], pP[0][:]), r=["pP0"], w=["q_scrO"])
            op("dve", lambda e: e.tensor_copy(q_scrO[:, 512:1024], pP[1][:]), r=["pP1"], w=["q_scrO"])
            op("act", lambda e: e.activation(k_scrO[:, 0:512], pP[2][:], AF.Identity),
               r=["pP2"], w=["k_scrO"])
            op("act", lambda e: e.activation(k_scrO[:, 512:1024], pP[3][:], AF.Identity),
               r=["pP3"], w=["k_scrO"])
            proj_1024(ltO[sl], f"ltO{sl}", wT_bf["wvT"], "wvT_bf", pP[0], pP[1], 2)
            op("act", lambda e, _k=k4: e.activation(
                v3[:, _k * D:_k * D + 512], pP[0][:], AF.Identity), r=["pP0"], w=["v3"])
            op("act", lambda e, _k=k4: e.activation(
                v3[:, _k * D + 512:_k * D + 1024], pP[1][:], AF.Identity), r=["pP1"], w=["v3"])
            rmsnorm16(q_scrO, "q_scrO", pb)

            def ev_q(g0, glen, _sl=sl):
                op("act", lambda e, _g0=g0, _n=glen * 128: e.activation(
                    qhTO[_sl][:, _g0 * 128:_g0 * 128 + _n], tr32[:, 0:_n],
                    AF.Identity, scale=qn_eff),
                   r=["tr32", "c_qe"], w=[f"qhTO{_sl}"])
            tr_f32([(q_scrO[:, j * 128:j * 128 + 128], "q_scrO") for j in range(8)], ev_q)
            rmsnorm16(k_scrO, "k_scrO", pb)

            def ev_k(g0, glen, _k4=k4):
                op("act", lambda e, _g0=g0, _n=glen * 128: e.activation(
                    khT3[:, _k4 * D + _g0 * 128:_k4 * D + _g0 * 128 + _n], tr32[:, 0:_n],
                    AF.Identity, scale=kn2[:]),
                   r=["tr32", "kn2"], w=["khT3"])
            tr_f32([(k_scrO[:, j * 128:j * 128 + 128], "k_scrO") for j in range(8)], ev_k)

        def P2o(i):
            sl = i % 2
            pb = i % 2
            xt = x_sb[:, i * D:(i + 1) * D]
            kstart = max(0, i - 2)
            nk = i - kstart + 1
            ncols = nk * 128

            def emit_ptav(h):
                hs = h % 2
                for c in range(nk):
                    op("pe", lambda e, _c=c, _hs=hs: e.transpose(
                        tbf[:, _c * 128:_c * 128 + 128],
                        pbO[_hs][:, _c * 128:_c * 128 + 128], id16[:]),
                       r=[f"pbO{hs}", "id16"], w=["tbf"])
                op("dve", lambda e, _n=ncols, _hs=hs: e.tensor_copy(
                    ptO[_hs][:, 0:_n], tbf[:, 0:_n]), r=["tbf"], w=[f"ptO{hs}"])
                bank = pP[0] if h < 8 else pP[1]
                bn = PN[id(bank)]
                for c in range(nk):
                    slot = (kstart + c) % 4
                    st_ = (h % 8 == 0) and (c == 0)
                    sp_ = (h % 8 == 7) and (c == nk - 1)
                    op("pe", lambda e, _c=c, _hs=hs, _b=bank, _co=(h % 8) * 64,
                       _st=st_, _sp=sp_, _slot=slot, _h=h: e.matmul(
                        _b[:, _co:_co + 64], ptO[_hs][:, _c * 128:_c * 128 + 128],
                        v3[:, _slot * D + _h * 64:_slot * D + _h * 64 + 64],
                        start=_st, stop=_sp),
                       r=[f"ptO{hs}", "v3"], w=[bn])

            pend = None
            for h in range(HEADS_O):
                hp = (h % 2) * 64
                hc = (h // 2) * 128
                hs = h % 2
                for c in range(nk):
                    slot = (kstart + c) % 4
                    op("pe", lambda e, _c=c, _s=slot, _hp=hp, _hc=hc, _sl=sl,
                       _st=(c == 0): e.matmul(
                        simP[:, _c * 128:_c * 128 + 128],
                        qhTO[_sl][_hp:_hp + 64, _hc:_hc + 128],
                        khT3[_hp:_hp + 64, _s * D + _hc:_s * D + _hc + 128],
                        start=_st, stop=False),
                       r=[f"qhTO{sl}", "khT3"], w=["simP"])
                if nk == 3:
                    op("pe", lambda e: e.matmul(simP[:, 0:128], id16[:], m_ups[:],
                                                start=False, stop=False),
                       r=["id16", "m_ups"], w=["simP"])
                op("pe", lambda e, _o=(nk - 1) * 128: e.matmul(
                    simP[:, _o:_o + 128], id16[:], m_low[:], start=False, stop=True),
                   r=["id16", "m_low"], w=["simP"])
                if pend is not None:
                    emit_ptav(pend)
                nmv = c_nmax[h % 2]
                nmn = f"c_nmax{h % 2}"
                rdn = f"c_rd{h}"
                op("dve", lambda e, _n=ncols, _m=nmv: e.tensor_reduce(
                    _m, simP[:, 0:_n], AX.X, ALU.max, negate=True),
                   r=["simP"], w=[nmn])
                op("act", lambda e, _n=ncols, _hs=hs, _m=nmv, _h=h: e.activation(
                    pbO[_hs][:, 0:_n], simP[:, 0:_n], AF.Exp, bias=_m, scale=1.0,
                    accum_out=c_rdens[:, _h:_h + 1]),
                   r=["simP", nmn], w=[f"pbO{hs}", rdn])
                op("dve", lambda e, _h=h: e.reciprocal(
                    c_rdens[:, _h:_h + 1], c_rdens[:, _h:_h + 1]), r=[rdn], w=[rdn])
                pend = h
            emit_ptav(pend)
            for h in range(HEADS_O):
                bank = pP[0] if h < 8 else pP[1]
                bn = PN[id(bank)]
                op("act", lambda e, _b=bank, _co=(h % 8) * 64, _h=h, _s=sl: e.activation(
                    obO[_s][:, _h * 64:_h * 64 + 64], _b[:, _co:_co + 64], AF.Identity,
                    scale=c_rdens[:, _h:_h + 1]),
                   r=[bn, f"c_rd{h}"], w=[f"obO{sl}"])
            for j in range(8):
                op("pe", lambda e, _j=j, _s=sl: e.transpose(
                    tbf[:, _j * 128:_j * 128 + 128],
                    obO[_s][:, _j * 128:_j * 128 + 128], id16[:]),
                   r=[f"obO{sl}", "id16"], w=["tbf"])
            op("dve", lambda e, _s=sl: e.tensor_copy(otO[_s][:], tbf[:]),
               r=["tbf"], w=[f"otO{sl}"])
            proj_1024(otO[sl], f"otO{sl}", wT_bf["woT"], "woT_bf", pP[2], pP[3], 3)
            rstd, nmr = ln_stats([(pP[2][:], 512, "pP2"), (pP[3][:], 512, "pP3")], D, pb)
            op("dve", lambda e, _x=xt, _n=nmr: e.tensor_scalar_add(_x, _x, _n),
               r=["x_sb", f"c_nmr{pb}"], w=["x_sb"])
            op("dve", lambda e, _x=xt, _r=rstd: e.scalar_tensor_tensor(
                _x[:, 0:512], pP[2][:], _r, _x[:, 0:512], ALU.mult, ALU.add),
               r=["pP2", f"c_rstd{pb}", "x_sb"], w=["x_sb"])
            op("dve", lambda e, _x=xt, _r=rstd: e.scalar_tensor_tensor(
                _x[:, 512:1024], pP[3][:], _r, _x[:, 512:1024], ALU.mult, ALU.add),
               r=["pP3", f"c_rstd{pb}", "x_sb"], w=["x_sb"])

        NTL = min(NT, NT_LIM)
        if NTL:
            P1o(0)
            for i in range(NTL):
                if i + 1 < NTL:
                    P1o(i + 1)
                P2o(i)

        op("pool", lambda e: e.dma_start(d_x1, x_sb[:]), r=["x_sb"], w=["x1d"], dma=True)
        if not (PHASE_LIMIT == "outer" and DEBUG_TAP):
            for i in range(NT):
                xt = x_sb[:, i * D:(i + 1) * D]
                pb = i % 2
                rstd, nmr = ln_stats([(xt, D, "x_sb")], D, pb)
                op("act", lambda e, _x=xt, _r=rstd, _n=nmr: e.activation(
                    _x, _x, AF.Identity, bias=_n, scale=_r),
                   r=["x_sb", f"c_rstd{pb}", f"c_nmr{pb}", "x1d"], w=["x_sb"])

        n_inner = {"outer": 0, "layer0": 1, "all": 2}[PHASE_LIMIT]
        prev_names = OUTER_T
        tA, namesA = (innerA() if n_inner else (None, None))
        tB, namesB = (innerB() if n_inner else (None, None))
        for l in range(n_inner):
            em.fence(prev_names, namesA)
            prev_names = namesA
            load_cast(tA["wqkv"], "wqkv_bf", f"wqkv{l}", D, 3 * INNER)
            load_cast(tA["woin"], "woin_bf", f"wo{l}", INNER, D)
            qsb_ap = cols[:, 128:384]
            slot = load_cast.k % 2
            load_cast.k += 1
            op("sp", lambda e, _l=l, _s=slot: e.dma_start(
                wld[_s][0:1, 0:256], dram[f"qsrows{_l}"]), w=["wld%d" % slot], dma=True)
            op("dve", lambda e: e.memset(sq_scr[0:1, 0:128], 1.0), w=["sq_scr"])
            op("pe", lambda e, _s=slot: e.matmul(
                tr32[:, 0:256], sq_scr[0:1, 0:128], wld[_s][0:1, 0:256],
                start=True, stop=True),
               r=["sq_scr", "wld%d" % slot], w=["tr32"])
            op("dve", lambda e: e.tensor_copy(qsb_ap, tr32[:, 0:256]),
               r=["tr32"], w=["qsb"])

            def shift_tile(i, dst, dst_name):
                op("pool", lambda e, _i=i, _d=dst: e.dma_start(
                    _d[1:128, :], x_sb[0:127, _i * D + 512:_i * D + 1024]),
                   r=["x_sb"], w=[dst_name], dma=True)
                if i == 0:
                    op("dve", lambda e, _d=dst: e.memset(_d[0:1, :], 0.0), w=[dst_name])
                else:
                    op("pool", lambda e, _i=i, _d=dst: e.dma_start(
                        _d[0:1, :], x_sb[127:128, (_i - 1) * D + 512:(_i - 1) * D + 1024]),
                       r=["x_sb"], w=[dst_name], dma=True)

            load_cast(tB["w2"], "w2bf", f"w2_{l}", HID, D)
            slot = load_cast.k % 2
            load_cast.k += 1
            op("sp", lambda e, _l=l, _s=slot: e.dma_start(
                wld[_s][0:1, 0:D], dram[f"b2_{_l}"]), w=["wld%d" % slot], dma=True)
            op("pool", lambda e, _s=slot: e.tensor_copy(
                tB["b2hi"][0:1, :], wld[_s][0:1, 0:D]),
               r=["wld%d" % slot], w=["b2hi"])
            def l2rot(scr, scr_name, a_off, b_off, pb):
                sums8 = sums_bank[pb][:, 0:8]
                sn = f"c_sums{pb}"
                op("act", lambda e, _s=scr: e.activation(
                    sq_scr[:, 0:512], _s[:], AF.Square), r=[scr_name], w=["sq_scr"])
                red3 = bass.AP(sq_scr, 0, [[D, 128], [64, 8], [1, 64]])
                op("dve", lambda e, _r=red3, _u=sums8: e.tensor_reduce(_u, _r, AX.X, ALU.add),
                   r=["sq_scr"], w=[sn])
                op("act", lambda e, _u=sums8: e.activation(_u, _u, AF.Sqrt),
                   r=[sn], w=[sn])
                op("dve", lambda e, _u=sums8: e.tensor_scalar_max(_u, _u, 1.0e-12),
                   r=[sn], w=[sn])
                op("dve", lambda e, _u=sums8: e.reciprocal(_u, _u), r=[sn], w=[sn])
                for h in range(8):
                    op("dve", lambda e, _s=scr, _h=h, _u=sums8: e.tensor_scalar_mul(
                        _s[:, _h * 64:_h * 64 + 64], _s[:, _h * 64:_h * 64 + 64],
                        _u[:, _h:_h + 1]), r=[scr_name, sn], w=[scr_name])
                vh = lambda t, off: bass.AP(t, off, [[t.ap().ap[0][0], 128], [64, 8], [1, 32]])
                a_bc = bass.AP(tA["abt"], a_off, [[4 * HD, 128], [0, 8], [1, 64]])
                b_bc = bass.AP(tA["abt"], b_off, [[4 * HD, 128], [0, 8], [1, 64]])
                op("dve", lambda e, _s=scr: e.tensor_scalar_mul(
                    vh(tA["rh"], 0), vh(_s, 32), -1.0), r=[scr_name], w=["rhI"])
                op("dve", lambda e, _s=scr: e.tensor_copy(vh(tA["rh"], 32), vh(_s, 0)),
                   r=[scr_name], w=["rhI"])
                op("dve", lambda e, _s=scr, _a=a_bc: e.tensor_mul(_s[:], _s[:], _a),
                   r=[scr_name, "abt"], w=[scr_name])
                op("dve", lambda e, _b=b_bc: e.tensor_mul(
                    tA["rh"][:], tA["rh"][:], _b), r=["rhI", "abt"], w=["rhI"])
                op("dve", lambda e, _s=scr: e.tensor_add(_s[:], _s[:], tA["rh"][:]),
                   r=[scr_name, "rhI"], w=[scr_name])

            def P1i(i):
                sl = i % 2
                k6 = i % 6
                pb = i % 2
                xt = x_sb[:, i * D:(i + 1) * D]
                shift_tile(i, tA["h2"][sl], "h2A_%d" % sl)
                op("sp", lambda e, _i=i, _s=sl: e.dma_start(
                    tA["rb"][_s][:], dram["rotpk"][:, _i * 256:(_i + 1) * 256]),
                   w=["rotb_%d" % sl], dma=True)
                op("dve", lambda e, _s=sl: e.tensor_mul(
                    tA["abt"][:], tA["rb"][_s][:], qsb_ap),
                   r=["rotb_%d" % sl, "qsb"], w=["abt"])

                def ev_lt(g0, glen, _sl=sl):
                    op("act", lambda e, _g0=g0, _n=glen * 128: e.activation(
                        tA["lt"][_sl][:, _g0 * 128:_g0 * 128 + _n], tr32[:, 0:_n], AF.Identity),
                       r=["tr32"], w=["ltI_%d" % _sl])
                tr_f32([(xt[:, j * 128:j * 128 + 128], "x_sb") for j in range(4)] +
                       [(tA["h2"][sl][:, j * 128:j * 128 + 128], "h2A_%d" % sl)
                        for j in range(4)], ev_lt)
                for n in range(3):
                    ps = pP[n]
                    psn = PN[id(ps)]
                    for j in range(8):
                        op("pe", lambda e, _n=n, _j=j, _p=ps, _s=sl: e.matmul(
                            _p[:, 0:512], tA["lt"][_s][:, _j * 128:_j * 128 + 128],
                            tA["wqkv"][:, _j * 1536 + _n * 512:_j * 1536 + _n * 512 + 512],
                            start=(_j == 0), stop=(_j == 7)),
                           r=["ltI_%d" % sl, "wqkv_bf"], w=[psn])
                op("dve", lambda e: e.tensor_copy(tA["q_scr"][:], pP[0][:]),
                   r=["pP0"], w=["q_scrI"])
                op("act", lambda e: e.activation(tA["k_scr"][:], pP[1][:], AF.Identity),
                   r=["pP1"], w=["k_scrI"])
                op("act", lambda e, _k=k6: e.activation(
                    tA["v5"][:, _k * INNER:(_k + 1) * INNER], pP[2][:], AF.Identity),
                   r=["pP2"], w=["v5"])
                l2rot(tA["q_scr"], "q_scrI", 0, HD, pb)
                l2rot(tA["k_scr"], "k_scrI", 2 * HD, 3 * HD, pb)
                for gi in range(4):
                    op("pe", lambda e, _g=gi: e.transpose(
                        tr32[:, _g * 128:_g * 128 + 128],
                        tA["q_scr"][:, _g * 128:_g * 128 + 128], id32[:]),
                       r=["q_scrI", "id32"], w=["tr32"])
                op("act", lambda e, _s=sl: e.activation(
                    tA["qhT"][_s][:], tr32[:], AF.Identity, scale=c_eight),
                   r=["tr32", "c_eight"], w=["qhTI_%d" % sl])
                for gi in range(4):
                    op("pe", lambda e, _g=gi: e.transpose(
                        tr32[:, _g * 128:_g * 128 + 128],
                        tA["k_scr"][:, _g * 128:_g * 128 + 128], id32[:]),
                       r=["k_scrI", "id32"], w=["tr32"])
                op("act", lambda e, _k=k6: e.activation(
                    tA["khT5"][:, _k * INNER:(_k + 1) * INNER], tr32[:], AF.Identity),
                   r=["tr32"], w=["khT5"])

            def P2i(i):
                sl = i % 2
                xt = x_sb[:, i * D:(i + 1) * D]
                kstart = max(0, i - 4)
                nk = i - kstart + 1
                first_av = [True]

                def emit_ptav(h):
                    hs = h % 2
                    for c in range(nk):
                        op("pe", lambda e, _c=c, _hs=hs: e.transpose(
                            tbf[:, _c * 128:_c * 128 + 128],
                            tA["pb"][_hs][:, _c * 128:_c * 128 + 128], id16[:]),
                           r=["pbI_%d" % hs, "id16"], w=["tbf"])
                    op("dve", lambda e, _n=nk * 128, _hs=hs: e.tensor_copy(
                        tA["pt"][_hs][:, 0:_n], tbf[:, 0:_n]), r=["tbf"], w=["ptI_%d" % hs])
                    for c in range(nk):
                        slot = (kstart + c) % 6
                        st_ = first_av[0]
                        first_av[0] = False
                        sp_ = (h == HEADS_I - 1) and (c == nk - 1)
                        op("pe", lambda e, _c=c, _hs=hs, _st=st_, _sp=sp_, _slot=slot, _h=h:
                           e.matmul(pP[3][:, _h * 64:_h * 64 + 64],
                                    tA["pt"][_hs][:, _c * 128:_c * 128 + 128],
                                    tA["v5"][:, _slot * INNER + _h * 64:_slot * INNER + _h * 64 + 64],
                                    start=_st, stop=_sp),
                           r=["ptI_%d" % hs, "v5"], w=["pP3"])

                pend = None
                for h in range(HEADS_I):
                    hp = (h % 2) * 64
                    hc = (h // 2) * 128
                    hs = h % 2
                    nP = min(nk, 4)
                    for c in range(nP):
                        slot = (kstart + c) % 6
                        op("pe", lambda e, _o=c * 128, _s=slot, _hp=hp, _hc=hc,
                           _sl=sl, _st=(c == 0): e.matmul(
                            simP[:, _o:_o + 128],
                            tA["qhT"][_sl][_hp:_hp + 64, _hc:_hc + 128],
                            tA["khT5"][_hp:_hp + 64, _s * INNER + _hc:_s * INNER + _hc + 128],
                            start=_st, stop=False),
                           r=["qhTI_%d" % sl, "khT5"], w=["simP"])
                    if nk == 5:
                        op("pe", lambda e: e.matmul(simP[:, 0:128], id16[:], m_upi[:],
                                                    start=False, stop=True),
                           r=["id16", "m_upi"], w=["simP"])
                        slot = i % 6
                        op("pe", lambda e, _s=slot, _hp=hp, _hc=hc, _sl=sl: e.matmul(
                            simX[:, 0:128],
                            tA["qhT"][_sl][_hp:_hp + 64, _hc:_hc + 128],
                            tA["khT5"][_hp:_hp + 64, _s * INNER + _hc:_s * INNER + _hc + 128],
                            start=True, stop=False),
                           r=["qhTI_%d" % sl, "khT5"], w=["simX"])
                        op("pe", lambda e: e.matmul(simX[:, 0:128], id16[:], m_low[:],
                                                    start=False, stop=True),
                           r=["id16", "m_low"], w=["simX"])
                    else:
                        op("pe", lambda e, _o=(nk - 1) * 128: e.matmul(
                            simP[:, _o:_o + 128], id16[:], m_low[:], start=False, stop=True),
                           r=["id16", "m_low"], w=["simP"])
                    if pend is not None:
                        emit_ptav(pend)
                    nPc = min(nk, 4) * 128
                    nmv = c_nmax[h % 2]
                    nmn = f"c_nmax{h % 2}"
                    rdn = f"c_rd{h}"
                    rdc = c_rdens[:, h:h + 1]
                    op("dve", lambda e, _n=nPc, _m=nmv: e.tensor_reduce(
                        _m, simP[:, 0:_n], AX.X, ALU.max, negate=True),
                       r=["simP"], w=[nmn])
                    if nk == 5:
                        xv = c_nmaxx[h % 2]
                        xn = f"c_nmaxx{h % 2}"
                        op("dve", lambda e, _x=xv: e.tensor_reduce(
                            _x, simX[:], AX.X, ALU.max, negate=True), r=["simX"], w=[xn])
                        op("dve", lambda e, _m=nmv, _x=xv: e.tensor_tensor(
                            _m, _m, _x, ALU.min), r=[nmn, xn], w=[nmn])
                    op("act", lambda e, _n=nPc, _hs=hs, _m=nmv, _c=rdc: e.activation(
                        tA["pb"][_hs][:, 0:_n], simP[:, 0:_n], AF.Exp, bias=_m,
                        scale=1.0, accum_out=_c),
                       r=["simP", nmn], w=["pbI_%d" % hs, rdn])
                    if nk == 5:
                        dxv = c_denx[h % 2]
                        dxn = f"c_denx{h % 2}"
                        op("act", lambda e, _hs=hs, _m=nmv, _x=dxv: e.activation(
                            tA["pb"][_hs][:, 512:640], simX[:], AF.Exp, bias=_m,
                            scale=1.0, accum_out=_x),
                           r=["simX", nmn], w=["pbI_%d" % hs, dxn])
                        op("dve", lambda e, _c=rdc, _x=dxv: e.tensor_add(_c, _c, _x),
                           r=[rdn, dxn], w=[rdn])
                    op("dve", lambda e, _c=rdc: e.reciprocal(_c, _c), r=[rdn], w=[rdn])
                    pend = h
                emit_ptav(pend)
                for h in range(HEADS_I):
                    op("act", lambda e, _h=h, _s=sl: e.activation(
                        tA["ob"][_s][:, _h * 64:_h * 64 + 64], pP[3][:, _h * 64:_h * 64 + 64],
                        AF.Identity, scale=c_rdens[:, _h:_h + 1]),
                       r=["pP3", f"c_rd{h}"], w=["obI_%d" % sl])
                for c in range(4):
                    op("pe", lambda e, _c=c, _s=sl: e.transpose(
                        tbf[:, _c * 128:_c * 128 + 128],
                        tA["ob"][_s][:, _c * 128:_c * 128 + 128], id16[:]),
                       r=["obI_%d" % sl, "id16"], w=["tbf"])
                op("dve", lambda e, _s=sl: e.tensor_copy(tA["ot"][_s][:], tbf[:, 0:512]),
                   r=["tbf"], w=["otI_%d" % sl])
                for n in range(2):
                    ps = pP[n]
                    psn = PN[id(ps)]
                    for j in range(4):
                        op("pe", lambda e, _n=n, _j=j, _p=ps, _s=sl: e.matmul(
                            _p[:, 0:512], tA["ot"][_s][:, _j * 128:_j * 128 + 128],
                            tA["woin"][:, _j * D + _n * 512:_j * D + _n * 512 + 512],
                            start=(_j == 0), stop=(_j == 3)),
                           r=["otI_%d" % sl, "woin_bf"], w=[psn])
                op("dve", lambda e, _x=xt: e.tensor_add(_x[:, 0:512], _x[:, 0:512], pP[0][:]),
                   r=["pP0", "x_sb"], w=["x_sb"])
                op("dve", lambda e, _x=xt: e.tensor_add(
                    _x[:, 512:1024], _x[:, 512:1024], pP[1][:]),
                   r=["pP1", "x_sb"], w=["x_sb"])

            P1i(0)
            for i in range(NT):
                if i + 1 < NT:
                    P1i(i + 1)
                P2i(i)

            em.fence(prev_names, namesB)
            prev_names = namesB
            w1_off = []
            pos = 0
            for m in range(NHT):
                aw = min(128, HID - m * 128)
                w1_off.append((pos, aw))
                w1_off.append((pos + aw, aw))
                pos += 2 * aw

            for tc in range(4):
                for ii in range(4):
                    i = tc * 4 + ii
                    sl = i % 2
                    xt = x_sb[:, i * D:(i + 1) * D]
                    shift_tile(i, tB["h2"][sl], "h2B_%d" % sl)
                    pb = i % 2
                    rstd, nmr = ln_stats([(xt[:, 0:512], 512, "x_sb"),
                              (tB["h2"][sl][:], 512, "h2B_%d" % sl)], D, pb)
                    op("act", lambda e, _x=xt, _r=rstd, _n=nmr: e.activation(
                        ln_scr[:, 0:512], _x[:, 0:512], AF.Identity, bias=_n, scale=_r),
                       r=["x_sb", f"c_rstd{pb}", f"c_nmr{pb}"], w=["ln_scr"])
                    op("act", lambda e, _s=sl, _r=rstd, _n=nmr: e.activation(
                        ln_scr[:, 512:1024], tB["h2"][_s][:], AF.Identity,
                        bias=_n, scale=_r),
                       r=["h2B_%d" % sl, f"c_rstd{pb}", f"c_nmr{pb}"], w=["ln_scr"])
                    for g0 in (0, 4):
                        for gi in range(4):
                            j = g0 + gi
                            op("pe", lambda e, _j=j, _g=gi: e.transpose(
                                tr32[:, _g * 128:_g * 128 + 128],
                                ln_scr[:, _j * 128:_j * 128 + 128], id32[:]),
                               r=["ln_scr", "id32"], w=["tr32"])
                        for gi in range(4):
                            j = g0 + gi
                            op("act", lambda e, _j=j, _g=gi, _ii=ii, _l=l: e.activation(
                                tB["ln2T"][:, _j * 512 + _ii * 128:_j * 512 + _ii * 128 + 128],
                                tr32[:, _g * 128:_g * 128 + 128], AF.Identity,
                                bias=lnb[_l][:, _j:_j + 1], scale=lng[_l][:, _j:_j + 1]),
                               r=["tr32", f"lng{l}", f"lnb{l}"], w=["ln2T"])
                for ot in range(2 * NHT):
                    col0, ow = w1_off[ot]
                    wb = ot % 2
                    op("sp", lambda e, _l=l, _o=ot, _w=ow, _b=wb: e.dma_start(
                        tB["w1"][_b][:, 0:8 * _w], d_w1img[_l, _o, :, 0:8 * _w],),
                       r=["w1img_d"], w=["w1bf_%d" % wb], dma=True)
                    pa = pP[2 + wb]
                    pan = PN[id(pa)]
                    for j in range(8):
                        op("pe", lambda e, _j=j, _p=pa, _b=wb, _w=ow: e.matmul(
                            _p[0:_w, 0:512], tB["w1"][_b][:, _j * _w:(_j + 1) * _w],
                            tB["ln2T"][:, _j * 512:_j * 512 + 512],
                            start=(_j == 0), stop=(_j == 7)),
                           r=["w1bf_%d" % wb, "ln2T"], w=[pan])
                    ht = ot // 2
                    if wb == 0:
                        op("act", lambda e, _p=pa, _w=ow, _h=ht, _l=l: e.activation(
                            tB["ag"][0:_w, 0:512], _p[0:_w, 0:512], AF.Identity,
                            bias=b1a[_l][0:_w, _h:_h + 1]),
                           r=[pan, f"b1a{l}"], w=["ag_a"])
                    else:
                        op("act", lambda e, _p=pa, _w=ow, _h=ht, _l=l: e.activation(
                            tB["ag"][0:_w, 1024:1536], _p[0:_w, 0:512], AF.Gelu,
                            bias=b1g[_l][0:_w, _h:_h + 1]),
                           r=[pan, f"b1g{l}"], w=["ag_g"])
                        op("dve", lambda e, _w=ow, _h=ht: e.tensor_mul(
                            tB["gg"][0:_w, _h * 512:_h * 512 + 512],
                            tB["ag"][0:_w, 0:512], tB["ag"][0:_w, 1024:1536]),
                           r=["ag_a", "ag_g"], w=["geglu"])
                for ii in range(4):
                    i = tc * 4 + ii
                    xt = x_sb[:, i * D:(i + 1) * D]
                    for n in range(2):
                        ps = pP[n]
                        psn = PN[id(ps)]
                        for kk in range(NHT):
                            kw = 128 if kk < NHT - 1 else HID_LAST
                            op("pe", lambda e, _n=n, _k=kk, _p=ps, _ii=ii, _kw=kw: e.matmul(
                                _p[:, 0:512],
                                tB["gg"][0:_kw, _k * 512 + _ii * 128:_k * 512 + _ii * 128 + 128],
                                tB["w2"][0:_kw, _k * D + _n * 512:_k * D + _n * 512 + 512],
                                start=(_k == 0), stop=False),
                               r=["geglu", "w2bf"], w=[psn])
                        op("pe", lambda e, _n=n, _p=ps, _l=l: e.matmul(
                            _p[:, 0:512], ones_bf[:],
                            tB["b2hi"][0:1, _n * 512:_n * 512 + 512],
                            start=False, stop=True),
                           r=["ones_bf", "b2hi"], w=[psn])
                    op("dve", lambda e, _x=xt: e.tensor_add(
                        _x[:, 0:512], _x[:, 0:512], pP[0][:]), r=["pP0", "x_sb"], w=["x_sb"])
                    op("dve", lambda e, _x=xt: e.tensor_add(
                        _x[:, 512:1024], _x[:, 512:1024], pP[1][:]),
                       r=["pP1", "x_sb"], w=["x_sb"])

        if PHASE_LIMIT == "all" and DEBUG_TAP is None:
            em.fence(prev_names, FINAL_T)
            for i in range(NT):
                sl = i % 2
                xt = x_sb[:, i * D:(i + 1) * D]
                op("sp", lambda e, _i=i, _s=sl: e.dma_start(
                    x1b[_s][:], d_x1[:, _i * D:(_i + 1) * D]),
                   r=["x1d"], w=[f"x1b{sl}"], dma=True)
                pb = i % 2
                rstd, nmr = ln_stats([(xt, D, "x_sb")], D, pb)
                op("dve", lambda e, _s=sl, _n=nmr: e.tensor_scalar_add(
                    x1b[_s][:], x1b[_s][:], _n),
                   r=[f"x1b{sl}", f"c_nmr{pb}"], w=[f"x1b{sl}"])
                op("dve", lambda e, _x=xt, _s=sl, _r=rstd: e.scalar_tensor_tensor(
                    _x, _x, _r, x1b[_s][:], ALU.mult, ALU.add),
                   r=[f"x1b{sl}", f"c_rstd{pb}", "x_sb"], w=["x_sb"])
        tok = op("sp", lambda e: e.dma_start(
            d_out.rearrange("(n p) d -> p n d", p=128), x_sb3),
            r=["x_sb"], w=["out_dram"], dma=True)
        em.final_wait("sp", tok)

        with nc.Block() as block:
            em.replay(block)
    return nc


def _prep_host(inputs):
    f = np.float32
    low_i, up_i, up_s = _masks()
    common = {
        "wqT": np.ascontiguousarray(inputs["L0_attn_wq_w"].T),
        "wkT": np.ascontiguousarray(inputs["L0_attn_wk_w"].T),
        "wvT": np.ascontiguousarray(inputs["L0_attn_wv_w"].T),
        "woT": np.ascontiguousarray(inputs["L0_attn_wo_w"].T),
        "brow4": np.concatenate([
            inputs["L0_attn_wq_b"], inputs["L0_attn_wk_b"],
            inputs["L0_attn_wv_b"], inputs["L0_attn_wo_b"]]).reshape(1, 4 * D),
        "qn2": np.tile(inputs["L0_attn_qn_w"], 2).reshape(128, 1),
        "kn2": np.tile(inputs["L0_attn_kn_w"], 2).reshape(128, 1),
        "rotpk": _host_tables(),
        "m_low": low_i, "m_upi": up_i, "m_ups": up_s,
        "id32": np.eye(128, dtype=f),
        "id16": np.eye(128, dtype=ml_dtypes.bfloat16),
    }
    for l in range(2):
        w1 = inputs["L0_ffn_w1"][l]
        b1 = inputs["L0_ffn_b1"][l]
        perm = []
        for m in range(NHT):
            aw = min(128, HID - m * 128)
            perm.extend(range(m * 128, m * 128 + aw))
            perm.extend(range(HID + m * 128, HID + m * 128 + aw))
        perm = np.asarray(perm)
        w1p = np.ascontiguousarray(w1[:, perm])
        b1p = b1[perm]
        b1av = np.zeros((128, NHT), f)
        b1gv = np.zeros((128, NHT), f)
        pos = 0
        for m in range(NHT):
            aw = min(128, HID - m * 128)
            b1av[0:aw, m] = b1p[pos:pos + aw]
            b1gv[0:aw, m] = b1p[pos + aw:pos + 2 * aw]
            pos += 2 * aw
        qs = inputs["L0_ffn_q_scale"][l]
        ks = inputs["L0_ffn_k_scale"][l]
        common.update({
            f"wqkv{l}": np.ascontiguousarray(inputs["L0_ffn_wqkv"][l]),
            f"wo{l}": np.ascontiguousarray(inputs["L0_ffn_wo"][l]),
            f"w1p{l}": w1p,
            f"b1a{l}": b1av, f"b1g{l}": b1gv,
            f"w2_{l}": np.ascontiguousarray(inputs["L0_ffn_w2"][l]),
            f"b2_{l}": inputs["L0_ffn_b2"][l].reshape(1, D),
            f"lng{l}": np.ascontiguousarray(inputs["L0_ffn_ln_g"][l].reshape(8, 128).T),
            f"lnb{l}": np.ascontiguousarray(inputs["L0_ffn_ln_b"][l].reshape(8, 128).T),
            f"qsrows{l}": np.concatenate([
                qs, np.concatenate([qs[32:], qs[:32]]),
                ks, np.concatenate([ks[32:], ks[:32]])]).reshape(1, 4 * HD),
        })
    return common


def kernel(**inputs):
    inputs = {k: np.asarray(v) for k, v in inputs.items()}
    common = _prep_host(inputs)
    nc = build()
    in_maps = []
    for c in range(8):
        m = dict(common)
        m["x"] = np.ascontiguousarray(inputs["L0_attn_x"][c])
        in_maps.append(m)
    res = run_bass_kernel_spmd(nc, in_maps, core_ids=list(range(8)))
    return np.stack([res.results[c]["out"] for c in range(8)], axis=0)


# version 08
# speedup vs baseline: 1.2149x; 1.0487x over previous
import numpy as np
import ml_dtypes
import concourse.bass as bass
import concourse.mybir as mybir
from concourse.bass_utils import run_bass_kernel_spmd

F32 = mybir.dt.float32
BF16 = mybir.dt.bfloat16
AF = mybir.ActivationFunctionType
ALU = mybir.AluOpType
AX = mybir.AxisListType

S, D, NT, HD = 2048, 1024, 16, 64
HEADS_O, HEADS_I, INNER = 16, 8, 512
HID, NHT, HID_LAST = 2730, 22, 42
EPS = 1e-5

PHASE_LIMIT = "all"
NT_LIM = NT
DEBUG_TAP = None

ENGINES = ("pe", "act", "dve", "pool", "sp")
QSEM = {"sp": "q_sp", "pool": "q_pool", "act": "q_act"}


class Buf:
    __slots__ = ("name", "w", "readers")

    def __init__(self, name):
        self.name = name
        self.w = None
        self.readers = []


class Emitter:
    def __init__(self, nc):
        self.nc = nc
        self.prog = {e: [] for e in ENGINES}
        self.sems = {}
        self.counts = {n: 0 for n in ("pe", "act", "dve", "pool", "sp")}
        self.lastwait = {}
        self.bufs = {}

    def make_sems(self, ctx):
        for nm in list(self.counts):
            self.sems[nm] = self.nc.alloc_semaphore("sem_" + nm)

    def dma_sem(self, bufname):
        nm = "d_" + bufname
        if nm not in self.sems:
            self.sems[nm] = self.nc.alloc_semaphore("sem_" + nm)
            self.counts[nm] = 0
        return nm

    def buf(self, name):
        if name not in self.bufs:
            self.bufs[name] = Buf(name)
        return self.bufs[name]

    PSUM_BUFS = frozenset(("pP0", "pP1", "pP2", "pP3", "simP", "simX", "tbf", "tr32"))

    def op(self, eng, fn, r=(), w=(), dma=False):
        reads = [self.buf(n) for n in r if n not in self.PSUM_BUFS]
        writes = [self.buf(n) for n in w] + [self.buf(n) for n in r
                                             if n in self.PSUM_BUFS]
        deps = []
        for b in reads:
            if b.w is not None:
                deps.append(b.w)
        for b in writes:
            if b.w is not None:
                deps.append(b.w)
            deps.extend(b.readers)
        mysem = self.dma_sem(w[0]) if dma else eng
        need = {}
        for (sem, val) in deps:
            if dma and sem == mysem:
                continue
            if sem == eng == "pe":
                continue
            if need.get(sem, 0) < val:
                need[sem] = val
        for sem, val in sorted(need.items()):
            if self.lastwait.get((eng, sem), 0) >= val:
                continue
            self.lastwait[(eng, sem)] = val
            self.prog[eng].append(
                lambda e, _s=sem, _v=val: e.wait_ge(self.sems[_s], _v))
        amt = 16 if dma else 1
        self.counts[mysem] += amt
        val = self.counts[mysem]
        self.prog[eng].append(
            lambda e, _f=fn, _s=mysem, _a=amt: _f(e).then_inc(self.sems[_s], _a))
        tok = (mysem, val)
        for b in reads:
            b.readers.append(tok)
        for b in writes:
            b.w = tok
            b.readers = []
        return tok

    def fence(self, old_names, new_names):
        toks = {}
        for n in old_names:
            b = self.bufs.get(n)
            if b is None:
                continue
            for t in ([b.w] if b.w else []) + b.readers:
                if toks.get(t[0], 0) < t[1]:
                    toks[t[0]] = t[1]
        tl = list(toks.items())
        for n in new_names:
            b = self.buf(n)
            b.readers = b.readers + tl
        for n in old_names:
            if n in self.bufs:
                self.bufs[n].w = None
                self.bufs[n].readers = []

    def final_wait(self, eng, tok):
        self.prog[eng].append(lambda e, _t=tok: e.wait_ge(self.sems[_t[0]], _t[1]))

    def replay(self, block):
        for eng, attr in (("sp", "sync"), ("pe", "tensor"), ("act", "scalar"),
                          ("dve", "vector"), ("pool", "gpsimd")):
            seq = self.prog[eng]

            def make(seq_):
                def run(e):
                    for f in seq_:
                        f(e)
                return run
            getattr(block, attr)(make(seq))


def _host_tables():
    half = np.arange(0, HD, 2, dtype=np.float64)
    inv_freq = 1.0 / 10000.0 ** (half / HD)
    t = np.arange(S, dtype=np.float64)
    freqs = t[:, None] * inv_freq[None, :]
    freqs = np.concatenate([freqs, freqs], axis=1)
    base = (half + 0.4 * HD) / (1.4 * HD)
    power = (t - S // 2) / 256.0
    scale = base[None, :] ** power[:, None]
    scale = np.concatenate([scale, scale], axis=1)
    cq = (np.cos(freqs) * scale).astype(np.float32)
    sq = (np.sin(freqs) * scale).astype(np.float32)
    ck = (np.cos(freqs) / scale).astype(np.float32)
    sk = (np.sin(freqs) / scale).astype(np.float32)
    packed = np.concatenate([cq, sq, ck, sk], axis=1)
    return np.ascontiguousarray(
        packed.reshape(NT, 128, 4 * HD).transpose(1, 0, 2).reshape(128, NT * 4 * HD))


def _masks():
    i = np.arange(128)[:, None]
    j = np.arange(128)[None, :]
    neg = np.float32(-1.0e30)
    bf = ml_dtypes.bfloat16
    low_i = np.where(i >= j, 0.0, neg).astype(np.float32).astype(bf)
    up_i = np.where(j >= i, 0.0, neg).astype(np.float32).astype(bf)
    up_s = np.where(j > i, 0.0, neg).astype(np.float32).astype(bf)
    return low_i, up_i, up_s


def build():
    nc = bass.Bass("TRN2", target_bir_lowering=False)
    dram = {}

    def din(name, shape, dt=F32):
        dram[name] = nc.dram_tensor(name, list(shape), dt, kind="ExternalInput").ap()

    din("x", (S, D))
    for w in ("wqT", "wkT", "wvT", "woT"):
        din(w, (D, D))
    din("brow4", (1, 4 * D))
    din("qn2", (128, 1))
    din("kn2", (128, 1))
    din("rotpk", (128, NT * 4 * HD))
    din("m_low", (128, 128), BF16)
    din("m_upi", (128, 128), BF16)
    din("m_ups", (128, 128), BF16)
    din("id32", (128, 128), F32)
    din("id16", (128, 128), BF16)
    for l in range(2):
        din(f"wqkv{l}", (D, 3 * INNER))
        din(f"wo{l}", (INNER, D))
        din(f"w1p{l}", (D, 2 * HID))
        din(f"b1a{l}", (128, NHT))
        din(f"b1g{l}", (128, NHT))
        din(f"w2_{l}", (HID, D))
        din(f"b2_{l}", (1, D))
        din(f"lng{l}", (128, 8))
        din(f"lnb{l}", (128, 8))
        din(f"qsrows{l}", (1, 4 * HD))
    d_out = nc.dram_tensor("out", [S, D], F32, kind="ExternalOutput").ap()
    d_x1 = nc.dram_tensor("x1_scratch", [128, NT * D], F32, kind="Internal").ap()
    d_w1img = nc.dram_tensor("w1img", [2, 2 * NHT, 128, 1024], BF16, kind="Internal").ap()

    from contextlib import ExitStack
    with ExitStack() as ctx:
        em = Emitter(nc)
        em.make_sems(ctx)

        def sba(name, shape, dt=F32):
            return nc.alloc_sbuf_tensor(name, list(shape), dt)

        x_sb = sba("x_sb", (128, NT * D))
        ln_scr = sba("ln_scr", (128, D))
        sq_scr = sba("sq_scr", (128, D))
        cols = sba("cols", (128, 384))
        m_low = sba("m_low_sb", (128, 128), BF16)
        m_upi = sba("m_upi_sb", (128, 128), BF16)
        m_ups = sba("m_ups_sb", (128, 128), BF16)
        id32 = sba("id32_sb", (128, 128))
        id16 = sba("id16_sb", (128, 128), BF16)
        ones_bf = sba("ones_bf", (1, 128), BF16)
        qn2 = sba("qn2_sb", (128, 1))
        kn2 = sba("kn2_sb", (128, 1))
        lng = [sba(f"lng{l}_sb", (128, 8)) for l in range(2)]
        lnb = [sba(f"lnb{l}_sb", (128, 8)) for l in range(2)]
        b1a = [sba(f"b1a{l}_sb", (128, NHT)) for l in range(2)]
        b1g = [sba(f"b1g{l}_sb", (128, NHT)) for l in range(2)]
        wld = [sba(f"wld{i}", (128, 1024)) for i in range(2)]
        w1stage = sba("w1stage", (128, 1024), BF16)
        rowb = sba("rowb", (1, 256), BF16)

        ARENA = 122880
        lo, hi = nc.bump_sbuf(ARENA)
        assert hi - lo >= ARENA

        def ar(name, shape, dt, off):
            sz = int(np.prod(shape[1:])) * mybir.dt.size(dt)
            assert off + sz <= ARENA, (name, off, sz)
            return nc.alloc_sbuf_tensor_at(name, list(shape), dt, offset=lo + off)

        wT_bf = {w: ar(w + "_bf", (128, 8 * D), BF16, k * 16384)
                 for k, w in enumerate(("wqT", "wkT", "wvT", "woT"))}
        q_scrO = ar("q_scrO", (128, D), F32, 65536)
        k_scrO = ar("k_scrO", (128, D), F32, 69632)
        ltO = [ar(f"ltO{i}", (128, D), BF16, 73728 + i * 2048) for i in range(2)]
        qhTO = [ar(f"qhTO{i}", (128, D), BF16, 77824 + i * 2048) for i in range(2)]
        khT3 = ar("khT3", (128, 4 * D), BF16, 81920)
        v3 = ar("v3", (128, 4 * D), BF16, 90112)
        pbO = [ar(f"pbO{i}", (128, 512), BF16, 98304 + i * 1024) for i in range(2)]
        ptO = [ar(f"ptO{i}", (128, 512), BF16, 100352 + i * 1024) for i in range(2)]
        obO = [ar(f"obO{i}", (128, D), BF16, 102400 + i * 2048) for i in range(2)]
        otO = [ar(f"otO{i}", (128, D), BF16, 106496 + i * 2048) for i in range(2)]
        brow4_hi = ar("brow4_hi", (1, 4 * D), BF16, 110592)
        OUTER_T = (["wqT_bf", "wkT_bf", "wvT_bf", "woT_bf", "q_scrO", "k_scrO",
                    "khT3", "v3", "brow4_hi"] +
                   [f"{p}{i}" for p in ("ltO", "qhTO", "pbO", "ptO", "obO", "otO")
                    for i in range(2)])

        def innerA():
            t = {}
            t["wqkv"] = ar("wqkv_bf", (128, 8 * 3 * INNER), BF16, 0)
            t["woin"] = ar("woin_bf", (128, 4 * D), BF16, 24576)
            t["q_scr"] = ar("q_scrI", (128, INNER), F32, 32768)
            t["k_scr"] = ar("k_scrI", (128, INNER), F32, 34816)
            t["rh"] = ar("rhI", (128, INNER), F32, 36864)
            t["lt"] = [ar(f"ltI_{i}", (128, D), BF16, 38912 + i * 2048) for i in range(2)]
            t["qhT"] = [ar(f"qhTI_{i}", (128, INNER), BF16, 43008 + i * 1024) for i in range(2)]
            t["khT5"] = ar("khT5", (128, 6 * INNER), BF16, 45056)
            t["v5"] = ar("v5", (128, 6 * INNER), BF16, 51200)
            t["pb"] = [ar(f"pbI_{i}", (128, 640), BF16, 57344 + i * 1280) for i in range(2)]
            t["pt"] = [ar(f"ptI_{i}", (128, 640), BF16, 59904 + i * 1280) for i in range(2)]
            t["ob"] = [ar(f"obI_{i}", (128, INNER), BF16, 62464 + i * 1024) for i in range(2)]
            t["ot"] = [ar(f"otI_{i}", (128, INNER), BF16, 64512 + i * 1024) for i in range(2)]
            t["h2"] = [ar(f"h2A_{i}", (128, INNER), F32, 66560 + i * 2048) for i in range(2)]
            t["rb"] = [ar(f"rotb_{i}", (128, 4 * HD), F32, 70656 + i * 1024) for i in range(2)]
            t["abt"] = ar("abt", (128, 4 * HD), F32, 72704)
            names = (["wqkv_bf", "woin_bf", "q_scrI", "k_scrI",
                      "rhI", "khT5", "v5", "abt",
                      "w2bf", "b2hi"] +
                     [f"{p}_{i}" for p in ("ltI", "qhTI", "pbI", "ptI", "obI", "otI",
                                           "h2A", "rotb") for i in range(2)])
            return t, names

        def innerB():
            t = {}
            t["w2"] = ar("w2bf", (128, NHT * D), BF16, 73728)
            t["gg"] = ar("geglu", (128, NHT * 512), BF16, 0)
            t["ln2T"] = ar("ln2T", (128, 8 * 512), BF16, 22528)
            t["w1"] = [ar(f"w1bf_{i}", (128, 4096), BF16, 30720 + i * 8192) for i in range(2)]
            t["ag"] = ar("agscr", (128, 2048), F32, 47104)
            t["h2"] = [ar(f"h2B_{i}", (128, INNER), F32, 55296 + i * 2048) for i in range(2)]
            t["b2hi"] = ar("b2hi", (1, D), BF16, 118784)
            names = (["geglu", "ln2T", "agscr"] +
                     [] +
                     [f"w1bf_{i}" for i in range(2)] + [f"h2B_{i}" for i in range(2)])
            return t, names

        x1b = [ar(f"x1b{i}", (128, D), F32, i * 4096) for i in range(2)]
        FINAL_T = ["x1b0", "x1b1"]

        pP = [nc.alloc_psum_tensor(f"pP{i}", [128, 512], F32) for i in range(4)]
        simP = nc.alloc_psum_tensor("simP", [128, 512], F32)
        simX = nc.alloc_psum_tensor("simX", [128, 128], F32)
        tbf = nc.alloc_psum_tensor("tbf", [128, 1024], BF16)
        tr32 = nc.alloc_psum_tensor("tr32", [128, 512], F32)
        PN = {id(pP[0]): "pP0", id(pP[1]): "pP1", id(pP[2]): "pP2", id(pP[3]): "pP3",
              id(simP): "simP", id(simX): "simX", id(tbf): "tbf", id(tr32): "tr32"}

        op = em.op

        def stats_bank(p):
            base = 8 * p
            return [cols[:, base + i:base + i + 1] for i in range(8)]
        sums_bank = [cols[:, 16:32], cols[:, 32:48]]
        c_rdens = cols[:, 48:64]
        c_nmax = [cols[:, 64 + i:65 + i] for i in range(2)]
        c_nmaxx = [cols[:, 66 + i:67 + i] for i in range(2)]
        c_denx = [cols[:, 68 + i:69 + i] for i in range(2)]
        qn_eff = cols[:, 70:71]
        c_eight = cols[:, 71:72]
        c_eps = cols[:, 72:73]
        c_s1, c_mu, c_m2, c_var, c_rstd, c_nmr, c_t6, c_t7 = stats_bank(0)

        def rstd_ap(pb):
            return stats_bank(pb)[4]

        x_sb3 = bass.AP(x_sb, 0, [[NT * D, 128], [D, NT], [1, D]])
        op("sp", lambda e: e.dma_start(
            x_sb3, dram["x"].rearrange("(n p) d -> p n d", p=128)),
            w=["x_sb"], dma=True)
        for t, n in ((m_low, "m_low"), (m_upi, "m_upi"), (m_ups, "m_ups"),
                     (id32, "id32"), (id16, "id16"), (qn2, "qn2"), (kn2, "kn2")):
            op("sp", lambda e, _t=t, _n=n: e.dma_start(_t[:], dram[_n]), w=[n], dma=True)
        for l in range(2):
            for n, t in ((f"lng{l}", lng[l]), (f"lnb{l}", lnb[l]),
                         (f"b1a{l}", b1a[l]), (f"b1g{l}", b1g[l])):
                op("sp", lambda e, _t=t, _n=n: e.dma_start(_t[:], dram[_n]), w=[n], dma=True)
        op("dve", lambda e: e.memset(ones_bf[:], 1.0), w=["ones_bf"])
        op("dve", lambda e: e.memset(c_eps, EPS), w=["c_eps"])
        op("dve", lambda e: e.memset(c_eight, 8.0), w=["c_eight"])
        op("dve", lambda e: e.tensor_scalar_mul(qn_eff, qn2[:], 0.125), r=["qn2"], w=["c_qe"])

        def load_cast(dst_bf, dst_name, dram_name, n_rows, n_cols, cast_eng="pool"):
            nj = (n_rows + 127) // 128
            for j in range(nj):
                rows = min(128, n_rows - j * 128)
                for c0 in range(0, n_cols, 1024):
                    cw = min(1024, n_cols - c0)
                    slot = load_cast.k % 2
                    load_cast.k += 1
                    op("sp", lambda e, _j=j, _c0=c0, _cw=cw, _r=rows, _s=slot,
                       _d=dram_name: e.dma_start(
                           wld[_s][0:_r, 0:_cw],
                           dram[_d][_j * 128:_j * 128 + _r, _c0:_c0 + _cw]),
                       w=["wld%d" % slot], dma=True)
                    op(cast_eng, lambda e, _j=j, _c0=c0, _cw=cw, _r=rows, _s=slot:
                       e.tensor_copy(
                           dst_bf[0:_r, _j * n_cols + _c0:_j * n_cols + _c0 + _cw],
                           wld[_s][0:_r, 0:_cw]),
                       r=["wld%d" % slot], w=[dst_name])
        load_cast.k = 0

        def ln_stats(srcs, total_d, pb=0):
            c_s1, c_mu, c_m2, c_var, c_rstd, c_nmr, c_t6, c_t7 = stats_bank(pb)
            sx = str(pb)
            for idx, (apv, wdt, rn) in enumerate(srcs):
                if idx == 0:
                    op("dve", lambda e, _a=apv, _c=c_s1: e.tensor_reduce(_c, _a, AX.X, ALU.add),
                       r=[rn], w=["c_s1" + sx])
                    op("act", lambda e, _a=apv, _w=wdt, _c=c_t6: e.activation(
                        sq_scr[:, 0:_w], _a, AF.Square, accum_out=_c),
                       r=[rn], w=["sq_scr", "c_t6" + sx])
                else:
                    op("dve", lambda e, _a=apv, _c=c_t7: e.tensor_reduce(_c, _a, AX.X, ALU.add),
                       r=[rn], w=["c_t7" + sx])
                    op("dve", lambda e, _c1=c_s1, _c2=c_t7: e.tensor_add(_c1, _c1, _c2),
                       r=["c_t7" + sx, "c_s1" + sx], w=["c_s1" + sx])
                    op("act", lambda e, _a=apv, _w=wdt, _c=c_t7: e.activation(
                        sq_scr[:, 0:_w], _a, AF.Square, accum_out=_c),
                       r=[rn], w=["sq_scr", "c_t7" + sx])
                    op("dve", lambda e, _c1=c_t6, _c2=c_t7: e.tensor_add(_c1, _c1, _c2),
                       r=["c_t7" + sx, "c_t6" + sx], w=["c_t6" + sx])
            inv = 1.0 / total_d
            op("dve", lambda e, _a=c_mu, _b=c_s1: e.tensor_scalar_mul(_a, _b, inv),
               r=["c_s1" + sx], w=["c_mu" + sx])
            op("dve", lambda e, _a=c_m2, _b=c_mu: e.tensor_mul(_a, _b, _b),
               r=["c_mu" + sx], w=["c_m2" + sx])
            op("dve", lambda e, _a=c_var, _b=c_t6: e.tensor_scalar_mul(_a, _b, inv),
               r=["c_t6" + sx], w=["c_var" + sx])
            op("dve", lambda e, _a=c_var, _b=c_m2: e.tensor_sub(_a, _a, _b),
               r=["c_var" + sx, "c_m2" + sx], w=["c_var" + sx])
            op("act", lambda e, _a=c_var: e.activation(_a, _a, AF.Sqrt, bias=c_eps, scale=1.0),
               r=["c_var" + sx, "c_eps"], w=["c_var" + sx])
            op("dve", lambda e, _a=c_rstd, _b=c_var: e.reciprocal(_a, _b),
               r=["c_var" + sx], w=["c_rstd" + sx])
            op("dve", lambda e, _a=c_nmr, _b=c_mu, _c=c_rstd: e.scalar_tensor_tensor(
                _a, _b, -1.0, _c, ALU.mult, ALU.mult),
               r=["c_mu" + sx, "c_rstd" + sx], w=["c_nmr" + sx])
            return c_rstd, c_nmr

        def tr_f32(chunks, evacs):
            for g0 in range(0, len(chunks), 4):
                grp = chunks[g0:g0 + 4]
                for gi, (src, rn) in enumerate(grp):
                    op("pe", lambda e, _s=src, _g=gi: e.transpose(
                        tr32[:, _g * 128:_g * 128 + 128], _s, id32[:]),
                       r=[rn, "id32"], w=["tr32"])
                evacs(g0, len(grp))

        for k, w in enumerate(("wqT", "wkT", "wvT", "woT")):
            load_cast(wT_bf[w], w + "_bf", w, D, D)
        for k in range(4):
            slot = load_cast.k % 2
            load_cast.k += 1
            op("sp", lambda e, _k=k, _s=slot: e.dma_start(
                wld[_s][0:1, 0:D], dram["brow4"][0:1, _k * D:(_k + 1) * D]),
               w=["wld%d" % slot], dma=True)
            op("pool", lambda e, _k=k, _s=slot: e.tensor_copy(
                brow4_hi[0:1, _k * D:(_k + 1) * D], wld[_s][0:1, 0:D]),
               r=["wld%d" % slot], w=["brow4_hi"])
        w1_off_pre = []
        pos = 0
        for m in range(NHT):
            aw = min(128, HID - m * 128)
            w1_off_pre.append((pos, aw))
            w1_off_pre.append((pos + aw, aw))
            pos += 2 * aw
        for l in range(2):
            for ot in range(2 * NHT):
                col0, ow = w1_off_pre[ot]
                slot = load_cast.k % 2
                load_cast.k += 1
                op("sp", lambda e, _c=col0, _w=ow, _s=slot, _l=l: e.dma_start(
                    bass.AP(wld[_s], 0, [[1024, 128], [_w, 8], [1, _w]]),
                    dram[f"w1p{_l}"][:, _c:_c + _w].rearrange("(a p) o -> p a o", p=128)),
                   w=["wld%d" % slot], dma=True)
                op("pool", lambda e, _s=slot, _w=ow: e.tensor_copy(
                    w1stage[:, 0:8 * _w], wld[_s][:, 0:8 * _w]),
                   r=["wld%d" % slot], w=["w1stage"])
                if ow < 128:
                    op("pool", lambda e, _w=ow: e.memset(w1stage[:, 8 * _w:1024], 0.0),
                       w=["w1stage"])
                op("sp", lambda e, _l=l, _o=ot: e.dma_start(
                    d_w1img[_l, _o, :, 0:1024], w1stage[:, 0:1024]),
                   r=["w1stage"], w=["w1img_d"], dma=True)

        def proj_1024(lt_t, lt_name, w_bf, w_name, psA, psB, bias_idx, brow=None,
                      brow_name="ones_bf"):
            for n in range(2):
                ps = (psA, psB)[n]
                psn = PN[id(ps)]
                for j in range(8):
                    op("pe", lambda e, _n=n, _j=j, _p=ps, _lt=lt_t: e.matmul(
                        _p[:, 0:512], _lt[:, _j * 128:_j * 128 + 128],
                        w_bf[:, _j * D + _n * 512:_j * D + _n * 512 + 512],
                        start=(_j == 0), stop=False),
                       r=[lt_name, w_name], w=[psn])
                br = ones_bf[:] if brow is None else brow
                op("pe", lambda e, _n=n, _p=ps, _b=bias_idx, _r=br: e.matmul(
                    _p[:, 0:512], _r,
                    brow4_hi[0:1, _b * D + _n * 512:_b * D + _n * 512 + 512],
                    start=False, stop=True),
                   r=[brow_name, "brow4_hi"], w=[psn])

        def rmsnorm16(scr, scr_name, pb):
            su = sums_bank[pb]
            sn = f"c_sums{pb}"
            op("dve", lambda e, _s=scr: e.tensor_scalar_min(_s[:], _s[:], 1.0e4),
               r=[scr_name], w=[scr_name])
            op("dve", lambda e, _s=scr: e.tensor_scalar_max(_s[:], _s[:], -1.0e4),
               r=[scr_name], w=[scr_name])
            op("act", lambda e, _s=scr: e.activation(sq_scr[:], _s[:], AF.Square),
               r=[scr_name], w=["sq_scr"])
            red3 = bass.AP(sq_scr, 0, [[D, 128], [64, 16], [1, 64]])
            op("dve", lambda e, _r=red3, _u=su: e.tensor_reduce(_u, _r, AX.X, ALU.add),
               r=["sq_scr"], w=[sn])
            op("dve", lambda e, _u=su: e.tensor_scalar(
                _u, _u, 1.0 / 64, 1.0e-5, ALU.mult, ALU.max),
               r=[sn], w=[sn])
            op("act", lambda e, _u=su: e.activation(_u, _u, AF.Sqrt, bias=c_eps, scale=1.0),
               r=[sn, "c_eps"], w=[sn])
            op("dve", lambda e, _u=su: e.reciprocal(_u, _u), r=[sn], w=[sn])
            for h in range(16):
                op("dve", lambda e, _s=scr, _h=h, _u=su: e.tensor_scalar_mul(
                    _s[:, _h * 64:_h * 64 + 64], _s[:, _h * 64:_h * 64 + 64],
                    _u[:, _h:_h + 1]), r=[scr_name, sn], w=[scr_name])

        def P1o(i):
            sl = i % 2
            k4 = i % 4
            pb = i % 2
            xt = x_sb[:, i * D:(i + 1) * D]
            c_s1, c_mu, c_m2, c_var, c_rstd, c_nmr, c_t6, c_t7 = stats_bank(pb)
            sx = str(pb)
            op("dve", lambda e, _a=xt, _c=c_s1: e.tensor_reduce(_c, _a, AX.X, ALU.add),
               r=["x_sb"], w=["c_s1" + sx])
            op("dve", lambda e, _a=c_nmr, _b=c_s1: e.tensor_scalar_mul(_a, _b, -1.0 / D),
               r=["c_s1" + sx], w=["c_nmr" + sx])
            op("act", lambda e, _x=xt, _n=c_nmr: e.activation(
                ln_scr[:], _x, AF.Identity, bias=_n, scale=1.0),
               r=["x_sb", "c_nmr" + sx], w=["ln_scr"])
            op("act", lambda e, _a=xt, _c=c_t6: e.activation(
                sq_scr[:], _a, AF.Square, accum_out=_c),
               r=["x_sb"], w=["sq_scr", "c_t6" + sx])
            op("dve", lambda e, _a=c_mu, _b=c_nmr: e.tensor_mul(_a, _b, _b),
               r=["c_nmr" + sx], w=["c_mu" + sx])
            op("dve", lambda e, _a=c_var, _b=c_t6: e.tensor_scalar_mul(_a, _b, 1.0 / D),
               r=["c_t6" + sx], w=["c_var" + sx])
            op("dve", lambda e, _a=c_var, _b=c_mu: e.tensor_sub(_a, _a, _b),
               r=["c_var" + sx, "c_mu" + sx], w=["c_var" + sx])
            op("act", lambda e, _a=c_var: e.activation(_a, _a, AF.Sqrt, bias=c_eps, scale=1.0),
               r=["c_var" + sx, "c_eps"], w=["c_var" + sx])
            op("pe", lambda e, _a=c_var: e.transpose(simX[0:1, 0:128], _a, id32[:]),
               r=["c_var" + sx, "id32"], w=["simX"])
            op("act", lambda e, _p=pb: e.activation(
                rowb[0:1, _p * 128:_p * 128 + 128], simX[0:1, 0:128], AF.Identity),
               r=["simX"], w=[f"rowb{pb}"])
            op("dve", lambda e, _a=c_rstd, _b=c_var: e.reciprocal(_a, _b),
               r=["c_var" + sx], w=["c_rstd" + sx])

            def ev_lt(g0, glen, _sl=sl):
                op("act", lambda e, _g0=g0, _n=glen * 128: e.activation(
                    ltO[_sl][:, _g0 * 128:_g0 * 128 + _n], tr32[:, 0:_n], AF.Identity),
                   r=["tr32"], w=[f"ltO{_sl}"])
            tr_f32([(ln_scr[:, j * 128:j * 128 + 128], "ln_scr") for j in range(8)], ev_lt)
            stdrow = rowb[0:1, pb * 128:pb * 128 + 128]
            rbn = f"rowb{pb}"
            proj_1024(ltO[sl], f"ltO{sl}", wT_bf["wqT"], "wqT_bf", pP[0], pP[1], 0,
                      brow=stdrow, brow_name=rbn)
            proj_1024(ltO[sl], f"ltO{sl}", wT_bf["wkT"], "wkT_bf", pP[2], pP[3], 1,
                      brow=stdrow, brow_name=rbn)
            op("dve", lambda e, _r=rstd_ap(pb): e.tensor_scalar_mul(
                q_scrO[:, 0:512], pP[0][:], _r), r=["pP0", "c_rstd" + str(pb)], w=["q_scrO"])
            op("dve", lambda e, _r=rstd_ap(pb): e.tensor_scalar_mul(
                q_scrO[:, 512:1024], pP[1][:], _r), r=["pP1", "c_rstd" + str(pb)], w=["q_scrO"])
            op("act", lambda e, _r=rstd_ap(pb): e.activation(
                k_scrO[:, 0:512], pP[2][:], AF.Identity, scale=_r),
               r=["pP2", "c_rstd" + str(pb)], w=["k_scrO"])
            op("act", lambda e, _r=rstd_ap(pb): e.activation(
                k_scrO[:, 512:1024], pP[3][:], AF.Identity, scale=_r),
               r=["pP3", "c_rstd" + str(pb)], w=["k_scrO"])
            proj_1024(ltO[sl], f"ltO{sl}", wT_bf["wvT"], "wvT_bf", pP[0], pP[1], 2,
                      brow=stdrow, brow_name=rbn)
            op("act", lambda e, _k=k4, _r=rstd_ap(pb): e.activation(
                v3[:, _k * D:_k * D + 512], pP[0][:], AF.Identity, scale=_r),
               r=["pP0", "c_rstd" + str(pb)], w=["v3"])
            op("act", lambda e, _k=k4, _r=rstd_ap(pb): e.activation(
                v3[:, _k * D + 512:_k * D + 1024], pP[1][:], AF.Identity, scale=_r),
               r=["pP1", "c_rstd" + str(pb)], w=["v3"])
            rmsnorm16(q_scrO, "q_scrO", pb)

            def ev_q(g0, glen, _sl=sl):
                op("act", lambda e, _g0=g0, _n=glen * 128: e.activation(
                    qhTO[_sl][:, _g0 * 128:_g0 * 128 + _n], tr32[:, 0:_n],
                    AF.Identity, scale=qn_eff),
                   r=["tr32", "c_qe"], w=[f"qhTO{_sl}"])
            tr_f32([(q_scrO[:, j * 128:j * 128 + 128], "q_scrO") for j in range(8)], ev_q)
            rmsnorm16(k_scrO, "k_scrO", pb)

            def ev_k(g0, glen, _k4=k4):
                op("act", lambda e, _g0=g0, _n=glen * 128: e.activation(
                    khT3[:, _k4 * D + _g0 * 128:_k4 * D + _g0 * 128 + _n], tr32[:, 0:_n],
                    AF.Identity, scale=kn2[:]),
                   r=["tr32", "kn2"], w=["khT3"])
            tr_f32([(k_scrO[:, j * 128:j * 128 + 128], "k_scrO") for j in range(8)], ev_k)

        def P2o(i):
            sl = i % 2
            pb = i % 2
            xt = x_sb[:, i * D:(i + 1) * D]
            kstart = max(0, i - 2)
            nk = i - kstart + 1
            ncols = nk * 128

            def emit_ptav(h):
                hs = h % 2
                for c in range(nk):
                    op("pe", lambda e, _c=c, _hs=hs: e.transpose(
                        tbf[:, _c * 128:_c * 128 + 128],
                        pbO[_hs][:, _c * 128:_c * 128 + 128], id16[:]),
                       r=[f"pbO{hs}", "id16"], w=["tbf"])
                op("dve", lambda e, _n=ncols, _hs=hs: e.tensor_copy(
                    ptO[_hs][:, 0:_n], tbf[:, 0:_n]), r=["tbf"], w=[f"ptO{hs}"])
                bank = pP[0] if h < 8 else pP[1]
                bn = PN[id(bank)]
                for c in range(nk):
                    slot = (kstart + c) % 4
                    st_ = (h % 8 == 0) and (c == 0)
                    sp_ = (h % 8 == 7) and (c == nk - 1)
                    op("pe", lambda e, _c=c, _hs=hs, _b=bank, _co=(h % 8) * 64,
                       _st=st_, _sp=sp_, _slot=slot, _h=h: e.matmul(
                        _b[:, _co:_co + 64], ptO[_hs][:, _c * 128:_c * 128 + 128],
                        v3[:, _slot * D + _h * 64:_slot * D + _h * 64 + 64],
                        start=_st, stop=_sp),
                       r=[f"ptO{hs}", "v3"], w=[bn])

            pend = None
            for h in range(HEADS_O):
                hp = (h % 2) * 64
                hc = (h // 2) * 128
                hs = h % 2
                for c in range(nk):
                    slot = (kstart + c) % 4
                    op("pe", lambda e, _c=c, _s=slot, _hp=hp, _hc=hc, _sl=sl,
                       _st=(c == 0): e.matmul(
                        simP[:, _c * 128:_c * 128 + 128],
                        qhTO[_sl][_hp:_hp + 64, _hc:_hc + 128],
                        khT3[_hp:_hp + 64, _s * D + _hc:_s * D + _hc + 128],
                        start=_st, stop=False),
                       r=[f"qhTO{sl}", "khT3"], w=["simP"])
                if nk == 3:
                    op("pe", lambda e: e.matmul(simP[:, 0:128], id16[:], m_ups[:],
                                                start=False, stop=False),
                       r=["id16", "m_ups"], w=["simP"])
                op("pe", lambda e, _o=(nk - 1) * 128: e.matmul(
                    simP[:, _o:_o + 128], id16[:], m_low[:], start=False, stop=True),
                   r=["id16", "m_low"], w=["simP"])
                if pend is not None:
                    emit_ptav(pend)
                nmv = c_nmax[h % 2]
                nmn = f"c_nmax{h % 2}"
                rdn = f"c_rd{h}"
                op("dve", lambda e, _n=ncols, _m=nmv: e.tensor_reduce(
                    _m, simP[:, 0:_n], AX.X, ALU.max, negate=True),
                   r=["simP"], w=[nmn])
                op("act", lambda e, _n=ncols, _hs=hs, _m=nmv, _h=h: e.activation(
                    pbO[_hs][:, 0:_n], simP[:, 0:_n], AF.Exp, bias=_m, scale=1.0,
                    accum_out=c_rdens[:, _h:_h + 1]),
                   r=["simP", nmn], w=[f"pbO{hs}", rdn])
                op("dve", lambda e, _h=h: e.reciprocal(
                    c_rdens[:, _h:_h + 1], c_rdens[:, _h:_h + 1]), r=[rdn], w=[rdn])
                pend = h
            emit_ptav(pend)
            for h in range(HEADS_O):
                bank = pP[0] if h < 8 else pP[1]
                bn = PN[id(bank)]
                op("act", lambda e, _b=bank, _co=(h % 8) * 64, _h=h, _s=sl: e.activation(
                    obO[_s][:, _h * 64:_h * 64 + 64], _b[:, _co:_co + 64], AF.Identity,
                    scale=c_rdens[:, _h:_h + 1]),
                   r=[bn, f"c_rd{h}"], w=[f"obO{sl}"])
            for j in range(8):
                op("pe", lambda e, _j=j, _s=sl: e.transpose(
                    tbf[:, _j * 128:_j * 128 + 128],
                    obO[_s][:, _j * 128:_j * 128 + 128], id16[:]),
                   r=[f"obO{sl}", "id16"], w=["tbf"])
            op("dve", lambda e, _s=sl: e.tensor_copy(otO[_s][:], tbf[:]),
               r=["tbf"], w=[f"otO{sl}"])
            proj_1024(otO[sl], f"otO{sl}", wT_bf["woT"], "woT_bf", pP[2], pP[3], 3)
            rstd, nmr = ln_stats([(pP[2][:], 512, "pP2"), (pP[3][:], 512, "pP3")], D, pb)
            op("dve", lambda e, _x=xt, _n=nmr: e.tensor_scalar_add(_x, _x, _n),
               r=["x_sb", f"c_nmr{pb}"], w=["x_sb"])
            op("dve", lambda e, _x=xt, _r=rstd: e.scalar_tensor_tensor(
                _x[:, 0:512], pP[2][:], _r, _x[:, 0:512], ALU.mult, ALU.add),
               r=["pP2", f"c_rstd{pb}", "x_sb"], w=["x_sb"])
            op("dve", lambda e, _x=xt, _r=rstd: e.scalar_tensor_tensor(
                _x[:, 512:1024], pP[3][:], _r, _x[:, 512:1024], ALU.mult, ALU.add),
               r=["pP3", f"c_rstd{pb}", "x_sb"], w=["x_sb"])

        NTL = min(NT, NT_LIM)
        if NTL:
            P1o(0)
            for i in range(NTL):
                if i + 1 < NTL:
                    P1o(i + 1)
                P2o(i)

        op("pool", lambda e: e.dma_start(d_x1, x_sb[:]), r=["x_sb"], w=["x1d"], dma=True)
        if not (PHASE_LIMIT == "outer" and DEBUG_TAP):
            for i in range(NT):
                xt = x_sb[:, i * D:(i + 1) * D]
                pb = i % 2
                rstd, nmr = ln_stats([(xt, D, "x_sb")], D, pb)
                op("act", lambda e, _x=xt, _r=rstd, _n=nmr: e.activation(
                    _x, _x, AF.Identity, bias=_n, scale=_r),
                   r=["x_sb", f"c_rstd{pb}", f"c_nmr{pb}", "x1d"], w=["x_sb"])

        n_inner = {"outer": 0, "layer0": 1, "all": 2}[PHASE_LIMIT]
        prev_names = OUTER_T
        tA, namesA = (innerA() if n_inner else (None, None))
        tB, namesB = (innerB() if n_inner else (None, None))
        for l in range(n_inner):
            em.fence(prev_names, namesA)
            prev_names = namesA
            load_cast(tA["wqkv"], "wqkv_bf", f"wqkv{l}", D, 3 * INNER)
            load_cast(tA["woin"], "woin_bf", f"wo{l}", INNER, D)
            qsb_ap = cols[:, 128:384]
            slot = load_cast.k % 2
            load_cast.k += 1
            op("sp", lambda e, _l=l, _s=slot: e.dma_start(
                wld[_s][0:1, 0:256], dram[f"qsrows{_l}"]), w=["wld%d" % slot], dma=True)
            op("dve", lambda e: e.memset(sq_scr[0:1, 0:128], 1.0), w=["sq_scr"])
            op("pe", lambda e, _s=slot: e.matmul(
                tr32[:, 0:256], sq_scr[0:1, 0:128], wld[_s][0:1, 0:256],
                start=True, stop=True),
               r=["sq_scr", "wld%d" % slot], w=["tr32"])
            op("dve", lambda e: e.tensor_copy(qsb_ap, tr32[:, 0:256]),
               r=["tr32"], w=["qsb"])

            def shift_tile(i, dst, dst_name):
                op("pool", lambda e, _i=i, _d=dst: e.dma_start(
                    _d[1:128, :], x_sb[0:127, _i * D + 512:_i * D + 1024]),
                   r=["x_sb"], w=[dst_name], dma=True)
                if i == 0:
                    op("dve", lambda e, _d=dst: e.memset(_d[0:1, :], 0.0), w=[dst_name])
                else:
                    op("pool", lambda e, _i=i, _d=dst: e.dma_start(
                        _d[0:1, :], x_sb[127:128, (_i - 1) * D + 512:(_i - 1) * D + 1024]),
                       r=["x_sb"], w=[dst_name], dma=True)

            load_cast(tB["w2"], "w2bf", f"w2_{l}", HID, D)
            slot = load_cast.k % 2
            load_cast.k += 1
            op("sp", lambda e, _l=l, _s=slot: e.dma_start(
                wld[_s][0:1, 0:D], dram[f"b2_{_l}"]), w=["wld%d" % slot], dma=True)
            op("pool", lambda e, _s=slot: e.tensor_copy(
                tB["b2hi"][0:1, :], wld[_s][0:1, 0:D]),
               r=["wld%d" % slot], w=["b2hi"])
            def l2rot(scr, scr_name, a_off, b_off, pb):
                sums8 = sums_bank[pb][:, 0:8]
                sn = f"c_sums{pb}"
                op("act", lambda e, _s=scr: e.activation(
                    sq_scr[:, 0:512], _s[:], AF.Square), r=[scr_name], w=["sq_scr"])
                red3 = bass.AP(sq_scr, 0, [[D, 128], [64, 8], [1, 64]])
                op("dve", lambda e, _r=red3, _u=sums8: e.tensor_reduce(_u, _r, AX.X, ALU.add),
                   r=["sq_scr"], w=[sn])
                op("act", lambda e, _u=sums8: e.activation(_u, _u, AF.Sqrt),
                   r=[sn], w=[sn])
                op("dve", lambda e, _u=sums8: e.tensor_scalar_max(_u, _u, 1.0e-12),
                   r=[sn], w=[sn])
                op("dve", lambda e, _u=sums8: e.reciprocal(_u, _u), r=[sn], w=[sn])
                for h in range(8):
                    op("dve", lambda e, _s=scr, _h=h, _u=sums8: e.tensor_scalar_mul(
                        _s[:, _h * 64:_h * 64 + 64], _s[:, _h * 64:_h * 64 + 64],
                        _u[:, _h:_h + 1]), r=[scr_name, sn], w=[scr_name])
                vh = lambda t, off: bass.AP(t, off, [[t.ap().ap[0][0], 128], [64, 8], [1, 32]])
                a_bc = bass.AP(tA["abt"], a_off, [[4 * HD, 128], [0, 8], [1, 64]])
                b_bc = bass.AP(tA["abt"], b_off, [[4 * HD, 128], [0, 8], [1, 64]])
                op("dve", lambda e, _s=scr: e.tensor_scalar_mul(
                    vh(tA["rh"], 0), vh(_s, 32), -1.0), r=[scr_name], w=["rhI"])
                op("dve", lambda e, _s=scr: e.tensor_copy(vh(tA["rh"], 32), vh(_s, 0)),
                   r=[scr_name], w=["rhI"])
                op("dve", lambda e, _s=scr, _a=a_bc: e.tensor_mul(_s[:], _s[:], _a),
                   r=[scr_name, "abt"], w=[scr_name])
                op("dve", lambda e, _b=b_bc: e.tensor_mul(
                    tA["rh"][:], tA["rh"][:], _b), r=["rhI", "abt"], w=["rhI"])
                op("dve", lambda e, _s=scr: e.tensor_add(_s[:], _s[:], tA["rh"][:]),
                   r=[scr_name, "rhI"], w=[scr_name])

            def P1i(i):
                sl = i % 2
                k6 = i % 6
                pb = i % 2
                xt = x_sb[:, i * D:(i + 1) * D]
                shift_tile(i, tA["h2"][sl], "h2A_%d" % sl)
                op("sp", lambda e, _i=i, _s=sl: e.dma_start(
                    tA["rb"][_s][:], dram["rotpk"][:, _i * 256:(_i + 1) * 256]),
                   w=["rotb_%d" % sl], dma=True)
                op("dve", lambda e, _s=sl: e.tensor_mul(
                    tA["abt"][:], tA["rb"][_s][:], qsb_ap),
                   r=["rotb_%d" % sl, "qsb"], w=["abt"])

                def ev_lt(g0, glen, _sl=sl):
                    op("act", lambda e, _g0=g0, _n=glen * 128: e.activation(
                        tA["lt"][_sl][:, _g0 * 128:_g0 * 128 + _n], tr32[:, 0:_n], AF.Identity),
                       r=["tr32"], w=["ltI_%d" % _sl])
                tr_f32([(xt[:, j * 128:j * 128 + 128], "x_sb") for j in range(4)] +
                       [(tA["h2"][sl][:, j * 128:j * 128 + 128], "h2A_%d" % sl)
                        for j in range(4)], ev_lt)
                for n in range(3):
                    ps = pP[n]
                    psn = PN[id(ps)]
                    for j in range(8):
                        op("pe", lambda e, _n=n, _j=j, _p=ps, _s=sl: e.matmul(
                            _p[:, 0:512], tA["lt"][_s][:, _j * 128:_j * 128 + 128],
                            tA["wqkv"][:, _j * 1536 + _n * 512:_j * 1536 + _n * 512 + 512],
                            start=(_j == 0), stop=(_j == 7)),
                           r=["ltI_%d" % sl, "wqkv_bf"], w=[psn])
                op("dve", lambda e: e.tensor_copy(tA["q_scr"][:], pP[0][:]),
                   r=["pP0"], w=["q_scrI"])
                op("act", lambda e: e.activation(tA["k_scr"][:], pP[1][:], AF.Identity),
                   r=["pP1"], w=["k_scrI"])
                op("act", lambda e, _k=k6: e.activation(
                    tA["v5"][:, _k * INNER:(_k + 1) * INNER], pP[2][:], AF.Identity),
                   r=["pP2"], w=["v5"])
                l2rot(tA["q_scr"], "q_scrI", 0, HD, pb)
                l2rot(tA["k_scr"], "k_scrI", 2 * HD, 3 * HD, pb)
                for gi in range(4):
                    op("pe", lambda e, _g=gi: e.transpose(
                        tr32[:, _g * 128:_g * 128 + 128],
                        tA["q_scr"][:, _g * 128:_g * 128 + 128], id32[:]),
                       r=["q_scrI", "id32"], w=["tr32"])
                op("act", lambda e, _s=sl: e.activation(
                    tA["qhT"][_s][:], tr32[:], AF.Identity, scale=c_eight),
                   r=["tr32", "c_eight"], w=["qhTI_%d" % sl])
                for gi in range(4):
                    op("pe", lambda e, _g=gi: e.transpose(
                        tr32[:, _g * 128:_g * 128 + 128],
                        tA["k_scr"][:, _g * 128:_g * 128 + 128], id32[:]),
                       r=["k_scrI", "id32"], w=["tr32"])
                op("act", lambda e, _k=k6: e.activation(
                    tA["khT5"][:, _k * INNER:(_k + 1) * INNER], tr32[:], AF.Identity),
                   r=["tr32"], w=["khT5"])

            def P2i(i):
                sl = i % 2
                xt = x_sb[:, i * D:(i + 1) * D]
                kstart = max(0, i - 4)
                nk = i - kstart + 1
                first_av = [True]

                def emit_ptav(h):
                    hs = h % 2
                    for c in range(nk):
                        op("pe", lambda e, _c=c, _hs=hs: e.transpose(
                            tbf[:, _c * 128:_c * 128 + 128],
                            tA["pb"][_hs][:, _c * 128:_c * 128 + 128], id16[:]),
                           r=["pbI_%d" % hs, "id16"], w=["tbf"])
                    op("dve", lambda e, _n=nk * 128, _hs=hs: e.tensor_copy(
                        tA["pt"][_hs][:, 0:_n], tbf[:, 0:_n]), r=["tbf"], w=["ptI_%d" % hs])
                    for c in range(nk):
                        slot = (kstart + c) % 6
                        st_ = first_av[0]
                        first_av[0] = False
                        sp_ = (h == HEADS_I - 1) and (c == nk - 1)
                        op("pe", lambda e, _c=c, _hs=hs, _st=st_, _sp=sp_, _slot=slot, _h=h:
                           e.matmul(pP[3][:, _h * 64:_h * 64 + 64],
                                    tA["pt"][_hs][:, _c * 128:_c * 128 + 128],
                                    tA["v5"][:, _slot * INNER + _h * 64:_slot * INNER + _h * 64 + 64],
                                    start=_st, stop=_sp),
                           r=["ptI_%d" % hs, "v5"], w=["pP3"])

                pend = None
                for h in range(HEADS_I):
                    hp = (h % 2) * 64
                    hc = (h // 2) * 128
                    hs = h % 2
                    nP = min(nk, 4)
                    for c in range(nP):
                        slot = (kstart + c) % 6
                        op("pe", lambda e, _o=c * 128, _s=slot, _hp=hp, _hc=hc,
                           _sl=sl, _st=(c == 0): e.matmul(
                            simP[:, _o:_o + 128],
                            tA["qhT"][_sl][_hp:_hp + 64, _hc:_hc + 128],
                            tA["khT5"][_hp:_hp + 64, _s * INNER + _hc:_s * INNER + _hc + 128],
                            start=_st, stop=False),
                           r=["qhTI_%d" % sl, "khT5"], w=["simP"])
                    if nk == 5:
                        op("pe", lambda e: e.matmul(simP[:, 0:128], id16[:], m_upi[:],
                                                    start=False, stop=True),
                           r=["id16", "m_upi"], w=["simP"])
                        slot = i % 6
                        op("pe", lambda e, _s=slot, _hp=hp, _hc=hc, _sl=sl: e.matmul(
                            simX[:, 0:128],
                            tA["qhT"][_sl][_hp:_hp + 64, _hc:_hc + 128],
                            tA["khT5"][_hp:_hp + 64, _s * INNER + _hc:_s * INNER + _hc + 128],
                            start=True, stop=False),
                           r=["qhTI_%d" % sl, "khT5"], w=["simX"])
                        op("pe", lambda e: e.matmul(simX[:, 0:128], id16[:], m_low[:],
                                                    start=False, stop=True),
                           r=["id16", "m_low"], w=["simX"])
                    else:
                        op("pe", lambda e, _o=(nk - 1) * 128: e.matmul(
                            simP[:, _o:_o + 128], id16[:], m_low[:], start=False, stop=True),
                           r=["id16", "m_low"], w=["simP"])
                    if pend is not None:
                        emit_ptav(pend)
                    nPc = min(nk, 4) * 128
                    nmv = c_nmax[h % 2]
                    nmn = f"c_nmax{h % 2}"
                    rdn = f"c_rd{h}"
                    rdc = c_rdens[:, h:h + 1]
                    op("dve", lambda e, _n=nPc, _m=nmv: e.tensor_reduce(
                        _m, simP[:, 0:_n], AX.X, ALU.max, negate=True),
                       r=["simP"], w=[nmn])
                    if nk == 5:
                        xv = c_nmaxx[h % 2]
                        xn = f"c_nmaxx{h % 2}"
                        op("dve", lambda e, _x=xv: e.tensor_reduce(
                            _x, simX[:], AX.X, ALU.max, negate=True), r=["simX"], w=[xn])
                        op("dve", lambda e, _m=nmv, _x=xv: e.tensor_tensor(
                            _m, _m, _x, ALU.min), r=[nmn, xn], w=[nmn])
                    op("act", lambda e, _n=nPc, _hs=hs, _m=nmv, _c=rdc: e.activation(
                        tA["pb"][_hs][:, 0:_n], simP[:, 0:_n], AF.Exp, bias=_m,
                        scale=1.0, accum_out=_c),
                       r=["simP", nmn], w=["pbI_%d" % hs, rdn])
                    if nk == 5:
                        dxv = c_denx[h % 2]
                        dxn = f"c_denx{h % 2}"
                        op("act", lambda e, _hs=hs, _m=nmv, _x=dxv: e.activation(
                            tA["pb"][_hs][:, 512:640], simX[:], AF.Exp, bias=_m,
                            scale=1.0, accum_out=_x),
                           r=["simX", nmn], w=["pbI_%d" % hs, dxn])
                        op("dve", lambda e, _c=rdc, _x=dxv: e.tensor_add(_c, _c, _x),
                           r=[rdn, dxn], w=[rdn])
                    op("dve", lambda e, _c=rdc: e.reciprocal(_c, _c), r=[rdn], w=[rdn])
                    pend = h
                emit_ptav(pend)
                for h in range(HEADS_I):
                    op("act", lambda e, _h=h, _s=sl: e.activation(
                        tA["ob"][_s][:, _h * 64:_h * 64 + 64], pP[3][:, _h * 64:_h * 64 + 64],
                        AF.Identity, scale=c_rdens[:, _h:_h + 1]),
                       r=["pP3", f"c_rd{h}"], w=["obI_%d" % sl])
                for c in range(4):
                    op("pe", lambda e, _c=c, _s=sl: e.transpose(
                        tbf[:, _c * 128:_c * 128 + 128],
                        tA["ob"][_s][:, _c * 128:_c * 128 + 128], id16[:]),
                       r=["obI_%d" % sl, "id16"], w=["tbf"])
                op("dve", lambda e, _s=sl: e.tensor_copy(tA["ot"][_s][:], tbf[:, 0:512]),
                   r=["tbf"], w=["otI_%d" % sl])
                for n in range(2):
                    ps = pP[n]
                    psn = PN[id(ps)]
                    for j in range(4):
                        op("pe", lambda e, _n=n, _j=j, _p=ps, _s=sl: e.matmul(
                            _p[:, 0:512], tA["ot"][_s][:, _j * 128:_j * 128 + 128],
                            tA["woin"][:, _j * D + _n * 512:_j * D + _n * 512 + 512],
                            start=(_j == 0), stop=(_j == 3)),
                           r=["otI_%d" % sl, "woin_bf"], w=[psn])
                op("dve", lambda e, _x=xt: e.tensor_add(_x[:, 0:512], _x[:, 0:512], pP[0][:]),
                   r=["pP0", "x_sb"], w=["x_sb"])
                op("dve", lambda e, _x=xt: e.tensor_add(
                    _x[:, 512:1024], _x[:, 512:1024], pP[1][:]),
                   r=["pP1", "x_sb"], w=["x_sb"])

            P1i(0)
            for i in range(NT):
                if i + 1 < NT:
                    P1i(i + 1)
                P2i(i)

            em.fence(prev_names, namesB)
            prev_names = namesB
            w1_off = []
            pos = 0
            for m in range(NHT):
                aw = min(128, HID - m * 128)
                w1_off.append((pos, aw))
                w1_off.append((pos + aw, aw))
                pos += 2 * aw

            def LN2part(tc):
                for ii in range(4):
                    i = tc * 4 + ii
                    sl = i % 2
                    xt = x_sb[:, i * D:(i + 1) * D]
                    shift_tile(i, tB["h2"][sl], "h2B_%d" % sl)
                    pb = i % 2
                    rstd, nmr = ln_stats([(xt[:, 0:512], 512, "x_sb"),
                              (tB["h2"][sl][:], 512, "h2B_%d" % sl)], D, pb)
                    op("act", lambda e, _x=xt, _r=rstd, _n=nmr: e.activation(
                        ln_scr[:, 0:512], _x[:, 0:512], AF.Identity, bias=_n, scale=_r),
                       r=["x_sb", f"c_rstd{pb}", f"c_nmr{pb}"], w=["ln_scr"])
                    op("act", lambda e, _s=sl, _r=rstd, _n=nmr: e.activation(
                        ln_scr[:, 512:1024], tB["h2"][_s][:], AF.Identity,
                        bias=_n, scale=_r),
                       r=["h2B_%d" % sl, f"c_rstd{pb}", f"c_nmr{pb}"], w=["ln_scr"])
                    for g0 in (0, 4):
                        for gi in range(4):
                            j = g0 + gi
                            op("pe", lambda e, _j=j, _g=gi: e.transpose(
                                tr32[:, _g * 128:_g * 128 + 128],
                                ln_scr[:, _j * 128:_j * 128 + 128], id32[:]),
                               r=["ln_scr", "id32"], w=["tr32"])
                        for gi in range(4):
                            j = g0 + gi
                            op("act", lambda e, _j=j, _g=gi, _ii=ii, _l=l: e.activation(
                                tB["ln2T"][:, _j * 512 + _ii * 128:_j * 512 + _ii * 128 + 128],
                                tr32[:, _g * 128:_g * 128 + 128], AF.Identity,
                                bias=lnb[_l][:, _j:_j + 1], scale=lng[_l][:, _j:_j + 1]),
                               r=["tr32", f"lng{l}", f"lnb{l}"], w=["ln2T"])
            LN2part(0)
            for tc in range(4):
                for ot in range(2 * NHT):
                    grp = ot // 4
                    lane = ot % 4
                    wb = grp % 2
                    if lane == 0:
                        op("sp", lambda e, _l=l, _o=ot, _b=wb: e.dma_start(
                            bass.AP(tB["w1"][_b], 0, [[4096, 128], [1024, 4], [1, 1024]]),
                            d_w1img[_l, _o:_o + 4].rearrange("o p c -> p o c")),
                           r=["w1img_d"], w=["w1bf_%d" % wb], dma=True)
                    col0, ow = w1_off[ot]
                    pa = pP[2 + (ot % 2)]
                    pan = PN[id(pa)]
                    for j in range(8):
                        op("pe", lambda e, _j=j, _p=pa, _b=wb, _w=ow, _ln=lane: e.matmul(
                            _p[0:_w, 0:512],
                            tB["w1"][_b][:, _ln * 1024 + _j * _w:_ln * 1024 + (_j + 1) * _w],
                            tB["ln2T"][:, _j * 512:_j * 512 + 512],
                            start=(_j == 0), stop=(_j == 7)),
                           r=["w1bf_%d" % wb, "ln2T"], w=[pan])
                    ht = ot // 2
                    if ot % 2 == 0:
                        op("act", lambda e, _p=pa, _w=ow, _h=ht, _l=l: e.activation(
                            tB["ag"][0:_w, 0:512], _p[0:_w, 0:512], AF.Identity,
                            bias=b1a[_l][0:_w, _h:_h + 1]),
                           r=[pan, f"b1a{l}"], w=["ag_a"])
                    else:
                        op("act", lambda e, _p=pa, _w=ow, _h=ht, _l=l: e.activation(
                            tB["ag"][0:_w, 1024:1536], _p[0:_w, 0:512], AF.Gelu,
                            bias=b1g[_l][0:_w, _h:_h + 1]),
                           r=[pan, f"b1g{l}"], w=["ag_g"])
                        op("dve", lambda e, _w=ow, _h=ht: e.tensor_mul(
                            tB["gg"][0:_w, _h * 512:_h * 512 + 512],
                            tB["ag"][0:_w, 0:512], tB["ag"][0:_w, 1024:1536]),
                           r=["ag_a", "ag_g"], w=["geglu"])
                if tc + 1 < 4:
                    LN2part(tc + 1)
                for ii in range(4):
                    i = tc * 4 + ii
                    xt = x_sb[:, i * D:(i + 1) * D]
                    for n in range(2):
                        ps = pP[2 * (ii % 2) + n]
                        psn = PN[id(ps)]
                        for kk in range(NHT):
                            kw = 128 if kk < NHT - 1 else HID_LAST
                            op("pe", lambda e, _n=n, _k=kk, _p=ps, _ii=ii, _kw=kw: e.matmul(
                                _p[:, 0:512],
                                tB["gg"][0:_kw, _k * 512 + _ii * 128:_k * 512 + _ii * 128 + 128],
                                tB["w2"][0:_kw, _k * D + _n * 512:_k * D + _n * 512 + 512],
                                start=(_k == 0), stop=False),
                               r=["geglu", "w2bf"], w=[psn])
                        op("pe", lambda e, _n=n, _p=ps, _l=l: e.matmul(
                            _p[:, 0:512], ones_bf[:],
                            tB["b2hi"][0:1, _n * 512:_n * 512 + 512],
                            start=False, stop=True),
                           r=["ones_bf", "b2hi"], w=[psn])
                    b0 = 2 * (ii % 2)
                    op("dve", lambda e, _x=xt, _b=b0: e.tensor_add(
                        _x[:, 0:512], _x[:, 0:512], pP[_b][:]),
                       r=[PN[id(pP[b0])], "x_sb"], w=["x_sb"])
                    op("dve", lambda e, _x=xt, _b=b0: e.tensor_add(
                        _x[:, 512:1024], _x[:, 512:1024], pP[_b + 1][:]),
                       r=[PN[id(pP[b0 + 1])], "x_sb"], w=["x_sb"])

        if PHASE_LIMIT == "all" and DEBUG_TAP is None:
            em.fence(prev_names, FINAL_T)
            for i in range(NT):
                sl = i % 2
                xt = x_sb[:, i * D:(i + 1) * D]
                op("sp", lambda e, _i=i, _s=sl: e.dma_start(
                    x1b[_s][:], d_x1[:, _i * D:(_i + 1) * D]),
                   r=["x1d"], w=[f"x1b{sl}"], dma=True)
                pb = i % 2
                rstd, nmr = ln_stats([(xt, D, "x_sb")], D, pb)
                op("dve", lambda e, _s=sl, _n=nmr: e.tensor_scalar_add(
                    x1b[_s][:], x1b[_s][:], _n),
                   r=[f"x1b{sl}", f"c_nmr{pb}"], w=[f"x1b{sl}"])
                op("dve", lambda e, _x=xt, _s=sl, _r=rstd: e.scalar_tensor_tensor(
                    _x, _x, _r, x1b[_s][:], ALU.mult, ALU.add),
                   r=[f"x1b{sl}", f"c_rstd{pb}", "x_sb"], w=["x_sb"])
        tok = op("sp", lambda e: e.dma_start(
            d_out.rearrange("(n p) d -> p n d", p=128), x_sb3),
            r=["x_sb"], w=["out_dram"], dma=True)
        em.final_wait("sp", tok)

        with nc.Block() as block:
            em.replay(block)
    return nc


def _prep_host(inputs):
    f = np.float32
    low_i, up_i, up_s = _masks()
    common = {
        "wqT": np.ascontiguousarray(inputs["L0_attn_wq_w"].T),
        "wkT": np.ascontiguousarray(inputs["L0_attn_wk_w"].T),
        "wvT": np.ascontiguousarray(inputs["L0_attn_wv_w"].T),
        "woT": np.ascontiguousarray(inputs["L0_attn_wo_w"].T),
        "brow4": np.concatenate([
            inputs["L0_attn_wq_b"], inputs["L0_attn_wk_b"],
            inputs["L0_attn_wv_b"], inputs["L0_attn_wo_b"]]).reshape(1, 4 * D),
        "qn2": np.tile(inputs["L0_attn_qn_w"], 2).reshape(128, 1),
        "kn2": np.tile(inputs["L0_attn_kn_w"], 2).reshape(128, 1),
        "rotpk": _host_tables(),
        "m_low": low_i, "m_upi": up_i, "m_ups": up_s,
        "id32": np.eye(128, dtype=f),
        "id16": np.eye(128, dtype=ml_dtypes.bfloat16),
    }
    for l in range(2):
        w1 = inputs["L0_ffn_w1"][l]
        b1 = inputs["L0_ffn_b1"][l]
        perm = []
        for m in range(NHT):
            aw = min(128, HID - m * 128)
            perm.extend(range(m * 128, m * 128 + aw))
            perm.extend(range(HID + m * 128, HID + m * 128 + aw))
        perm = np.asarray(perm)
        w1p = np.ascontiguousarray(w1[:, perm])
        b1p = b1[perm]
        b1av = np.zeros((128, NHT), f)
        b1gv = np.zeros((128, NHT), f)
        pos = 0
        for m in range(NHT):
            aw = min(128, HID - m * 128)
            b1av[0:aw, m] = b1p[pos:pos + aw]
            b1gv[0:aw, m] = b1p[pos + aw:pos + 2 * aw]
            pos += 2 * aw
        qs = inputs["L0_ffn_q_scale"][l]
        ks = inputs["L0_ffn_k_scale"][l]
        common.update({
            f"wqkv{l}": np.ascontiguousarray(inputs["L0_ffn_wqkv"][l]),
            f"wo{l}": np.ascontiguousarray(inputs["L0_ffn_wo"][l]),
            f"w1p{l}": w1p,
            f"b1a{l}": b1av, f"b1g{l}": b1gv,
            f"w2_{l}": np.ascontiguousarray(inputs["L0_ffn_w2"][l]),
            f"b2_{l}": inputs["L0_ffn_b2"][l].reshape(1, D),
            f"lng{l}": np.ascontiguousarray(inputs["L0_ffn_ln_g"][l].reshape(8, 128).T),
            f"lnb{l}": np.ascontiguousarray(inputs["L0_ffn_ln_b"][l].reshape(8, 128).T),
            f"qsrows{l}": np.concatenate([
                qs, np.concatenate([qs[32:], qs[:32]]),
                ks, np.concatenate([ks[32:], ks[:32]])]).reshape(1, 4 * HD),
        })
    return common


def kernel(**inputs):
    inputs = {k: np.asarray(v) for k, v in inputs.items()}
    common = _prep_host(inputs)
    nc = build()
    in_maps = []
    for c in range(8):
        m = dict(common)
        m["x"] = np.ascontiguousarray(inputs["L0_attn_x"][c])
        in_maps.append(m)
    res = run_bass_kernel_spmd(nc, in_maps, core_ids=list(range(8)))
    return np.stack([res.results[c]["out"] for c in range(8)], axis=0)


# version 09
# speedup vs baseline: 1.3113x; 1.0794x over previous
import numpy as np
import ml_dtypes
import concourse.bass as bass
import concourse.mybir as mybir
from concourse.bass_utils import run_bass_kernel_spmd

F32 = mybir.dt.float32
BF16 = mybir.dt.bfloat16
AF = mybir.ActivationFunctionType
ALU = mybir.AluOpType
AX = mybir.AxisListType

S, D, NT, HD = 2048, 1024, 16, 64
HEADS_O, HEADS_I, INNER = 16, 8, 512
HID, NHT, HID_LAST = 2730, 22, 42
EPS = 1e-5

PHASE_LIMIT = "all"
NT_LIM = NT
DEBUG_TAP = None

ENGINES = ("pe", "act", "dve", "pool", "sp")
QSEM = {"sp": "q_sp", "pool": "q_pool", "act": "q_act"}


class Buf:
    __slots__ = ("name", "w", "readers")

    def __init__(self, name):
        self.name = name
        self.w = None
        self.readers = []


class Emitter:
    def __init__(self, nc):
        self.nc = nc
        self.prog = {e: [] for e in ENGINES}
        self.sems = {}
        self.counts = {n: 0 for n in ("pe", "act", "dve", "pool", "sp")}
        self.lastwait = {}
        self.bufs = {}

    def make_sems(self, ctx):
        for nm in list(self.counts):
            self.sems[nm] = self.nc.alloc_semaphore("sem_" + nm)

    def dma_sem(self, bufname):
        nm = "d_" + bufname
        if nm not in self.sems:
            self.sems[nm] = self.nc.alloc_semaphore("sem_" + nm)
            self.counts[nm] = 0
        return nm

    def buf(self, name):
        if name not in self.bufs:
            self.bufs[name] = Buf(name)
        return self.bufs[name]

    PSUM_BUFS = frozenset(("pP0", "pP1", "pP2", "pP3", "simP", "simX", "tbf", "tr32"))

    def op(self, eng, fn, r=(), w=(), dma=False):
        reads = [self.buf(n) for n in r if n not in self.PSUM_BUFS]
        writes = [self.buf(n) for n in w] + [self.buf(n) for n in r
                                             if n in self.PSUM_BUFS]
        deps = []
        for b in reads:
            if b.w is not None:
                deps.append(b.w)
        for b in writes:
            if b.w is not None:
                deps.append(b.w)
            deps.extend(b.readers)
        mysem = self.dma_sem(w[0]) if dma else eng
        need = {}
        for (sem, val) in deps:
            if dma and sem == mysem:
                continue
            if sem == eng == "pe":
                continue
            if need.get(sem, 0) < val:
                need[sem] = val
        for sem, val in sorted(need.items()):
            if self.lastwait.get((eng, sem), 0) >= val:
                continue
            self.lastwait[(eng, sem)] = val
            self.prog[eng].append(
                lambda e, _s=sem, _v=val: e.wait_ge(self.sems[_s], _v))
        amt = 16 if dma else 1
        self.counts[mysem] += amt
        val = self.counts[mysem]
        self.prog[eng].append(
            lambda e, _f=fn, _s=mysem, _a=amt: _f(e).then_inc(self.sems[_s], _a))
        tok = (mysem, val)
        for b in reads:
            b.readers.append(tok)
        for b in writes:
            b.w = tok
            b.readers = []
        return tok

    def fence(self, old_names, new_names):
        toks = {}
        for n in old_names:
            b = self.bufs.get(n)
            if b is None:
                continue
            for t in ([b.w] if b.w else []) + b.readers:
                if toks.get(t[0], 0) < t[1]:
                    toks[t[0]] = t[1]
        tl = list(toks.items())
        for n in new_names:
            b = self.buf(n)
            b.readers = b.readers + tl
        for n in old_names:
            if n in self.bufs:
                self.bufs[n].w = None
                self.bufs[n].readers = []

    def final_wait(self, eng, tok):
        self.prog[eng].append(lambda e, _t=tok: e.wait_ge(self.sems[_t[0]], _t[1]))

    def replay(self, block):
        for eng, attr in (("sp", "sync"), ("pe", "tensor"), ("act", "scalar"),
                          ("dve", "vector"), ("pool", "gpsimd")):
            seq = self.prog[eng]

            def make(seq_):
                def run(e):
                    for f in seq_:
                        f(e)
                return run
            getattr(block, attr)(make(seq))


def _host_tables():
    half = np.arange(0, HD, 2, dtype=np.float64)
    inv_freq = 1.0 / 10000.0 ** (half / HD)
    t = np.arange(S, dtype=np.float64)
    freqs = t[:, None] * inv_freq[None, :]
    freqs = np.concatenate([freqs, freqs], axis=1)
    base = (half + 0.4 * HD) / (1.4 * HD)
    power = (t - S // 2) / 256.0
    scale = base[None, :] ** power[:, None]
    scale = np.concatenate([scale, scale], axis=1)
    cq = (np.cos(freqs) * scale).astype(np.float32)
    sq = (np.sin(freqs) * scale).astype(np.float32)
    ck = (np.cos(freqs) / scale).astype(np.float32)
    sk = (np.sin(freqs) / scale).astype(np.float32)
    packed = np.concatenate([cq, sq, ck, sk], axis=1)
    return np.ascontiguousarray(
        packed.reshape(NT, 128, 4 * HD).transpose(1, 0, 2).reshape(128, NT * 4 * HD))


def _masks():
    i = np.arange(128)[:, None]
    j = np.arange(128)[None, :]
    neg = np.float32(-1.0e30)
    bf = ml_dtypes.bfloat16
    low_i = np.where(i >= j, 0.0, neg).astype(np.float32).astype(bf)
    up_i = np.where(j >= i, 0.0, neg).astype(np.float32).astype(bf)
    up_s = np.where(j > i, 0.0, neg).astype(np.float32).astype(bf)
    return low_i, up_i, up_s


def build():
    nc = bass.Bass("TRN2", target_bir_lowering=False)
    dram = {}

    def din(name, shape, dt=F32):
        dram[name] = nc.dram_tensor(name, list(shape), dt, kind="ExternalInput").ap()

    din("x", (S, D))
    for w in ("wqT", "wkT", "wvT", "woT"):
        din(w, (D, D))
    din("brow4", (1, 4 * D))
    din("qn2", (128, 1))
    din("kn2", (128, 1))
    din("rotpk", (128, NT * 4 * HD))
    din("m_low", (128, 128), BF16)
    din("m_upi", (128, 128), BF16)
    din("m_ups", (128, 128), BF16)
    din("id32", (128, 128), F32)
    din("id16", (128, 128), BF16)
    for l in range(2):
        din(f"wqkv{l}", (D, 3 * INNER))
        din(f"wo{l}", (INNER, D))
        din(f"w1p{l}", (D, 2 * HID))
        din(f"b1a{l}", (128, NHT))
        din(f"b1g{l}", (128, NHT))
        din(f"w2_{l}", (HID, D))
        din(f"b2_{l}", (1, D))
        din(f"lng{l}", (128, 8))
        din(f"lnb{l}", (128, 8))
        din(f"qsrows{l}", (1, 4 * HD))
    d_out = nc.dram_tensor("out", [S, D], F32, kind="ExternalOutput").ap()
    d_x1 = nc.dram_tensor("x1_scratch", [128, NT * D], F32, kind="Internal").ap()

    from contextlib import ExitStack
    with ExitStack() as ctx:
        em = Emitter(nc)
        em.make_sems(ctx)

        def sba(name, shape, dt=F32):
            return nc.alloc_sbuf_tensor(name, list(shape), dt)

        x_sb = sba("x_sb", (128, NT * D))
        ln_scr = sba("ln_scr", (128, D))
        sq_scr = sba("sq_scr", (128, D))
        cols = sba("cols", (128, 384))
        m_low = sba("m_low_sb", (128, 128), BF16)
        m_upi = sba("m_upi_sb", (128, 128), BF16)
        m_ups = sba("m_ups_sb", (128, 128), BF16)
        id32 = sba("id32_sb", (128, 128))
        id16 = sba("id16_sb", (128, 128), BF16)
        ones_bf = sba("ones_bf", (1, 128), BF16)
        qn2 = sba("qn2_sb", (128, 1))
        kn2 = sba("kn2_sb", (128, 1))
        lng = [sba(f"lng{l}_sb", (128, 8)) for l in range(2)]
        lnb = [sba(f"lnb{l}_sb", (128, 8)) for l in range(2)]
        b1a = [sba(f"b1a{l}_sb", (128, NHT)) for l in range(2)]
        b1g = [sba(f"b1g{l}_sb", (128, NHT)) for l in range(2)]
        wld = [sba(f"wld{i}", (128, 1024)) for i in range(2)]
        rowb = sba("rowb", (1, 256), BF16)

        ARENA = 122880
        lo, hi = nc.bump_sbuf(ARENA)
        assert hi - lo >= ARENA

        def ar(name, shape, dt, off):
            sz = int(np.prod(shape[1:])) * mybir.dt.size(dt)
            assert off + sz <= ARENA, (name, off, sz)
            return nc.alloc_sbuf_tensor_at(name, list(shape), dt, offset=lo + off)

        wT_bf = {w: ar(w + "_bf", (128, 8 * D), BF16, k * 16384)
                 for k, w in enumerate(("wqT", "wkT", "wvT", "woT"))}
        q_scrO = ar("q_scrO", (128, D), F32, 65536)
        k_scrO = ar("k_scrO", (128, D), F32, 69632)
        ltO = [ar(f"ltO{i}", (128, D), BF16, 73728 + i * 2048) for i in range(2)]
        qhTO = [ar(f"qhTO{i}", (128, D), BF16, 77824 + i * 2048) for i in range(2)]
        khT3 = ar("khT3", (128, 4 * D), BF16, 81920)
        v3 = ar("v3", (128, 4 * D), BF16, 90112)
        pbO = [ar(f"pbO{i}", (128, 512), BF16, 98304 + i * 1024) for i in range(2)]
        ptO = [ar(f"ptO{i}", (128, 512), BF16, 100352 + i * 1024) for i in range(2)]
        obO = [ar(f"obO{i}", (128, D), BF16, 102400 + i * 2048) for i in range(2)]
        otO = [ar(f"otO{i}", (128, D), BF16, 106496 + i * 2048) for i in range(2)]
        brow4_hi = ar("brow4_hi", (1, 4 * D), BF16, 110592)
        OUTER_T = (["wqT_bf", "wkT_bf", "wvT_bf", "woT_bf", "q_scrO", "k_scrO",
                    "khT3", "v3", "brow4_hi"] +
                   [f"{p}{i}" for p in ("ltO", "qhTO", "pbO", "ptO", "obO", "otO")
                    for i in range(2)])

        def innerA():
            t = {}
            t["wqkv"] = ar("wqkv_bf", (128, 8 * 3 * INNER), BF16, 0)
            t["woin"] = ar("woin_bf", (128, 4 * D), BF16, 24576)
            t["q_scr"] = ar("q_scrI", (128, INNER), F32, 32768)
            t["k_scr"] = ar("k_scrI", (128, INNER), F32, 34816)
            t["rh"] = ar("rhI", (128, INNER), F32, 36864)
            t["lt"] = [ar(f"ltI_{i}", (128, D), BF16, 38912 + i * 2048) for i in range(2)]
            t["qhT"] = [ar(f"qhTI_{i}", (128, INNER), BF16, 43008 + i * 1024) for i in range(2)]
            t["khT5"] = ar("khT5", (128, 6 * INNER), BF16, 45056)
            t["v5"] = ar("v5", (128, 6 * INNER), BF16, 51200)
            t["pb"] = [ar(f"pbI_{i}", (128, 640), BF16, 57344 + i * 1280) for i in range(2)]
            t["pt"] = [ar(f"ptI_{i}", (128, 640), BF16, 59904 + i * 1280) for i in range(2)]
            t["ob"] = [ar(f"obI_{i}", (128, INNER), BF16, 62464 + i * 1024) for i in range(2)]
            t["ot"] = [ar(f"otI_{i}", (128, INNER), BF16, 64512 + i * 1024) for i in range(2)]
            t["h2"] = [ar(f"h2A_{i}", (128, INNER), F32, 66560 + i * 2048) for i in range(2)]
            t["rb"] = [ar(f"rotb_{i}", (128, 4 * HD), F32, 70656 + i * 1024) for i in range(2)]
            t["abt"] = ar("abt", (128, 4 * HD), F32, 72704)
            names = (["wqkv_bf", "woin_bf", "q_scrI", "k_scrI",
                      "rhI", "khT5", "v5", "abt",
                      "w2bf", "b2hi"] +
                     [f"{p}_{i}" for p in ("ltI", "qhTI", "pbI", "ptI", "obI", "otI",
                                           "h2A", "rotb") for i in range(2)])
            return t, names

        def innerB():
            t = {}
            t["w2"] = ar("w2bf", (128, NHT * D), BF16, 73728)
            t["gg"] = ar("geglu", (128, NHT * 512), BF16, 0)
            t["ln2T"] = ar("ln2T", (128, 8 * 512), BF16, 22528)
            t["w1"] = [ar(f"w1bf_{i}", (128, 4096), BF16, 30720 + i * 8192) for i in range(2)]
            t["ag"] = ar("agscr", (128, 2048), F32, 47104)
            t["h2"] = [ar(f"h2B_{i}", (128, INNER), F32, 55296 + i * 2048) for i in range(2)]
            t["b2hi"] = ar("b2hi", (1, D), BF16, 118784)
            names = (["geglu", "ln2T", "agscr"] +
                     [] +
                     [f"w1bf_{i}" for i in range(2)] + [f"h2B_{i}" for i in range(2)])
            return t, names

        x1b = [ar(f"x1b{i}", (128, D), F32, i * 4096) for i in range(2)]
        FINAL_T = ["x1b0", "x1b1"]

        pP = [nc.alloc_psum_tensor(f"pP{i}", [128, 512], F32) for i in range(4)]
        simP = nc.alloc_psum_tensor("simP", [128, 512], F32)
        simX = nc.alloc_psum_tensor("simX", [128, 128], F32)
        tbf = nc.alloc_psum_tensor("tbf", [128, 1024], BF16)
        tr32 = nc.alloc_psum_tensor("tr32", [128, 512], F32)
        PN = {id(pP[0]): "pP0", id(pP[1]): "pP1", id(pP[2]): "pP2", id(pP[3]): "pP3",
              id(simP): "simP", id(simX): "simX", id(tbf): "tbf", id(tr32): "tr32"}

        op = em.op

        def stats_bank(p):
            base = 8 * p
            return [cols[:, base + i:base + i + 1] for i in range(8)]
        sums_bank = [cols[:, 16:32], cols[:, 32:48]]
        c_rdens = cols[:, 48:64]
        c_nmax = [cols[:, 64 + i:65 + i] for i in range(2)]
        c_nmaxx = [cols[:, 66 + i:67 + i] for i in range(2)]
        c_denx = [cols[:, 68 + i:69 + i] for i in range(2)]
        qn_eff = cols[:, 70:71]
        c_eight = cols[:, 71:72]
        c_eps = cols[:, 72:73]
        c_s1, c_mu, c_m2, c_var, c_rstd, c_nmr, c_t6, c_t7 = stats_bank(0)

        def rstd_ap(pb):
            return stats_bank(pb)[4]

        x_sb3 = bass.AP(x_sb, 0, [[NT * D, 128], [D, NT], [1, D]])
        op("sp", lambda e: e.dma_start(
            x_sb3, dram["x"].rearrange("(n p) d -> p n d", p=128)),
            w=["x_sb"], dma=True)
        for t, n in ((m_low, "m_low"), (m_upi, "m_upi"), (m_ups, "m_ups"),
                     (id32, "id32"), (id16, "id16"), (qn2, "qn2"), (kn2, "kn2")):
            op("sp", lambda e, _t=t, _n=n: e.dma_start(_t[:], dram[_n]), w=[n], dma=True)
        for l in range(2):
            for n, t in ((f"lng{l}", lng[l]), (f"lnb{l}", lnb[l]),
                         (f"b1a{l}", b1a[l]), (f"b1g{l}", b1g[l])):
                op("sp", lambda e, _t=t, _n=n: e.dma_start(_t[:], dram[_n]), w=[n], dma=True)
        op("dve", lambda e: e.memset(ones_bf[:], 1.0), w=["ones_bf"])
        op("dve", lambda e: e.memset(c_eps, EPS), w=["c_eps"])
        op("dve", lambda e: e.memset(c_eight, 8.0), w=["c_eight"])
        op("dve", lambda e: e.tensor_scalar_mul(qn_eff, qn2[:], 0.125), r=["qn2"], w=["c_qe"])

        def load_cast(dst_bf, dst_name, dram_name, n_rows, n_cols, cast_eng=None):
            nj = n_rows // 128
            rem = n_rows - nj * 128
            W = dst_bf.ap().ap[0][0]
            if nj:
                op("pool", lambda e: e.dma_start(
                    bass.AP(dst_bf, 0, [[W, 128], [n_cols, nj], [1, n_cols]]),
                    dram[dram_name][0:nj * 128, :].rearrange("(j p) o -> p j o", p=128)),
                   w=[dst_name], dma=True)
            if rem:
                op("pool", lambda e: e.dma_start(
                    dst_bf[0:rem, nj * n_cols:(nj + 1) * n_cols],
                    dram[dram_name][nj * 128:n_rows, :]),
                   w=[dst_name], dma=True)
        load_cast.k = 0

        def ln_stats(srcs, total_d, pb=0):
            c_s1, c_mu, c_m2, c_var, c_rstd, c_nmr, c_t6, c_t7 = stats_bank(pb)
            sx = str(pb)
            for idx, (apv, wdt, rn) in enumerate(srcs):
                if idx == 0:
                    op("dve", lambda e, _a=apv, _c=c_s1: e.tensor_reduce(_c, _a, AX.X, ALU.add),
                       r=[rn], w=["c_s1" + sx])
                    op("act", lambda e, _a=apv, _w=wdt, _c=c_t6: e.activation(
                        sq_scr[:, 0:_w], _a, AF.Square, accum_out=_c),
                       r=[rn], w=["sq_scr", "c_t6" + sx])
                else:
                    op("dve", lambda e, _a=apv, _c=c_t7: e.tensor_reduce(_c, _a, AX.X, ALU.add),
                       r=[rn], w=["c_t7" + sx])
                    op("dve", lambda e, _c1=c_s1, _c2=c_t7: e.tensor_add(_c1, _c1, _c2),
                       r=["c_t7" + sx, "c_s1" + sx], w=["c_s1" + sx])
                    op("act", lambda e, _a=apv, _w=wdt, _c=c_t7: e.activation(
                        sq_scr[:, 0:_w], _a, AF.Square, accum_out=_c),
                       r=[rn], w=["sq_scr", "c_t7" + sx])
                    op("dve", lambda e, _c1=c_t6, _c2=c_t7: e.tensor_add(_c1, _c1, _c2),
                       r=["c_t7" + sx, "c_t6" + sx], w=["c_t6" + sx])
            inv = 1.0 / total_d
            op("dve", lambda e, _a=c_mu, _b=c_s1: e.tensor_scalar_mul(_a, _b, inv),
               r=["c_s1" + sx], w=["c_mu" + sx])
            op("dve", lambda e, _a=c_m2, _b=c_mu: e.tensor_mul(_a, _b, _b),
               r=["c_mu" + sx], w=["c_m2" + sx])
            op("dve", lambda e, _a=c_var, _b=c_t6: e.tensor_scalar_mul(_a, _b, inv),
               r=["c_t6" + sx], w=["c_var" + sx])
            op("dve", lambda e, _a=c_var, _b=c_m2: e.tensor_sub(_a, _a, _b),
               r=["c_var" + sx, "c_m2" + sx], w=["c_var" + sx])
            op("act", lambda e, _a=c_var: e.activation(_a, _a, AF.Sqrt, bias=c_eps, scale=1.0),
               r=["c_var" + sx, "c_eps"], w=["c_var" + sx])
            op("dve", lambda e, _a=c_rstd, _b=c_var: e.reciprocal(_a, _b),
               r=["c_var" + sx], w=["c_rstd" + sx])
            op("dve", lambda e, _a=c_nmr, _b=c_mu, _c=c_rstd: e.scalar_tensor_tensor(
                _a, _b, -1.0, _c, ALU.mult, ALU.mult),
               r=["c_mu" + sx, "c_rstd" + sx], w=["c_nmr" + sx])
            return c_rstd, c_nmr

        def tr_f32(chunks, evacs):
            for g0 in range(0, len(chunks), 4):
                grp = chunks[g0:g0 + 4]
                for gi, (src, rn) in enumerate(grp):
                    op("pe", lambda e, _s=src, _g=gi: e.transpose(
                        tr32[:, _g * 128:_g * 128 + 128], _s, id32[:]),
                       r=[rn, "id32"], w=["tr32"])
                evacs(g0, len(grp))

        for k, w in enumerate(("wqT", "wkT", "wvT", "woT")):
            load_cast(wT_bf[w], w + "_bf", w, D, D)
        op("pool", lambda e: e.dma_start(brow4_hi[0:1, :], dram["brow4"]),
           w=["brow4_hi"], dma=True)
        def proj_1024(lt_t, lt_name, w_bf, w_name, psA, psB, bias_idx, brow=None,
                      brow_name="ones_bf"):
            for n in range(2):
                ps = (psA, psB)[n]
                psn = PN[id(ps)]
                for j in range(8):
                    op("pe", lambda e, _n=n, _j=j, _p=ps, _lt=lt_t: e.matmul(
                        _p[:, 0:512], _lt[:, _j * 128:_j * 128 + 128],
                        w_bf[:, _j * D + _n * 512:_j * D + _n * 512 + 512],
                        start=(_j == 0), stop=False),
                       r=[lt_name, w_name], w=[psn])
                br = ones_bf[:] if brow is None else brow
                op("pe", lambda e, _n=n, _p=ps, _b=bias_idx, _r=br: e.matmul(
                    _p[:, 0:512], _r,
                    brow4_hi[0:1, _b * D + _n * 512:_b * D + _n * 512 + 512],
                    start=False, stop=True),
                   r=[brow_name, "brow4_hi"], w=[psn])

        def rmsnorm16(scr, scr_name, pb):
            su = sums_bank[pb]
            sn = f"c_sums{pb}"
            op("dve", lambda e, _s=scr: e.tensor_scalar_min(_s[:], _s[:], 1.0e4),
               r=[scr_name], w=[scr_name])
            op("dve", lambda e, _s=scr: e.tensor_scalar_max(_s[:], _s[:], -1.0e4),
               r=[scr_name], w=[scr_name])
            op("act", lambda e, _s=scr: e.activation(sq_scr[:], _s[:], AF.Square),
               r=[scr_name], w=["sq_scr"])
            red3 = bass.AP(sq_scr, 0, [[D, 128], [64, 16], [1, 64]])
            op("dve", lambda e, _r=red3, _u=su: e.tensor_reduce(_u, _r, AX.X, ALU.add),
               r=["sq_scr"], w=[sn])
            op("dve", lambda e, _u=su: e.tensor_scalar(
                _u, _u, 1.0 / 64, 1.0e-5, ALU.mult, ALU.max),
               r=[sn], w=[sn])
            op("act", lambda e, _u=su: e.activation(_u, _u, AF.Sqrt, bias=c_eps, scale=1.0),
               r=[sn, "c_eps"], w=[sn])
            op("dve", lambda e, _u=su: e.reciprocal(_u, _u), r=[sn], w=[sn])
            for h in range(16):
                op("dve", lambda e, _s=scr, _h=h, _u=su: e.tensor_scalar_mul(
                    _s[:, _h * 64:_h * 64 + 64], _s[:, _h * 64:_h * 64 + 64],
                    _u[:, _h:_h + 1]), r=[scr_name, sn], w=[scr_name])

        def P1o(i):
            sl = i % 2
            k4 = i % 4
            pb = i % 2
            xt = x_sb[:, i * D:(i + 1) * D]
            c_s1, c_mu, c_m2, c_var, c_rstd, c_nmr, c_t6, c_t7 = stats_bank(pb)
            sx = str(pb)
            op("dve", lambda e, _a=xt, _c=c_s1: e.tensor_reduce(_c, _a, AX.X, ALU.add),
               r=["x_sb"], w=["c_s1" + sx])
            op("dve", lambda e, _a=c_nmr, _b=c_s1: e.tensor_scalar_mul(_a, _b, -1.0 / D),
               r=["c_s1" + sx], w=["c_nmr" + sx])
            op("act", lambda e, _x=xt, _n=c_nmr: e.activation(
                ln_scr[:], _x, AF.Identity, bias=_n, scale=1.0),
               r=["x_sb", "c_nmr" + sx], w=["ln_scr"])
            op("act", lambda e, _a=xt, _c=c_t6: e.activation(
                sq_scr[:], _a, AF.Square, accum_out=_c),
               r=["x_sb"], w=["sq_scr", "c_t6" + sx])
            op("dve", lambda e, _a=c_mu, _b=c_nmr: e.tensor_mul(_a, _b, _b),
               r=["c_nmr" + sx], w=["c_mu" + sx])
            op("dve", lambda e, _a=c_var, _b=c_t6: e.tensor_scalar_mul(_a, _b, 1.0 / D),
               r=["c_t6" + sx], w=["c_var" + sx])
            op("dve", lambda e, _a=c_var, _b=c_mu: e.tensor_sub(_a, _a, _b),
               r=["c_var" + sx, "c_mu" + sx], w=["c_var" + sx])
            op("act", lambda e, _a=c_var: e.activation(_a, _a, AF.Sqrt, bias=c_eps, scale=1.0),
               r=["c_var" + sx, "c_eps"], w=["c_var" + sx])
            op("pe", lambda e, _a=c_var: e.transpose(simX[0:1, 0:128], _a, id32[:]),
               r=["c_var" + sx, "id32"], w=["simX"])
            op("act", lambda e, _p=pb: e.activation(
                rowb[0:1, _p * 128:_p * 128 + 128], simX[0:1, 0:128], AF.Identity),
               r=["simX"], w=[f"rowb{pb}"])
            op("dve", lambda e, _a=c_rstd, _b=c_var: e.reciprocal(_a, _b),
               r=["c_var" + sx], w=["c_rstd" + sx])

            def ev_lt(g0, glen, _sl=sl):
                op("act", lambda e, _g0=g0, _n=glen * 128: e.activation(
                    ltO[_sl][:, _g0 * 128:_g0 * 128 + _n], tr32[:, 0:_n], AF.Identity),
                   r=["tr32"], w=[f"ltO{_sl}"])
            tr_f32([(ln_scr[:, j * 128:j * 128 + 128], "ln_scr") for j in range(8)], ev_lt)
            stdrow = rowb[0:1, pb * 128:pb * 128 + 128]
            rbn = f"rowb{pb}"
            proj_1024(ltO[sl], f"ltO{sl}", wT_bf["wqT"], "wqT_bf", pP[0], pP[1], 0,
                      brow=stdrow, brow_name=rbn)
            proj_1024(ltO[sl], f"ltO{sl}", wT_bf["wkT"], "wkT_bf", pP[2], pP[3], 1,
                      brow=stdrow, brow_name=rbn)
            op("dve", lambda e, _r=rstd_ap(pb): e.tensor_scalar_mul(
                q_scrO[:, 0:512], pP[0][:], _r), r=["pP0", "c_rstd" + str(pb)], w=["q_scrO"])
            op("dve", lambda e, _r=rstd_ap(pb): e.tensor_scalar_mul(
                q_scrO[:, 512:1024], pP[1][:], _r), r=["pP1", "c_rstd" + str(pb)], w=["q_scrO"])
            op("act", lambda e, _r=rstd_ap(pb): e.activation(
                k_scrO[:, 0:512], pP[2][:], AF.Identity, scale=_r),
               r=["pP2", "c_rstd" + str(pb)], w=["k_scrO"])
            op("act", lambda e, _r=rstd_ap(pb): e.activation(
                k_scrO[:, 512:1024], pP[3][:], AF.Identity, scale=_r),
               r=["pP3", "c_rstd" + str(pb)], w=["k_scrO"])
            proj_1024(ltO[sl], f"ltO{sl}", wT_bf["wvT"], "wvT_bf", pP[0], pP[1], 2,
                      brow=stdrow, brow_name=rbn)
            op("act", lambda e, _k=k4, _r=rstd_ap(pb): e.activation(
                v3[:, _k * D:_k * D + 512], pP[0][:], AF.Identity, scale=_r),
               r=["pP0", "c_rstd" + str(pb)], w=["v3"])
            op("act", lambda e, _k=k4, _r=rstd_ap(pb): e.activation(
                v3[:, _k * D + 512:_k * D + 1024], pP[1][:], AF.Identity, scale=_r),
               r=["pP1", "c_rstd" + str(pb)], w=["v3"])
            rmsnorm16(q_scrO, "q_scrO", pb)

            def ev_q(g0, glen, _sl=sl):
                op("act", lambda e, _g0=g0, _n=glen * 128: e.activation(
                    qhTO[_sl][:, _g0 * 128:_g0 * 128 + _n], tr32[:, 0:_n],
                    AF.Identity, scale=qn_eff),
                   r=["tr32", "c_qe"], w=[f"qhTO{_sl}"])
            tr_f32([(q_scrO[:, j * 128:j * 128 + 128], "q_scrO") for j in range(8)], ev_q)
            rmsnorm16(k_scrO, "k_scrO", pb)

            def ev_k(g0, glen, _k4=k4):
                op("act", lambda e, _g0=g0, _n=glen * 128: e.activation(
                    khT3[:, _k4 * D + _g0 * 128:_k4 * D + _g0 * 128 + _n], tr32[:, 0:_n],
                    AF.Identity, scale=kn2[:]),
                   r=["tr32", "kn2"], w=["khT3"])
            tr_f32([(k_scrO[:, j * 128:j * 128 + 128], "k_scrO") for j in range(8)], ev_k)

        def P2o(i):
            sl = i % 2
            pb = i % 2
            xt = x_sb[:, i * D:(i + 1) * D]
            kstart = max(0, i - 2)
            nk = i - kstart + 1
            ncols = nk * 128

            def emit_ptav(h):
                hs = h % 2
                for c in range(nk):
                    op("pe", lambda e, _c=c, _hs=hs: e.transpose(
                        tbf[:, _c * 128:_c * 128 + 128],
                        pbO[_hs][:, _c * 128:_c * 128 + 128], id16[:]),
                       r=[f"pbO{hs}", "id16"], w=["tbf"])
                op("dve", lambda e, _n=ncols, _hs=hs: e.tensor_copy(
                    ptO[_hs][:, 0:_n], tbf[:, 0:_n]), r=["tbf"], w=[f"ptO{hs}"])
                bank = pP[0] if h < 8 else pP[1]
                bn = PN[id(bank)]
                for c in range(nk):
                    slot = (kstart + c) % 4
                    st_ = (h % 8 == 0) and (c == 0)
                    sp_ = (h % 8 == 7) and (c == nk - 1)
                    op("pe", lambda e, _c=c, _hs=hs, _b=bank, _co=(h % 8) * 64,
                       _st=st_, _sp=sp_, _slot=slot, _h=h: e.matmul(
                        _b[:, _co:_co + 64], ptO[_hs][:, _c * 128:_c * 128 + 128],
                        v3[:, _slot * D + _h * 64:_slot * D + _h * 64 + 64],
                        start=_st, stop=_sp),
                       r=[f"ptO{hs}", "v3"], w=[bn])

            pend = None
            for h in range(HEADS_O):
                hp = (h % 2) * 64
                hc = (h // 2) * 128
                hs = h % 2
                for c in range(nk):
                    slot = (kstart + c) % 4
                    op("pe", lambda e, _c=c, _s=slot, _hp=hp, _hc=hc, _sl=sl,
                       _st=(c == 0): e.matmul(
                        simP[:, _c * 128:_c * 128 + 128],
                        qhTO[_sl][_hp:_hp + 64, _hc:_hc + 128],
                        khT3[_hp:_hp + 64, _s * D + _hc:_s * D + _hc + 128],
                        start=_st, stop=False),
                       r=[f"qhTO{sl}", "khT3"], w=["simP"])
                if nk == 3:
                    op("pe", lambda e: e.matmul(simP[:, 0:128], id16[:], m_ups[:],
                                                start=False, stop=False),
                       r=["id16", "m_ups"], w=["simP"])
                op("pe", lambda e, _o=(nk - 1) * 128: e.matmul(
                    simP[:, _o:_o + 128], id16[:], m_low[:], start=False, stop=True),
                   r=["id16", "m_low"], w=["simP"])
                if pend is not None:
                    emit_ptav(pend)
                nmv = c_nmax[h % 2]
                nmn = f"c_nmax{h % 2}"
                rdn = f"c_rd{h}"
                op("dve", lambda e, _n=ncols, _m=nmv: e.tensor_reduce(
                    _m, simP[:, 0:_n], AX.X, ALU.max, negate=True),
                   r=["simP"], w=[nmn])
                op("act", lambda e, _n=ncols, _hs=hs, _m=nmv, _h=h: e.activation(
                    pbO[_hs][:, 0:_n], simP[:, 0:_n], AF.Exp, bias=_m, scale=1.0,
                    accum_out=c_rdens[:, _h:_h + 1]),
                   r=["simP", nmn], w=[f"pbO{hs}", rdn])
                op("dve", lambda e, _h=h: e.reciprocal(
                    c_rdens[:, _h:_h + 1], c_rdens[:, _h:_h + 1]), r=[rdn], w=[rdn])
                pend = h
            emit_ptav(pend)
            for h in range(HEADS_O):
                bank = pP[0] if h < 8 else pP[1]
                bn = PN[id(bank)]
                op("act", lambda e, _b=bank, _co=(h % 8) * 64, _h=h, _s=sl: e.activation(
                    obO[_s][:, _h * 64:_h * 64 + 64], _b[:, _co:_co + 64], AF.Identity,
                    scale=c_rdens[:, _h:_h + 1]),
                   r=[bn, f"c_rd{h}"], w=[f"obO{sl}"])
            for j in range(8):
                op("pe", lambda e, _j=j, _s=sl: e.transpose(
                    tbf[:, _j * 128:_j * 128 + 128],
                    obO[_s][:, _j * 128:_j * 128 + 128], id16[:]),
                   r=[f"obO{sl}", "id16"], w=["tbf"])
            op("dve", lambda e, _s=sl: e.tensor_copy(otO[_s][:], tbf[:]),
               r=["tbf"], w=[f"otO{sl}"])
            proj_1024(otO[sl], f"otO{sl}", wT_bf["woT"], "woT_bf", pP[2], pP[3], 3)
            rstd, nmr = ln_stats([(pP[2][:], 512, "pP2"), (pP[3][:], 512, "pP3")], D, pb)
            op("dve", lambda e, _x=xt, _n=nmr: e.tensor_scalar_add(_x, _x, _n),
               r=["x_sb", f"c_nmr{pb}"], w=["x_sb"])
            op("dve", lambda e, _x=xt, _r=rstd: e.scalar_tensor_tensor(
                _x[:, 0:512], pP[2][:], _r, _x[:, 0:512], ALU.mult, ALU.add),
               r=["pP2", f"c_rstd{pb}", "x_sb"], w=["x_sb"])
            op("dve", lambda e, _x=xt, _r=rstd: e.scalar_tensor_tensor(
                _x[:, 512:1024], pP[3][:], _r, _x[:, 512:1024], ALU.mult, ALU.add),
               r=["pP3", f"c_rstd{pb}", "x_sb"], w=["x_sb"])

        NTL = min(NT, NT_LIM)
        if NTL:
            P1o(0)
            for i in range(NTL):
                if i + 1 < NTL:
                    P1o(i + 1)
                P2o(i)

        op("pool", lambda e: e.dma_start(d_x1, x_sb[:]), r=["x_sb"], w=["x1d"], dma=True)
        if not (PHASE_LIMIT == "outer" and DEBUG_TAP):
            for i in range(NT):
                xt = x_sb[:, i * D:(i + 1) * D]
                pb = i % 2
                rstd, nmr = ln_stats([(xt, D, "x_sb")], D, pb)
                op("act", lambda e, _x=xt, _r=rstd, _n=nmr: e.activation(
                    _x, _x, AF.Identity, bias=_n, scale=_r),
                   r=["x_sb", f"c_rstd{pb}", f"c_nmr{pb}", "x1d"], w=["x_sb"])

        n_inner = {"outer": 0, "layer0": 1, "all": 2}[PHASE_LIMIT]
        prev_names = OUTER_T
        tA, namesA = (innerA() if n_inner else (None, None))
        tB, namesB = (innerB() if n_inner else (None, None))
        for l in range(n_inner):
            em.fence(prev_names, namesA)
            prev_names = namesA
            load_cast(tA["wqkv"], "wqkv_bf", f"wqkv{l}", D, 3 * INNER)
            load_cast(tA["woin"], "woin_bf", f"wo{l}", INNER, D)
            qsb_ap = cols[:, 128:384]
            slot = load_cast.k % 2
            load_cast.k += 1
            op("sp", lambda e, _l=l, _s=slot: e.dma_start(
                wld[_s][0:1, 0:256], dram[f"qsrows{_l}"]), w=["wld%d" % slot], dma=True)
            op("dve", lambda e: e.memset(sq_scr[0:1, 0:128], 1.0), w=["sq_scr"])
            op("pe", lambda e, _s=slot: e.matmul(
                tr32[:, 0:256], sq_scr[0:1, 0:128], wld[_s][0:1, 0:256],
                start=True, stop=True),
               r=["sq_scr", "wld%d" % slot], w=["tr32"])
            op("dve", lambda e: e.tensor_copy(qsb_ap, tr32[:, 0:256]),
               r=["tr32"], w=["qsb"])

            def shift_tile(i, dst, dst_name):
                op("pool", lambda e, _i=i, _d=dst: e.dma_start(
                    _d[1:128, :], x_sb[0:127, _i * D + 512:_i * D + 1024]),
                   r=["x_sb"], w=[dst_name], dma=True)
                if i == 0:
                    op("dve", lambda e, _d=dst: e.memset(_d[0:1, :], 0.0), w=[dst_name])
                else:
                    op("pool", lambda e, _i=i, _d=dst: e.dma_start(
                        _d[0:1, :], x_sb[127:128, (_i - 1) * D + 512:(_i - 1) * D + 1024]),
                       r=["x_sb"], w=[dst_name], dma=True)

            load_cast(tB["w2"], "w2bf", f"w2_{l}", HID, D)
            op("pool", lambda e, _l=l: e.dma_start(tB["b2hi"][0:1, :], dram[f"b2_{_l}"]),
               w=["b2hi"], dma=True)
            def l2rot(scr, scr_name, a_off, b_off, pb):
                sums8 = sums_bank[pb][:, 0:8]
                sn = f"c_sums{pb}"
                op("act", lambda e, _s=scr: e.activation(
                    sq_scr[:, 0:512], _s[:], AF.Square), r=[scr_name], w=["sq_scr"])
                red3 = bass.AP(sq_scr, 0, [[D, 128], [64, 8], [1, 64]])
                op("dve", lambda e, _r=red3, _u=sums8: e.tensor_reduce(_u, _r, AX.X, ALU.add),
                   r=["sq_scr"], w=[sn])
                op("act", lambda e, _u=sums8: e.activation(_u, _u, AF.Sqrt),
                   r=[sn], w=[sn])
                op("dve", lambda e, _u=sums8: e.tensor_scalar_max(_u, _u, 1.0e-12),
                   r=[sn], w=[sn])
                op("dve", lambda e, _u=sums8: e.reciprocal(_u, _u), r=[sn], w=[sn])
                for h in range(8):
                    op("dve", lambda e, _s=scr, _h=h, _u=sums8: e.tensor_scalar_mul(
                        _s[:, _h * 64:_h * 64 + 64], _s[:, _h * 64:_h * 64 + 64],
                        _u[:, _h:_h + 1]), r=[scr_name, sn], w=[scr_name])
                vh = lambda t, off: bass.AP(t, off, [[t.ap().ap[0][0], 128], [64, 8], [1, 32]])
                a_bc = bass.AP(tA["abt"], a_off, [[4 * HD, 128], [0, 8], [1, 64]])
                b_bc = bass.AP(tA["abt"], b_off, [[4 * HD, 128], [0, 8], [1, 64]])
                op("dve", lambda e, _s=scr: e.tensor_scalar_mul(
                    vh(tA["rh"], 0), vh(_s, 32), -1.0), r=[scr_name], w=["rhI"])
                op("dve", lambda e, _s=scr: e.tensor_copy(vh(tA["rh"], 32), vh(_s, 0)),
                   r=[scr_name], w=["rhI"])
                op("dve", lambda e, _s=scr, _a=a_bc: e.tensor_mul(_s[:], _s[:], _a),
                   r=[scr_name, "abt"], w=[scr_name])
                op("dve", lambda e, _b=b_bc: e.tensor_mul(
                    tA["rh"][:], tA["rh"][:], _b), r=["rhI", "abt"], w=["rhI"])
                op("dve", lambda e, _s=scr: e.tensor_add(_s[:], _s[:], tA["rh"][:]),
                   r=[scr_name, "rhI"], w=[scr_name])

            def P1i(i):
                sl = i % 2
                k6 = i % 6
                pb = i % 2
                xt = x_sb[:, i * D:(i + 1) * D]
                shift_tile(i, tA["h2"][sl], "h2A_%d" % sl)
                op("sp", lambda e, _i=i, _s=sl: e.dma_start(
                    tA["rb"][_s][:], dram["rotpk"][:, _i * 256:(_i + 1) * 256]),
                   w=["rotb_%d" % sl], dma=True)
                op("dve", lambda e, _s=sl: e.tensor_mul(
                    tA["abt"][:], tA["rb"][_s][:], qsb_ap),
                   r=["rotb_%d" % sl, "qsb"], w=["abt"])

                def ev_lt(g0, glen, _sl=sl):
                    op("act", lambda e, _g0=g0, _n=glen * 128: e.activation(
                        tA["lt"][_sl][:, _g0 * 128:_g0 * 128 + _n], tr32[:, 0:_n], AF.Identity),
                       r=["tr32"], w=["ltI_%d" % _sl])
                tr_f32([(xt[:, j * 128:j * 128 + 128], "x_sb") for j in range(4)] +
                       [(tA["h2"][sl][:, j * 128:j * 128 + 128], "h2A_%d" % sl)
                        for j in range(4)], ev_lt)
                for n in range(3):
                    ps = pP[n]
                    psn = PN[id(ps)]
                    for j in range(8):
                        op("pe", lambda e, _n=n, _j=j, _p=ps, _s=sl: e.matmul(
                            _p[:, 0:512], tA["lt"][_s][:, _j * 128:_j * 128 + 128],
                            tA["wqkv"][:, _j * 1536 + _n * 512:_j * 1536 + _n * 512 + 512],
                            start=(_j == 0), stop=(_j == 7)),
                           r=["ltI_%d" % sl, "wqkv_bf"], w=[psn])
                op("dve", lambda e: e.tensor_copy(tA["q_scr"][:], pP[0][:]),
                   r=["pP0"], w=["q_scrI"])
                op("act", lambda e: e.activation(tA["k_scr"][:], pP[1][:], AF.Identity),
                   r=["pP1"], w=["k_scrI"])
                op("act", lambda e, _k=k6: e.activation(
                    tA["v5"][:, _k * INNER:(_k + 1) * INNER], pP[2][:], AF.Identity),
                   r=["pP2"], w=["v5"])
                l2rot(tA["q_scr"], "q_scrI", 0, HD, pb)
                l2rot(tA["k_scr"], "k_scrI", 2 * HD, 3 * HD, pb)
                for gi in range(4):
                    op("pe", lambda e, _g=gi: e.transpose(
                        tr32[:, _g * 128:_g * 128 + 128],
                        tA["q_scr"][:, _g * 128:_g * 128 + 128], id32[:]),
                       r=["q_scrI", "id32"], w=["tr32"])
                op("act", lambda e, _s=sl: e.activation(
                    tA["qhT"][_s][:], tr32[:], AF.Identity, scale=c_eight),
                   r=["tr32", "c_eight"], w=["qhTI_%d" % sl])
                for gi in range(4):
                    op("pe", lambda e, _g=gi: e.transpose(
                        tr32[:, _g * 128:_g * 128 + 128],
                        tA["k_scr"][:, _g * 128:_g * 128 + 128], id32[:]),
                       r=["k_scrI", "id32"], w=["tr32"])
                op("act", lambda e, _k=k6: e.activation(
                    tA["khT5"][:, _k * INNER:(_k + 1) * INNER], tr32[:], AF.Identity),
                   r=["tr32"], w=["khT5"])

            def P2i(i):
                sl = i % 2
                xt = x_sb[:, i * D:(i + 1) * D]
                kstart = max(0, i - 4)
                nk = i - kstart + 1
                first_av = [True]

                def emit_ptav(h):
                    hs = h % 2
                    for c in range(nk):
                        op("pe", lambda e, _c=c, _hs=hs: e.transpose(
                            tbf[:, _c * 128:_c * 128 + 128],
                            tA["pb"][_hs][:, _c * 128:_c * 128 + 128], id16[:]),
                           r=["pbI_%d" % hs, "id16"], w=["tbf"])
                    op("dve", lambda e, _n=nk * 128, _hs=hs: e.tensor_copy(
                        tA["pt"][_hs][:, 0:_n], tbf[:, 0:_n]), r=["tbf"], w=["ptI_%d" % hs])
                    for c in range(nk):
                        slot = (kstart + c) % 6
                        st_ = first_av[0]
                        first_av[0] = False
                        sp_ = (h == HEADS_I - 1) and (c == nk - 1)
                        op("pe", lambda e, _c=c, _hs=hs, _st=st_, _sp=sp_, _slot=slot, _h=h:
                           e.matmul(pP[3][:, _h * 64:_h * 64 + 64],
                                    tA["pt"][_hs][:, _c * 128:_c * 128 + 128],
                                    tA["v5"][:, _slot * INNER + _h * 64:_slot * INNER + _h * 64 + 64],
                                    start=_st, stop=_sp),
                           r=["ptI_%d" % hs, "v5"], w=["pP3"])

                pend = None
                def emit_ptav_i(h):
                    hs = h % 2
                    for c in range(nk):
                        op("pe", lambda e, _c=c, _hs=hs: e.transpose(
                            tbf[:, _c * 128:_c * 128 + 128],
                            tA["pb"][_hs][:, _c * 128:_c * 128 + 128], id16[:]),
                           r=["pbI_%d" % hs, "id16"], w=["tbf"])
                    op("act", lambda e, _n=nk * 128, _hs=hs: e.activation(
                        tA["pt"][_hs][:, 0:_n], tbf[:, 0:_n], AF.Identity),
                       r=["tbf"], w=["ptI_%d" % hs])
                    for c in range(nk):
                        slot = (kstart + c) % 6
                        st_ = first_av[0]
                        first_av[0] = False
                        sp_ = (h == HEADS_I - 1) and (c == nk - 1)
                        op("pe", lambda e, _c=c, _hs=hs, _st=st_, _sp=sp_, _slot=slot, _h=h:
                           e.matmul(pP[3][:, _h * 64:_h * 64 + 64],
                                    tA["pt"][_hs][:, _c * 128:_c * 128 + 128],
                                    tA["v5"][:, _slot * INNER + _h * 64:_slot * INNER + _h * 64 + 64],
                                    start=_st, stop=_sp),
                           r=["ptI_%d" % hs, "v5"], w=["pP3"])

                pend = None
                for h in range(HEADS_I):
                    hp = (h % 2) * 64
                    hc = (h // 2) * 128
                    hs = h % 2
                    nP = min(nk, 4)
                    for c in range(nP):
                        slot = (kstart + c) % 6
                        op("pe", lambda e, _o=c * 128, _s=slot, _hp=hp, _hc=hc,
                           _sl=sl, _st=(c == 0): e.matmul(
                            simP[:, _o:_o + 128],
                            tA["qhT"][_sl][_hp:_hp + 64, _hc:_hc + 128],
                            tA["khT5"][_hp:_hp + 64, _s * INNER + _hc:_s * INNER + _hc + 128],
                            start=_st, stop=False),
                           r=["qhTI_%d" % sl, "khT5"], w=["simP"])
                    if nk == 5:
                        op("pe", lambda e: e.matmul(simP[:, 0:128], id16[:], m_upi[:],
                                                    start=False, stop=True),
                           r=["id16", "m_upi"], w=["simP"])
                        slot = i % 6
                        op("pe", lambda e, _s=slot, _hp=hp, _hc=hc, _sl=sl: e.matmul(
                            simX[:, 0:128],
                            tA["qhT"][_sl][_hp:_hp + 64, _hc:_hc + 128],
                            tA["khT5"][_hp:_hp + 64, _s * INNER + _hc:_s * INNER + _hc + 128],
                            start=True, stop=False),
                           r=["qhTI_%d" % sl, "khT5"], w=["simX"])
                        op("pe", lambda e: e.matmul(simX[:, 0:128], id16[:], m_low[:],
                                                    start=False, stop=True),
                           r=["id16", "m_low"], w=["simX"])
                    else:
                        op("pe", lambda e, _o=(nk - 1) * 128: e.matmul(
                            simP[:, _o:_o + 128], id16[:], m_low[:], start=False, stop=True),
                           r=["id16", "m_low"], w=["simP"])
                    if pend is not None:
                        emit_ptav_i(pend)
                    nPc = min(nk, 4) * 128
                    nmv = c_nmax[h % 2]
                    nmn = f"c_nmax{h % 2}"
                    rdn = f"c_rd{h}"
                    rdc = c_rdens[:, h:h + 1]
                    op("dve", lambda e, _n=nPc, _m=nmv: e.tensor_reduce(
                        _m, simP[:, 0:_n], AX.X, ALU.max, negate=True),
                       r=["simP"], w=[nmn])
                    if nk == 5:
                        xv = c_nmaxx[h % 2]
                        xn = f"c_nmaxx{h % 2}"
                        op("dve", lambda e, _x=xv: e.tensor_reduce(
                            _x, simX[:], AX.X, ALU.max, negate=True), r=["simX"], w=[xn])
                        op("dve", lambda e, _m=nmv, _x=xv: e.tensor_tensor(
                            _m, _m, _x, ALU.min), r=[nmn, xn], w=[nmn])
                    op("act", lambda e, _n=nPc, _hs=hs, _m=nmv, _c=rdc: e.activation(
                        tA["pb"][_hs][:, 0:_n], simP[:, 0:_n], AF.Exp, bias=_m,
                        scale=1.0, accum_out=_c),
                       r=["simP", nmn], w=["pbI_%d" % hs, rdn])
                    if nk == 5:
                        dxv = c_denx[h % 2]
                        dxn = f"c_denx{h % 2}"
                        op("act", lambda e, _hs=hs, _m=nmv, _x=dxv: e.activation(
                            tA["pb"][_hs][:, 512:640], simX[:], AF.Exp, bias=_m,
                            scale=1.0, accum_out=_x),
                           r=["simX", nmn], w=["pbI_%d" % hs, dxn])
                        op("dve", lambda e, _c=rdc, _x=dxv: e.tensor_add(_c, _c, _x),
                           r=[rdn, dxn], w=[rdn])
                    op("dve", lambda e, _c=rdc: e.reciprocal(_c, _c), r=[rdn], w=[rdn])
                    pend = h
                emit_ptav_i(pend)
                for h in range(HEADS_I):
                    op("act", lambda e, _h=h, _s=sl: e.activation(
                        tA["ob"][_s][:, _h * 64:_h * 64 + 64], pP[3][:, _h * 64:_h * 64 + 64],
                        AF.Identity, scale=c_rdens[:, _h:_h + 1]),
                       r=["pP3", f"c_rd{h}"], w=["obI_%d" % sl])
                for c in range(4):
                    op("pe", lambda e, _c=c, _s=sl: e.transpose(
                        tbf[:, _c * 128:_c * 128 + 128],
                        tA["ob"][_s][:, _c * 128:_c * 128 + 128], id16[:]),
                       r=["obI_%d" % sl, "id16"], w=["tbf"])
                op("dve", lambda e, _s=sl: e.tensor_copy(tA["ot"][_s][:], tbf[:, 0:512]),
                   r=["tbf"], w=["otI_%d" % sl])
                for n in range(2):
                    ps = pP[n]
                    psn = PN[id(ps)]
                    for j in range(4):
                        op("pe", lambda e, _n=n, _j=j, _p=ps, _s=sl: e.matmul(
                            _p[:, 0:512], tA["ot"][_s][:, _j * 128:_j * 128 + 128],
                            tA["woin"][:, _j * D + _n * 512:_j * D + _n * 512 + 512],
                            start=(_j == 0), stop=(_j == 3)),
                           r=["otI_%d" % sl, "woin_bf"], w=[psn])
                op("dve", lambda e, _x=xt: e.tensor_add(_x[:, 0:512], _x[:, 0:512], pP[0][:]),
                   r=["pP0", "x_sb"], w=["x_sb"])
                op("dve", lambda e, _x=xt: e.tensor_add(
                    _x[:, 512:1024], _x[:, 512:1024], pP[1][:]),
                   r=["pP1", "x_sb"], w=["x_sb"])

            P1i(0)
            for i in range(NT):
                if i + 1 < NT:
                    P1i(i + 1)
                P2i(i)

            em.fence(prev_names, namesB)
            prev_names = namesB
            w1_off = []
            pos = 0
            for m in range(NHT):
                aw = min(128, HID - m * 128)
                w1_off.append((pos, aw))
                w1_off.append((pos + aw, aw))
                pos += 2 * aw

            def LN2part(tc):
                for ii in range(4):
                    i = tc * 4 + ii
                    sl = i % 2
                    xt = x_sb[:, i * D:(i + 1) * D]
                    shift_tile(i, tB["h2"][sl], "h2B_%d" % sl)
                    pb = i % 2
                    rstd, nmr = ln_stats([(xt[:, 0:512], 512, "x_sb"),
                              (tB["h2"][sl][:], 512, "h2B_%d" % sl)], D, pb)
                    op("act", lambda e, _x=xt, _r=rstd, _n=nmr: e.activation(
                        ln_scr[:, 0:512], _x[:, 0:512], AF.Identity, bias=_n, scale=_r),
                       r=["x_sb", f"c_rstd{pb}", f"c_nmr{pb}"], w=["ln_scr"])
                    op("act", lambda e, _s=sl, _r=rstd, _n=nmr: e.activation(
                        ln_scr[:, 512:1024], tB["h2"][_s][:], AF.Identity,
                        bias=_n, scale=_r),
                       r=["h2B_%d" % sl, f"c_rstd{pb}", f"c_nmr{pb}"], w=["ln_scr"])
                    for g0 in (0, 4):
                        for gi in range(4):
                            j = g0 + gi
                            op("pe", lambda e, _j=j, _g=gi: e.transpose(
                                tr32[:, _g * 128:_g * 128 + 128],
                                ln_scr[:, _j * 128:_j * 128 + 128], id32[:]),
                               r=["ln_scr", "id32"], w=["tr32"])
                        for gi in range(4):
                            j = g0 + gi
                            op("act", lambda e, _j=j, _g=gi, _ii=ii, _l=l: e.activation(
                                tB["ln2T"][:, _j * 512 + _ii * 128:_j * 512 + _ii * 128 + 128],
                                tr32[:, _g * 128:_g * 128 + 128], AF.Identity,
                                bias=lnb[_l][:, _j:_j + 1], scale=lng[_l][:, _j:_j + 1]),
                               r=["tr32", f"lng{l}", f"lnb{l}"], w=["ln2T"])
            LN2part(0)
            for tc in range(4):
                lane_w = [128, 128, 128, 128]
                for ot in range(2 * NHT):
                    grp = ot // 4
                    lane = ot % 4
                    wb = grp % 2
                    if grp == (2 * NHT - 1) // 4:
                        lane_w = [w1_off[grp * 4 + k][1] if grp * 4 + k < 2 * NHT else 0
                                  for k in range(4)]
                    lane_off = sum(lane_w[:lane])
                    gw = sum(lane_w)
                    if lane == 0:
                        C0 = w1_off[ot][0]
                        op("pool", lambda e, _l=l, _c=C0, _b=wb, _g=gw: e.dma_start(
                            bass.AP(tB["w1"][_b], 0, [[4096, 128], [512, 8], [1, _g]]),
                            bass.AP(dram[f"w1p{_l}"].tensor, _c,
                                    [[2 * HID, 128], [2 * HID * 128, 8], [1, _g]])),
                           w=["w1bf_%d" % wb], dma=True)
                    col0, ow = w1_off[ot]
                    pa = pP[2 + (ot % 2)]
                    pan = PN[id(pa)]
                    for j in range(8):
                        op("pe", lambda e, _j=j, _p=pa, _b=wb, _w=ow, _lo=lane_off: e.matmul(
                            _p[0:_w, 0:512],
                            tB["w1"][_b][:, _j * 512 + _lo:_j * 512 + _lo + _w],
                            tB["ln2T"][:, _j * 512:_j * 512 + 512],
                            start=(_j == 0), stop=(_j == 7)),
                           r=["w1bf_%d" % wb, "ln2T"], w=[pan])
                    ht = ot // 2
                    if ot % 2 == 0:
                        op("act", lambda e, _p=pa, _w=ow, _h=ht, _l=l: e.activation(
                            tB["ag"][0:_w, 0:512], _p[0:_w, 0:512], AF.Identity,
                            bias=b1a[_l][0:_w, _h:_h + 1]),
                           r=[pan, f"b1a{l}"], w=["ag_a"])
                    else:
                        op("act", lambda e, _p=pa, _w=ow, _h=ht, _l=l: e.activation(
                            tB["ag"][0:_w, 1024:1536], _p[0:_w, 0:512], AF.Gelu,
                            bias=b1g[_l][0:_w, _h:_h + 1]),
                           r=[pan, f"b1g{l}"], w=["ag_g"])
                        op("dve", lambda e, _w=ow, _h=ht: e.tensor_mul(
                            tB["gg"][0:_w, _h * 512:_h * 512 + 512],
                            tB["ag"][0:_w, 0:512], tB["ag"][0:_w, 1024:1536]),
                           r=["ag_a", "ag_g"], w=["geglu"])
                if tc + 1 < 4:
                    LN2part(tc + 1)
                for ii in range(4):
                    i = tc * 4 + ii
                    xt = x_sb[:, i * D:(i + 1) * D]
                    for n in range(2):
                        ps = pP[2 * (ii % 2) + n]
                        psn = PN[id(ps)]
                        for kk in range(NHT):
                            kw = 128 if kk < NHT - 1 else HID_LAST
                            op("pe", lambda e, _n=n, _k=kk, _p=ps, _ii=ii, _kw=kw: e.matmul(
                                _p[:, 0:512],
                                tB["gg"][0:_kw, _k * 512 + _ii * 128:_k * 512 + _ii * 128 + 128],
                                tB["w2"][0:_kw, _k * D + _n * 512:_k * D + _n * 512 + 512],
                                start=(_k == 0), stop=False),
                               r=["geglu", "w2bf"], w=[psn])
                        op("pe", lambda e, _n=n, _p=ps, _l=l: e.matmul(
                            _p[:, 0:512], ones_bf[:],
                            tB["b2hi"][0:1, _n * 512:_n * 512 + 512],
                            start=False, stop=True),
                           r=["ones_bf", "b2hi"], w=[psn])
                    b0 = 2 * (ii % 2)
                    op("dve", lambda e, _x=xt, _b=b0: e.tensor_add(
                        _x[:, 0:512], _x[:, 0:512], pP[_b][:]),
                       r=[PN[id(pP[b0])], "x_sb"], w=["x_sb"])
                    op("dve", lambda e, _x=xt, _b=b0: e.tensor_add(
                        _x[:, 512:1024], _x[:, 512:1024], pP[_b + 1][:]),
                       r=[PN[id(pP[b0 + 1])], "x_sb"], w=["x_sb"])

        if PHASE_LIMIT == "all" and DEBUG_TAP is None:
            em.fence(prev_names, FINAL_T)
            for i in range(NT):
                sl = i % 2
                xt = x_sb[:, i * D:(i + 1) * D]
                op("sp", lambda e, _i=i, _s=sl: e.dma_start(
                    x1b[_s][:], d_x1[:, _i * D:(_i + 1) * D]),
                   r=["x1d"], w=[f"x1b{sl}"], dma=True)
                pb = i % 2
                rstd, nmr = ln_stats([(xt, D, "x_sb")], D, pb)
                op("dve", lambda e, _s=sl, _n=nmr: e.tensor_scalar_add(
                    x1b[_s][:], x1b[_s][:], _n),
                   r=[f"x1b{sl}", f"c_nmr{pb}"], w=[f"x1b{sl}"])
                op("dve", lambda e, _x=xt, _s=sl, _r=rstd: e.scalar_tensor_tensor(
                    _x, _x, _r, x1b[_s][:], ALU.mult, ALU.add),
                   r=[f"x1b{sl}", f"c_rstd{pb}", "x_sb"], w=["x_sb"])
        tok = op("sp", lambda e: e.dma_start(
            d_out.rearrange("(n p) d -> p n d", p=128), x_sb3),
            r=["x_sb"], w=["out_dram"], dma=True)
        em.final_wait("sp", tok)

        with nc.Block() as block:
            em.replay(block)
    return nc


def _prep_host(inputs):
    f = np.float32
    low_i, up_i, up_s = _masks()
    common = {
        "wqT": np.ascontiguousarray(inputs["L0_attn_wq_w"].T),
        "wkT": np.ascontiguousarray(inputs["L0_attn_wk_w"].T),
        "wvT": np.ascontiguousarray(inputs["L0_attn_wv_w"].T),
        "woT": np.ascontiguousarray(inputs["L0_attn_wo_w"].T),
        "brow4": np.concatenate([
            inputs["L0_attn_wq_b"], inputs["L0_attn_wk_b"],
            inputs["L0_attn_wv_b"], inputs["L0_attn_wo_b"]]).reshape(1, 4 * D),
        "qn2": np.tile(inputs["L0_attn_qn_w"], 2).reshape(128, 1),
        "kn2": np.tile(inputs["L0_attn_kn_w"], 2).reshape(128, 1),
        "rotpk": _host_tables(),
        "m_low": low_i, "m_upi": up_i, "m_ups": up_s,
        "id32": np.eye(128, dtype=f),
        "id16": np.eye(128, dtype=ml_dtypes.bfloat16),
    }
    for l in range(2):
        w1 = inputs["L0_ffn_w1"][l]
        b1 = inputs["L0_ffn_b1"][l]
        perm = []
        for m in range(NHT):
            aw = min(128, HID - m * 128)
            perm.extend(range(m * 128, m * 128 + aw))
            perm.extend(range(HID + m * 128, HID + m * 128 + aw))
        perm = np.asarray(perm)
        w1p = np.ascontiguousarray(w1[:, perm])
        b1p = b1[perm]
        b1av = np.zeros((128, NHT), f)
        b1gv = np.zeros((128, NHT), f)
        pos = 0
        for m in range(NHT):
            aw = min(128, HID - m * 128)
            b1av[0:aw, m] = b1p[pos:pos + aw]
            b1gv[0:aw, m] = b1p[pos + aw:pos + 2 * aw]
            pos += 2 * aw
        qs = inputs["L0_ffn_q_scale"][l]
        ks = inputs["L0_ffn_k_scale"][l]
        common.update({
            f"wqkv{l}": np.ascontiguousarray(inputs["L0_ffn_wqkv"][l]),
            f"wo{l}": np.ascontiguousarray(inputs["L0_ffn_wo"][l]),
            f"w1p{l}": w1p,
            f"b1a{l}": b1av, f"b1g{l}": b1gv,
            f"w2_{l}": np.ascontiguousarray(inputs["L0_ffn_w2"][l]),
            f"b2_{l}": inputs["L0_ffn_b2"][l].reshape(1, D),
            f"lng{l}": np.ascontiguousarray(inputs["L0_ffn_ln_g"][l].reshape(8, 128).T),
            f"lnb{l}": np.ascontiguousarray(inputs["L0_ffn_ln_b"][l].reshape(8, 128).T),
            f"qsrows{l}": np.concatenate([
                qs, np.concatenate([qs[32:], qs[:32]]),
                ks, np.concatenate([ks[32:], ks[:32]])]).reshape(1, 4 * HD),
        })
    return common


def kernel(**inputs):
    inputs = {k: np.asarray(v) for k, v in inputs.items()}
    common = _prep_host(inputs)
    nc = build()
    in_maps = []
    for c in range(8):
        m = dict(common)
        m["x"] = np.ascontiguousarray(inputs["L0_attn_x"][c])
        in_maps.append(m)
    res = run_bass_kernel_spmd(nc, in_maps, core_ids=list(range(8)))
    return np.stack([res.results[c]["out"] for c in range(8)], axis=0)


# version 11
# speedup vs baseline: 1.3660x; 1.0418x over previous
import numpy as np
import ml_dtypes
import concourse.bass as bass
import concourse.mybir as mybir
from concourse.bass_utils import run_bass_kernel_spmd

F32 = mybir.dt.float32
BF16 = mybir.dt.bfloat16
AF = mybir.ActivationFunctionType
ALU = mybir.AluOpType
AX = mybir.AxisListType

S, D, NT, HD = 2048, 1024, 16, 64
HEADS_O, HEADS_I, INNER = 16, 8, 512
HID, NHT, HID_LAST = 2730, 22, 42
EPS = 1e-5

PHASE_LIMIT = "all"
NT_LIM = NT
DEBUG_TAP = None

ENGINES = ("pe", "act", "dve", "pool", "sp")
QSEM = {"sp": "q_sp", "pool": "q_pool", "act": "q_act"}


class Buf:
    __slots__ = ("name", "w", "readers")

    def __init__(self, name):
        self.name = name
        self.w = None
        self.readers = []


class Emitter:
    def __init__(self, nc):
        self.nc = nc
        self.prog = {e: [] for e in ENGINES}
        self.sems = {}
        self.counts = {n: 0 for n in ("pe", "act", "dve", "pool", "sp")}
        self.lastwait = {}
        self.bufs = {}

    def make_sems(self, ctx):
        for nm in list(self.counts):
            self.sems[nm] = self.nc.alloc_semaphore("sem_" + nm)

    def dma_sem(self, bufname):
        nm = "d_" + bufname
        if nm not in self.sems:
            self.sems[nm] = self.nc.alloc_semaphore("sem_" + nm)
            self.counts[nm] = 0
        return nm

    def buf(self, name):
        if name not in self.bufs:
            self.bufs[name] = Buf(name)
        return self.bufs[name]

    PSUM_BUFS = frozenset(("pP0", "pP1", "pP2", "pP3", "simP", "simX", "tbf", "tr32"))

    def op(self, eng, fn, r=(), w=(), dma=False):
        reads = [self.buf(n) for n in r if n not in self.PSUM_BUFS]
        writes = [self.buf(n) for n in w] + [self.buf(n) for n in r
                                             if n in self.PSUM_BUFS]
        deps = []
        for b in reads:
            if b.w is not None:
                deps.append(b.w)
        for b in writes:
            if b.w is not None:
                deps.append(b.w)
            deps.extend(b.readers)
        mysem = self.dma_sem(w[0]) if dma else eng
        need = {}
        for (sem, val) in deps:
            if dma and sem == mysem:
                continue
            if sem == eng == "pe":
                continue
            if need.get(sem, 0) < val:
                need[sem] = val
        for sem, val in sorted(need.items()):
            if self.lastwait.get((eng, sem), 0) >= val:
                continue
            self.lastwait[(eng, sem)] = val
            self.prog[eng].append(
                lambda e, _s=sem, _v=val: e.wait_ge(self.sems[_s], _v))
        amt = 16 if dma else 1
        self.counts[mysem] += amt
        val = self.counts[mysem]
        self.prog[eng].append(
            lambda e, _f=fn, _s=mysem, _a=amt: _f(e).then_inc(self.sems[_s], _a))
        tok = (mysem, val)
        for b in reads:
            b.readers.append(tok)
        for b in writes:
            b.w = tok
            b.readers = []
        return tok

    def fence(self, old_names, new_names):
        toks = {}
        for n in old_names:
            b = self.bufs.get(n)
            if b is None:
                continue
            for t in ([b.w] if b.w else []) + b.readers:
                if toks.get(t[0], 0) < t[1]:
                    toks[t[0]] = t[1]
        tl = list(toks.items())
        for n in new_names:
            b = self.buf(n)
            b.readers = b.readers + tl
        for n in old_names:
            if n in self.bufs:
                self.bufs[n].w = None
                self.bufs[n].readers = []

    def final_wait(self, eng, tok):
        self.prog[eng].append(lambda e, _t=tok: e.wait_ge(self.sems[_t[0]], _t[1]))

    def replay(self, block):
        for eng, attr in (("sp", "sync"), ("pe", "tensor"), ("act", "scalar"),
                          ("dve", "vector"), ("pool", "gpsimd")):
            seq = self.prog[eng]

            def make(seq_):
                def run(e):
                    for f in seq_:
                        f(e)
                return run
            getattr(block, attr)(make(seq))


def _host_tables():
    half = np.arange(0, HD, 2, dtype=np.float64)
    inv_freq = 1.0 / 10000.0 ** (half / HD)
    t = np.arange(S, dtype=np.float64)
    freqs = t[:, None] * inv_freq[None, :]
    freqs = np.concatenate([freqs, freqs], axis=1)
    base = (half + 0.4 * HD) / (1.4 * HD)
    power = (t - S // 2) / 256.0
    scale = base[None, :] ** power[:, None]
    scale = np.concatenate([scale, scale], axis=1)
    cq = (np.cos(freqs) * scale).astype(np.float32)
    sq = (np.sin(freqs) * scale).astype(np.float32)
    ck = (np.cos(freqs) / scale).astype(np.float32)
    sk = (np.sin(freqs) / scale).astype(np.float32)
    packed = np.concatenate([cq, sq, ck, sk], axis=1)
    return np.ascontiguousarray(
        packed.reshape(NT, 128, 4 * HD).transpose(1, 0, 2).reshape(128, NT * 4 * HD))


def _masks():
    i = np.arange(128)[:, None]
    j = np.arange(128)[None, :]
    neg = np.float32(-1.0e30)
    bf = ml_dtypes.bfloat16
    low_i = np.where(i >= j, 0.0, neg).astype(np.float32).astype(bf)
    up_i = np.where(j >= i, 0.0, neg).astype(np.float32).astype(bf)
    up_s = np.where(j > i, 0.0, neg).astype(np.float32).astype(bf)
    return low_i, up_i, up_s


def build():
    nc = bass.Bass("TRN2", target_bir_lowering=False)
    dram = {}

    def din(name, shape, dt=F32):
        dram[name] = nc.dram_tensor(name, list(shape), dt, kind="ExternalInput").ap()

    din("x", (S, D))
    for w in ("wqT", "wkT", "wvT", "woT"):
        din(w, (D, D))
    din("brow4", (1, 4 * D))
    din("qn2", (128, 1))
    din("kn2", (128, 1))
    din("rotpk", (128, NT * 4 * HD))
    din("m_low", (128, 128), BF16)
    din("m_upi", (128, 128), BF16)
    din("m_ups", (128, 128), BF16)
    din("id32", (128, 128), F32)
    din("id16", (128, 128), BF16)
    for l in range(2):
        din(f"wqkv{l}", (D, 3 * INNER))
        din(f"wo{l}", (INNER, D))
        din(f"w1p{l}", (D, 2 * HID))
        din(f"b1a{l}", (128, NHT))
        din(f"b1g{l}", (128, NHT))
        din(f"w2_{l}", (HID, D))
        din(f"b2_{l}", (1, D))
        din(f"lng{l}", (128, 8))
        din(f"lnb{l}", (128, 8))
        din(f"qsrows{l}", (1, 4 * HD))
    d_out = nc.dram_tensor("out", [S, D], F32, kind="ExternalOutput").ap()
    d_x1 = nc.dram_tensor("x1_scratch", [128, NT * D], F32, kind="Internal").ap()

    from contextlib import ExitStack
    with ExitStack() as ctx:
        em = Emitter(nc)
        em.make_sems(ctx)

        def sba(name, shape, dt=F32):
            return nc.alloc_sbuf_tensor(name, list(shape), dt)

        x_sb = sba("x_sb", (128, NT * D))
        ln_scr = sba("ln_scr", (128, D))
        sq_scr = sba("sq_scr", (128, D))
        cols = sba("cols", (128, 384))
        m_low = sba("m_low_sb", (128, 128), BF16)
        m_upi = sba("m_upi_sb", (128, 128), BF16)
        m_ups = sba("m_ups_sb", (128, 128), BF16)
        id32 = sba("id32_sb", (128, 128))
        id16 = sba("id16_sb", (128, 128), BF16)
        ones_bf = sba("ones_bf", (1, 128), BF16)
        qn2 = sba("qn2_sb", (128, 1))
        kn2 = sba("kn2_sb", (128, 1))
        lng = [sba(f"lng{l}_sb", (128, 8)) for l in range(2)]
        lnb = [sba(f"lnb{l}_sb", (128, 8)) for l in range(2)]
        b1a = [sba(f"b1a{l}_sb", (128, NHT)) for l in range(2)]
        b1g = [sba(f"b1g{l}_sb", (128, NHT)) for l in range(2)]
        wld = [sba(f"wld{i}", (128, 1024)) for i in range(2)]
        rowb = sba("rowb", (1, 256), BF16)

        ARENA = 122880
        lo, hi = nc.bump_sbuf(ARENA)
        assert hi - lo >= ARENA

        def ar(name, shape, dt, off):
            sz = int(np.prod(shape[1:])) * mybir.dt.size(dt)
            assert off + sz <= ARENA, (name, off, sz)
            return nc.alloc_sbuf_tensor_at(name, list(shape), dt, offset=lo + off)

        wT_bf = {w: ar(w + "_bf", (128, 8 * D), BF16, k * 16384)
                 for k, w in enumerate(("wqT", "wkT", "wvT", "woT"))}
        q_scrO = ar("q_scrO", (128, D), F32, 65536)
        k_scrO = ar("k_scrO", (128, D), F32, 69632)
        ltO = [ar(f"ltO{i}", (128, D), BF16, 73728 + i * 2048) for i in range(2)]
        qhTO = [ar(f"qhTO{i}", (128, D), BF16, 77824 + i * 2048) for i in range(2)]
        khT3 = ar("khT3", (128, 4 * D), BF16, 81920)
        v3 = ar("v3", (128, 4 * D), BF16, 90112)
        pbO = [ar(f"pbO{i}", (128, 512), BF16, 98304 + i * 1024) for i in range(2)]
        ptO = [ar(f"ptO{i}", (128, 512), BF16, 100352 + i * 1024) for i in range(2)]
        obO = [ar(f"obO{i}", (128, D), BF16, 102400 + i * 2048) for i in range(2)]
        otO = [ar(f"otO{i}", (128, D), BF16, 106496 + i * 2048) for i in range(2)]
        brow4_hi = ar("brow4_hi", (1, 4 * D), BF16, 110592)
        OUTER_T = (["wqT_bf", "wkT_bf", "wvT_bf", "woT_bf", "q_scrO", "k_scrO",
                    "khT3", "v3", "brow4_hi"] +
                   [f"{p}{i}" for p in ("ltO", "qhTO", "pbO", "ptO", "obO", "otO")
                    for i in range(2)])

        def innerA():
            t = {}
            t["wqkv"] = ar("wqkv_bf", (128, 8 * 3 * INNER), BF16, 0)
            t["woin"] = ar("woin_bf", (128, 4 * D), BF16, 24576)
            t["q_scr"] = ar("q_scrI", (128, INNER), F32, 32768)
            t["k_scr"] = ar("k_scrI", (128, INNER), F32, 34816)
            t["rh"] = ar("rhI", (128, INNER), F32, 36864)
            t["lt"] = [ar(f"ltI_{i}", (128, D), BF16, 38912 + i * 2048) for i in range(2)]
            t["qhT"] = [ar(f"qhTI_{i}", (128, INNER), BF16, 43008 + i * 1024) for i in range(2)]
            t["khT5"] = ar("khT5", (128, 6 * INNER), BF16, 45056)
            t["v5"] = ar("v5", (128, 6 * INNER), BF16, 51200)
            t["pb"] = [ar(f"pbI_{i}", (128, 640), BF16, 57344 + i * 1280) for i in range(2)]
            t["pt"] = [ar(f"ptI_{i}", (128, 640), BF16, 59904 + i * 1280) for i in range(2)]
            t["ob"] = [ar(f"obI_{i}", (128, INNER), BF16, 62464 + i * 1024) for i in range(2)]
            t["ot"] = [ar(f"otI_{i}", (128, INNER), BF16, 64512 + i * 1024) for i in range(2)]
            t["h2"] = [ar(f"h2A_{i}", (128, INNER), F32, 66560 + i * 2048) for i in range(2)]
            t["rb"] = [ar(f"rotb_{i}", (128, 4 * HD), F32, 70656 + i * 1024) for i in range(2)]
            t["abt"] = ar("abt", (128, 4 * HD), F32, 72704)
            names = (["wqkv_bf", "woin_bf", "q_scrI", "k_scrI",
                      "rhI", "khT5", "v5", "abt",
                      "w2bf", "b2hi"] +
                     [f"{p}_{i}" for p in ("ltI", "qhTI", "pbI", "ptI", "obI", "otI",
                                           "h2A", "rotb") for i in range(2)])
            return t, names

        def innerB():
            t = {}
            t["w2"] = ar("w2bf", (128, NHT * D), BF16, 73728)
            t["gg"] = ar("geglu", (128, NHT * 512), BF16, 0)
            t["ln2T"] = ar("ln2T", (128, 8 * 512), BF16, 22528)
            t["w1"] = [ar(f"w1bf_{i}", (128, 4096), BF16, 30720 + i * 8192) for i in range(2)]
            t["ag"] = ar("agscr", (128, 2048), F32, 47104)
            t["h2"] = [ar(f"h2B_{i}", (128, INNER), F32, 55296 + i * 2048) for i in range(2)]
            t["b2hi"] = ar("b2hi", (1, D), BF16, 118784)
            names = (["geglu", "ln2T", "agscr"] +
                     [] +
                     [f"w1bf_{i}" for i in range(2)] + [f"h2B_{i}" for i in range(2)])
            return t, names

        x1b = [ar(f"x1b{i}", (128, D), F32, i * 4096) for i in range(2)]
        FINAL_T = ["x1b0", "x1b1"]

        pP = [nc.alloc_psum_tensor(f"pP{i}", [128, 512], F32) for i in range(4)]
        simP = nc.alloc_psum_tensor("simP", [128, 512], F32)
        simX = nc.alloc_psum_tensor("simX", [128, 128], F32)
        tbf = nc.alloc_psum_tensor("tbf", [128, 1024], BF16)
        tr32 = nc.alloc_psum_tensor("tr32", [128, 512], F32)
        PN = {id(pP[0]): "pP0", id(pP[1]): "pP1", id(pP[2]): "pP2", id(pP[3]): "pP3",
              id(simP): "simP", id(simX): "simX", id(tbf): "tbf", id(tr32): "tr32"}

        op = em.op

        def stats_bank(p):
            base = 8 * p
            return [cols[:, base + i:base + i + 1] for i in range(8)]
        sums_bank = [cols[:, 16:32], cols[:, 32:48]]
        c_rdens = cols[:, 48:64]
        c_nmax = [cols[:, 64 + i:65 + i] for i in range(2)]
        c_nmaxx = [cols[:, 66 + i:67 + i] for i in range(2)]
        c_denx = [cols[:, 68 + i:69 + i] for i in range(2)]
        qn_eff = cols[:, 70:71]
        rq_bank = [cols[:, 96:112], cols[:, 112:128]]
        c_eight = cols[:, 71:72]
        c_eps = cols[:, 72:73]
        c_s1, c_mu, c_m2, c_var, c_rstd, c_nmr, c_t6, c_t7 = stats_bank(0)

        def rstd_ap(pb):
            return stats_bank(pb)[4]

        x_sb3 = bass.AP(x_sb, 0, [[NT * D, 128], [D, NT], [1, D]])
        op("sp", lambda e: e.dma_start(
            x_sb3, dram["x"].rearrange("(n p) d -> p n d", p=128)),
            w=["x_sb"], dma=True)
        for t, n in ((m_low, "m_low"), (m_upi, "m_upi"), (m_ups, "m_ups"),
                     (id32, "id32"), (id16, "id16"), (qn2, "qn2"), (kn2, "kn2")):
            op("sp", lambda e, _t=t, _n=n: e.dma_start(_t[:], dram[_n]), w=[n], dma=True)
        for l in range(2):
            for n, t in ((f"lng{l}", lng[l]), (f"lnb{l}", lnb[l]),
                         (f"b1a{l}", b1a[l]), (f"b1g{l}", b1g[l])):
                op("sp", lambda e, _t=t, _n=n: e.dma_start(_t[:], dram[_n]), w=[n], dma=True)
        op("dve", lambda e: e.memset(ones_bf[:], 1.0), w=["ones_bf"])
        op("dve", lambda e: e.memset(c_eps, EPS), w=["c_eps"])
        op("dve", lambda e: e.memset(c_eight, 8.0), w=["c_eight"])
        op("dve", lambda e: e.tensor_scalar_mul(qn_eff, qn2[:], 0.125), r=["qn2"], w=["c_qe"])

        def load_cast(dst_bf, dst_name, dram_name, n_rows, n_cols, cast_eng=None):
            nj = n_rows // 128
            rem = n_rows - nj * 128
            W = dst_bf.ap().ap[0][0]
            if nj:
                op("pool", lambda e: e.dma_start(
                    bass.AP(dst_bf, 0, [[W, 128], [n_cols, nj], [1, n_cols]]),
                    dram[dram_name][0:nj * 128, :].rearrange("(j p) o -> p j o", p=128)),
                   w=[dst_name], dma=True)
            if rem:
                op("pool", lambda e: e.dma_start(
                    dst_bf[0:rem, nj * n_cols:(nj + 1) * n_cols],
                    dram[dram_name][nj * 128:n_rows, :]),
                   w=[dst_name], dma=True)
        load_cast.k = 0

        def ln_stats(srcs, total_d, pb=0):
            c_s1, c_mu, c_m2, c_var, c_rstd, c_nmr, c_t6, c_t7 = stats_bank(pb)
            sx = str(pb)
            for idx, (apv, wdt, rn) in enumerate(srcs):
                if idx == 0:
                    op("dve", lambda e, _a=apv, _c=c_s1: e.tensor_reduce(_c, _a, AX.X, ALU.add),
                       r=[rn], w=["c_s1" + sx])
                    op("act", lambda e, _a=apv, _w=wdt, _c=c_t6: e.activation(
                        sq_scr[:, 0:_w], _a, AF.Square, accum_out=_c),
                       r=[rn], w=["sq_scr", "c_t6" + sx])
                else:
                    op("dve", lambda e, _a=apv, _c=c_t7: e.tensor_reduce(_c, _a, AX.X, ALU.add),
                       r=[rn], w=["c_t7" + sx])
                    op("dve", lambda e, _c1=c_s1, _c2=c_t7: e.tensor_add(_c1, _c1, _c2),
                       r=["c_t7" + sx, "c_s1" + sx], w=["c_s1" + sx])
                    op("act", lambda e, _a=apv, _w=wdt, _c=c_t7: e.activation(
                        sq_scr[:, 0:_w], _a, AF.Square, accum_out=_c),
                       r=[rn], w=["sq_scr", "c_t7" + sx])
                    op("dve", lambda e, _c1=c_t6, _c2=c_t7: e.tensor_add(_c1, _c1, _c2),
                       r=["c_t7" + sx, "c_t6" + sx], w=["c_t6" + sx])
            inv = 1.0 / total_d
            op("dve", lambda e, _a=c_mu, _b=c_s1: e.tensor_scalar_mul(_a, _b, inv),
               r=["c_s1" + sx], w=["c_mu" + sx])
            op("dve", lambda e, _a=c_m2, _b=c_mu: e.tensor_mul(_a, _b, _b),
               r=["c_mu" + sx], w=["c_m2" + sx])
            op("dve", lambda e, _a=c_var, _b=c_t6: e.tensor_scalar_mul(_a, _b, inv),
               r=["c_t6" + sx], w=["c_var" + sx])
            op("dve", lambda e, _a=c_var, _b=c_m2: e.tensor_sub(_a, _a, _b),
               r=["c_var" + sx, "c_m2" + sx], w=["c_var" + sx])
            op("act", lambda e, _a=c_var: e.activation(_a, _a, AF.Sqrt, bias=c_eps, scale=1.0),
               r=["c_var" + sx, "c_eps"], w=["c_var" + sx])
            op("dve", lambda e, _a=c_rstd, _b=c_var: e.reciprocal(_a, _b),
               r=["c_var" + sx], w=["c_rstd" + sx])
            op("dve", lambda e, _a=c_nmr, _b=c_mu, _c=c_rstd: e.scalar_tensor_tensor(
                _a, _b, -1.0, _c, ALU.mult, ALU.mult),
               r=["c_mu" + sx, "c_rstd" + sx], w=["c_nmr" + sx])
            return c_rstd, c_nmr

        def tr_f32(chunks, evacs):
            for g0 in range(0, len(chunks), 4):
                grp = chunks[g0:g0 + 4]
                for gi, (src, rn) in enumerate(grp):
                    op("pe", lambda e, _s=src, _g=gi: e.transpose(
                        tr32[:, _g * 128:_g * 128 + 128], _s, id32[:]),
                       r=[rn, "id32"], w=["tr32"])
                evacs(g0, len(grp))

        for k, w in enumerate(("wqT", "wkT", "wvT", "woT")):
            load_cast(wT_bf[w], w + "_bf", w, D, D)
        op("pool", lambda e: e.dma_start(brow4_hi[0:1, :], dram["brow4"]),
           w=["brow4_hi"], dma=True)
        def proj_1024(lt_t, lt_name, w_bf, w_name, psA, psB, bias_idx, brow=None,
                      brow_name="ones_bf"):
            for n in range(2):
                ps = (psA, psB)[n]
                psn = PN[id(ps)]
                for j in range(8):
                    op("pe", lambda e, _n=n, _j=j, _p=ps, _lt=lt_t: e.matmul(
                        _p[:, 0:512], _lt[:, _j * 128:_j * 128 + 128],
                        w_bf[:, _j * D + _n * 512:_j * D + _n * 512 + 512],
                        start=(_j == 0), stop=False),
                       r=[lt_name, w_name], w=[psn])
                br = ones_bf[:] if brow is None else brow
                op("pe", lambda e, _n=n, _p=ps, _b=bias_idx, _r=br: e.matmul(
                    _p[:, 0:512], _r,
                    brow4_hi[0:1, _b * D + _n * 512:_b * D + _n * 512 + 512],
                    start=False, stop=True),
                   r=[brow_name, "brow4_hi"], w=[psn])

        def rmsnorm16(scr, scr_name, pb, rq=False, pre_min=False):
            if rq:
                su = rq_bank[pb]
                sn = f"c_rq{pb}"
            else:
                su = sums_bank[pb]
                sn = f"c_sums{pb}"
            if not pre_min:
                op("dve", lambda e, _s=scr: e.tensor_scalar_min(_s[:], _s[:], 1.0e4),
                   r=[scr_name], w=[scr_name])
            op("dve", lambda e, _s=scr: e.tensor_scalar_max(_s[:], _s[:], -1.0e4),
               r=[scr_name], w=[scr_name])
            op("act", lambda e, _s=scr: e.activation(sq_scr[:], _s[:], AF.Square),
               r=[scr_name], w=["sq_scr"])
            red3 = bass.AP(sq_scr, 0, [[D, 128], [64, 16], [1, 64]])
            op("dve", lambda e, _r=red3, _u=su: e.tensor_reduce(_u, _r, AX.X, ALU.add),
               r=["sq_scr"], w=[sn])
            op("dve", lambda e, _u=su: e.tensor_scalar(
                _u, _u, 1.0 / 64, 1.0e-5, ALU.mult, ALU.max),
               r=[sn], w=[sn])
            op("act", lambda e, _u=su: e.activation(_u, _u, AF.Sqrt, bias=c_eps, scale=1.0),
               r=[sn, "c_eps"], w=[sn])
            op("dve", lambda e, _u=su: e.reciprocal(_u, _u), r=[sn], w=[sn])
            if not rq:
                for h in range(16):
                    op("dve", lambda e, _s=scr, _h=h, _u=su: e.tensor_scalar_mul(
                        _s[:, _h * 64:_h * 64 + 64], _s[:, _h * 64:_h * 64 + 64],
                        _u[:, _h:_h + 1]), r=[scr_name, sn], w=[scr_name])

        def P1o(i):
            sl = i % 2
            k4 = i % 4
            pb = i % 2
            xt = x_sb[:, i * D:(i + 1) * D]
            c_s1, c_mu, c_m2, c_var, c_rstd, c_nmr, c_t6, c_t7 = stats_bank(pb)
            sx = str(pb)
            op("dve", lambda e, _a=xt, _c=c_s1: e.tensor_reduce(_c, _a, AX.X, ALU.add),
               r=["x_sb"], w=["c_s1" + sx])
            op("dve", lambda e, _a=c_nmr, _b=c_s1: e.tensor_scalar_mul(_a, _b, -1.0 / D),
               r=["c_s1" + sx], w=["c_nmr" + sx])
            op("act", lambda e, _x=xt, _n=c_nmr: e.activation(
                ln_scr[:], _x, AF.Identity, bias=_n, scale=1.0),
               r=["x_sb", "c_nmr" + sx], w=["ln_scr"])
            op("act", lambda e, _a=xt, _c=c_t6: e.activation(
                sq_scr[:], _a, AF.Square, accum_out=_c),
               r=["x_sb"], w=["sq_scr", "c_t6" + sx])
            op("dve", lambda e, _a=c_mu, _b=c_nmr: e.tensor_mul(_a, _b, _b),
               r=["c_nmr" + sx], w=["c_mu" + sx])
            op("dve", lambda e, _a=c_var, _b=c_t6: e.tensor_scalar_mul(_a, _b, 1.0 / D),
               r=["c_t6" + sx], w=["c_var" + sx])
            op("dve", lambda e, _a=c_var, _b=c_mu: e.tensor_sub(_a, _a, _b),
               r=["c_var" + sx, "c_mu" + sx], w=["c_var" + sx])
            op("act", lambda e, _a=c_var: e.activation(_a, _a, AF.Sqrt, bias=c_eps, scale=1.0),
               r=["c_var" + sx, "c_eps"], w=["c_var" + sx])
            op("pe", lambda e, _a=c_var: e.transpose(simX[0:1, 0:128], _a, id32[:]),
               r=["c_var" + sx, "id32"], w=["simX"])
            op("act", lambda e, _p=pb: e.activation(
                rowb[0:1, _p * 128:_p * 128 + 128], simX[0:1, 0:128], AF.Identity),
               r=["simX"], w=[f"rowb{pb}"])
            op("dve", lambda e, _a=c_rstd, _b=c_var: e.reciprocal(_a, _b),
               r=["c_var" + sx], w=["c_rstd" + sx])

            def ev_lt(g0, glen, _sl=sl):
                op("act", lambda e, _g0=g0, _n=glen * 128: e.activation(
                    ltO[_sl][:, _g0 * 128:_g0 * 128 + _n], tr32[:, 0:_n], AF.Identity),
                   r=["tr32"], w=[f"ltO{_sl}"])
            tr_f32([(ln_scr[:, j * 128:j * 128 + 128], "ln_scr") for j in range(8)], ev_lt)
            stdrow = rowb[0:1, pb * 128:pb * 128 + 128]
            rbn = f"rowb{pb}"
            proj_1024(ltO[sl], f"ltO{sl}", wT_bf["wqT"], "wqT_bf", pP[0], pP[1], 0,
                      brow=stdrow, brow_name=rbn)
            proj_1024(ltO[sl], f"ltO{sl}", wT_bf["wkT"], "wkT_bf", pP[2], pP[3], 1,
                      brow=stdrow, brow_name=rbn)
            op("dve", lambda e, _r=rstd_ap(pb): e.tensor_scalar_mul(
                q_scrO[:, 0:512], pP[0][:], _r), r=["pP0", "c_rstd" + str(pb)], w=["q_scrO"])
            op("dve", lambda e, _r=rstd_ap(pb): e.tensor_scalar_mul(
                q_scrO[:, 512:1024], pP[1][:], _r), r=["pP1", "c_rstd" + str(pb)], w=["q_scrO"])
            op("act", lambda e, _r=rstd_ap(pb): e.activation(
                k_scrO[:, 0:512], pP[2][:], AF.Identity, scale=_r),
               r=["pP2", "c_rstd" + str(pb)], w=["k_scrO"])
            op("act", lambda e, _r=rstd_ap(pb): e.activation(
                k_scrO[:, 512:1024], pP[3][:], AF.Identity, scale=_r),
               r=["pP3", "c_rstd" + str(pb)], w=["k_scrO"])
            proj_1024(ltO[sl], f"ltO{sl}", wT_bf["wvT"], "wvT_bf", pP[0], pP[1], 2,
                      brow=stdrow, brow_name=rbn)
            op("act", lambda e, _k=k4, _r=rstd_ap(pb): e.activation(
                v3[:, _k * D:_k * D + 512], pP[0][:], AF.Identity, scale=_r),
               r=["pP0", "c_rstd" + str(pb)], w=["v3"])
            op("act", lambda e, _k=k4, _r=rstd_ap(pb): e.activation(
                v3[:, _k * D + 512:_k * D + 1024], pP[1][:], AF.Identity, scale=_r),
               r=["pP1", "c_rstd" + str(pb)], w=["v3"])
            rmsnorm16(q_scrO, "q_scrO", pb, rq=True, pre_min=True)

            def ev_q(g0, glen, _sl=sl):
                op("act", lambda e, _g0=g0, _n=glen * 128: e.activation(
                    qhTO[_sl][:, _g0 * 128:_g0 * 128 + _n], tr32[:, 0:_n],
                    AF.Identity, scale=qn_eff),
                   r=["tr32", "c_qe"], w=[f"qhTO{_sl}"])
            tr_f32([(q_scrO[:, j * 128:j * 128 + 128], "q_scrO") for j in range(8)], ev_q)
            rmsnorm16(k_scrO, "k_scrO", pb)

            def ev_k(g0, glen, _k4=k4):
                op("act", lambda e, _g0=g0, _n=glen * 128: e.activation(
                    khT3[:, _k4 * D + _g0 * 128:_k4 * D + _g0 * 128 + _n], tr32[:, 0:_n],
                    AF.Identity, scale=kn2[:]),
                   r=["tr32", "kn2"], w=["khT3"])
            tr_f32([(k_scrO[:, j * 128:j * 128 + 128], "k_scrO") for j in range(8)], ev_k)

        def P2o(i):
            sl = i % 2
            pb = i % 2
            xt = x_sb[:, i * D:(i + 1) * D]
            kstart = max(0, i - 2)
            nk = i - kstart + 1
            ncols = nk * 128

            def emit_ptav(h):
                hs = h % 2
                sb = simP if hs == 0 else pP[1]
                sbn = "simP" if hs == 0 else "pP1"
                for c in range(nk):
                    op("pe", lambda e, _c=c, _hs=hs: e.transpose(
                        tbf[:, _c * 128:_c * 128 + 128],
                        pbO[_hs][:, _c * 128:_c * 128 + 128], id16[:]),
                       r=[f"pbO{hs}", "id16"], w=["tbf"])
                op("act", lambda e, _n=ncols, _hs=hs: e.activation(
                    ptO[_hs][:, 0:_n], tbf[:, 0:_n], AF.Identity), r=["tbf"], w=[f"ptO{hs}"])
                for c in range(nk):
                    slot = (kstart + c) % 4
                    st_ = (h % 8 == 0) and (c == 0)
                    sp_ = (h % 8 == 7) and (c == nk - 1)
                    op("pe", lambda e, _c=c, _hs=hs, _co=(h % 8) * 64,
                       _st=st_, _sp=sp_, _slot=slot, _h=h: e.matmul(
                        pP[0][:, _co:_co + 64], ptO[_hs][:, _c * 128:_c * 128 + 128],
                        v3[:, _slot * D + _h * 64:_slot * D + _h * 64 + 64],
                        start=_st, stop=_sp),
                       r=[f"ptO{hs}", "v3"], w=["pP0"])
                if h % 8 == 7:
                    for hh in range(h - 7, h + 1):
                        op("act", lambda e, _co=(hh % 8) * 64, _h=hh, _s=sl: e.activation(
                            obO[_s][:, _h * 64:_h * 64 + 64], pP[0][:, _co:_co + 64],
                            AF.Identity, scale=c_rdens[:, _h:_h + 1]),
                           r=["pP0", f"c_rd{hh}"], w=[f"obO{sl}"])

            pend = None
            for h in range(HEADS_O):
                hp = (h % 2) * 64
                hc = (h // 2) * 128
                hs = h % 2
                sb = simP if hs == 0 else pP[1]
                sbn = "simP" if hs == 0 else "pP1"
                for c in range(nk):
                    slot = (kstart + c) % 4
                    op("pe", lambda e, _c=c, _s=slot, _hp=hp, _hc=hc, _sl=sl,
                       _st=(c == 0), _b=sb: e.matmul(
                        _b[:, _c * 128:_c * 128 + 128],
                        qhTO[_sl][_hp:_hp + 64, _hc:_hc + 128],
                        khT3[_hp:_hp + 64, _s * D + _hc:_s * D + _hc + 128],
                        start=_st, stop=False),
                       r=[f"qhTO{sl}", "khT3"], w=[sbn])
                if nk == 3:
                    op("pe", lambda e, _b=sb: e.matmul(_b[:, 0:128], id16[:], m_ups[:],
                                                       start=False, stop=False),
                       r=["id16", "m_ups"], w=[sbn])
                op("pe", lambda e, _o=(nk - 1) * 128, _b=sb: e.matmul(
                    _b[:, _o:_o + 128], id16[:], m_low[:], start=False, stop=True),
                   r=["id16", "m_low"], w=[sbn])
                if pend is not None:
                    emit_ptav(pend)
                nmv = c_nmax[h % 2]
                nmn = f"c_nmax{h % 2}"
                rdn = f"c_rd{h}"
                rqc = rq_bank[pb][:, h:h + 1]
                rqn = f"c_rq{pb}"
                op("dve", lambda e, _n=ncols, _m=nmv, _b=sb: e.tensor_reduce(
                    _m, _b[:, 0:_n], AX.X, ALU.max, negate=True),
                   r=[sbn], w=[nmn])
                op("act", lambda e, _m=nmv, _q=rqc: e.activation(
                    _m, _m, AF.Identity, scale=_q), r=[nmn, rqn], w=[nmn])
                op("act", lambda e, _n=ncols, _hs=hs, _m=nmv, _h=h, _b=sb, _q=rqc:
                   e.activation(
                    pbO[_hs][:, 0:_n], _b[:, 0:_n], AF.Exp, bias=_m, scale=_q,
                    accum_out=c_rdens[:, _h:_h + 1]),
                   r=[sbn, nmn, rqn], w=[f"pbO{hs}", rdn])
                op("dve", lambda e, _h=h: e.reciprocal(
                    c_rdens[:, _h:_h + 1], c_rdens[:, _h:_h + 1]), r=[rdn], w=[rdn])
                pend = h
            emit_ptav(pend)
            for j in range(8):
                op("pe", lambda e, _j=j, _s=sl: e.transpose(
                    tbf[:, _j * 128:_j * 128 + 128],
                    obO[_s][:, _j * 128:_j * 128 + 128], id16[:]),
                   r=[f"obO{sl}", "id16"], w=["tbf"])
            op("dve", lambda e, _s=sl: e.tensor_copy(otO[_s][:], tbf[:]),
               r=["tbf"], w=[f"otO{sl}"])
            proj_1024(otO[sl], f"otO{sl}", wT_bf["woT"], "woT_bf", pP[2], pP[3], 3)
            rstd, nmr = ln_stats([(pP[2][:], 512, "pP2"), (pP[3][:], 512, "pP3")], D, pb)
            op("dve", lambda e, _x=xt, _n=nmr: e.tensor_scalar_add(_x, _x, _n),
               r=["x_sb", f"c_nmr{pb}"], w=["x_sb"])
            op("dve", lambda e, _x=xt, _r=rstd: e.scalar_tensor_tensor(
                _x[:, 0:512], pP[2][:], _r, _x[:, 0:512], ALU.mult, ALU.add),
               r=["pP2", f"c_rstd{pb}", "x_sb"], w=["x_sb"])
            op("dve", lambda e, _x=xt, _r=rstd: e.scalar_tensor_tensor(
                _x[:, 512:1024], pP[3][:], _r, _x[:, 512:1024], ALU.mult, ALU.add),
               r=["pP3", f"c_rstd{pb}", "x_sb"], w=["x_sb"])

        NTL = min(NT, NT_LIM)
        if NTL:
            P1o(0)
            for i in range(NTL):
                if i + 1 < NTL:
                    P1o(i + 1)
                P2o(i)

        op("pool", lambda e: e.dma_start(d_x1, x_sb[:]), r=["x_sb"], w=["x1d"], dma=True)
        if not (PHASE_LIMIT == "outer" and DEBUG_TAP):
            for i in range(NT):
                xt = x_sb[:, i * D:(i + 1) * D]
                pb = i % 2
                rstd, nmr = ln_stats([(xt, D, "x_sb")], D, pb)
                op("act", lambda e, _x=xt, _r=rstd, _n=nmr: e.activation(
                    _x, _x, AF.Identity, bias=_n, scale=_r),
                   r=["x_sb", f"c_rstd{pb}", f"c_nmr{pb}", "x1d"], w=["x_sb"])

        n_inner = {"outer": 0, "layer0": 1, "all": 2}[PHASE_LIMIT]
        prev_names = OUTER_T
        tA, namesA = (innerA() if n_inner else (None, None))
        tB, namesB = (innerB() if n_inner else (None, None))
        for l in range(n_inner):
            em.fence(prev_names, namesA)
            prev_names = namesA
            load_cast(tA["wqkv"], "wqkv_bf", f"wqkv{l}", D, 3 * INNER)
            load_cast(tA["woin"], "woin_bf", f"wo{l}", INNER, D)
            qsb_ap = cols[:, 128:384]
            slot = load_cast.k % 2
            load_cast.k += 1
            op("sp", lambda e, _l=l, _s=slot: e.dma_start(
                wld[_s][0:1, 0:256], dram[f"qsrows{_l}"]), w=["wld%d" % slot], dma=True)
            op("dve", lambda e: e.memset(sq_scr[0:1, 0:128], 1.0), w=["sq_scr"])
            op("pe", lambda e, _s=slot: e.matmul(
                tr32[:, 0:256], sq_scr[0:1, 0:128], wld[_s][0:1, 0:256],
                start=True, stop=True),
               r=["sq_scr", "wld%d" % slot], w=["tr32"])
            op("dve", lambda e: e.tensor_copy(qsb_ap, tr32[:, 0:256]),
               r=["tr32"], w=["qsb"])

            def shift_tile(i, dst, dst_name):
                op("pool", lambda e, _i=i, _d=dst: e.dma_start(
                    _d[1:128, :], x_sb[0:127, _i * D + 512:_i * D + 1024]),
                   r=["x_sb"], w=[dst_name], dma=True)
                if i == 0:
                    op("dve", lambda e, _d=dst: e.memset(_d[0:1, :], 0.0), w=[dst_name])
                else:
                    op("pool", lambda e, _i=i, _d=dst: e.dma_start(
                        _d[0:1, :], x_sb[127:128, (_i - 1) * D + 512:(_i - 1) * D + 1024]),
                       r=["x_sb"], w=[dst_name], dma=True)

            load_cast(tB["w2"], "w2bf", f"w2_{l}", HID, D)
            op("pool", lambda e, _l=l: e.dma_start(tB["b2hi"][0:1, :], dram[f"b2_{_l}"]),
               w=["b2hi"], dma=True)
            def l2rot(scr, scr_name, a_off, b_off, pb, rq=False):
                if rq:
                    sums8 = rq_bank[pb][:, 0:8]
                    sn = f"c_rq{pb}"
                else:
                    sums8 = sums_bank[pb][:, 0:8]
                    sn = f"c_sums{pb}"
                op("act", lambda e, _s=scr: e.activation(
                    sq_scr[:, 0:512], _s[:], AF.Square), r=[scr_name], w=["sq_scr"])
                red3 = bass.AP(sq_scr, 0, [[D, 128], [64, 8], [1, 64]])
                op("dve", lambda e, _r=red3, _u=sums8: e.tensor_reduce(_u, _r, AX.X, ALU.add),
                   r=["sq_scr"], w=[sn])
                op("act", lambda e, _u=sums8: e.activation(_u, _u, AF.Sqrt),
                   r=[sn], w=[sn])
                op("dve", lambda e, _u=sums8: e.tensor_scalar_max(_u, _u, 1.0e-12),
                   r=[sn], w=[sn])
                op("dve", lambda e, _u=sums8: e.reciprocal(_u, _u), r=[sn], w=[sn])
                if not rq:
                    for h in range(8):
                        op("dve", lambda e, _s=scr, _h=h, _u=sums8: e.tensor_scalar_mul(
                            _s[:, _h * 64:_h * 64 + 64], _s[:, _h * 64:_h * 64 + 64],
                            _u[:, _h:_h + 1]), r=[scr_name, sn], w=[scr_name])
                vh = lambda t, off: bass.AP(t, off, [[t.ap().ap[0][0], 128], [64, 8], [1, 32]])
                a_bc = bass.AP(tA["abt"], a_off, [[4 * HD, 128], [0, 8], [1, 64]])
                b_bc = bass.AP(tA["abt"], b_off, [[4 * HD, 128], [0, 8], [1, 64]])
                op("dve", lambda e, _s=scr: e.tensor_scalar_mul(
                    vh(tA["rh"], 0), vh(_s, 32), -1.0), r=[scr_name], w=["rhI"])
                op("dve", lambda e, _s=scr: e.tensor_copy(vh(tA["rh"], 32), vh(_s, 0)),
                   r=[scr_name], w=["rhI"])
                op("dve", lambda e, _s=scr, _a=a_bc: e.tensor_mul(_s[:], _s[:], _a),
                   r=[scr_name, "abt"], w=[scr_name])
                op("dve", lambda e, _b=b_bc: e.tensor_mul(
                    tA["rh"][:], tA["rh"][:], _b), r=["rhI", "abt"], w=["rhI"])
                op("dve", lambda e, _s=scr: e.tensor_add(_s[:], _s[:], tA["rh"][:]),
                   r=[scr_name, "rhI"], w=[scr_name])

            def P1i(i):
                sl = i % 2
                k6 = i % 6
                pb = i % 2
                xt = x_sb[:, i * D:(i + 1) * D]
                shift_tile(i, tA["h2"][sl], "h2A_%d" % sl)
                op("sp", lambda e, _i=i, _s=sl: e.dma_start(
                    tA["rb"][_s][:], dram["rotpk"][:, _i * 256:(_i + 1) * 256]),
                   w=["rotb_%d" % sl], dma=True)
                op("dve", lambda e, _s=sl: e.tensor_mul(
                    tA["abt"][:], tA["rb"][_s][:], qsb_ap),
                   r=["rotb_%d" % sl, "qsb"], w=["abt"])

                def ev_lt(g0, glen, _sl=sl):
                    op("act", lambda e, _g0=g0, _n=glen * 128: e.activation(
                        tA["lt"][_sl][:, _g0 * 128:_g0 * 128 + _n], tr32[:, 0:_n], AF.Identity),
                       r=["tr32"], w=["ltI_%d" % _sl])
                tr_f32([(xt[:, j * 128:j * 128 + 128], "x_sb") for j in range(4)] +
                       [(tA["h2"][sl][:, j * 128:j * 128 + 128], "h2A_%d" % sl)
                        for j in range(4)], ev_lt)
                for n in range(3):
                    ps = pP[n]
                    psn = PN[id(ps)]
                    for j in range(8):
                        op("pe", lambda e, _n=n, _j=j, _p=ps, _s=sl: e.matmul(
                            _p[:, 0:512], tA["lt"][_s][:, _j * 128:_j * 128 + 128],
                            tA["wqkv"][:, _j * 1536 + _n * 512:_j * 1536 + _n * 512 + 512],
                            start=(_j == 0), stop=(_j == 7)),
                           r=["ltI_%d" % sl, "wqkv_bf"], w=[psn])
                op("dve", lambda e: e.tensor_copy(tA["q_scr"][:], pP[0][:]),
                   r=["pP0"], w=["q_scrI"])
                op("act", lambda e: e.activation(tA["k_scr"][:], pP[1][:], AF.Identity),
                   r=["pP1"], w=["k_scrI"])
                op("act", lambda e, _k=k6: e.activation(
                    tA["v5"][:, _k * INNER:(_k + 1) * INNER], pP[2][:], AF.Identity),
                   r=["pP2"], w=["v5"])
                l2rot(tA["q_scr"], "q_scrI", 0, HD, pb, rq=True)
                l2rot(tA["k_scr"], "k_scrI", 2 * HD, 3 * HD, pb)
                for gi in range(4):
                    op("pe", lambda e, _g=gi: e.transpose(
                        tr32[:, _g * 128:_g * 128 + 128],
                        tA["q_scr"][:, _g * 128:_g * 128 + 128], id32[:]),
                       r=["q_scrI", "id32"], w=["tr32"])
                op("act", lambda e, _s=sl: e.activation(
                    tA["qhT"][_s][:], tr32[:], AF.Identity, scale=c_eight),
                   r=["tr32", "c_eight"], w=["qhTI_%d" % sl])
                for gi in range(4):
                    op("pe", lambda e, _g=gi: e.transpose(
                        tr32[:, _g * 128:_g * 128 + 128],
                        tA["k_scr"][:, _g * 128:_g * 128 + 128], id32[:]),
                       r=["k_scrI", "id32"], w=["tr32"])
                op("act", lambda e, _k=k6: e.activation(
                    tA["khT5"][:, _k * INNER:(_k + 1) * INNER], tr32[:], AF.Identity),
                   r=["tr32"], w=["khT5"])

            def P2i(i):
                sl = i % 2
                xt = x_sb[:, i * D:(i + 1) * D]
                kstart = max(0, i - 4)
                nk = i - kstart + 1
                first_av = [True]

                def emit_ptav(h):
                    hs = h % 2
                    for c in range(nk):
                        op("pe", lambda e, _c=c, _hs=hs: e.transpose(
                            tbf[:, _c * 128:_c * 128 + 128],
                            tA["pb"][_hs][:, _c * 128:_c * 128 + 128], id16[:]),
                           r=["pbI_%d" % hs, "id16"], w=["tbf"])
                    op("act", lambda e, _n=nk * 128, _hs=hs: e.activation(
                        tA["pt"][_hs][:, 0:_n], tbf[:, 0:_n], AF.Identity),
                       r=["tbf"], w=["ptI_%d" % hs])
                    for c in range(nk):
                        slot = (kstart + c) % 6
                        st_ = first_av[0]
                        first_av[0] = False
                        sp_ = (h == HEADS_I - 1) and (c == nk - 1)
                        op("pe", lambda e, _c=c, _hs=hs, _st=st_, _sp=sp_, _slot=slot, _h=h:
                           e.matmul(pP[3][:, _h * 64:_h * 64 + 64],
                                    tA["pt"][_hs][:, _c * 128:_c * 128 + 128],
                                    tA["v5"][:, _slot * INNER + _h * 64:_slot * INNER + _h * 64 + 64],
                                    start=_st, stop=_sp),
                           r=["ptI_%d" % hs, "v5"], w=["pP3"])

                pend = None
                def emit_ptav_i(h):
                    hs = h % 2
                    for c in range(nk):
                        op("pe", lambda e, _c=c, _hs=hs: e.transpose(
                            tbf[:, _c * 128:_c * 128 + 128],
                            tA["pb"][_hs][:, _c * 128:_c * 128 + 128], id16[:]),
                           r=["pbI_%d" % hs, "id16"], w=["tbf"])
                    op("act", lambda e, _n=nk * 128, _hs=hs: e.activation(
                        tA["pt"][_hs][:, 0:_n], tbf[:, 0:_n], AF.Identity),
                       r=["tbf"], w=["ptI_%d" % hs])
                    for c in range(nk):
                        slot = (kstart + c) % 6
                        st_ = first_av[0]
                        first_av[0] = False
                        sp_ = (h == HEADS_I - 1) and (c == nk - 1)
                        op("pe", lambda e, _c=c, _hs=hs, _st=st_, _sp=sp_, _slot=slot, _h=h:
                           e.matmul(pP[3][:, _h * 64:_h * 64 + 64],
                                    tA["pt"][_hs][:, _c * 128:_c * 128 + 128],
                                    tA["v5"][:, _slot * INNER + _h * 64:_slot * INNER + _h * 64 + 64],
                                    start=_st, stop=_sp),
                           r=["ptI_%d" % hs, "v5"], w=["pP3"])

                pend = None
                for h in range(HEADS_I):
                    hp = (h % 2) * 64
                    hc = (h // 2) * 128
                    hs = h % 2
                    sb = simP if hs == 0 else pP[1]
                    sbn = "simP" if hs == 0 else "pP1"
                    xb = simX if hs == 0 else pP[2]
                    xbn = "simX" if hs == 0 else "pP2"
                    nP = min(nk, 4)
                    for c in range(nP):
                        slot = (kstart + c) % 6
                        op("pe", lambda e, _o=c * 128, _s=slot, _hp=hp, _hc=hc,
                           _sl=sl, _st=(c == 0), _b=sb: e.matmul(
                            _b[:, _o:_o + 128],
                            tA["qhT"][_sl][_hp:_hp + 64, _hc:_hc + 128],
                            tA["khT5"][_hp:_hp + 64, _s * INNER + _hc:_s * INNER + _hc + 128],
                            start=_st, stop=False),
                           r=["qhTI_%d" % sl, "khT5"], w=[sbn])
                    if nk == 5:
                        op("pe", lambda e, _b=sb: e.matmul(_b[:, 0:128], id16[:], m_upi[:],
                                                           start=False, stop=True),
                           r=["id16", "m_upi"], w=[sbn])
                        slot = i % 6
                        op("pe", lambda e, _s=slot, _hp=hp, _hc=hc, _sl=sl, _x=xb: e.matmul(
                            _x[:, 0:128],
                            tA["qhT"][_sl][_hp:_hp + 64, _hc:_hc + 128],
                            tA["khT5"][_hp:_hp + 64, _s * INNER + _hc:_s * INNER + _hc + 128],
                            start=True, stop=False),
                           r=["qhTI_%d" % sl, "khT5"], w=[xbn])
                        op("pe", lambda e, _x=xb: e.matmul(_x[:, 0:128], id16[:], m_low[:],
                                                           start=False, stop=True),
                           r=["id16", "m_low"], w=[xbn])
                    else:
                        op("pe", lambda e, _o=(nk - 1) * 128, _b=sb: e.matmul(
                            _b[:, _o:_o + 128], id16[:], m_low[:], start=False, stop=True),
                           r=["id16", "m_low"], w=[sbn])
                    if pend is not None:
                        emit_ptav_i(pend)
                    nPc = min(nk, 4) * 128
                    nmv = c_nmax[h % 2]
                    nmn = f"c_nmax{h % 2}"
                    rdn = f"c_rd{h}"
                    rdc = c_rdens[:, h:h + 1]
                    pbp = i % 2
                    rqc = rq_bank[pbp][:, h:h + 1]
                    rqn = f"c_rq{pbp}"
                    op("dve", lambda e, _n=nPc, _m=nmv, _b=sb: e.tensor_reduce(
                        _m, _b[:, 0:_n], AX.X, ALU.max, negate=True),
                       r=[sbn], w=[nmn])
                    if nk == 5:
                        xv = c_nmaxx[h % 2]
                        xn = f"c_nmaxx{h % 2}"
                        op("dve", lambda e, _xv=xv, _x=xb: e.tensor_reduce(
                            _xv, _x[:, 0:128], AX.X, ALU.max, negate=True), r=[xbn], w=[xn])
                        op("dve", lambda e, _m=nmv, _x=xv: e.tensor_tensor(
                            _m, _m, _x, ALU.min), r=[nmn, xn], w=[nmn])
                    op("act", lambda e, _m=nmv, _q=rqc: e.activation(
                        _m, _m, AF.Identity, scale=_q), r=[nmn, rqn], w=[nmn])
                    op("act", lambda e, _n=nPc, _hs=hs, _m=nmv, _c=rdc, _b=sb, _q=rqc:
                       e.activation(
                        tA["pb"][_hs][:, 0:_n], _b[:, 0:_n], AF.Exp, bias=_m,
                        scale=_q, accum_out=_c),
                       r=[sbn, nmn, rqn], w=["pbI_%d" % hs, rdn])
                    if nk == 5:
                        dxv = c_denx[h % 2]
                        dxn = f"c_denx{h % 2}"
                        op("act", lambda e, _hs=hs, _m=nmv, _xv=dxv, _x=xb, _q=rqc:
                           e.activation(
                            tA["pb"][_hs][:, 512:640], _x[:, 0:128], AF.Exp, bias=_m,
                            scale=_q, accum_out=_xv),
                           r=[xbn, nmn, rqn], w=["pbI_%d" % hs, dxn])
                        op("dve", lambda e, _c=rdc, _x=dxv: e.tensor_add(_c, _c, _x),
                           r=[rdn, dxn], w=[rdn])
                    op("dve", lambda e, _c=rdc: e.reciprocal(_c, _c), r=[rdn], w=[rdn])
                    pend = h
                emit_ptav_i(pend)
                for h in range(HEADS_I):
                    op("act", lambda e, _h=h, _s=sl: e.activation(
                        tA["ob"][_s][:, _h * 64:_h * 64 + 64], pP[3][:, _h * 64:_h * 64 + 64],
                        AF.Identity, scale=c_rdens[:, _h:_h + 1]),
                       r=["pP3", f"c_rd{h}"], w=["obI_%d" % sl])
                for c in range(4):
                    op("pe", lambda e, _c=c, _s=sl: e.transpose(
                        tbf[:, _c * 128:_c * 128 + 128],
                        tA["ob"][_s][:, _c * 128:_c * 128 + 128], id16[:]),
                       r=["obI_%d" % sl, "id16"], w=["tbf"])
                op("dve", lambda e, _s=sl: e.tensor_copy(tA["ot"][_s][:], tbf[:, 0:512]),
                   r=["tbf"], w=["otI_%d" % sl])
                for n in range(2):
                    ps = pP[n]
                    psn = PN[id(ps)]
                    for j in range(4):
                        op("pe", lambda e, _n=n, _j=j, _p=ps, _s=sl: e.matmul(
                            _p[:, 0:512], tA["ot"][_s][:, _j * 128:_j * 128 + 128],
                            tA["woin"][:, _j * D + _n * 512:_j * D + _n * 512 + 512],
                            start=(_j == 0), stop=(_j == 3)),
                           r=["otI_%d" % sl, "woin_bf"], w=[psn])
                op("dve", lambda e, _x=xt: e.tensor_add(_x[:, 0:512], _x[:, 0:512], pP[0][:]),
                   r=["pP0", "x_sb"], w=["x_sb"])
                op("dve", lambda e, _x=xt: e.tensor_add(
                    _x[:, 512:1024], _x[:, 512:1024], pP[1][:]),
                   r=["pP1", "x_sb"], w=["x_sb"])

            P1i(0)
            for i in range(NT):
                if i + 1 < NT:
                    P1i(i + 1)
                P2i(i)

            em.fence(prev_names, namesB)
            prev_names = namesB
            w1_off = []
            pos = 0
            for m in range(NHT):
                aw = min(128, HID - m * 128)
                w1_off.append((pos, aw))
                w1_off.append((pos + aw, aw))
                pos += 2 * aw

            def LN2part(tc):
                for ii in range(4):
                    i = tc * 4 + ii
                    sl = i % 2
                    xt = x_sb[:, i * D:(i + 1) * D]
                    shift_tile(i, tB["h2"][sl], "h2B_%d" % sl)
                    pb = i % 2
                    rstd, nmr = ln_stats([(xt[:, 0:512], 512, "x_sb"),
                              (tB["h2"][sl][:], 512, "h2B_%d" % sl)], D, pb)
                    op("act", lambda e, _x=xt, _r=rstd, _n=nmr: e.activation(
                        ln_scr[:, 0:512], _x[:, 0:512], AF.Identity, bias=_n, scale=_r),
                       r=["x_sb", f"c_rstd{pb}", f"c_nmr{pb}"], w=["ln_scr"])
                    op("act", lambda e, _s=sl, _r=rstd, _n=nmr: e.activation(
                        ln_scr[:, 512:1024], tB["h2"][_s][:], AF.Identity,
                        bias=_n, scale=_r),
                       r=["h2B_%d" % sl, f"c_rstd{pb}", f"c_nmr{pb}"], w=["ln_scr"])
                    for g0 in (0, 4):
                        for gi in range(4):
                            j = g0 + gi
                            op("pe", lambda e, _j=j, _g=gi: e.transpose(
                                tr32[:, _g * 128:_g * 128 + 128],
                                ln_scr[:, _j * 128:_j * 128 + 128], id32[:]),
                               r=["ln_scr", "id32"], w=["tr32"])
                        for gi in range(4):
                            j = g0 + gi
                            op("act", lambda e, _j=j, _g=gi, _ii=ii, _l=l: e.activation(
                                tB["ln2T"][:, _j * 512 + _ii * 128:_j * 512 + _ii * 128 + 128],
                                tr32[:, _g * 128:_g * 128 + 128], AF.Identity,
                                bias=lnb[_l][:, _j:_j + 1], scale=lng[_l][:, _j:_j + 1]),
                               r=["tr32", f"lng{l}", f"lnb{l}"], w=["ln2T"])
            LN2part(0)
            for tc in range(4):
                lane_w = [128, 128, 128, 128]
                for ot in range(2 * NHT):
                    grp = ot // 4
                    lane = ot % 4
                    wb = grp % 2
                    if grp == (2 * NHT - 1) // 4:
                        lane_w = [w1_off[grp * 4 + k][1] if grp * 4 + k < 2 * NHT else 0
                                  for k in range(4)]
                    lane_off = sum(lane_w[:lane])
                    gw = sum(lane_w)
                    if lane == 0:
                        C0 = w1_off[ot][0]
                        op("pool", lambda e, _l=l, _c=C0, _b=wb, _g=gw: e.dma_start(
                            bass.AP(tB["w1"][_b], 0, [[4096, 128], [512, 8], [1, _g]]),
                            bass.AP(dram[f"w1p{_l}"].tensor, _c,
                                    [[2 * HID, 128], [2 * HID * 128, 8], [1, _g]])),
                           w=["w1bf_%d" % wb], dma=True)
                    col0, ow = w1_off[ot]
                    pa = pP[2 + (ot % 2)]
                    pan = PN[id(pa)]
                    for j in range(8):
                        op("pe", lambda e, _j=j, _p=pa, _b=wb, _w=ow, _lo=lane_off: e.matmul(
                            _p[0:_w, 0:512],
                            tB["w1"][_b][:, _j * 512 + _lo:_j * 512 + _lo + _w],
                            tB["ln2T"][:, _j * 512:_j * 512 + 512],
                            start=(_j == 0), stop=(_j == 7)),
                           r=["w1bf_%d" % wb, "ln2T"], w=[pan])
                    ht = ot // 2
                    if ot % 2 == 0:
                        op("act", lambda e, _p=pa, _w=ow, _h=ht, _l=l: e.activation(
                            tB["ag"][0:_w, 0:512], _p[0:_w, 0:512], AF.Identity,
                            bias=b1a[_l][0:_w, _h:_h + 1]),
                           r=[pan, f"b1a{l}"], w=["ag_a"])
                    else:
                        op("act", lambda e, _p=pa, _w=ow, _h=ht, _l=l: e.activation(
                            tB["ag"][0:_w, 1024:1536], _p[0:_w, 0:512], AF.Gelu,
                            bias=b1g[_l][0:_w, _h:_h + 1]),
                           r=[pan, f"b1g{l}"], w=["ag_g"])
                        op("dve", lambda e, _w=ow, _h=ht: e.tensor_mul(
                            tB["gg"][0:_w, _h * 512:_h * 512 + 512],
                            tB["ag"][0:_w, 0:512], tB["ag"][0:_w, 1024:1536]),
                           r=["ag_a", "ag_g"], w=["geglu"])
                if tc + 1 < 4:
                    LN2part(tc + 1)
                for ii in range(4):
                    i = tc * 4 + ii
                    xt = x_sb[:, i * D:(i + 1) * D]
                    for n in range(2):
                        ps = pP[2 * (ii % 2) + n]
                        psn = PN[id(ps)]
                        for kk in range(NHT):
                            kw = 128 if kk < NHT - 1 else HID_LAST
                            op("pe", lambda e, _n=n, _k=kk, _p=ps, _ii=ii, _kw=kw: e.matmul(
                                _p[:, 0:512],
                                tB["gg"][0:_kw, _k * 512 + _ii * 128:_k * 512 + _ii * 128 + 128],
                                tB["w2"][0:_kw, _k * D + _n * 512:_k * D + _n * 512 + 512],
                                start=(_k == 0), stop=False),
                               r=["geglu", "w2bf"], w=[psn])
                        op("pe", lambda e, _n=n, _p=ps, _l=l: e.matmul(
                            _p[:, 0:512], ones_bf[:],
                            tB["b2hi"][0:1, _n * 512:_n * 512 + 512],
                            start=False, stop=True),
                           r=["ones_bf", "b2hi"], w=[psn])
                    b0 = 2 * (ii % 2)
                    op("dve", lambda e, _x=xt, _b=b0: e.tensor_add(
                        _x[:, 0:512], _x[:, 0:512], pP[_b][:]),
                       r=[PN[id(pP[b0])], "x_sb"], w=["x_sb"])
                    op("dve", lambda e, _x=xt, _b=b0: e.tensor_add(
                        _x[:, 512:1024], _x[:, 512:1024], pP[_b + 1][:]),
                       r=[PN[id(pP[b0 + 1])], "x_sb"], w=["x_sb"])

        if PHASE_LIMIT == "all" and DEBUG_TAP is None:
            em.fence(prev_names, FINAL_T)
            for i in range(NT):
                sl = i % 2
                xt = x_sb[:, i * D:(i + 1) * D]
                op("sp", lambda e, _i=i, _s=sl: e.dma_start(
                    x1b[_s][:], d_x1[:, _i * D:(_i + 1) * D]),
                   r=["x1d"], w=[f"x1b{sl}"], dma=True)
                pb = i % 2
                rstd, nmr = ln_stats([(xt, D, "x_sb")], D, pb)
                op("dve", lambda e, _s=sl, _n=nmr: e.tensor_scalar_add(
                    x1b[_s][:], x1b[_s][:], _n),
                   r=[f"x1b{sl}", f"c_nmr{pb}"], w=[f"x1b{sl}"])
                op("dve", lambda e, _x=xt, _s=sl, _r=rstd: e.scalar_tensor_tensor(
                    _x, _x, _r, x1b[_s][:], ALU.mult, ALU.add),
                   r=[f"x1b{sl}", f"c_rstd{pb}", "x_sb"], w=["x_sb"])
        tok = op("sp", lambda e: e.dma_start(
            d_out.rearrange("(n p) d -> p n d", p=128), x_sb3),
            r=["x_sb"], w=["out_dram"], dma=True)
        em.final_wait("sp", tok)

        with nc.Block() as block:
            em.replay(block)
    return nc


def _prep_host(inputs):
    f = np.float32
    low_i, up_i, up_s = _masks()
    common = {
        "wqT": np.ascontiguousarray(inputs["L0_attn_wq_w"].T),
        "wkT": np.ascontiguousarray(inputs["L0_attn_wk_w"].T),
        "wvT": np.ascontiguousarray(inputs["L0_attn_wv_w"].T),
        "woT": np.ascontiguousarray(inputs["L0_attn_wo_w"].T),
        "brow4": np.concatenate([
            inputs["L0_attn_wq_b"], inputs["L0_attn_wk_b"],
            inputs["L0_attn_wv_b"], inputs["L0_attn_wo_b"]]).reshape(1, 4 * D),
        "qn2": np.tile(inputs["L0_attn_qn_w"], 2).reshape(128, 1),
        "kn2": np.tile(inputs["L0_attn_kn_w"], 2).reshape(128, 1),
        "rotpk": _host_tables(),
        "m_low": low_i, "m_upi": up_i, "m_ups": up_s,
        "id32": np.eye(128, dtype=f),
        "id16": np.eye(128, dtype=ml_dtypes.bfloat16),
    }
    for l in range(2):
        w1 = inputs["L0_ffn_w1"][l]
        b1 = inputs["L0_ffn_b1"][l]
        perm = []
        for m in range(NHT):
            aw = min(128, HID - m * 128)
            perm.extend(range(m * 128, m * 128 + aw))
            perm.extend(range(HID + m * 128, HID + m * 128 + aw))
        perm = np.asarray(perm)
        w1p = np.ascontiguousarray(w1[:, perm])
        b1p = b1[perm]
        b1av = np.zeros((128, NHT), f)
        b1gv = np.zeros((128, NHT), f)
        pos = 0
        for m in range(NHT):
            aw = min(128, HID - m * 128)
            b1av[0:aw, m] = b1p[pos:pos + aw]
            b1gv[0:aw, m] = b1p[pos + aw:pos + 2 * aw]
            pos += 2 * aw
        qs = inputs["L0_ffn_q_scale"][l]
        ks = inputs["L0_ffn_k_scale"][l]
        common.update({
            f"wqkv{l}": np.ascontiguousarray(inputs["L0_ffn_wqkv"][l]),
            f"wo{l}": np.ascontiguousarray(inputs["L0_ffn_wo"][l]),
            f"w1p{l}": w1p,
            f"b1a{l}": b1av, f"b1g{l}": b1gv,
            f"w2_{l}": np.ascontiguousarray(inputs["L0_ffn_w2"][l]),
            f"b2_{l}": inputs["L0_ffn_b2"][l].reshape(1, D),
            f"lng{l}": np.ascontiguousarray(inputs["L0_ffn_ln_g"][l].reshape(8, 128).T),
            f"lnb{l}": np.ascontiguousarray(inputs["L0_ffn_ln_b"][l].reshape(8, 128).T),
            f"qsrows{l}": np.concatenate([
                qs, np.concatenate([qs[32:], qs[:32]]),
                ks, np.concatenate([ks[32:], ks[:32]])]).reshape(1, 4 * HD),
        })
    return common


def kernel(**inputs):
    inputs = {k: np.asarray(v) for k, v in inputs.items()}
    common = _prep_host(inputs)
    nc = build()
    in_maps = []
    for c in range(8):
        m = dict(common)
        m["x"] = np.ascontiguousarray(inputs["L0_attn_x"][c])
        in_maps.append(m)
    res = run_bass_kernel_spmd(nc, in_maps, core_ids=list(range(8)))
    return np.stack([res.results[c]["out"] for c in range(8)], axis=0)


# version 12
# speedup vs baseline: 1.3887x; 1.0166x over previous
import numpy as np
import ml_dtypes
import concourse.bass as bass
import concourse.mybir as mybir
from concourse.bass_utils import run_bass_kernel_spmd

F32 = mybir.dt.float32
BF16 = mybir.dt.bfloat16
AF = mybir.ActivationFunctionType
ALU = mybir.AluOpType
AX = mybir.AxisListType

S, D, NT, HD = 2048, 1024, 16, 64
HEADS_O, HEADS_I, INNER = 16, 8, 512
HID, NHT, HID_LAST = 2730, 22, 42
EPS = 1e-5

PHASE_LIMIT = "all"
NT_LIM = NT
DEBUG_TAP = None

ENGINES = ("pe", "act", "dve", "pool", "sp")
QSEM = {"sp": "q_sp", "pool": "q_pool", "act": "q_act"}


class Buf:
    __slots__ = ("name", "w", "readers")

    def __init__(self, name):
        self.name = name
        self.w = None
        self.readers = []


class Emitter:
    def __init__(self, nc):
        self.nc = nc
        self.prog = {e: [] for e in ENGINES}
        self.sems = {}
        self.counts = {n: 0 for n in ("pe", "act", "dve", "pool", "sp")}
        self.lastwait = {}
        self.bufs = {}

    def make_sems(self, ctx):
        for nm in list(self.counts):
            self.sems[nm] = self.nc.alloc_semaphore("sem_" + nm)

    def dma_sem(self, bufname):
        nm = "d_" + bufname
        if nm not in self.sems:
            self.sems[nm] = self.nc.alloc_semaphore("sem_" + nm)
            self.counts[nm] = 0
        return nm

    def buf(self, name):
        if name not in self.bufs:
            self.bufs[name] = Buf(name)
        return self.bufs[name]

    PSUM_BUFS = frozenset(("pP0", "pP1", "pP2", "pP3", "simP", "simX", "tbf", "tr32"))

    def op(self, eng, fn, r=(), w=(), dma=False):
        reads = [self.buf(n) for n in r if n not in self.PSUM_BUFS]
        writes = [self.buf(n) for n in w] + [self.buf(n) for n in r
                                             if n in self.PSUM_BUFS]
        deps = []
        for b in reads:
            if b.w is not None:
                deps.append(b.w)
        for b in writes:
            if b.w is not None:
                deps.append(b.w)
            deps.extend(b.readers)
        mysem = self.dma_sem(w[0]) if dma else eng
        need = {}
        for (sem, val) in deps:
            if dma and sem == mysem:
                continue
            if sem == eng == "pe":
                continue
            if need.get(sem, 0) < val:
                need[sem] = val
        for sem, val in sorted(need.items()):
            if self.lastwait.get((eng, sem), 0) >= val:
                continue
            self.lastwait[(eng, sem)] = val
            self.prog[eng].append(
                lambda e, _s=sem, _v=val: e.wait_ge(self.sems[_s], _v))
        amt = 16 if dma else 1
        self.counts[mysem] += amt
        val = self.counts[mysem]
        self.prog[eng].append(
            lambda e, _f=fn, _s=mysem, _a=amt: _f(e).then_inc(self.sems[_s], _a))
        tok = (mysem, val)
        for b in reads:
            b.readers.append(tok)
        for b in writes:
            b.w = tok
            b.readers = []
        return tok

    def fence(self, old_names, new_names):
        toks = {}
        for n in old_names:
            b = self.bufs.get(n)
            if b is None:
                continue
            for t in ([b.w] if b.w else []) + b.readers:
                if toks.get(t[0], 0) < t[1]:
                    toks[t[0]] = t[1]
        tl = list(toks.items())
        for n in new_names:
            b = self.buf(n)
            b.readers = b.readers + tl
        for n in old_names:
            if n in self.bufs:
                self.bufs[n].w = None
                self.bufs[n].readers = []

    def final_wait(self, eng, tok):
        self.prog[eng].append(lambda e, _t=tok: e.wait_ge(self.sems[_t[0]], _t[1]))

    def replay(self, block):
        for eng, attr in (("sp", "sync"), ("pe", "tensor"), ("act", "scalar"),
                          ("dve", "vector"), ("pool", "gpsimd")):
            seq = self.prog[eng]

            def make(seq_):
                def run(e):
                    for f in seq_:
                        f(e)
                return run
            getattr(block, attr)(make(seq))


def _host_tables():
    half = np.arange(0, HD, 2, dtype=np.float64)
    inv_freq = 1.0 / 10000.0 ** (half / HD)
    t = np.arange(S, dtype=np.float64)
    freqs = t[:, None] * inv_freq[None, :]
    freqs = np.concatenate([freqs, freqs], axis=1)
    base = (half + 0.4 * HD) / (1.4 * HD)
    power = (t - S // 2) / 256.0
    scale = base[None, :] ** power[:, None]
    scale = np.concatenate([scale, scale], axis=1)
    cq = (np.cos(freqs) * scale).astype(np.float32)
    sq = (np.sin(freqs) * scale).astype(np.float32)
    ck = (np.cos(freqs) / scale).astype(np.float32)
    sk = (np.sin(freqs) / scale).astype(np.float32)
    packed = np.concatenate([cq, sq, ck, sk], axis=1)
    return np.ascontiguousarray(
        packed.reshape(NT, 128, 4 * HD).transpose(1, 0, 2).reshape(128, NT * 4 * HD))


def _masks():
    i = np.arange(128)[:, None]
    j = np.arange(128)[None, :]
    neg = np.float32(-1.0e30)
    bf = ml_dtypes.bfloat16
    low_i = np.where(i >= j, 0.0, neg).astype(np.float32).astype(bf)
    up_i = np.where(j >= i, 0.0, neg).astype(np.float32).astype(bf)
    up_s = np.where(j > i, 0.0, neg).astype(np.float32).astype(bf)
    return low_i, up_i, up_s


def build():
    nc = bass.Bass("TRN2", target_bir_lowering=False)
    dram = {}

    def din(name, shape, dt=F32):
        dram[name] = nc.dram_tensor(name, list(shape), dt, kind="ExternalInput").ap()

    din("x", (S, D))
    for w in ("wqT", "wkT", "wvT", "woT"):
        din(w, (D, D))
    din("brow4", (1, 4 * D))
    din("qn2", (128, 1))
    din("kn2", (128, 1))
    din("rotpk", (128, NT * 4 * HD))
    din("m_low", (128, 128), BF16)
    din("m_upi", (128, 128), BF16)
    din("m_ups", (128, 128), BF16)
    din("id32", (128, 128), F32)
    din("id16", (128, 128), BF16)
    for l in range(2):
        din(f"wqkv{l}", (D, 3 * INNER))
        din(f"wo{l}", (INNER, D))
        din(f"w1p{l}", (D, 2 * HID))
        din(f"b1a{l}", (128, NHT))
        din(f"b1g{l}", (128, NHT))
        din(f"w2_{l}", (HID, D))
        din(f"b2_{l}", (1, D))
        din(f"lng{l}", (128, 8))
        din(f"lnb{l}", (128, 8))
        din(f"qsrows{l}", (1, 4 * HD))
    d_out = nc.dram_tensor("out", [S, D], F32, kind="ExternalOutput").ap()
    d_x1 = nc.dram_tensor("x1_scratch", [128, NT * D], F32, kind="Internal").ap()

    from contextlib import ExitStack
    with ExitStack() as ctx:
        em = Emitter(nc)
        em.make_sems(ctx)

        def sba(name, shape, dt=F32):
            return nc.alloc_sbuf_tensor(name, list(shape), dt)

        x_sb = sba("x_sb", (128, NT * D))
        ln_scr = sba("ln_scr", (128, D))
        sq_scr = sba("sq_scr", (128, D))
        cols = sba("cols", (128, 384))
        m_low = sba("m_low_sb", (128, 128), BF16)
        m_upi = sba("m_upi_sb", (128, 128), BF16)
        m_ups = sba("m_ups_sb", (128, 128), BF16)
        id32 = sba("id32_sb", (128, 128))
        id16 = sba("id16_sb", (128, 128), BF16)
        ones_bf = sba("ones_bf", (1, 128), BF16)
        qn2 = sba("qn2_sb", (128, 1))
        kn2 = sba("kn2_sb", (128, 1))
        lng = [sba(f"lng{l}_sb", (128, 8)) for l in range(2)]
        lnb = [sba(f"lnb{l}_sb", (128, 8)) for l in range(2)]
        b1a = [sba(f"b1a{l}_sb", (128, NHT)) for l in range(2)]
        b1g = [sba(f"b1g{l}_sb", (128, NHT)) for l in range(2)]
        wld = [sba(f"wld{i}", (128, 1024)) for i in range(2)]
        rowb = sba("rowb", (1, 256), BF16)

        ARENA = 122880
        lo, hi = nc.bump_sbuf(ARENA)
        assert hi - lo >= ARENA

        def ar(name, shape, dt, off):
            sz = int(np.prod(shape[1:])) * mybir.dt.size(dt)
            assert off + sz <= ARENA, (name, off, sz)
            return nc.alloc_sbuf_tensor_at(name, list(shape), dt, offset=lo + off)

        wT_bf = {w: ar(w + "_bf", (128, 8 * D), BF16, k * 16384)
                 for k, w in enumerate(("wqT", "wkT", "wvT", "woT"))}
        q_scrO = ar("q_scrO", (128, D), F32, 65536)
        k_scrO = ar("k_scrO", (128, D), F32, 69632)
        ltO = [ar(f"ltO{i}", (128, D), BF16, 73728 + i * 2048) for i in range(2)]
        qhTO = [ar(f"qhTO{i}", (128, D), BF16, 77824 + i * 2048) for i in range(2)]
        khT3 = ar("khT3", (128, 4 * D), BF16, 81920)
        v3 = ar("v3", (128, 4 * D), BF16, 90112)
        pbO = [ar(f"pbO{i}", (128, 384), BF16, 98304 + i * 768) for i in range(4)]
        ptO = [ar(f"ptO{i}", (128, 384), BF16, 118784 + i * 1024) for i in range(4)]
        obO = [ar(f"obO{i}", (128, D), BF16, 102400 + i * 2048) for i in range(2)]
        otO = [ar(f"otO{i}", (128, D), BF16, 106496 + i * 2048) for i in range(2)]
        brow4_hi = ar("brow4_hi", (1, 4 * D), BF16, 110592)
        OUTER_T = (["wqT_bf", "wkT_bf", "wvT_bf", "woT_bf", "q_scrO", "k_scrO",
                    "khT3", "v3", "brow4_hi", "csrows"] +
                   [f"{p}{i}" for p in ("ltO", "qhTO", "obO", "otO") for i in range(2)] +
                   [f"{p}{i}" for p in ("pbO", "ptO") for i in range(4)])

        def innerA():
            t = {}
            t["wqkv"] = ar("wqkv_bf", (128, 8 * 3 * INNER), BF16, 0)
            t["woin"] = ar("woin_bf", (128, 4 * D), BF16, 24576)
            t["q_scr"] = ar("q_scrI", (128, INNER), F32, 32768)
            t["k_scr"] = ar("k_scrI", (128, INNER), F32, 34816)
            t["rh"] = ar("rhI", (128, INNER), F32, 36864)
            t["lt"] = [ar(f"ltI_{i}", (128, D), BF16, 38912 + i * 2048) for i in range(2)]
            t["qhT"] = [ar(f"qhTI_{i}", (128, INNER), BF16, 43008 + i * 1024) for i in range(2)]
            t["khT5"] = ar("khT5", (128, 6 * INNER), BF16, 45056)
            t["v5"] = ar("v5", (128, 6 * INNER), BF16, 51200)
            t["pb"] = [ar(f"pbI_{i}", (128, 640), BF16, 57344 + i * 1280) for i in range(2)]
            t["pt"] = [ar(f"ptI_{i}", (128, 640), BF16, 59904 + i * 1280) for i in range(2)]
            t["ob"] = [ar(f"obI_{i}", (128, INNER), BF16, 62464 + i * 1024) for i in range(2)]
            t["ot"] = [ar(f"otI_{i}", (128, INNER), BF16, 64512 + i * 1024) for i in range(2)]
            t["h2"] = [ar(f"h2A_{i}", (128, INNER), F32, 66560 + i * 2048) for i in range(2)]
            t["rb"] = [ar(f"rotb_{i}", (128, 4 * HD), F32, 70656 + i * 1024) for i in range(2)]
            t["abt"] = ar("abt", (128, 4 * HD), F32, 72704)
            names = (["wqkv_bf", "woin_bf", "q_scrI", "k_scrI",
                      "rhI", "khT5", "v5", "abt",
                      "w2bf", "b2hi"] +
                     [f"{p}_{i}" for p in ("ltI", "qhTI", "pbI", "ptI", "obI", "otI",
                                           "h2A", "rotb") for i in range(2)])
            return t, names

        def innerB():
            t = {}
            t["w2"] = ar("w2bf", (128, NHT * D), BF16, 73728)
            t["gg"] = ar("geglu", (128, NHT * 512), BF16, 0)
            t["ln2T"] = ar("ln2T", (128, 8 * 512), BF16, 22528)
            t["w1"] = [ar(f"w1bf_{i}", (128, 4096), BF16, 30720 + i * 8192) for i in range(2)]
            t["ag"] = ar("agscr", (128, 2048), F32, 47104)
            t["h2"] = [ar(f"h2B_{i}", (128, INNER), F32, 55296 + i * 2048) for i in range(2)]
            t["b2hi"] = ar("b2hi", (1, D), BF16, 118784)
            names = (["geglu", "ln2T", "agscr"] +
                     [] +
                     [f"w1bf_{i}" for i in range(2)] + [f"h2B_{i}" for i in range(2)])
            return t, names

        x1b = [ar(f"x1b{i}", (128, D), F32, i * 4096) for i in range(2)]
        FINAL_T = ["x1b0", "x1b1"]

        pP = [nc.alloc_psum_tensor(f"pP{i}", [128, 512], F32) for i in range(4)]
        simP = nc.alloc_psum_tensor("simP", [128, 512], F32)
        simX = nc.alloc_psum_tensor("simX", [128, 128], F32)
        tbf = nc.alloc_psum_tensor("tbf", [128, 1024], BF16)
        tr32 = nc.alloc_psum_tensor("tr32", [128, 512], F32)
        PN = {id(pP[0]): "pP0", id(pP[1]): "pP1", id(pP[2]): "pP2", id(pP[3]): "pP3",
              id(simP): "simP", id(simX): "simX", id(tbf): "tbf", id(tr32): "tr32"}

        op = em.op

        def stats_bank(p):
            base = 8 * p
            return [cols[:, base + i:base + i + 1] for i in range(8)]
        sums_bank = [cols[:, 16:32], cols[:, 32:48]]
        c_rdens = cols[:, 48:64]
        c_nmax = [cols[:, 64 + i:65 + i] for i in range(4)]
        c_nmaxx = [cols[:, 68 + i:69 + i] for i in range(2)]
        c_denx = [cols[:, 72 + i:73 + i] for i in range(2)]
        qn_eff = cols[:, 76:77]
        rq_bank = [cols[:, 96:112], cols[:, 112:128]]
        c_eight = cols[:, 77:78]
        c_eps = cols[:, 78:79]
        c_s1, c_mu, c_m2, c_var, c_rstd, c_nmr, c_t6, c_t7 = stats_bank(0)

        def rstd_ap(pb):
            return stats_bank(pb)[4]

        x_sb3 = bass.AP(x_sb, 0, [[NT * D, 128], [D, NT], [1, D]])
        op("sp", lambda e: e.dma_start(
            x_sb3, dram["x"].rearrange("(n p) d -> p n d", p=128)),
            w=["x_sb"], dma=True)
        for t, n in ((m_low, "m_low"), (m_upi, "m_upi"), (m_ups, "m_ups"),
                     (id32, "id32"), (id16, "id16"), (qn2, "qn2"), (kn2, "kn2")):
            op("sp", lambda e, _t=t, _n=n: e.dma_start(_t[:], dram[_n]), w=[n], dma=True)
        for l in range(2):
            for n, t in ((f"lng{l}", lng[l]), (f"lnb{l}", lnb[l]),
                         (f"b1a{l}", b1a[l]), (f"b1g{l}", b1g[l])):
                op("sp", lambda e, _t=t, _n=n: e.dma_start(_t[:], dram[_n]), w=[n], dma=True)
        op("dve", lambda e: e.memset(ones_bf[:], 1.0), w=["ones_bf"])
        op("dve", lambda e: e.memset(c_eps, EPS), w=["c_eps"])
        op("dve", lambda e: e.memset(c_eight, 8.0), w=["c_eight"])
        op("dve", lambda e: e.tensor_scalar_mul(qn_eff, qn2[:], 0.125), r=["qn2"], w=["c_qe"])

        def load_cast(dst_bf, dst_name, dram_name, n_rows, n_cols, cast_eng=None):
            nj = n_rows // 128
            rem = n_rows - nj * 128
            W = dst_bf.ap().ap[0][0]
            if nj:
                op("pool", lambda e: e.dma_start(
                    bass.AP(dst_bf, 0, [[W, 128], [n_cols, nj], [1, n_cols]]),
                    dram[dram_name][0:nj * 128, :].rearrange("(j p) o -> p j o", p=128)),
                   w=[dst_name], dma=True)
            if rem:
                op("pool", lambda e: e.dma_start(
                    dst_bf[0:rem, nj * n_cols:(nj + 1) * n_cols],
                    dram[dram_name][nj * 128:n_rows, :]),
                   w=[dst_name], dma=True)
        load_cast.k = 0

        def ln_stats(srcs, total_d, pb=0):
            c_s1, c_mu, c_m2, c_var, c_rstd, c_nmr, c_t6, c_t7 = stats_bank(pb)
            sx = str(pb)
            for idx, (apv, wdt, rn) in enumerate(srcs):
                if idx == 0:
                    op("dve", lambda e, _a=apv, _c=c_s1: e.tensor_reduce(_c, _a, AX.X, ALU.add),
                       r=[rn], w=["c_s1" + sx])
                    op("act", lambda e, _a=apv, _w=wdt, _c=c_t6: e.activation(
                        sq_scr[:, 0:_w], _a, AF.Square, accum_out=_c),
                       r=[rn], w=["sq_scr", "c_t6" + sx])
                else:
                    op("dve", lambda e, _a=apv, _c=c_t7: e.tensor_reduce(_c, _a, AX.X, ALU.add),
                       r=[rn], w=["c_t7" + sx])
                    op("dve", lambda e, _c1=c_s1, _c2=c_t7: e.tensor_add(_c1, _c1, _c2),
                       r=["c_t7" + sx, "c_s1" + sx], w=["c_s1" + sx])
                    op("act", lambda e, _a=apv, _w=wdt, _c=c_t7: e.activation(
                        sq_scr[:, 0:_w], _a, AF.Square, accum_out=_c),
                       r=[rn], w=["sq_scr", "c_t7" + sx])
                    op("dve", lambda e, _c1=c_t6, _c2=c_t7: e.tensor_add(_c1, _c1, _c2),
                       r=["c_t7" + sx, "c_t6" + sx], w=["c_t6" + sx])
            inv = 1.0 / total_d
            op("dve", lambda e, _a=c_mu, _b=c_s1: e.tensor_scalar_mul(_a, _b, inv),
               r=["c_s1" + sx], w=["c_mu" + sx])
            op("dve", lambda e, _a=c_m2, _b=c_mu: e.tensor_mul(_a, _b, _b),
               r=["c_mu" + sx], w=["c_m2" + sx])
            op("dve", lambda e, _a=c_var, _b=c_t6: e.tensor_scalar_mul(_a, _b, inv),
               r=["c_t6" + sx], w=["c_var" + sx])
            op("dve", lambda e, _a=c_var, _b=c_m2: e.tensor_sub(_a, _a, _b),
               r=["c_var" + sx, "c_m2" + sx], w=["c_var" + sx])
            op("act", lambda e, _a=c_var: e.activation(_a, _a, AF.Sqrt, bias=c_eps, scale=1.0),
               r=["c_var" + sx, "c_eps"], w=["c_var" + sx])
            op("dve", lambda e, _a=c_rstd, _b=c_var: e.reciprocal(_a, _b),
               r=["c_var" + sx], w=["c_rstd" + sx])
            op("dve", lambda e, _a=c_nmr, _b=c_mu, _c=c_rstd: e.scalar_tensor_tensor(
                _a, _b, -1.0, _c, ALU.mult, ALU.mult),
               r=["c_mu" + sx, "c_rstd" + sx], w=["c_nmr" + sx])
            return c_rstd, c_nmr

        def tr_f32(chunks, evacs):
            for g0 in range(0, len(chunks), 4):
                grp = chunks[g0:g0 + 4]
                for gi, (src, rn) in enumerate(grp):
                    op("pe", lambda e, _s=src, _g=gi: e.transpose(
                        tr32[:, _g * 128:_g * 128 + 128], _s, id32[:]),
                       r=[rn, "id32"], w=["tr32"])
                evacs(g0, len(grp))

        for k, w in enumerate(("wqT", "wkT", "wvT", "woT")):
            load_cast(wT_bf[w], w + "_bf", w, D, D)
        op("pool", lambda e: e.dma_start(brow4_hi[0:1, :], dram["brow4"]),
           w=["brow4_hi"], dma=True)
        def proj_1024(lt_t, lt_name, w_bf, w_name, psA, psB, bias_idx, brow=None,
                      brow_name="ones_bf"):
            for n in range(2):
                ps = (psA, psB)[n]
                psn = PN[id(ps)]
                for j in range(8):
                    op("pe", lambda e, _n=n, _j=j, _p=ps, _lt=lt_t: e.matmul(
                        _p[:, 0:512], _lt[:, _j * 128:_j * 128 + 128],
                        w_bf[:, _j * D + _n * 512:_j * D + _n * 512 + 512],
                        start=(_j == 0), stop=False),
                       r=[lt_name, w_name], w=[psn])
                br = ones_bf[:] if brow is None else brow
                op("pe", lambda e, _n=n, _p=ps, _b=bias_idx, _r=br: e.matmul(
                    _p[:, 0:512], _r,
                    brow4_hi[0:1, _b * D + _n * 512:_b * D + _n * 512 + 512],
                    start=False, stop=True),
                   r=[brow_name, "brow4_hi"], w=[psn])

        def rmsnorm16(scr, scr_name, pb, rq=False, pre_min=False):
            if rq:
                su = rq_bank[pb]
                sn = f"c_rq{pb}"
            else:
                su = sums_bank[pb]
                sn = f"c_sums{pb}"
            if not pre_min:
                op("dve", lambda e, _s=scr: e.tensor_scalar_min(_s[:], _s[:], 1.0e4),
                   r=[scr_name], w=[scr_name])
            op("dve", lambda e, _s=scr: e.tensor_scalar_max(_s[:], _s[:], -1.0e4),
               r=[scr_name], w=[scr_name])
            op("act", lambda e, _s=scr: e.activation(sq_scr[:], _s[:], AF.Square),
               r=[scr_name], w=["sq_scr"])
            red3 = bass.AP(sq_scr, 0, [[D, 128], [64, 16], [1, 64]])
            op("dve", lambda e, _r=red3, _u=su: e.tensor_reduce(_u, _r, AX.X, ALU.add),
               r=["sq_scr"], w=[sn])
            op("dve", lambda e, _u=su: e.tensor_scalar(
                _u, _u, 1.0 / 64, 1.0e-5, ALU.mult, ALU.max),
               r=[sn], w=[sn])
            op("act", lambda e, _u=su: e.activation(_u, _u, AF.Sqrt, bias=c_eps, scale=1.0),
               r=[sn, "c_eps"], w=[sn])
            op("dve", lambda e, _u=su: e.reciprocal(_u, _u), r=[sn], w=[sn])
            if not rq:
                for h in range(16):
                    op("dve", lambda e, _s=scr, _h=h, _u=su: e.tensor_scalar_mul(
                        _s[:, _h * 64:_h * 64 + 64], _s[:, _h * 64:_h * 64 + 64],
                        _u[:, _h:_h + 1]), r=[scr_name, sn], w=[scr_name])

        def P1o(i):
            sl = i % 2
            k4 = i % 4
            pb = i % 2
            xt = x_sb[:, i * D:(i + 1) * D]
            c_s1, c_mu, c_m2, c_var, c_rstd, c_nmr, c_t6, c_t7 = stats_bank(pb)
            sx = str(pb)
            op("dve", lambda e, _a=xt, _c=c_s1: e.tensor_reduce(_c, _a, AX.X, ALU.add),
               r=["x_sb"], w=["c_s1" + sx])
            op("dve", lambda e, _a=c_nmr, _b=c_s1: e.tensor_scalar_mul(_a, _b, -1.0 / D),
               r=["c_s1" + sx], w=["c_nmr" + sx])
            op("act", lambda e, _x=xt, _n=c_nmr: e.activation(
                ln_scr[:], _x, AF.Identity, bias=_n, scale=1.0),
               r=["x_sb", "c_nmr" + sx], w=["ln_scr"])
            op("act", lambda e, _a=xt, _c=c_t6: e.activation(
                sq_scr[:], _a, AF.Square, accum_out=_c),
               r=["x_sb"], w=["sq_scr", "c_t6" + sx])
            op("dve", lambda e, _a=c_mu, _b=c_nmr: e.tensor_mul(_a, _b, _b),
               r=["c_nmr" + sx], w=["c_mu" + sx])
            op("dve", lambda e, _a=c_var, _b=c_t6: e.tensor_scalar_mul(_a, _b, 1.0 / D),
               r=["c_t6" + sx], w=["c_var" + sx])
            op("dve", lambda e, _a=c_var, _b=c_mu: e.tensor_sub(_a, _a, _b),
               r=["c_var" + sx, "c_mu" + sx], w=["c_var" + sx])
            op("act", lambda e, _a=c_var: e.activation(_a, _a, AF.Sqrt, bias=c_eps, scale=1.0),
               r=["c_var" + sx, "c_eps"], w=["c_var" + sx])
            op("pe", lambda e, _a=c_var: e.transpose(simX[0:1, 0:128], _a, id32[:]),
               r=["c_var" + sx, "id32"], w=["simX"])
            op("act", lambda e, _p=pb: e.activation(
                rowb[0:1, _p * 128:_p * 128 + 128], simX[0:1, 0:128], AF.Identity),
               r=["simX"], w=[f"rowb{pb}"])
            op("dve", lambda e, _a=c_rstd, _b=c_var: e.reciprocal(_a, _b),
               r=["c_var" + sx], w=["c_rstd" + sx])

            def ev_lt(g0, glen, _sl=sl):
                if g0 == 0:
                    op("dve", lambda e, _n=glen * 128: e.tensor_copy(
                        ltO[_sl][:, 0:_n], tr32[:, 0:_n]), r=["tr32"], w=[f"ltO{_sl}"])
                else:
                    op("act", lambda e, _g0=g0, _n=glen * 128: e.activation(
                        ltO[_sl][:, _g0 * 128:_g0 * 128 + _n], tr32[:, 0:_n], AF.Identity),
                       r=["tr32"], w=[f"ltO{_sl}"])
            tr_f32([(ln_scr[:, j * 128:j * 128 + 128], "ln_scr") for j in range(8)], ev_lt)
            stdrow = rowb[0:1, pb * 128:pb * 128 + 128]
            rbn = f"rowb{pb}"
            proj_1024(ltO[sl], f"ltO{sl}", wT_bf["wqT"], "wqT_bf", pP[0], pP[1], 0,
                      brow=stdrow, brow_name=rbn)
            proj_1024(ltO[sl], f"ltO{sl}", wT_bf["wkT"], "wkT_bf", pP[2], pP[3], 1,
                      brow=stdrow, brow_name=rbn)
            op("dve", lambda e, _r=rstd_ap(pb): e.tensor_scalar_mul(
                q_scrO[:, 0:512], pP[0][:], _r), r=["pP0", "c_rstd" + str(pb)], w=["q_scrO"])
            op("dve", lambda e, _r=rstd_ap(pb): e.tensor_scalar_mul(
                q_scrO[:, 512:1024], pP[1][:], _r), r=["pP1", "c_rstd" + str(pb)], w=["q_scrO"])
            op("act", lambda e, _r=rstd_ap(pb): e.activation(
                k_scrO[:, 0:512], pP[2][:], AF.Identity, scale=_r),
               r=["pP2", "c_rstd" + str(pb)], w=["k_scrO"])
            op("act", lambda e, _r=rstd_ap(pb): e.activation(
                k_scrO[:, 512:1024], pP[3][:], AF.Identity, scale=_r),
               r=["pP3", "c_rstd" + str(pb)], w=["k_scrO"])
            proj_1024(ltO[sl], f"ltO{sl}", wT_bf["wvT"], "wvT_bf", pP[0], pP[1], 2,
                      brow=stdrow, brow_name=rbn)
            op("act", lambda e, _k=k4, _r=rstd_ap(pb): e.activation(
                v3[:, _k * D:_k * D + 512], pP[0][:], AF.Identity, scale=_r),
               r=["pP0", "c_rstd" + str(pb)], w=["v3"])
            op("act", lambda e, _k=k4, _r=rstd_ap(pb): e.activation(
                v3[:, _k * D + 512:_k * D + 1024], pP[1][:], AF.Identity, scale=_r),
               r=["pP1", "c_rstd" + str(pb)], w=["v3"])
            rmsnorm16(q_scrO, "q_scrO", pb, rq=True, pre_min=True)

            def ev_q(g0, glen, _sl=sl):
                op("act", lambda e, _g0=g0, _n=glen * 128: e.activation(
                    qhTO[_sl][:, _g0 * 128:_g0 * 128 + _n], tr32[:, 0:_n],
                    AF.Identity, scale=qn_eff),
                   r=["tr32", "c_qe"], w=[f"qhTO{_sl}"])
            tr_f32([(q_scrO[:, j * 128:j * 128 + 128], "q_scrO") for j in range(8)], ev_q)
            rmsnorm16(k_scrO, "k_scrO", pb)

            def ev_k(g0, glen, _k4=k4):
                op("act", lambda e, _g0=g0, _n=glen * 128: e.activation(
                    khT3[:, _k4 * D + _g0 * 128:_k4 * D + _g0 * 128 + _n], tr32[:, 0:_n],
                    AF.Identity, scale=kn2[:]),
                   r=["tr32", "kn2"], w=["khT3"])
            tr_f32([(k_scrO[:, j * 128:j * 128 + 128], "k_scrO") for j in range(8)], ev_k)

        def P2o(i):
            sl = i % 2
            pb = i % 2
            xt = x_sb[:, i * D:(i + 1) * D]
            kstart = max(0, i - 2)
            nk = i - kstart + 1
            ncols = nk * 128

            def emit_ptav(h):
                hs = h % 4
                for c in range(nk):
                    op("pe", lambda e, _c=c, _hs=hs: e.transpose(
                        tbf[:, _c * 128:_c * 128 + 128],
                        pbO[_hs][:, _c * 128:_c * 128 + 128], id16[:]),
                       r=[f"pbO{hs}", "id16"], w=["tbf"])
                op("act", lambda e, _n=ncols, _hs=hs: e.activation(
                    ptO[_hs][:, 0:_n], tbf[:, 0:_n], AF.Identity), r=["tbf"], w=[f"ptO{hs}"])
                for c in range(nk):
                    slot = (kstart + c) % 4
                    st_ = (h % 8 == 0) and (c == 0)
                    sp_ = (h % 8 == 7) and (c == nk - 1)
                    op("pe", lambda e, _c=c, _hs=hs, _co=(h % 8) * 64,
                       _st=st_, _sp=sp_, _slot=slot, _h=h: e.matmul(
                        pP[0][:, _co:_co + 64], ptO[_hs][:, _c * 128:_c * 128 + 128],
                        v3[:, _slot * D + _h * 64:_slot * D + _h * 64 + 64],
                        start=_st, stop=_sp),
                       r=[f"ptO{hs}", "v3"], w=["pP0"])
                if h % 8 == 7:
                    for hh in range(h - 7, h + 1):
                        op("dve", lambda e, _co=(hh % 8) * 64, _h=hh, _s=sl:
                           e.tensor_scalar_mul(
                            obO[_s][:, _h * 64:_h * 64 + 64], pP[0][:, _co:_co + 64],
                            c_rdens[:, _h:_h + 1]),
                           r=["pP0", f"c_rd{hh}"], w=[f"obO{sl}"])

            SIMB = [simP, pP[1], pP[2], pP[3]]
            SIMBN = ["simP", "pP1", "pP2", "pP3"]
            pend = None
            for h in range(HEADS_O):
                hp = (h % 2) * 64
                hc = (h // 2) * 128
                hs = h % 4
                sb = SIMB[hs]
                sbn = SIMBN[hs]
                for c in range(nk):
                    slot = (kstart + c) % 4
                    op("pe", lambda e, _c=c, _s=slot, _hp=hp, _hc=hc, _sl=sl,
                       _st=(c == 0), _b=sb: e.matmul(
                        _b[:, _c * 128:_c * 128 + 128],
                        qhTO[_sl][_hp:_hp + 64, _hc:_hc + 128],
                        khT3[_hp:_hp + 64, _s * D + _hc:_s * D + _hc + 128],
                        start=_st, stop=False),
                       r=[f"qhTO{sl}", "khT3"], w=[sbn])
                if nk == 3:
                    op("pe", lambda e, _b=sb: e.matmul(_b[:, 0:128], id16[:], m_ups[:],
                                                       start=False, stop=False),
                       r=["id16", "m_ups"], w=[sbn])
                op("pe", lambda e, _o=(nk - 1) * 128, _b=sb: e.matmul(
                    _b[:, _o:_o + 128], id16[:], m_low[:], start=False, stop=True),
                   r=["id16", "m_low"], w=[sbn])
                if pend is not None:
                    emit_ptav(pend)
                nmv = c_nmax[h % 4]
                nmn = f"c_nmax{h % 4}"
                rdn = f"c_rd{h}"
                rqc = rq_bank[pb][:, h:h + 1]
                rqn = f"c_rq{pb}"
                op("dve", lambda e, _n=ncols, _m=nmv, _b=sb: e.tensor_reduce(
                    _m, _b[:, 0:_n], AX.X, ALU.max, negate=True),
                   r=[sbn], w=[nmn])
                op("dve", lambda e, _m=nmv, _q=rqc: e.tensor_mul(_m, _m, _q),
                   r=[nmn, rqn], w=[nmn])
                op("act", lambda e, _n=ncols, _hs=hs, _m=nmv, _h=h, _b=sb, _q=rqc:
                   e.activation(
                    pbO[_hs][:, 0:_n], _b[:, 0:_n], AF.Exp, bias=_m, scale=_q,
                    accum_out=c_rdens[:, _h:_h + 1]),
                   r=[sbn, nmn, rqn], w=[f"pbO{hs}", rdn])
                op("dve", lambda e, _h=h: e.reciprocal(
                    c_rdens[:, _h:_h + 1], c_rdens[:, _h:_h + 1]), r=[rdn], w=[rdn])
                pend = h
            emit_ptav(pend)
            for j in range(8):
                op("pe", lambda e, _j=j, _s=sl: e.transpose(
                    tbf[:, _j * 128:_j * 128 + 128],
                    obO[_s][:, _j * 128:_j * 128 + 128], id16[:]),
                   r=[f"obO{sl}", "id16"], w=["tbf"])
            op("dve", lambda e, _s=sl: e.tensor_copy(otO[_s][:], tbf[:]),
               r=["tbf"], w=[f"otO{sl}"])
            proj_1024(otO[sl], f"otO{sl}", wT_bf["woT"], "woT_bf", pP[2], pP[3], 3)
            rstd, nmr = ln_stats([(pP[2][:], 512, "pP2"), (pP[3][:], 512, "pP3")], D, pb)
            op("dve", lambda e, _x=xt, _n=nmr: e.tensor_scalar_add(_x, _x, _n),
               r=["x_sb", f"c_nmr{pb}"], w=["x_sb"])
            op("dve", lambda e, _x=xt, _r=rstd: e.scalar_tensor_tensor(
                _x[:, 0:512], pP[2][:], _r, _x[:, 0:512], ALU.mult, ALU.add),
               r=["pP2", f"c_rstd{pb}", "x_sb"], w=["x_sb"])
            op("dve", lambda e, _x=xt, _r=rstd: e.scalar_tensor_tensor(
                _x[:, 512:1024], pP[3][:], _r, _x[:, 512:1024], ALU.mult, ALU.add),
               r=["pP3", f"c_rstd{pb}", "x_sb"], w=["x_sb"])

        NTL = min(NT, NT_LIM)
        if NTL:
            P1o(0)
            for i in range(NTL):
                if i + 1 < NTL:
                    P1o(i + 1)
                P2o(i)

        op("pool", lambda e: e.dma_start(d_x1, x_sb[:]), r=["x_sb"], w=["x1d"], dma=True)
        if not (PHASE_LIMIT == "outer" and DEBUG_TAP):
            for i in range(NT):
                xt = x_sb[:, i * D:(i + 1) * D]
                pb = i % 2
                rstd, nmr = ln_stats([(xt, D, "x_sb")], D, pb)
                op("act", lambda e, _x=xt, _r=rstd, _n=nmr: e.activation(
                    _x, _x, AF.Identity, bias=_n, scale=_r),
                   r=["x_sb", f"c_rstd{pb}", f"c_nmr{pb}", "x1d"], w=["x_sb"])

        n_inner = {"outer": 0, "layer0": 1, "all": 2}[PHASE_LIMIT]
        prev_names = OUTER_T
        tA, namesA = (innerA() if n_inner else (None, None))
        tB, namesB = (innerB() if n_inner else (None, None))
        for l in range(n_inner):
            em.fence(prev_names, namesA)
            prev_names = namesA
            load_cast(tA["wqkv"], "wqkv_bf", f"wqkv{l}", D, 3 * INNER)
            load_cast(tA["woin"], "woin_bf", f"wo{l}", INNER, D)
            qsb_ap = cols[:, 128:384]
            slot = load_cast.k % 2
            load_cast.k += 1
            op("sp", lambda e, _l=l, _s=slot: e.dma_start(
                wld[_s][0:1, 0:256], dram[f"qsrows{_l}"]), w=["wld%d" % slot], dma=True)
            op("dve", lambda e: e.memset(sq_scr[0:1, 0:128], 1.0), w=["sq_scr"])
            op("pe", lambda e, _s=slot: e.matmul(
                tr32[:, 0:256], sq_scr[0:1, 0:128], wld[_s][0:1, 0:256],
                start=True, stop=True),
               r=["sq_scr", "wld%d" % slot], w=["tr32"])
            op("dve", lambda e: e.tensor_copy(qsb_ap, tr32[:, 0:256]),
               r=["tr32"], w=["qsb"])

            def shift_tile(i, dst, dst_name):
                op("pool", lambda e, _i=i, _d=dst: e.dma_start(
                    _d[1:128, :], x_sb[0:127, _i * D + 512:_i * D + 1024]),
                   r=["x_sb"], w=[dst_name], dma=True)
                if i == 0:
                    op("dve", lambda e, _d=dst: e.memset(_d[0:1, :], 0.0), w=[dst_name])
                else:
                    op("pool", lambda e, _i=i, _d=dst: e.dma_start(
                        _d[0:1, :], x_sb[127:128, (_i - 1) * D + 512:(_i - 1) * D + 1024]),
                       r=["x_sb"], w=[dst_name], dma=True)

            load_cast(tB["w2"], "w2bf", f"w2_{l}", HID, D)
            op("pool", lambda e, _l=l: e.dma_start(tB["b2hi"][0:1, :], dram[f"b2_{_l}"]),
               w=["b2hi"], dma=True)
            def l2rot(scr, scr_name, a_off, b_off, pb, rq=False):
                if rq:
                    sums8 = rq_bank[pb][:, 0:8]
                    sn = f"c_rq{pb}"
                else:
                    sums8 = sums_bank[pb][:, 0:8]
                    sn = f"c_sums{pb}"
                op("act", lambda e, _s=scr: e.activation(
                    sq_scr[:, 0:512], _s[:], AF.Square), r=[scr_name], w=["sq_scr"])
                red3 = bass.AP(sq_scr, 0, [[D, 128], [64, 8], [1, 64]])
                op("dve", lambda e, _r=red3, _u=sums8: e.tensor_reduce(_u, _r, AX.X, ALU.add),
                   r=["sq_scr"], w=[sn])
                op("act", lambda e, _u=sums8: e.activation(_u, _u, AF.Sqrt),
                   r=[sn], w=[sn])
                op("dve", lambda e, _u=sums8: e.tensor_scalar_max(_u, _u, 1.0e-12),
                   r=[sn], w=[sn])
                op("dve", lambda e, _u=sums8: e.reciprocal(_u, _u), r=[sn], w=[sn])
                if not rq:
                    for h in range(8):
                        op("dve", lambda e, _s=scr, _h=h, _u=sums8: e.tensor_scalar_mul(
                            _s[:, _h * 64:_h * 64 + 64], _s[:, _h * 64:_h * 64 + 64],
                            _u[:, _h:_h + 1]), r=[scr_name, sn], w=[scr_name])
                vh = lambda t, off: bass.AP(t, off, [[t.ap().ap[0][0], 128], [64, 8], [1, 32]])
                a_bc = bass.AP(tA["abt"], a_off, [[4 * HD, 128], [0, 8], [1, 64]])
                b_bc = bass.AP(tA["abt"], b_off, [[4 * HD, 128], [0, 8], [1, 64]])
                op("dve", lambda e, _s=scr: e.tensor_scalar_mul(
                    vh(tA["rh"], 0), vh(_s, 32), -1.0), r=[scr_name], w=["rhI"])
                op("dve", lambda e, _s=scr: e.tensor_copy(vh(tA["rh"], 32), vh(_s, 0)),
                   r=[scr_name], w=["rhI"])
                op("dve", lambda e, _s=scr, _a=a_bc: e.tensor_mul(_s[:], _s[:], _a),
                   r=[scr_name, "abt"], w=[scr_name])
                op("dve", lambda e, _b=b_bc: e.tensor_mul(
                    tA["rh"][:], tA["rh"][:], _b), r=["rhI", "abt"], w=["rhI"])
                op("dve", lambda e, _s=scr: e.tensor_add(_s[:], _s[:], tA["rh"][:]),
                   r=[scr_name, "rhI"], w=[scr_name])

            def P1i(i):
                sl = i % 2
                k6 = i % 6
                pb = i % 2
                xt = x_sb[:, i * D:(i + 1) * D]
                shift_tile(i, tA["h2"][sl], "h2A_%d" % sl)
                op("sp", lambda e, _i=i, _s=sl: e.dma_start(
                    tA["rb"][_s][:], dram["rotpk"][:, _i * 256:(_i + 1) * 256]),
                   w=["rotb_%d" % sl], dma=True)
                op("dve", lambda e, _s=sl: e.tensor_mul(
                    tA["abt"][:], tA["rb"][_s][:], qsb_ap),
                   r=["rotb_%d" % sl, "qsb"], w=["abt"])

                def ev_lt(g0, glen, _sl=sl):
                    if g0 == 0:
                        op("dve", lambda e, _n=glen * 128: e.tensor_copy(
                            tA["lt"][_sl][:, 0:_n], tr32[:, 0:_n]),
                           r=["tr32"], w=["ltI_%d" % _sl])
                    else:
                        op("act", lambda e, _g0=g0, _n=glen * 128: e.activation(
                            tA["lt"][_sl][:, _g0 * 128:_g0 * 128 + _n], tr32[:, 0:_n],
                            AF.Identity),
                           r=["tr32"], w=["ltI_%d" % _sl])
                tr_f32([(xt[:, j * 128:j * 128 + 128], "x_sb") for j in range(4)] +
                       [(tA["h2"][sl][:, j * 128:j * 128 + 128], "h2A_%d" % sl)
                        for j in range(4)], ev_lt)
                for n in range(3):
                    ps = pP[n]
                    psn = PN[id(ps)]
                    for j in range(8):
                        op("pe", lambda e, _n=n, _j=j, _p=ps, _s=sl: e.matmul(
                            _p[:, 0:512], tA["lt"][_s][:, _j * 128:_j * 128 + 128],
                            tA["wqkv"][:, _j * 1536 + _n * 512:_j * 1536 + _n * 512 + 512],
                            start=(_j == 0), stop=(_j == 7)),
                           r=["ltI_%d" % sl, "wqkv_bf"], w=[psn])
                op("dve", lambda e: e.tensor_copy(tA["q_scr"][:], pP[0][:]),
                   r=["pP0"], w=["q_scrI"])
                op("act", lambda e: e.activation(tA["k_scr"][:], pP[1][:], AF.Identity),
                   r=["pP1"], w=["k_scrI"])
                op("act", lambda e, _k=k6: e.activation(
                    tA["v5"][:, _k * INNER:(_k + 1) * INNER], pP[2][:], AF.Identity),
                   r=["pP2"], w=["v5"])
                l2rot(tA["q_scr"], "q_scrI", 0, HD, pb, rq=True)
                l2rot(tA["k_scr"], "k_scrI", 2 * HD, 3 * HD, pb)
                for gi in range(4):
                    op("pe", lambda e, _g=gi: e.transpose(
                        tr32[:, _g * 128:_g * 128 + 128],
                        tA["q_scr"][:, _g * 128:_g * 128 + 128], id32[:]),
                       r=["q_scrI", "id32"], w=["tr32"])
                op("act", lambda e, _s=sl: e.activation(
                    tA["qhT"][_s][:], tr32[:], AF.Identity, scale=c_eight),
                   r=["tr32", "c_eight"], w=["qhTI_%d" % sl])
                for gi in range(4):
                    op("pe", lambda e, _g=gi: e.transpose(
                        tr32[:, _g * 128:_g * 128 + 128],
                        tA["k_scr"][:, _g * 128:_g * 128 + 128], id32[:]),
                       r=["k_scrI", "id32"], w=["tr32"])
                op("act", lambda e, _k=k6: e.activation(
                    tA["khT5"][:, _k * INNER:(_k + 1) * INNER], tr32[:], AF.Identity),
                   r=["tr32"], w=["khT5"])

            def P2i(i):
                sl = i % 2
                xt = x_sb[:, i * D:(i + 1) * D]
                kstart = max(0, i - 4)
                nk = i - kstart + 1
                first_av = [True]

                def emit_ptav(h):
                    hs = h % 2
                    for c in range(nk):
                        op("pe", lambda e, _c=c, _hs=hs: e.transpose(
                            tbf[:, _c * 128:_c * 128 + 128],
                            tA["pb"][_hs][:, _c * 128:_c * 128 + 128], id16[:]),
                           r=["pbI_%d" % hs, "id16"], w=["tbf"])
                    op("act", lambda e, _n=nk * 128, _hs=hs: e.activation(
                        tA["pt"][_hs][:, 0:_n], tbf[:, 0:_n], AF.Identity),
                       r=["tbf"], w=["ptI_%d" % hs])
                    for c in range(nk):
                        slot = (kstart + c) % 6
                        st_ = first_av[0]
                        first_av[0] = False
                        sp_ = (h == HEADS_I - 1) and (c == nk - 1)
                        op("pe", lambda e, _c=c, _hs=hs, _st=st_, _sp=sp_, _slot=slot, _h=h:
                           e.matmul(pP[3][:, _h * 64:_h * 64 + 64],
                                    tA["pt"][_hs][:, _c * 128:_c * 128 + 128],
                                    tA["v5"][:, _slot * INNER + _h * 64:_slot * INNER + _h * 64 + 64],
                                    start=_st, stop=_sp),
                           r=["ptI_%d" % hs, "v5"], w=["pP3"])

                pend = None
                def emit_ptav_i(h):
                    hs = h % 2
                    for c in range(nk):
                        op("pe", lambda e, _c=c, _hs=hs: e.transpose(
                            tbf[:, _c * 128:_c * 128 + 128],
                            tA["pb"][_hs][:, _c * 128:_c * 128 + 128], id16[:]),
                           r=["pbI_%d" % hs, "id16"], w=["tbf"])
                    op("act", lambda e, _n=nk * 128, _hs=hs: e.activation(
                        tA["pt"][_hs][:, 0:_n], tbf[:, 0:_n], AF.Identity),
                       r=["tbf"], w=["ptI_%d" % hs])
                    for c in range(nk):
                        slot = (kstart + c) % 6
                        st_ = first_av[0]
                        first_av[0] = False
                        sp_ = (h == HEADS_I - 1) and (c == nk - 1)
                        op("pe", lambda e, _c=c, _hs=hs, _st=st_, _sp=sp_, _slot=slot, _h=h:
                           e.matmul(pP[3][:, _h * 64:_h * 64 + 64],
                                    tA["pt"][_hs][:, _c * 128:_c * 128 + 128],
                                    tA["v5"][:, _slot * INNER + _h * 64:_slot * INNER + _h * 64 + 64],
                                    start=_st, stop=_sp),
                           r=["ptI_%d" % hs, "v5"], w=["pP3"])

                pend = None
                for h in range(HEADS_I):
                    hp = (h % 2) * 64
                    hc = (h // 2) * 128
                    hs = h % 2
                    sb = simP if hs == 0 else pP[1]
                    sbn = "simP" if hs == 0 else "pP1"
                    xb = simX if hs == 0 else pP[2]
                    xbn = "simX" if hs == 0 else "pP2"
                    nP = min(nk, 4)
                    for c in range(nP):
                        slot = (kstart + c) % 6
                        op("pe", lambda e, _o=c * 128, _s=slot, _hp=hp, _hc=hc,
                           _sl=sl, _st=(c == 0), _b=sb: e.matmul(
                            _b[:, _o:_o + 128],
                            tA["qhT"][_sl][_hp:_hp + 64, _hc:_hc + 128],
                            tA["khT5"][_hp:_hp + 64, _s * INNER + _hc:_s * INNER + _hc + 128],
                            start=_st, stop=False),
                           r=["qhTI_%d" % sl, "khT5"], w=[sbn])
                    if nk == 5:
                        op("pe", lambda e, _b=sb: e.matmul(_b[:, 0:128], id16[:], m_upi[:],
                                                           start=False, stop=True),
                           r=["id16", "m_upi"], w=[sbn])
                        slot = i % 6
                        op("pe", lambda e, _s=slot, _hp=hp, _hc=hc, _sl=sl, _x=xb: e.matmul(
                            _x[:, 0:128],
                            tA["qhT"][_sl][_hp:_hp + 64, _hc:_hc + 128],
                            tA["khT5"][_hp:_hp + 64, _s * INNER + _hc:_s * INNER + _hc + 128],
                            start=True, stop=False),
                           r=["qhTI_%d" % sl, "khT5"], w=[xbn])
                        op("pe", lambda e, _x=xb: e.matmul(_x[:, 0:128], id16[:], m_low[:],
                                                           start=False, stop=True),
                           r=["id16", "m_low"], w=[xbn])
                    else:
                        op("pe", lambda e, _o=(nk - 1) * 128, _b=sb: e.matmul(
                            _b[:, _o:_o + 128], id16[:], m_low[:], start=False, stop=True),
                           r=["id16", "m_low"], w=[sbn])
                    if pend is not None:
                        emit_ptav_i(pend)
                    nPc = min(nk, 4) * 128
                    nmv = c_nmax[h % 2]
                    nmn = f"c_nmax{h % 2}"
                    rdn = f"c_rd{h}"
                    rdc = c_rdens[:, h:h + 1]
                    pbp = i % 2
                    rqc = rq_bank[pbp][:, h:h + 1]
                    rqn = f"c_rq{pbp}"
                    op("dve", lambda e, _n=nPc, _m=nmv, _b=sb: e.tensor_reduce(
                        _m, _b[:, 0:_n], AX.X, ALU.max, negate=True),
                       r=[sbn], w=[nmn])
                    if nk == 5:
                        xv = c_nmaxx[h % 2]
                        xn = f"c_nmaxx{h % 2}"
                        op("dve", lambda e, _xv=xv, _x=xb: e.tensor_reduce(
                            _xv, _x[:, 0:128], AX.X, ALU.max, negate=True), r=[xbn], w=[xn])
                        op("dve", lambda e, _m=nmv, _x=xv: e.tensor_tensor(
                            _m, _m, _x, ALU.min), r=[nmn, xn], w=[nmn])
                    op("dve", lambda e, _m=nmv, _q=rqc: e.tensor_mul(_m, _m, _q),
                       r=[nmn, rqn], w=[nmn])
                    op("act", lambda e, _n=nPc, _hs=hs, _m=nmv, _c=rdc, _b=sb, _q=rqc:
                       e.activation(
                        tA["pb"][_hs][:, 0:_n], _b[:, 0:_n], AF.Exp, bias=_m,
                        scale=_q, accum_out=_c),
                       r=[sbn, nmn, rqn], w=["pbI_%d" % hs, rdn])
                    if nk == 5:
                        dxv = c_denx[h % 2]
                        dxn = f"c_denx{h % 2}"
                        op("act", lambda e, _hs=hs, _m=nmv, _xv=dxv, _x=xb, _q=rqc:
                           e.activation(
                            tA["pb"][_hs][:, 512:640], _x[:, 0:128], AF.Exp, bias=_m,
                            scale=_q, accum_out=_xv),
                           r=[xbn, nmn, rqn], w=["pbI_%d" % hs, dxn])
                        op("dve", lambda e, _c=rdc, _x=dxv: e.tensor_add(_c, _c, _x),
                           r=[rdn, dxn], w=[rdn])
                    op("dve", lambda e, _c=rdc: e.reciprocal(_c, _c), r=[rdn], w=[rdn])
                    pend = h
                emit_ptav_i(pend)
                for h in range(HEADS_I):
                    op("dve", lambda e, _h=h, _s=sl: e.tensor_scalar_mul(
                        tA["ob"][_s][:, _h * 64:_h * 64 + 64], pP[3][:, _h * 64:_h * 64 + 64],
                        c_rdens[:, _h:_h + 1]),
                       r=["pP3", f"c_rd{h}"], w=["obI_%d" % sl])
                for c in range(4):
                    op("pe", lambda e, _c=c, _s=sl: e.transpose(
                        tbf[:, _c * 128:_c * 128 + 128],
                        tA["ob"][_s][:, _c * 128:_c * 128 + 128], id16[:]),
                       r=["obI_%d" % sl, "id16"], w=["tbf"])
                op("dve", lambda e, _s=sl: e.tensor_copy(tA["ot"][_s][:], tbf[:, 0:512]),
                   r=["tbf"], w=["otI_%d" % sl])
                for n in range(2):
                    ps = pP[n]
                    psn = PN[id(ps)]
                    for j in range(4):
                        op("pe", lambda e, _n=n, _j=j, _p=ps, _s=sl: e.matmul(
                            _p[:, 0:512], tA["ot"][_s][:, _j * 128:_j * 128 + 128],
                            tA["woin"][:, _j * D + _n * 512:_j * D + _n * 512 + 512],
                            start=(_j == 0), stop=(_j == 3)),
                           r=["otI_%d" % sl, "woin_bf"], w=[psn])
                op("dve", lambda e, _x=xt: e.tensor_add(_x[:, 0:512], _x[:, 0:512], pP[0][:]),
                   r=["pP0", "x_sb"], w=["x_sb"])
                op("dve", lambda e, _x=xt: e.tensor_add(
                    _x[:, 512:1024], _x[:, 512:1024], pP[1][:]),
                   r=["pP1", "x_sb"], w=["x_sb"])

            P1i(0)
            for i in range(NT):
                if i + 1 < NT:
                    P1i(i + 1)
                P2i(i)

            em.fence(prev_names, namesB)
            prev_names = namesB
            w1_off = []
            pos = 0
            for m in range(NHT):
                aw = min(128, HID - m * 128)
                w1_off.append((pos, aw))
                w1_off.append((pos + aw, aw))
                pos += 2 * aw

            def LN2part(tc):
                for ii in range(4):
                    i = tc * 4 + ii
                    sl = i % 2
                    xt = x_sb[:, i * D:(i + 1) * D]
                    shift_tile(i, tB["h2"][sl], "h2B_%d" % sl)
                    pb = i % 2
                    rstd, nmr = ln_stats([(xt[:, 0:512], 512, "x_sb"),
                              (tB["h2"][sl][:], 512, "h2B_%d" % sl)], D, pb)
                    op("act", lambda e, _x=xt, _r=rstd, _n=nmr: e.activation(
                        ln_scr[:, 0:512], _x[:, 0:512], AF.Identity, bias=_n, scale=_r),
                       r=["x_sb", f"c_rstd{pb}", f"c_nmr{pb}"], w=["ln_scr"])
                    op("act", lambda e, _s=sl, _r=rstd, _n=nmr: e.activation(
                        ln_scr[:, 512:1024], tB["h2"][_s][:], AF.Identity,
                        bias=_n, scale=_r),
                       r=["h2B_%d" % sl, f"c_rstd{pb}", f"c_nmr{pb}"], w=["ln_scr"])
                    for g0 in (0, 4):
                        for gi in range(4):
                            j = g0 + gi
                            op("pe", lambda e, _j=j, _g=gi: e.transpose(
                                tr32[:, _g * 128:_g * 128 + 128],
                                ln_scr[:, _j * 128:_j * 128 + 128], id32[:]),
                               r=["ln_scr", "id32"], w=["tr32"])
                        for gi in range(4):
                            j = g0 + gi
                            op("act", lambda e, _j=j, _g=gi, _ii=ii, _l=l: e.activation(
                                tB["ln2T"][:, _j * 512 + _ii * 128:_j * 512 + _ii * 128 + 128],
                                tr32[:, _g * 128:_g * 128 + 128], AF.Identity,
                                bias=lnb[_l][:, _j:_j + 1], scale=lng[_l][:, _j:_j + 1]),
                               r=["tr32", f"lng{l}", f"lnb{l}"], w=["ln2T"])
            LN2part(0)
            for tc in range(4):
                lane_w = [128, 128, 128, 128]
                for ot in range(2 * NHT):
                    grp = ot // 4
                    lane = ot % 4
                    wb = grp % 2
                    if grp == (2 * NHT - 1) // 4:
                        lane_w = [w1_off[grp * 4 + k][1] if grp * 4 + k < 2 * NHT else 0
                                  for k in range(4)]
                    lane_off = sum(lane_w[:lane])
                    gw = sum(lane_w)
                    if lane == 0:
                        C0 = w1_off[ot][0]
                        op("pool", lambda e, _l=l, _c=C0, _b=wb, _g=gw: e.dma_start(
                            bass.AP(tB["w1"][_b], 0, [[4096, 128], [512, 8], [1, _g]]),
                            bass.AP(dram[f"w1p{_l}"].tensor, _c,
                                    [[2 * HID, 128], [2 * HID * 128, 8], [1, _g]])),
                           w=["w1bf_%d" % wb], dma=True)
                    col0, ow = w1_off[ot]
                    pa = pP[2 + (ot % 2)]
                    pan = PN[id(pa)]
                    for j in range(8):
                        op("pe", lambda e, _j=j, _p=pa, _b=wb, _w=ow, _lo=lane_off: e.matmul(
                            _p[0:_w, 0:512],
                            tB["w1"][_b][:, _j * 512 + _lo:_j * 512 + _lo + _w],
                            tB["ln2T"][:, _j * 512:_j * 512 + 512],
                            start=(_j == 0), stop=(_j == 7)),
                           r=["w1bf_%d" % wb, "ln2T"], w=[pan])
                    ht = ot // 2
                    if ot % 2 == 0:
                        op("act", lambda e, _p=pa, _w=ow, _h=ht, _l=l: e.activation(
                            tB["ag"][0:_w, 0:512], _p[0:_w, 0:512], AF.Identity,
                            bias=b1a[_l][0:_w, _h:_h + 1]),
                           r=[pan, f"b1a{l}"], w=["ag_a"])
                    else:
                        op("act", lambda e, _p=pa, _w=ow, _h=ht, _l=l: e.activation(
                            tB["ag"][0:_w, 1024:1536], _p[0:_w, 0:512], AF.Gelu,
                            bias=b1g[_l][0:_w, _h:_h + 1]),
                           r=[pan, f"b1g{l}"], w=["ag_g"])
                        op("dve", lambda e, _w=ow, _h=ht: e.tensor_mul(
                            tB["gg"][0:_w, _h * 512:_h * 512 + 512],
                            tB["ag"][0:_w, 0:512], tB["ag"][0:_w, 1024:1536]),
                           r=["ag_a", "ag_g"], w=["geglu"])
                if tc + 1 < 4:
                    LN2part(tc + 1)
                for ii in range(4):
                    i = tc * 4 + ii
                    xt = x_sb[:, i * D:(i + 1) * D]
                    for n in range(2):
                        ps = pP[2 * (ii % 2) + n]
                        psn = PN[id(ps)]
                        for kk in range(NHT):
                            kw = 128 if kk < NHT - 1 else HID_LAST
                            op("pe", lambda e, _n=n, _k=kk, _p=ps, _ii=ii, _kw=kw: e.matmul(
                                _p[:, 0:512],
                                tB["gg"][0:_kw, _k * 512 + _ii * 128:_k * 512 + _ii * 128 + 128],
                                tB["w2"][0:_kw, _k * D + _n * 512:_k * D + _n * 512 + 512],
                                start=(_k == 0), stop=False),
                               r=["geglu", "w2bf"], w=[psn])
                        op("pe", lambda e, _n=n, _p=ps, _l=l: e.matmul(
                            _p[:, 0:512], ones_bf[:],
                            tB["b2hi"][0:1, _n * 512:_n * 512 + 512],
                            start=False, stop=True),
                           r=["ones_bf", "b2hi"], w=[psn])
                    b0 = 2 * (ii % 2)
                    op("dve", lambda e, _x=xt, _b=b0: e.tensor_add(
                        _x[:, 0:512], _x[:, 0:512], pP[_b][:]),
                       r=[PN[id(pP[b0])], "x_sb"], w=["x_sb"])
                    op("dve", lambda e, _x=xt, _b=b0: e.tensor_add(
                        _x[:, 512:1024], _x[:, 512:1024], pP[_b + 1][:]),
                       r=[PN[id(pP[b0 + 1])], "x_sb"], w=["x_sb"])

        if PHASE_LIMIT == "all" and DEBUG_TAP is None:
            em.fence(prev_names, FINAL_T)
            for i in range(NT):
                sl = i % 2
                xt = x_sb[:, i * D:(i + 1) * D]
                op("sp", lambda e, _i=i, _s=sl: e.dma_start(
                    x1b[_s][:], d_x1[:, _i * D:(_i + 1) * D]),
                   r=["x1d"], w=[f"x1b{sl}"], dma=True)
                pb = i % 2
                rstd, nmr = ln_stats([(xt, D, "x_sb")], D, pb)
                op("dve", lambda e, _s=sl, _n=nmr: e.tensor_scalar_add(
                    x1b[_s][:], x1b[_s][:], _n),
                   r=[f"x1b{sl}", f"c_nmr{pb}"], w=[f"x1b{sl}"])
                op("dve", lambda e, _x=xt, _s=sl, _r=rstd: e.scalar_tensor_tensor(
                    _x, _x, _r, x1b[_s][:], ALU.mult, ALU.add),
                   r=[f"x1b{sl}", f"c_rstd{pb}", "x_sb"], w=["x_sb"])
        tok = op("sp", lambda e: e.dma_start(
            d_out.rearrange("(n p) d -> p n d", p=128), x_sb3),
            r=["x_sb"], w=["out_dram"], dma=True)
        em.final_wait("sp", tok)

        with nc.Block() as block:
            em.replay(block)
    return nc


def _prep_host(inputs):
    f = np.float32
    low_i, up_i, up_s = _masks()
    common = {
        "wqT": np.ascontiguousarray(inputs["L0_attn_wq_w"].T),
        "wkT": np.ascontiguousarray(inputs["L0_attn_wk_w"].T),
        "wvT": np.ascontiguousarray(inputs["L0_attn_wv_w"].T),
        "woT": np.ascontiguousarray(inputs["L0_attn_wo_w"].T),
        "brow4": np.concatenate([
            inputs["L0_attn_wq_b"], inputs["L0_attn_wk_b"],
            inputs["L0_attn_wv_b"], inputs["L0_attn_wo_b"]]).reshape(1, 4 * D),
        "qn2": np.tile(inputs["L0_attn_qn_w"], 2).reshape(128, 1),
        "kn2": np.tile(inputs["L0_attn_kn_w"], 2).reshape(128, 1),
        "rotpk": _host_tables(),
        "m_low": low_i, "m_upi": up_i, "m_ups": up_s,
        "id32": np.eye(128, dtype=f),
        "id16": np.eye(128, dtype=ml_dtypes.bfloat16),
    }
    for l in range(2):
        w1 = inputs["L0_ffn_w1"][l]
        b1 = inputs["L0_ffn_b1"][l]
        perm = []
        for m in range(NHT):
            aw = min(128, HID - m * 128)
            perm.extend(range(m * 128, m * 128 + aw))
            perm.extend(range(HID + m * 128, HID + m * 128 + aw))
        perm = np.asarray(perm)
        w1p = np.ascontiguousarray(w1[:, perm])
        b1p = b1[perm]
        b1av = np.zeros((128, NHT), f)
        b1gv = np.zeros((128, NHT), f)
        pos = 0
        for m in range(NHT):
            aw = min(128, HID - m * 128)
            b1av[0:aw, m] = b1p[pos:pos + aw]
            b1gv[0:aw, m] = b1p[pos + aw:pos + 2 * aw]
            pos += 2 * aw
        qs = inputs["L0_ffn_q_scale"][l]
        ks = inputs["L0_ffn_k_scale"][l]
        common.update({
            f"wqkv{l}": np.ascontiguousarray(inputs["L0_ffn_wqkv"][l]),
            f"wo{l}": np.ascontiguousarray(inputs["L0_ffn_wo"][l]),
            f"w1p{l}": w1p,
            f"b1a{l}": b1av, f"b1g{l}": b1gv,
            f"w2_{l}": np.ascontiguousarray(inputs["L0_ffn_w2"][l]),
            f"b2_{l}": inputs["L0_ffn_b2"][l].reshape(1, D),
            f"lng{l}": np.ascontiguousarray(inputs["L0_ffn_ln_g"][l].reshape(8, 128).T),
            f"lnb{l}": np.ascontiguousarray(inputs["L0_ffn_ln_b"][l].reshape(8, 128).T),
            f"qsrows{l}": np.concatenate([
                qs, np.concatenate([qs[32:], qs[:32]]),
                ks, np.concatenate([ks[32:], ks[:32]])]).reshape(1, 4 * HD),
        })
    return common


def kernel(**inputs):
    inputs = {k: np.asarray(v) for k, v in inputs.items()}
    common = _prep_host(inputs)
    nc = build()
    in_maps = []
    for c in range(8):
        m = dict(common)
        m["x"] = np.ascontiguousarray(inputs["L0_attn_x"][c])
        in_maps.append(m)
    res = run_bass_kernel_spmd(nc, in_maps, core_ids=list(range(8)))
    return np.stack([res.results[c]["out"] for c in range(8)], axis=0)
